# version 01
# baseline (speedup 1.0000x reference)
import math
from contextlib import ExitStack

import numpy as np
import ml_dtypes
import concourse.bass as bass
import concourse.mybir as mybir
from concourse.bass_utils import run_bass_kernel_spmd

F32 = mybir.dt.float32
BF16 = mybir.dt.bfloat16
AF = mybir.ActivationFunctionType
ALU = mybir.AluOpType
AXX = mybir.AxisListType.X

T = 1024
D = 2048
FF = 8192
S = 4096

CVALS = [0.0, 1.0, -1.0, 0.5, 2.0, -20.0, 1e-5,
         float(np.float32(0.5) * np.float32(1.0 / np.sqrt(8192.0))),
         -1e9]


ENGINES = ("tensor", "vector", "scalar", "gpsimd", "sync")

L1_EVENKV = [h for h in range(32) if (h // 4) % 2 == 0]
L1_ODDKV = [h for h in range(32) if (h // 4) % 2 == 1]


def l1_qpos(h):
    if (h // 4) % 2 == 0:
        return L1_EVENKV.index(h), 0
    return L1_ODDKV.index(h), 64


class Ctx:

    def __init__(self, nc):
        self.nc = nc
        self.es = ExitStack()
        self.eng = dict(tensor=nc.tensor, vector=nc.vector, scalar=nc.scalar,
                        gpsimd=nc.gpsimd, sync=nc.sync)
        self.sem = {e: self.es.enter_context(nc.semaphore(f"sem_{e}")) for e in ENGINES}
        self.cnt = {e: 0 for e in ENGINES}
        self.waited = {}
        self.res = {}

    def sb(self, name, shape, dtype):
        return self.es.enter_context(self.nc.sbuf_tensor(name, shape, dtype))

    def ps(self, name, shape, dtype):
        return self.es.enter_context(self.nc.psum_tensor(name, shape, dtype))

    def _dma_semkey(self, key):
        sk = f"d:{key}"
        if sk not in self.sem:
            self.sem[sk] = self.es.enter_context(self.nc.semaphore(sk))
            self.cnt[sk] = 0
        return sk

    def _need(self, eng, key, write, need):
        r = self.res.get(key)
        if r is None:
            return
        w = r.get("w")
        if w is not None:
            wsem, wval = w
            if not (wsem == "tensor" and eng == "tensor"):
                need[wsem] = max(need.get(wsem, 0), wval)
        if write:
            for rsem, rval in r.get("r", {}).items():
                if rsem == "tensor" and eng == "tensor":
                    continue
                need[rsem] = max(need.get(rsem, 0), rval)

    def _record(self, semkey, key, write, val):
        r = self.res.setdefault(key, {"w": None, "r": {}})
        if write:
            r["w"] = (semkey, val)
            r["r"] = {}
        else:
            r["r"][semkey] = max(r["r"].get(semkey, 0), val)

    def _emit_waits(self, eng, need):
        e = self.eng[eng]
        for tgt, val in sorted(need.items()):
            if val <= 0 or self.waited.get((eng, tgt), 0) >= val:
                continue
            if val > self.cnt[tgt]:
                raise RuntimeError(f"{eng} waits {tgt}>={val} but cnt={self.cnt[tgt]}")
            e.wait_ge(self.sem[tgt], val)
            self.waited[(eng, tgt)] = val

    def op(self, eng, method, *args, reads=(), writes=(), inc=True, **kw):
        need = {}
        for k in reads:
            self._need(eng, k, False, need)
        for k in writes:
            self._need(eng, k, True, need)
        self._emit_waits(eng, need)
        ins = getattr(self.eng[eng], method)(*args, **kw)
        is_dma = method in ("dma_start", "collective_compute", "dma_start_transpose")
        if is_dma:
            assert writes, f"dma op needs a write resource ({method})"
            semkey = self._dma_semkey(writes[0])
            ins.then_inc(self.sem[semkey], 16)
            self.cnt[semkey] += 16
            val = self.cnt[semkey]
        elif inc:
            semkey = eng
            ins.then_inc(self.sem[eng], 1)
            self.cnt[eng] += 1
            val = self.cnt[eng]
        else:
            semkey = eng
            val = self.cnt[eng] + 1
        for k in reads:
            self._record(semkey, k, False, val)
        for k in writes:
            self._record(semkey, k, True, val)
        return ins

    def barrier(self):
        for a in ENGINES:
            for b, cnt in self.cnt.items():
                if a == b or cnt == 0:
                    continue
                if self.waited.get((a, b), 0) < cnt:
                    self.eng[a].wait_ge(self.sem[b], cnt)
                    self.waited[(a, b)] = cnt
        self.res.clear()



def _stat_tiles(wT, kt=128, ct=128):
    K, M = wT.shape
    a = wT.reshape(K // kt, kt, M // ct, ct).transpose(2, 1, 0, 3)
    return np.ascontiguousarray(a.reshape(M // ct * kt, (K // kt) * ct))


def _stat_flat(wT, kt=128, ct=128):
    K, M = wT.shape
    a = wT.reshape(K // kt, kt, M // ct, ct).transpose(1, 2, 0, 3)
    return np.ascontiguousarray(a.reshape(kt, (M // ct) * (K // kt) * ct))


def prep_weights(w):
    o = {}
    o["l0_W1Tt"] = _stat_tiles(w["L0_ffn_W1"].T)
    o["l0_W2T"] = np.ascontiguousarray(w["L0_ffn_W2"].T)
    o["l0_WrTt"] = _stat_tiles(w["L0_ffn_W_r"].T, ct=8)
    o["l0_br"] = np.ascontiguousarray(w["L0_ffn_b_r"].reshape(1, 8))
    o["l0_A1Tt"] = _stat_tiles(w["L0_ffn_A1"].reshape(64, D).T, ct=64)
    B1rf = w["L0_ffn_B1"].transpose(0, 2, 1).reshape(64, FF)
    o["l0_B1aug"] = np.ascontiguousarray(
        np.concatenate([B1rf, w["L0_ffn_b1"].reshape(1, FF)], 0))
    o["l0_A2Tt"] = _stat_tiles(w["L0_ffn_A2"].reshape(64, FF).T, ct=64)
    B2rf = w["L0_ffn_B2"].transpose(0, 2, 1).reshape(64, D)
    o["l0_B2aug"] = np.ascontiguousarray(
        np.concatenate([B2rf, w["L0_ffn_b2"].reshape(1, D)], 0))
    H0, QKD, NOPE, VD0, QR, KVR = 16, 128, 64, 128, 1536, 512
    o["l0_qaTt"] = _stat_tiles(w["L0_attn_q_a_w"].T)
    qb = w["L0_attn_q_b_w"].reshape(H0, QKD, QR)
    perm = (list(range(NOPE)) + [NOPE + 2 * i for i in range(32)]
            + [NOPE + 2 * i + 1 for i in range(32)])
    o["l0_qbTt"] = _stat_tiles(
        np.ascontiguousarray(qb[:, perm, :].reshape(H0 * QKD, QR).T))
    kva = w["L0_attn_kv_a_w"]
    rot = kva[KVR:]
    kva_p = np.concatenate([kva[:KVR], rot[0::2], rot[1::2]], 0)
    kvaT = np.ascontiguousarray(kva_p.T)
    o["l0_kvaTaf"] = _stat_flat(kvaT[:, 0:KVR])
    o["l0_kvaTbf"] = _stat_flat(kvaT[:, KVR:], ct=64)
    kvb = w["L0_attn_kv_b_w"].reshape(H0, NOPE + VD0, KVR)
    o["l0_WkbTf"] = _stat_flat(
        np.ascontiguousarray(kvb[:, :NOPE, :].reshape(H0 * NOPE, KVR).T), ct=64)
    o["l0_WvbTf"] = _stat_flat(
        np.ascontiguousarray(kvb[:, NOPE:, :].reshape(H0 * VD0, KVR).T), ct=2048)
    o["l0_owT"] = np.ascontiguousarray(w["L0_attn_o_w"].T)
    order = []
    for j in range(16):
        order += [L1_EVENKV[j], L1_ODDKV[j]]
    rowperm = np.concatenate([np.arange(h * 64, (h + 1) * 64) for h in order])
    o["l1_WqTt"] = _stat_tiles(np.ascontiguousarray(w["L1_attn_Wq"][rowperm].T))
    o["l1_WkTf"] = _stat_flat(w["L1_attn_Wk"].T)
    o["l1_WvTf"] = _stat_flat(w["L1_attn_Wv"].T, ct=512)
    o["l1_WoT"] = np.ascontiguousarray(w["L1_attn_Wo"].T[rowperm])
    o["l1_W1Tt"] = _stat_tiles(w["L1_ffn_W1"].T)
    o["l1_W3Tt"] = _stat_tiles(w["L1_ffn_W3"].T)
    o["l1_W2G"] = np.ascontiguousarray(w["L1_ffn_W2"].T.reshape(64, 128 * D))
    return o


def host_consts(core_id):
    c = {}
    c["CONSTS"] = np.tile(np.array(CVALS, np.float32), (128, 1))
    c["ID128"] = np.eye(128, dtype=np.float32)
    c["ONESROW"] = np.ones((1, 128), np.float32)
    c["ONESROWF"] = np.ones((1, 128), np.float32)
    c["ONES128"] = np.ones((128, 1), np.float32)
    e8 = np.zeros((8, 64), np.float32)
    for e in range(8):
        e8[e, e * 8:(e + 1) * 8] = 2.0
    c["E8X2"] = e8
    c["EPSIDX"] = (np.arange(8, dtype=np.float32) * np.float32(-1e-6)).reshape(1, 8)

    def had(n):
        h = np.array([[1.0]], np.float32)
        while h.shape[0] < n:
            h = np.block([[h, h], [h, -h]])
        return np.ascontiguousarray(h)
    c["H128"] = had(128)
    c["H64"] = had(64)
    q0 = (core_id % 4) * T + np.arange(T, dtype=np.float64)
    kpos = np.arange(S, dtype=np.float64)
    th = 1.0 / (10000.0 ** (np.arange(0, 64, 2, dtype=np.float64) / 64.0))
    for nm, pos, scl, rep in (("L0Q", q0, 1.0, 1), ("L0K", kpos, 1.0, 1),
                              ("L1Q", q0, 0.125, 2), ("L1K", kpos, 1.0, 2)):
        ang = (pos[None, :] * th[:, None]).astype(np.float32)
        cs = np.cos(ang).astype(np.float32) * np.float32(scl)
        sn = np.sin(ang).astype(np.float32) * np.float32(scl)
        blkC = np.concatenate([cs, cs], 0)
        blkS = np.concatenate([-sn, sn], 0)
        c[f"{nm}_C"] = np.ascontiguousarray(np.concatenate([blkC] * rep, 0))
        c[f"{nm}_S"] = np.ascontiguousarray(np.concatenate([blkS] * rep, 0))
    return c



class Prog:
    def __init__(self, nc):
        self.nc = nc
        self.ctx = Ctx(nc)
        self.inp = {}

    def dram_in(self, name, shape, dtype=F32):
        if name in self.inp:
            return self.inp[name]
        t = self.nc.dram_tensor(name, list(shape), dtype, kind="ExternalInput")
        self.inp[name] = t.ap()
        return self.inp[name]

    def dram_out(self, name, shape, dtype=F32):
        t = self.nc.dram_tensor(name, list(shape), dtype, kind="ExternalOutput")
        return t.ap()

    def dram_int(self, name, shape, dtype):
        return self.nc.dram_tensor(name, list(shape), dtype, kind="Internal")

    def setup_common(self):
        ctx = self.ctx
        self.pb = [ctx.ps(f"pb{i}", [128, 512], F32) for i in range(8)]
        self.c_consts = ctx.sb("c_consts", [128, len(CVALS)], F32)
        for j, v in enumerate(CVALS):
            self.nc.const_aps.register(F32, v, self.c_consts[:, j:j + 1])
        self.CI = {v: j for j, v in enumerate(CVALS)}
        self.c_id = ctx.sb("c_id", [128, 128], F32)
        self.c_onesrow = ctx.sb("c_onesrow", [1, 128], BF16)
        self.c_onesrowf = ctx.sb("c_onesrowf", [1, 128], F32)
        self.c_ones = ctx.sb("c_ones", [128, 1], BF16)
        self.c_e8 = ctx.sb("c_e8", [8, 64], BF16)
        self.c_eps = ctx.sb("c_eps", [1, 8], F32)
        self.c_h128 = ctx.sb("c_h128", [128, 128], BF16)
        self.c_h64 = ctx.sb("c_h64", [64, 64], BF16)
        loads = [("CONSTS", self.c_consts, [128, len(CVALS)]),
                 ("ID128", self.c_id, [128, 128]),
                 ("ONESROW", self.c_onesrow, [1, 128]),
                 ("ONESROWF", self.c_onesrowf, [1, 128]),
                 ("ONES128", self.c_ones, [128, 1]),
                 ("E8X2", self.c_e8, [8, 64]),
                 ("EPSIDX", self.c_eps, [1, 8]),
                 ("H128", self.c_h128, [128, 128]),
                 ("H64", self.c_h64, [64, 64])]
        for nm, dst, shp in loads:
            src = self.dram_in(nm, shp, F32)
            ctx.op("gpsimd", "dma_start", dst[:, :], src, writes=(f"c:{nm}",))
        ctx.barrier()

    def const(self, v):
        j = self.CI[v]
        return self.c_consts[:, j:j + 1]



def ln_tile(p, x_ap, x_key, xn_tm, xn_key, stat, stat_key, tag):
    ctx = p.ctx
    bn = stat
    for j in range(4):
        ctx.op("vector", "bn_stats", stat[:, 8 + j * 6: 8 + (j + 1) * 6],
               x_ap[:, j * 512:(j + 1) * 512],
               reads=(x_key,), writes=(stat_key,))
    ctx.op("vector", "bn_aggr", stat[:, 0:2], stat[:, 8:32],
           reads=(stat_key,), writes=(stat_key,))
    ctx.op("scalar", "activation", stat[:, 2:3], stat[:, 1:2], AF.Sqrt,
           bias=1e-5, reads=(stat_key,), writes=(stat_key,))
    ctx.op("vector", "reciprocal", stat[:, 3:4], stat[:, 2:3],
           reads=(stat_key,), writes=(stat_key,))
    ctx.op("vector", "tensor_scalar", xn_tm[:, :], x_ap, stat[:, 0:1], stat[:, 3:4],
           ALU.subtract, ALU.mult, reads=(x_key, stat_key), writes=(xn_key,))


def transpose_tm_to_fm(p, src_tm, src_key, dst_fm, dst_key, ti, tc, n_dt=16,
                       dst_bf16=True, banks=(0, 1)):
    ctx = p.ctx
    for g in range(n_dt // 4):
        b = p.pb[banks[g % len(banks)]]
        bk = f"pb{banks[g % len(banks)]}"
        for q in range(4):
            dt = g * 4 + q
            ctx.op("tensor", "transpose", b[:, q * 128:(q + 1) * 128],
                   src_tm[:, dt * 128:(dt + 1) * 128], p.c_id[:, :],
                   reads=(src_key, "c:ID128"), writes=(bk,), inc=(q == 3))
        for q in range(4):
            dt = g * 4 + q
            ctx.op("scalar", "copy",
                   dst_fm[:, dt * tc + ti * 128: dt * tc + (ti + 1) * 128],
                   b[:, q * 128:(q + 1) * 128],
                   reads=(bk,), writes=(dst_key,))


def mm_movs(p, psum_ap, bank_keys, stat_ap, stat_keys, mov_ap, mov_keys, ncols,
            start, stop, inc_last=True, tile_position=None):
    ctx = p.ctx
    nch = (ncols + 511) // 512
    for j in range(nch):
        c0, c1 = j * 512, min(ncols, (j + 1) * 512)
        kw = {}
        if tile_position is not None:
            kw["tile_position"] = tile_position
        ctx.op("tensor", "matmul", psum_ap[:, c0:c1], stat_ap, mov_ap[:, c0:c1],
               start=start and j == 0, stop=stop and j == nch - 1,
               reads=tuple(stat_keys) + tuple(mov_keys), writes=tuple(bank_keys),
               inc=inc_last and (j == nch - 1), **kw)



def phase_l0ffn(p, xres_src, xres_dst, tag):
    nc, ctx = p.nc, p.ctx
    tc = 256
    nt = tc // 128
    nch = T // tc
    es = ExitStack()
    sb = lambda n, s, d: es.enter_context(nc.sbuf_tensor(f"{tag}_{n}", s, d))

    wr = sb("wr", [128, 16 * 8], BF16)
    a1t = sb("a1t", [128, 16 * 64], BF16)
    a2t = sb("a2t", [128, 64 * 64], BF16)
    b1aug = sb("b1aug", [65, FF], BF16)
    b2aug = sb("b2aug", [65, D], BF16)
    br = sb("br", [1, 8], BF16)
    epsb = sb("epsb", [128, 8], F32)
    w1b = [sb(f"w1b{i}", [128, 16 * 128], BF16) for i in range(2)]
    w2b = [sb(f"w2b{i}", [128, D], BF16) for i in range(2)]
    xb = [sb(f"xb{i}", [128, D], F32) for i in range(nt)]
    xn_tm = sb("xn_tm", [128, D], F32)
    xn_fm = sb("xn_fm", [128, 16 * tc], BF16)
    stat = sb("stat", [128, 32], F32)
    lg = sb("lg", [128, 8], F32)
    sel = sb("sel", [128, 8], F32)
    m1 = sb("m1", [128, 8], F32)
    m2 = sb("m2", [128, 8], F32)
    ee = sb("ee", [128, 8], F32)
    ee2 = sb("ee2", [128, 8], F32)
    rr = sb("rr", [128, 8], F32)
    w12 = sb("w12", [128, 2 * nt], F32)
    m1t = sb("m1t", [8, tc], BF16)
    m2t = sb("m2t", [8, tc], BF16)
    mex = [sb(f"mex{k}", [64, tc], BF16) for k in range(2)]
    mu1a = [sb(f"mu1a{k}", [65, tc], BF16) for k in range(2)]
    mz2a = [sb(f"mz2a{k}", [65, tc], BF16) for k in range(2)]
    h = [sb(f"h{k}", [128, 64 * tc], BF16) for k in range(2)]
    base1 = [sb(f"base1_{j}", [128, tc], BF16) for j in range(2)]
    v01 = [sb(f"v01_{k}", [128, tc], F32) for k in range(2)]
    yacc = [sb(f"yacc{i}", [128, D], F32) for i in range(nt)]

    W1T = p.dram_in("l0_W1Tt", [64 * 128, 16 * 128])
    W2T = p.dram_in("l0_W2T", [FF, D])
    ctx.op("gpsimd", "dma_start", wr[:, :],
           p.dram_in("l0_WrTt", [128, 16 * 8]), writes=("wr",))
    ctx.op("gpsimd", "dma_start", a1t[:, :],
           p.dram_in("l0_A1Tt", [128, 16 * 64]), writes=("a1t",))
    ctx.op("gpsimd", "dma_start", a2t[:, :],
           p.dram_in("l0_A2Tt", [128, 64 * 64]), writes=("a2t",))
    ctx.op("gpsimd", "dma_start", b1aug[:, :], p.dram_in("l0_B1aug", [65, FF]),
           writes=("b1aug",))
    ctx.op("gpsimd", "dma_start", b2aug[:, :], p.dram_in("l0_B2aug", [65, D]),
           writes=("b2aug",))
    ctx.op("gpsimd", "dma_start", br[:, :], p.dram_in("l0_br", [1, 8]),
           writes=("br",))
    epsbf = sb("epsbf", [1, 8], BF16)
    ctx.op("vector", "tensor_copy", epsbf[:, :], p.c_eps[:, :],
           reads=("c:EPSIDX",), writes=("epsbf",))
    ctx.op("tensor", "matmul", p.pb[0][0:128, 0:8], p.c_onesrow[:, :], epsbf[:, :],
           start=True, stop=True, reads=("c:ONESROW", "epsbf"), writes=("pb0",))
    ctx.op("vector", "tensor_copy", epsb[:, :], p.pb[0][0:128, 0:8],
           reads=("pb0",), writes=("epsb",))
    for k in range(2):
        ctx.op("gpsimd", "memset", mu1a[k][64:65, :], 1.0, writes=(f"mu1a{k}",))
        ctx.op("gpsimd", "memset", mz2a[k][64:65, :], 1.0, writes=(f"mz2a{k}",))

    for ch in range(nch):
        tok0 = ch * tc
        for i in range(nt):
            ctx.op("sync", "dma_start", xb[i][:, :],
                   xres_src[tok0 + i * 128: tok0 + (i + 1) * 128, :],
                   reads=("xres_src",), writes=(f"xb{i}",))
        for i in range(nt):
            ln_tile(p, xb[i][:, :], f"xb{i}", xn_tm, "xn_tm", stat, "stat", tag)
            transpose_tm_to_fm(p, xn_tm, "xn_tm", xn_fm, "xn_fm", i, tc)
        for i in range(nt):
            for dt in range(16):
                ctx.op("tensor", "matmul", p.pb[4][0:128, 0:8],
                       xn_fm[:, dt * tc + i * 128: dt * tc + (i + 1) * 128],
                       wr[:, dt * 8:(dt + 1) * 8],
                       start=(dt == 0), stop=False,
                       reads=("xn_fm", "wr"), writes=("pb4",), inc=False)
            ctx.op("tensor", "matmul", p.pb[4][0:128, 0:8], p.c_onesrow[:, :],
                   br[:, :], start=False, stop=True,
                   reads=("c:ONESROW", "br"), writes=("pb4",))
            ctx.op("vector", "tensor_copy", lg[:, :], p.pb[4][0:128, 0:8],
                   reads=("pb4",), writes=("lg",))
            ctx.op("vector", "tensor_add", sel[:, :], lg[:, :], epsb[:, :],
                   reads=("lg", "epsb"), writes=("sel",))
            ctx.op("vector", "reduce_max", rr[:, 0:1], sel[:, :], axis=AXX,
                   reads=("sel",), writes=("rr",))
            ctx.op("vector", "tensor_scalar", m1[:, :], sel[:, :], rr[:, 0:1], None,
                   ALU.is_ge, reads=("sel", "rr"), writes=("m1",))
            ctx.op("vector", "scalar_tensor_tensor", sel[:, :], m1[:, :], -1e9,
                   sel[:, :], ALU.mult, ALU.add,
                   reads=("m1", "sel"), writes=("sel",))
            ctx.op("vector", "reduce_max", rr[:, 1:2], sel[:, :], axis=AXX,
                   reads=("sel",), writes=("rr",))
            ctx.op("vector", "tensor_scalar", m2[:, :], sel[:, :], rr[:, 1:2], None,
                   ALU.is_ge, reads=("sel", "rr"), writes=("m2",))
            ctx.op("vector", "reduce_max", rr[:, 2:3], lg[:, :], axis=AXX,
                   negate=True, reads=("lg",), writes=("rr",))
            ctx.op("scalar", "activation", ee[:, :], lg[:, :], AF.Exp,
                   bias=rr[:, 2:3], accum_out=rr[:, 3:4],
                   reads=("lg", "rr"), writes=("ee", "rr"))
            ctx.op("vector", "reciprocal", rr[:, 4:5], rr[:, 3:4],
                   reads=("rr",), writes=("rr",))
            for k, mk in ((0, m1), (1, m2)):
                ctx.op("vector", "tensor_mul", ee2[:, :], mk[:, :], ee[:, :],
                       reads=(("m1", "m2")[k], "ee"), writes=("ee2",))
                ctx.op("vector", "reduce_sum", rr[:, 6:7], ee2[:, :], axis=AXX,
                       reads=("ee2",), writes=("rr",))
                ctx.op("vector", "tensor_scalar", w12[:, 2 * i + k: 2 * i + k + 1],
                       rr[:, 6:7], rr[:, 4:5], 0.5, ALU.mult, ALU.mult,
                       reads=("rr",), writes=("w12",))
            for mk, mt, mkey, mtk in ((m1, m1t, "m1", "m1t"), (m2, m2t, "m2", "m2t")):
                ctx.op("tensor", "transpose", p.pb[5][0:8, i * 128:(i + 1) * 128],
                       mk[:, :], p.c_id[:, :],
                       reads=(mkey, "c:ID128"), writes=("pb5",))
                ctx.op("vector", "tensor_copy", mt[0:8, i * 128:(i + 1) * 128],
                       p.pb[5][0:8, i * 128:(i + 1) * 128],
                       reads=("pb5",), writes=(mtk,))
        for k, (mt, mtk) in enumerate(((m1t, "m1t"), (m2t, "m2t"))):
            ctx.op("tensor", "matmul", p.pb[5][0:64, 0:tc], p.c_e8[:, :], mt[:, :],
                   start=True, stop=True, reads=("c:E8X2", mtk), writes=("pb5",))
            ctx.op("vector", "tensor_copy", mex[k][:, :], p.pb[5][0:64, 0:tc],
                   reads=("pb5",), writes=(f"mex{k}",))
        for dt in range(16):
            ctx.op("tensor", "matmul", p.pb[6][0:64, 0:tc],
                   a1t[:, dt * 64:(dt + 1) * 64],
                   xn_fm[:, dt * tc:(dt + 1) * tc],
                   start=(dt == 0), stop=(dt == 15),
                   reads=("a1t", "xn_fm"), writes=("pb6",), inc=(dt == 15))
        for k in range(2):
            ctx.op("vector", "tensor_mul", mu1a[k][0:64, :], p.pb[6][0:64, 0:tc],
                   mex[k][:, :], reads=("pb6", f"mex{k}"), writes=(f"mu1a{k}",))
        ctx.op("gpsimd", "dma_start", w1b[0][:, :], W1T[0:128, :],
               reads=("W1T",), writes=("w1b0",))
        for ft in range(FF // 128):
            cur = w1b[ft % 2]
            ck = f"w1b{ft % 2}"
            if ft + 1 < 64:
                nxt = w1b[(ft + 1) % 2]
                ctx.op("gpsimd", "dma_start", nxt[:, :],
                       W1T[(ft + 1) * 128:(ft + 2) * 128, :],
                       reads=("W1T",), writes=(f"w1b{(ft + 1) % 2}",))
            bA, bB, bC = (0, 1, 2) if ft % 2 == 0 else (3, 4, 5)
            kA, kB, kC = f"pb{bA}", f"pb{bB}", f"pb{bC}"
            fb = ft % 2
            for dt in range(16):
                ctx.op("tensor", "matmul", p.pb[bA][:, 0:tc],
                       cur[:, dt * 128:(dt + 1) * 128],
                       xn_fm[:, dt * tc:(dt + 1) * tc],
                       start=(dt == 0), stop=(dt == 15),
                       reads=(ck, "xn_fm"), writes=(kA,), inc=(dt == 15))
            for k, bZ, kZ in ((0, bB, kB), (1, bC, kC)):
                ctx.op("tensor", "matmul", p.pb[bZ][:, 0:tc],
                       b1aug[:, ft * 128:(ft + 1) * 128], mu1a[k][:, :],
                       start=True, stop=True,
                       reads=("b1aug", f"mu1a{k}"), writes=(kZ,))
            ctx.op("scalar", "copy", base1[fb][:, :], p.pb[bA][:, 0:tc],
                   reads=(kA,), writes=(f"base1_{fb}",))
            for k, bZ, kZ in ((0, bB, kB), (1, bC, kC)):
                ctx.op("vector", "tensor_add", v01[k][:, :], p.pb[bZ][:, 0:tc],
                       base1[fb][:, :], reads=(kZ, f"base1_{fb}"),
                       writes=(f"v01_{k}",))
            for k in range(2):
                ctx.op("scalar", "activation", h[k][:, ft * tc:(ft + 1) * tc],
                       v01[k][:, :], AF.Gelu_apprx_tanh,
                       reads=(f"v01_{k}",), writes=(f"h{k}",))
            for k, bU in ((0, 6), (1, 7)):
                ctx.op("tensor", "matmul", p.pb[bU][0:64, 0:tc],
                       a2t[:, ft * 64:(ft + 1) * 64],
                       h[k][:, ft * tc:(ft + 1) * tc],
                       start=(ft == 0), stop=(ft == 63),
                       reads=("a2t", f"h{k}"), writes=(f"pb{bU}",),
                       inc=(ft == 63))
        for k, bU in ((0, 6), (1, 7)):
            ctx.op("vector", "tensor_mul", mz2a[k][0:64, :], p.pb[bU][0:64, 0:tc],
                   mex[k][:, :], reads=(f"pb{bU}", f"mex{k}"), writes=(f"mz2a{k}",))
        for k in range(2):
            for i in range(nt):
                for dc in range(4):
                    b = i * 4 + dc
                    ctx.op("tensor", "matmul", p.pb[b][:, 0:512],
                           mz2a[k][:, i * 128:(i + 1) * 128],
                           b2aug[:, dc * 512:(dc + 1) * 512],
                           start=True, stop=False,
                           reads=(f"mz2a{k}", "b2aug"), writes=(f"pb{b}",),
                           inc=False)
            ctx.op("gpsimd", "dma_start", w2b[0][:, :], W2T[0:128, :],
                   reads=("W2T",), writes=("w2b0",))
            for ft in range(64):
                cur, ck = w2b[ft % 2], f"w2b{ft % 2}"
                if ft + 1 < 64:
                    ctx.op("gpsimd", "dma_start", w2b[(ft + 1) % 2][:, :],
                           W2T[(ft + 1) * 128:(ft + 2) * 128, :],
                           reads=("W2T",), writes=(f"w2b{(ft + 1) % 2}",))
                for i in range(nt):
                    for dc in range(4):
                        b = i * 4 + dc
                        ctx.op("tensor", "matmul", p.pb[b][:, 0:512],
                               h[k][:, ft * tc + i * 128: ft * tc + (i + 1) * 128],
                               cur[:, dc * 512:(dc + 1) * 512],
                               start=False, stop=(ft == 63),
                               reads=(f"h{k}", ck), writes=(f"pb{b}",),
                               inc=(i == nt - 1 and dc == 3))
            for i in range(nt):
                if k == 0:
                    for dc in range(4):
                        b = i * 4 + dc
                        ctx.op("vector", "tensor_scalar",
                               yacc[i][:, dc * 512:(dc + 1) * 512],
                               p.pb[b][:, 0:512], w12[:, 2 * i:2 * i + 1], None,
                               ALU.mult, reads=(f"pb{b}", "w12"),
                               writes=(f"yacc{i}",))
                else:
                    for dc in range(4):
                        b = i * 4 + dc
                        ctx.op("vector", "scalar_tensor_tensor",
                               yacc[i][:, dc * 512:(dc + 1) * 512],
                               p.pb[b][:, 0:512], w12[:, 2 * i + 1:2 * i + 2],
                               yacc[i][:, dc * 512:(dc + 1) * 512],
                               ALU.mult, ALU.add,
                               reads=(f"pb{b}", "w12", f"yacc{i}"),
                               writes=(f"yacc{i}",))
                    ctx.op("vector", "tensor_add", xn_tm[:, :], yacc[i][:, :],
                           xb[i][:, :], reads=(f"yacc{i}", f"xb{i}"),
                           writes=("xn_tm",))
                    ctx.op("sync", "dma_start",
                           xres_dst[tok0 + i * 128: tok0 + (i + 1) * 128, :],
                           xn_tm[:, :], reads=("xn_tm",), writes=("xres_dst",))
    ctx.barrier()
    es.close()



def phase_w2prep(p, w2p_dram, tag="w2p"):
    nc, ctx = p.nc, p.ctx
    es = ExitStack()
    sb = lambda n, s, d: es.enter_context(nc.sbuf_tensor(f"{tag}_{n}", s, d))
    W2G = p.dram_in("l1_W2G", [64, 128 * D])
    gin = [sb(f"gin{i}", [64, D], BF16) for i in range(2)]
    gout = [sb(f"gout{i}", [64, D], BF16) for i in range(2)]
    w2p_v = w2p_dram.ap().rearrange("(a b) d -> a b d", a=64)
    for b in range(128):
        i = b % 2
        ctx.op("gpsimd", "dma_start", gin[i][:, :],
               W2G[:, b * D:(b + 1) * D],
               reads=("W2G",), writes=(f"{tag}gin{i}",))
        for dc in range(4):
            ctx.op("tensor", "matmul", p.pb[dc][0:64, 0:512], p.c_h64[:, :],
                   gin[i][:, dc * 512:(dc + 1) * 512],
                   start=True, stop=True,
                   reads=("c:H64", f"{tag}gin{i}"), writes=(f"pb{dc}",))
            ctx.op("vector", "tensor_copy", gout[i][:, dc * 512:(dc + 1) * 512],
                   p.pb[dc][0:64, 0:512],
                   reads=(f"pb{dc}",), writes=(f"{tag}gout{i}",))
        ctx.op("sync", "dma_start", w2p_v[:, b, :], gout[i][:, :],
               reads=(f"{tag}gout{i}",), writes=("w2p",))
    ctx.barrier()
    es.close()



def phase_l1ffn(p, xres_src, xres_dst, w2p_dram, tag):
    nc, ctx = p.nc, p.ctx
    tc = 512
    nt = tc // 128
    nch = T // tc
    es = ExitStack()
    sb = lambda n, s, d: es.enter_context(nc.sbuf_tensor(f"{tag}_{n}", s, d))

    W1T = p.dram_in("l1_W1Tt", [64 * 128, 16 * 128])
    W3T = p.dram_in("l1_W3Tt", [64 * 128, 16 * 128])
    w2p_ap = w2p_dram.ap()

    w1b = [sb(f"w1b{i}", [128, 2048], BF16) for i in range(2)]
    w3b = [sb(f"w3b{i}", [128, 2048], BF16) for i in range(2)]
    w2b = [sb(f"w2b{i}", [128, D], BF16) for i in range(2)]
    xb = [sb(f"xb{i}", [128, D], F32) for i in range(nt)]
    xn_tm = sb("xn_tm", [128, D], F32)
    xn_fm = sb("xn_fm", [128, 16 * tc], BF16)
    stat = sb("stat", [128, 32], F32)
    sg = sb("sg", [128, tc], F32)
    hp = [sb(f"hp{i}", [128, tc], BF16) for i in range(2)]
    hidp = sb("hidp", [128, 64 * tc], BF16)
    xo = sb("xo", [128, D], F32)

    for ch in range(nch):
        tok0 = ch * tc
        for i in range(nt):
            ctx.op("sync", "dma_start", xb[i][:, :],
                   xres_src[tok0 + i * 128: tok0 + (i + 1) * 128, :],
                   reads=("xres_src",), writes=(f"xb{i}",))
        for i in range(nt):
            ln_tile(p, xb[i][:, :], f"xb{i}", xn_tm, "xn_tm", stat, "stat", tag)
            transpose_tm_to_fm(p, xn_tm, "xn_tm", xn_fm, "xn_fm", i, tc)
        ctx.op("gpsimd", "dma_start", w1b[0][:, :], W1T[0:128, :],
               reads=("W1T",), writes=("w1b0",))
        ctx.op("gpsimd", "dma_start", w3b[0][:, :], W3T[0:128, :],
               reads=("W3T",), writes=("w3b0",))
        for ft in range(64):
            fb = ft % 2
            cg, cl = w1b[fb], w3b[fb]
            kg, kl = f"w1b{fb}", f"w3b{fb}"
            if ft + 1 < 64:
                nfb = (ft + 1) % 2
                ctx.op("gpsimd", "dma_start", w1b[nfb][:, :],
                       W1T[(ft + 1) * 128:(ft + 2) * 128, :],
                       reads=("W1T",), writes=(f"w1b{nfb}",))
                ctx.op("gpsimd", "dma_start", w3b[nfb][:, :],
                       W3T[(ft + 1) * 128:(ft + 2) * 128, :],
                       reads=("W3T",), writes=(f"w3b{nfb}",))
            bG, bL, bH = (0, 1, 2) if fb == 0 else (3, 4, 5)
            for dt in range(16):
                ctx.op("tensor", "matmul", p.pb[bG][:, 0:tc],
                       cg[:, dt * 128:(dt + 1) * 128],
                       xn_fm[:, dt * tc:(dt + 1) * tc],
                       start=(dt == 0), stop=(dt == 15),
                       reads=(kg, "xn_fm"), writes=(f"pb{bG}",), inc=(dt == 15))
            for dt in range(16):
                ctx.op("tensor", "matmul", p.pb[bL][:, 0:tc],
                       cl[:, dt * 128:(dt + 1) * 128],
                       xn_fm[:, dt * tc:(dt + 1) * tc],
                       start=(dt == 0), stop=(dt == 15),
                       reads=(kl, "xn_fm"), writes=(f"pb{bL}",), inc=(dt == 15))
            ctx.op("scalar", "activation", sg[:, :], p.pb[bG][:, 0:tc], AF.Silu,
                   reads=(f"pb{bG}",), writes=("sg",))
            ctx.op("vector", "tensor_mul", hp[fb][:, :], sg[:, :],
                   p.pb[bL][:, 0:tc], reads=("sg", f"pb{bL}"), writes=(f"hp{fb}",))
            ctx.op("tensor", "matmul", p.pb[bH][:, 0:tc], p.c_h128[:, :],
                   hp[fb][:, :], start=True, stop=True,
                   reads=("c:H128", f"hp{fb}"), writes=(f"pb{bH}",))
            ctx.op("scalar", "activation", hidp[:, ft * tc:(ft + 1) * tc],
                   p.pb[bH][:, 0:tc], AF.Copy, scale=CVALS[7],
                   reads=(f"pb{bH}",), writes=("hidp",))
        for pair in range(nt // 2):
            ctx.op("sync", "dma_start", w2b[0][:, :], w2p_ap[0:128, :],
                   reads=("w2p",), writes=("w2b0",))
            for ft in range(64):
                fb = ft % 2
                cur, ck = w2b[fb], f"w2b{fb}"
                if ft + 1 < 64:
                    nfb = (ft + 1) % 2
                    ctx.op("sync", "dma_start", w2b[nfb][:, :],
                           w2p_ap[(ft + 1) * 128:(ft + 2) * 128, :],
                           reads=("w2p",), writes=(f"w2b{nfb}",))
                for ii in range(2):
                    i = pair * 2 + ii
                    for dc in range(4):
                        b = ii * 4 + dc
                        ctx.op("tensor", "matmul", p.pb[b][:, 0:512],
                               hidp[:, ft * tc + i * 128: ft * tc + (i + 1) * 128],
                               cur[:, dc * 512:(dc + 1) * 512],
                               start=(ft == 0), stop=(ft == 63),
                               reads=("hidp", ck), writes=(f"pb{b}",),
                               inc=(ii == 1 and dc == 3))
            for ii in range(2):
                i = pair * 2 + ii
                for dc in range(4):
                    b = ii * 4 + dc
                    ctx.op("vector", "tensor_add", xo[:, dc * 512:(dc + 1) * 512],
                           p.pb[b][:, 0:512],
                           xb[i][:, dc * 512:(dc + 1) * 512],
                           reads=(f"pb{b}", f"xb{i}"), writes=("xo",))
                ctx.op("sync", "dma_start",
                       xres_dst[tok0 + i * 128: tok0 + (i + 1) * 128, :],
                       xo[:, :], reads=("xo",), writes=("xres_dst",))
    ctx.barrier()
    es.close()



def phase_xnfull(p, src_ap, src_key, dst_ap, dst_key, ranks, tag):
    nc, ctx = p.nc, p.ctx
    tc = 512
    es = ExitStack()
    sb = lambda n, s, d: es.enter_context(nc.sbuf_tensor(f"{tag}_{n}", s, d))
    xb = [sb(f"xb{i}", [128, D], F32) for i in range(2)]
    xn_tm = sb("xn_tm", [128, D], F32)
    xn_fm = sb("xn_fm", [128, 16 * tc], BF16)
    stat = sb("stat", [128, 32], F32)
    for rb, st0 in ranks:
        for ch in range(1024 // tc):
            for i in range(tc // 128):
                ib = i % 2
                ctx.op("sync", "dma_start", xb[ib][:, :],
                       src_ap[st0 + ch * tc + i * 128: st0 + ch * tc + (i + 1) * 128, :],
                       reads=(src_key,), writes=(f"{tag}xb{ib}",))
                ln_tile(p, xb[ib][:, :], f"{tag}xb{ib}", xn_tm, f"{tag}xn_tm",
                        stat, f"{tag}stat", tag)
                transpose_tm_to_fm(p, xn_tm, f"{tag}xn_tm", xn_fm, f"{tag}xn_fm",
                                   i, tc)
            for dt in range(16):
                ctx.op("sync", "dma_start",
                       dst_ap[rb + dt * 128: rb + (dt + 1) * 128,
                              ch * tc:(ch + 1) * tc],
                       xn_fm[:, dt * tc:(dt + 1) * tc],
                       reads=(f"{tag}xn_fm",), writes=(dst_key,))
    ctx.barrier()
    es.close()



def phase_l0attn(p, xres_src, xres_dst, xnfull_ap, xnfull_key, xnown_ap, xnown_key, tag):
    nc, ctx = p.nc, p.ctx
    es = ExitStack()
    sb = lambda n, s, d: es.enter_context(nc.sbuf_tensor(f"{tag}_{n}", s, d))

    Q = sb("Q", [128, 16 * T], BF16)
    ckv = sb("ckv", [128, 4 * S], BF16)
    krot = sb("krot", [64, S], BF16)

    esB = ExitStack()
    sbB = lambda n, s, d: esB.enter_context(nc.sbuf_tensor(f"{tag}B_{n}", s, d))
    xnq = sbB("xnq", [128, 16 * T], BF16)
    qlat = sbB("qlat", [128, 12 * T], BF16)
    qab = [sbB(f"qab{i}", [128, 2048], BF16) for i in range(2)]
    qbb = [sbB(f"qbb{i}", [128, 1536], BF16) for i in range(2)]
    qs = sbB("qs", [64, T], F32)
    qsw = sbB("qsw", [64, T], F32)
    c0q = sbB("c0q", [64, T], F32)
    s0q = sbB("s0q", [64, T], F32)
    QAT = p.dram_in("l0_qaTt", [12 * 128, 16 * 128])
    QBT = p.dram_in("l0_qbTt", [16 * 128, 12 * 128])
    ctx.op("sync", "dma_start", c0q[:, :], p.dram_in("L0Q_C", [64, T]),
           writes=("c0q",))
    ctx.op("sync", "dma_start", s0q[:, :], p.dram_in("L0Q_S", [64, T]),
           writes=("s0q",))
    for dt in range(16):
        ctx.op("sync", "dma_start", xnq[:, dt * T:(dt + 1) * T],
               xnown_ap[dt * 128:(dt + 1) * 128, :],
               reads=(xnown_key,), writes=("xnq",))
    ctx.op("gpsimd", "dma_start", qab[0][:, :], QAT[0:128, :],
           reads=("QAT",), writes=("qab0",))
    for lt in range(12):
        fb = lt % 2
        if lt + 1 < 12:
            ctx.op("gpsimd", "dma_start", qab[(lt + 1) % 2][:, :],
                   QAT[(lt + 1) * 128:(lt + 2) * 128, :],
                   reads=("QAT",), writes=(f"qab{(lt + 1) % 2}",))
        for half in range(2):
            bb = 2 * half
            for dt in range(16):
                ctx.op("tensor", "matmul", p.pb[bb][:, 0:512],
                       qab[fb][:, dt * 128:(dt + 1) * 128],
                       xnq[:, dt * T + half * 512: dt * T + (half + 1) * 512],
                       start=(dt == 0), stop=(dt == 15),
                       reads=(f"qab{fb}", "xnq"), writes=(f"pb{bb}",),
                       inc=(dt == 15))
            ctx.op("scalar", "copy",
                   qlat[:, lt * T + half * 512: lt * T + (half + 1) * 512],
                   p.pb[bb][:, 0:512], reads=(f"pb{bb}",), writes=("qlat",))
    ctx.op("gpsimd", "dma_start", qbb[0][:, :], QBT[0:128, :],
           reads=("QBT",), writes=("qbb0",))
    for qt in range(16):
        fb = qt % 2
        if qt + 1 < 16:
            ctx.op("gpsimd", "dma_start", qbb[(qt + 1) % 2][:, :],
                   QBT[(qt + 1) * 128:(qt + 2) * 128, :],
                   reads=("QBT",), writes=(f"qbb{(qt + 1) % 2}",))
        for half in range(2):
            bb = 4 + 2 * half
            for lt in range(12):
                ctx.op("tensor", "matmul", p.pb[bb][:, 0:512],
                       qbb[fb][:, lt * 128:(lt + 1) * 128],
                       qlat[:, lt * T + half * 512: lt * T + (half + 1) * 512],
                       start=(lt == 0), stop=(lt == 11),
                       reads=(f"qbb{fb}", "qlat"), writes=(f"pb{bb}",),
                       inc=(lt == 11))
        for half in range(2):
            bb = 4 + 2 * half
            ctx.op("scalar", "copy",
                   Q[0:64, qt * T + half * 512: qt * T + (half + 1) * 512],
                   p.pb[bb][0:64, 0:512], reads=(f"pb{bb}",), writes=("Q",))
            ctx.op("vector", "tensor_copy", qs[:, half * 512:(half + 1) * 512],
                   p.pb[bb][64:128, 0:512], reads=(f"pb{bb}",), writes=("qs",))
        for b in range(1):
            ctx.op("vector", "tensor_copy", qsw[0:32, :], qs[32:64, :],
                   reads=("qs",), writes=("qsw",))
            ctx.op("vector", "tensor_copy", qsw[32:64, :], qs[0:32, :],
                   reads=("qs",), writes=("qsw",))
        ctx.op("vector", "tensor_mul", qs[:, :], qs[:, :], c0q[:, :],
               reads=("qs", "c0q"), writes=("qs",))
        ctx.op("vector", "tensor_mul", qsw[:, :], qsw[:, :], s0q[:, :],
               reads=("qsw", "s0q"), writes=("qsw",))
        ctx.op("vector", "tensor_add", Q[64:128, qt * T:(qt + 1) * T],
               qs[:, :], qsw[:, :], reads=("qs", "qsw"), writes=("Q",))
    ctx.barrier()
    esB.close()

    esC = ExitStack()
    sbC = lambda n, s, d: esC.enter_context(nc.sbuf_tensor(f"{tag}C_{n}", s, d))
    kvaa = sbC("kvaa", [128, 4 * 2048], BF16)
    kvab = sbC("kvab", [128, 16 * 64], BF16)
    xk = [sbC(f"xk{i}", [128, 16 * 512], BF16) for i in range(2)]
    ckc = [sbC(f"ckc{i}", [64, 512], F32) for i in range(2)]
    sks = [sbC(f"sks{i}", [64, 512], F32) for i in range(2)]
    ks = sbC("ks", [64, 512], F32)
    ksw = sbC("ksw", [64, 512], F32)
    ctx.op("gpsimd", "dma_start", kvaa[:, :],
           p.dram_in("l0_kvaTaf", [128, 4 * 2048]), writes=("kvaa",))
    ctx.op("gpsimd", "dma_start", kvab[:, :],
           p.dram_in("l0_kvaTbf", [128, 16 * 64]), writes=("kvab",))
    C0K = p.dram_in("L0K_C", [64, S])
    S0K = p.dram_in("L0K_S", [64, S])

    def load_xk(kc, buf):
        r = kc // 2
        for dt in range(16):
            ctx.op("sync", "dma_start", xk[buf][:, dt * 512:(dt + 1) * 512],
                   xnfull_ap[r * 2048 + dt * 128: r * 2048 + (dt + 1) * 128,
                             (kc % 2) * 512:(kc % 2 + 1) * 512],
                   reads=(xnfull_key,), writes=(f"xk{buf}",))

    load_xk(0, 0)
    ctx.op("sync", "dma_start", ckc[0][:, :], C0K[:, 0:512], writes=("ckc0",))
    ctx.op("sync", "dma_start", sks[0][:, :], S0K[:, 0:512], writes=("sks0",))
    for kc in range(8):
        fb = kc % 2
        if kc + 1 < 8:
            load_xk(kc + 1, (kc + 1) % 2)
            ctx.op("sync", "dma_start", ckc[(kc + 1) % 2][:, :],
                   C0K[:, (kc + 1) * 512:(kc + 2) * 512],
                   writes=(f"ckc{(kc + 1) % 2}",))
            ctx.op("sync", "dma_start", sks[(kc + 1) % 2][:, :],
                   S0K[:, (kc + 1) * 512:(kc + 2) * 512],
                   writes=(f"sks{(kc + 1) % 2}",))
        for ct in range(4):
            for dt in range(16):
                ctx.op("tensor", "matmul", p.pb[ct][:, 0:512],
                       kvaa[:, ct * 2048 + dt * 128: ct * 2048 + (dt + 1) * 128],
                       xk[fb][:, dt * 512:(dt + 1) * 512],
                       start=(dt == 0), stop=(dt == 15),
                       reads=("kvaa", f"xk{fb}"), writes=(f"pb{ct}",),
                       inc=(dt == 15))
        for dt in range(16):
            ctx.op("tensor", "matmul", p.pb[4][0:64, 0:512],
                   kvab[:, dt * 64:(dt + 1) * 64],
                   xk[fb][:, dt * 512:(dt + 1) * 512],
                   start=(dt == 0), stop=(dt == 15),
                   reads=("kvab", f"xk{fb}"), writes=("pb4",), inc=(dt == 15))
        for ct in range(4):
            ctx.op("scalar", "copy", ckv[:, ct * S + kc * 512:(ct * S + (kc + 1) * 512)],
                   p.pb[ct][:, 0:512], reads=(f"pb{ct}",), writes=("ckv",))
        ctx.op("vector", "tensor_copy", ks[:, :], p.pb[4][0:64, 0:512],
               reads=("pb4",), writes=("ks",))
        ctx.op("vector", "tensor_copy", ksw[0:32, :], ks[32:64, :],
               reads=("ks",), writes=("ksw",))
        ctx.op("vector", "tensor_copy", ksw[32:64, :], ks[0:32, :],
               reads=("ks",), writes=("ksw",))
        ctx.op("vector", "tensor_mul", ks[:, :], ks[:, :], ckc[fb][:, :],
               reads=("ks", f"ckc{fb}"), writes=("ks",))
        ctx.op("vector", "tensor_mul", ksw[:, :], ksw[:, :], sks[fb][:, :],
               reads=("ksw", f"sks{fb}"), writes=("ksw",))
        ctx.op("vector", "tensor_add", krot[:, kc * 512:(kc + 1) * 512],
               ks[:, :], ksw[:, :], reads=("ks", "ksw"), writes=("krot",))
    ctx.barrier()
    esC.close()

    esD = ExitStack()
    sbD = lambda n, s, d: esD.enter_context(nc.sbuf_tensor(f"{tag}D_{n}", s, d))
    wkb = sbD("wkb", [128, 16 * 256], BF16)
    wvb = sbD("wvb", [128, 4 * 2048], BF16)
    Vq = sbD("Vq", [128, 32 * 512], BF16)
    Kf = [sbD(f"Kf{i}", [128, S], BF16) for i in range(2)]
    Eb = [sbD(f"Eb{i}", [128, 512], BF16) for i in range(2)]
    den = sbD("den", [1, 512], F32)
    denb = sbD("denb", [1, 512], F32)
    bc = sbD("bc", [128, 512], F32)
    attn_out = sbD("attn_out", [128, 16 * T], BF16)
    ctx.op("gpsimd", "dma_start", wkb[:, :],
           p.dram_in("l0_WkbTf", [128, 16 * 256]), writes=("wkb",))
    ctx.op("gpsimd", "dma_start", wvb[:, :],
           p.dram_in("l0_WvbTf", [128, 4 * 2048]), writes=("wvb",))
    for g in range(4):
        for kt in range(32):
            ctx.op("tensor", "matmul", p.pb[6][:, 0:512],
                   ckv[:, 0 * S + kt * 128: 0 * S + (kt + 1) * 128],
                   wvb[:, 0 * 2048 + g * 512:(0 * 2048 + (g + 1) * 512)],
                   start=True, stop=False,
                   reads=("ckv", "wvb"), writes=("pb6",), inc=False)
            for ct in range(1, 4):
                ctx.op("tensor", "matmul", p.pb[6][:, 0:512],
                       ckv[:, ct * S + kt * 128: ct * S + (kt + 1) * 128],
                       wvb[:, ct * 2048 + g * 512: ct * 2048 + (g + 1) * 512],
                       start=False, stop=(ct == 3),
                       reads=("ckv", "wvb"), writes=("pb6",), inc=(ct == 3))
            ctx.op("scalar", "copy", Vq[:, kt * 512:(kt + 1) * 512],
                   p.pb[6][:, 0:512], reads=("pb6",), writes=("Vq",))
        for hh in range(4):
            h = g * 4 + hh
            hb = hh % 2
            for kc in range(8):
                for ct in range(4):
                    ctx.op("tensor", "matmul", p.pb[7][0:64, 0:512],
                           wkb[:, h * 256 + ct * 64: h * 256 + (ct + 1) * 64],
                           ckv[:, ct * S + kc * 512: ct * S + (kc + 1) * 512],
                           start=(ct == 0), stop=(ct == 3),
                           reads=("wkb", "ckv"), writes=("pb7",), inc=(ct == 3))
                ctx.op("scalar", "copy", Kf[hb][0:64, kc * 512:(kc + 1) * 512],
                       p.pb[7][0:64, 0:512], reads=("pb7",), writes=(f"Kf{hb}",))
            ctx.op("vector", "tensor_copy", Kf[hb][64:128, :], krot[:, :],
                   reads=("krot",), writes=(f"Kf{hb}",))
            for u in range(2):
                for kt in range(32):
                    sb_ = 4 + kt % 2
                    ctx.op("tensor", "matmul", p.pb[sb_][:, 0:512],
                           Kf[hb][:, kt * 128:(kt + 1) * 128],
                           Q[:, h * T + u * 512: h * T + (u + 1) * 512],
                           start=True, stop=True,
                           reads=(f"Kf{hb}", "Q"), writes=(f"pb{sb_}",))
                    ctx.op("scalar", "activation", Eb[kt % 2][:, :],
                           p.pb[sb_][:, 0:512], AF.Exp, bias=-20.0,
                           reads=(f"pb{sb_}",), writes=(f"Eb{kt % 2}",))
                    ctx.op("tensor", "matmul", p.pb[2][:, 0:512],
                           Vq[:, kt * 512 + hh * 128: kt * 512 + (hh + 1) * 128],
                           Eb[kt % 2][:, :], start=(kt == 0), stop=(kt == 31),
                           reads=("Vq", f"Eb{kt % 2}"), writes=("pb2",),
                           inc=(kt == 31))
                    ctx.op("tensor", "matmul", p.pb[3][0:1, 0:512],
                           p.c_ones[:, :], Eb[kt % 2][:, :],
                           start=(kt == 0), stop=(kt == 31),
                           reads=("c:ONES128", f"Eb{kt % 2}"), writes=("pb3",),
                           inc=(kt == 31))
                ctx.op("vector", "reciprocal", den[:, :], p.pb[3][0:1, 0:512],
                       reads=("pb3",), writes=("den",))
                ctx.op("tensor", "matmul", p.pb[6][:, 0:512],
                       p.c_onesrowf[:, :], den[:, :], start=True, stop=True,
                       reads=("c:ONESROWF", "den"), writes=("pb6",))
                ctx.op("vector", "tensor_copy", bc[:, :], p.pb[6][:, 0:512],
                       reads=("pb6",), writes=("bc",))
                ctx.op("vector", "tensor_mul",
                       attn_out[:, h * T + u * 512: h * T + (u + 1) * 512],
                       p.pb[2][:, 0:512], bc[:, :],
                       reads=("pb2", "bc"), writes=("attn_out",))
    ctx.barrier()
    esD.close()

    esE = ExitStack()
    sbE = lambda n, s, d: esE.enter_context(nc.sbuf_tensor(f"{tag}E_{n}", s, d))
    owb = [sbE(f"owb{i}", [128, D], BF16) for i in range(2)]
    xo = sbE("xo", [128, D], F32)
    xob = [sbE(f"xob{i}", [128, D], F32) for i in range(2)]
    OWT = p.dram_in("l0_owT", [D, D])
    for tt in range(8):
        ctx.op("sync", "dma_start", xob[tt % 2][:, :],
               xres_src[tt * 128:(tt + 1) * 128, :],
               reads=("xres_src",), writes=(f"xob{tt % 2}",))
        ctx.op("gpsimd", "dma_start", owb[0][:, :], OWT[0:128, :],
               reads=("OWT",), writes=("owb0",))
        for ht in range(16):
            fb = ht % 2
            if ht + 1 < 16:
                ctx.op("gpsimd", "dma_start", owb[(ht + 1) % 2][:, :],
                       OWT[(ht + 1) * 128:(ht + 2) * 128, :],
                       reads=("OWT",), writes=(f"owb{(ht + 1) % 2}",))
            for dc in range(4):
                b = 4 * (tt % 2) + dc
                ctx.op("tensor", "matmul", p.pb[b][:, 0:512],
                       attn_out[:, ht * T + tt * 128: ht * T + (tt + 1) * 128],
                       owb[fb][:, dc * 512:(dc + 1) * 512],
                       start=(ht == 0), stop=(ht == 15),
                       reads=("attn_out", f"owb{fb}"), writes=(f"pb{b}",),
                       inc=(dc == 3))
        for dc in range(4):
            b = 4 * (tt % 2) + dc
            ctx.op("vector", "tensor_add", xo[:, dc * 512:(dc + 1) * 512],
                   p.pb[b][:, 0:512], xob[tt % 2][:, dc * 512:(dc + 1) * 512],
                   reads=(f"pb{b}", f"xob{tt % 2}"), writes=("xo",))
        ctx.op("sync", "dma_start", xres_dst[tt * 128:(tt + 1) * 128, :],
               xo[:, :], reads=("xo",), writes=("xres_dst",))
    ctx.barrier()
    esE.close()
    es.close()



def phase_l1attn(p, xres_src, xres_dst, xnfull_ap, xnfull_key, xnown_ap, xnown_key, tag):
    nc, ctx = p.nc, p.ctx
    es = ExitStack()
    sb = lambda n, s, d: es.enter_context(nc.sbuf_tensor(f"{tag}_{n}", s, d))
    Q = sb("Q", [128, 16 * T], BF16)
    Ksb = sb("Ksb", [128, 4 * S], BF16)
    Vsb = sb("Vsb", [128, 32 * 520], BF16)
    attn_out = sb("attn_out", [128, 16 * T], BF16)

    esB = ExitStack()
    sbB = lambda n, s, d: esB.enter_context(nc.sbuf_tensor(f"{tag}B_{n}", s, d))
    xnq = sbB("xnq", [128, 16 * T], BF16)
    wqb = [sbB(f"wqb{i}", [128, 2048], BF16) for i in range(2)]
    qs = sbB("qs", [128, T], F32)
    qsw = sbB("qsw", [128, T], F32)
    c1q = sbB("c1q", [128, T], F32)
    s1q = sbB("s1q", [128, T], F32)
    WQT = p.dram_in("l1_WqTt", [16 * 128, 16 * 128])
    ctx.op("sync", "dma_start", c1q[:, :], p.dram_in("L1Q_C", [128, T]),
           writes=("c1q",))
    ctx.op("sync", "dma_start", s1q[:, :], p.dram_in("L1Q_S", [128, T]),
           writes=("s1q",))
    for dt in range(16):
        ctx.op("sync", "dma_start", xnq[:, dt * T:(dt + 1) * T],
               xnown_ap[dt * 128:(dt + 1) * 128, :],
               reads=(xnown_key,), writes=("xnq",))
    ctx.op("gpsimd", "dma_start", wqb[0][:, :], WQT[0:128, :],
           reads=("WQT",), writes=("wqb0",))
    for qt in range(16):
        fb = qt % 2
        if qt + 1 < 16:
            ctx.op("gpsimd", "dma_start", wqb[(qt + 1) % 2][:, :],
                   WQT[(qt + 1) * 128:(qt + 2) * 128, :],
                   reads=("WQT",), writes=(f"wqb{(qt + 1) % 2}",))
        for half in range(2):
            bb = 2 * half
            for dt in range(16):
                ctx.op("tensor", "matmul", p.pb[bb][:, 0:512],
                       wqb[fb][:, dt * 128:(dt + 1) * 128],
                       xnq[:, dt * T + half * 512: dt * T + (half + 1) * 512],
                       start=(dt == 0), stop=(dt == 15),
                       reads=(f"wqb{fb}", "xnq"), writes=(f"pb{bb}",),
                       inc=(dt == 15))
            ctx.op("vector", "tensor_copy", qs[:, half * 512:(half + 1) * 512],
                   p.pb[bb][:, 0:512], reads=(f"pb{bb}",), writes=("qs",))
        for b in range(4):
            ctx.op("vector", "tensor_copy", qsw[b * 32:(b + 1) * 32, :],
                   qs[b * 32 + 32 - 64 * (b % 2):b * 32 + 64 - 64 * (b % 2), :],
                   reads=("qs",), writes=("qsw",))
        ctx.op("vector", "tensor_mul", qs[:, :], qs[:, :], c1q[:, :],
               reads=("qs", "c1q"), writes=("qs",))
        ctx.op("vector", "tensor_mul", qsw[:, :], qsw[:, :], s1q[:, :],
               reads=("qsw", "s1q"), writes=("qsw",))
        ctx.op("vector", "tensor_add", Q[:, qt * T:(qt + 1) * T],
               qs[:, :], qsw[:, :], reads=("qs", "qsw"), writes=("Q",))
    ctx.barrier()
    esB.close()

    esC = ExitStack()
    sbC = lambda n, s, d: esC.enter_context(nc.sbuf_tensor(f"{tag}C_{n}", s, d))
    wk = sbC("wk", [128, 4 * 2048], BF16)
    wv = sbC("wv", [128, 16 * 512], BF16)
    xk = [sbC(f"xk{i}", [128, 16 * 512], BF16) for i in range(2)]
    c1k = [sbC(f"c1k{i}", [128, 512], F32) for i in range(2)]
    s1k = [sbC(f"s1k{i}", [128, 512], F32) for i in range(2)]
    ks = sbC("ks", [128, 512], F32)
    ksw = sbC("ksw", [128, 512], F32)
    ctx.op("gpsimd", "dma_start", wk[:, :],
           p.dram_in("l1_WkTf", [128, 4 * 2048]), writes=("wk",))
    ctx.op("gpsimd", "dma_start", wv[:, :],
           p.dram_in("l1_WvTf", [128, 16 * 512]), writes=("wv",))
    C1K = p.dram_in("L1K_C", [128, S])
    S1K = p.dram_in("L1K_S", [128, S])
    ctx.op("vector", "memset",
           Vsb[:, :].rearrange("p (t g c) -> p t g c", g=8, c=65)[:, :, :, 64:65],
           1.0, writes=("Vsb",))

    def load_xk(kc, buf):
        r = kc // 2
        for dt in range(16):
            ctx.op("sync", "dma_start", xk[buf][:, dt * 512:(dt + 1) * 512],
                   xnfull_ap[r * 2048 + dt * 128: r * 2048 + (dt + 1) * 128,
                             (kc % 2) * 512:(kc % 2 + 1) * 512],
                   reads=(xnfull_key,), writes=(f"xk{buf}",))

    load_xk(0, 0)
    ctx.op("sync", "dma_start", c1k[0][:, :], C1K[:, 0:512], writes=("c1k0",))
    ctx.op("sync", "dma_start", s1k[0][:, :], S1K[:, 0:512], writes=("s1k0",))
    for kc in range(8):
        fb = kc % 2
        if kc + 1 < 8:
            load_xk(kc + 1, (kc + 1) % 2)
            ctx.op("sync", "dma_start", c1k[(kc + 1) % 2][:, :],
                   C1K[:, (kc + 1) * 512:(kc + 2) * 512],
                   writes=(f"c1k{(kc + 1) % 2}",))
            ctx.op("sync", "dma_start", s1k[(kc + 1) % 2][:, :],
                   S1K[:, (kc + 1) * 512:(kc + 2) * 512],
                   writes=(f"s1k{(kc + 1) % 2}",))
        for kb in range(4):
            for dt in range(16):
                ctx.op("tensor", "matmul", p.pb[kb][:, 0:512],
                       wk[:, kb * 2048 + dt * 128: kb * 2048 + (dt + 1) * 128],
                       xk[fb][:, dt * 512:(dt + 1) * 512],
                       start=(dt == 0), stop=(dt == 15),
                       reads=("wk", f"xk{fb}"), writes=(f"pb{kb}",),
                       inc=(dt == 15))
            ctx.op("vector", "tensor_copy", ks[:, :], p.pb[kb][:, 0:512],
                   reads=(f"pb{kb}",), writes=("ks",))
            for b in range(4):
                ctx.op("vector", "tensor_copy", ksw[b * 32:(b + 1) * 32, :],
                       ks[b * 32 + 32 - 64 * (b % 2):b * 32 + 64 - 64 * (b % 2), :],
                       reads=("ks",), writes=("ksw",))
            ctx.op("vector", "tensor_mul", ks[:, :], ks[:, :], c1k[fb][:, :],
                   reads=("ks", f"c1k{fb}"), writes=("ks",))
            ctx.op("vector", "tensor_mul", ksw[:, :], ksw[:, :], s1k[fb][:, :],
                   reads=("ksw", f"s1k{fb}"), writes=("ksw",))
            ctx.op("vector", "tensor_add",
                   Ksb[:, kb * S + kc * 512: kb * S + (kc + 1) * 512],
                   ks[:, :], ksw[:, :], reads=("ks", "ksw"), writes=("Ksb",))
        for k4 in range(4):
            kt = kc * 4 + k4
            for dt in range(16):
                ctx.op("tensor", "matmul", p.pb[4][:, 0:512],
                       xk[fb][:, dt * 512 + k4 * 128: dt * 512 + (k4 + 1) * 128],
                       wv[:, dt * 512:(dt + 1) * 512],
                       start=(dt == 0), stop=(dt == 15),
                       reads=(f"xk{fb}", "wv"), writes=("pb4",), inc=(dt == 15))
            ctx.op("vector", "tensor_copy",
                   Vsb[:, kt * 520:(kt + 1) * 520]
                   .rearrange("p (g c) -> p g c", c=65)[:, :, 0:64],
                   p.pb[4][:, 0:512].rearrange("p (g c) -> p g c", c=64),
                   reads=("pb4",), writes=("Vsb",))
    ctx.barrier()
    esC.close()

    esD = ExitStack()
    sbD = lambda n, s, d: esD.enter_context(nc.sbuf_tensor(f"{tag}D_{n}", s, d))
    Eb = [sbD(f"Eb{i}", [128, 512], BF16) for i in range(2)]
    den = sbD("den", [1, 512], F32)
    bc = sbD("bc", [64, 512], F32)
    for kvh in range(8):
        for u in range(2):
            for hh in range(4):
                h = kvh * 4 + hh
                qt_, qb_ = l1_qpos(h)
                assert qb_ == (kvh % 2) * 64
                for kt in range(32):
                    sb_ = 4 + kt % 2
                    ctx.op("tensor", "matmul", p.pb[sb_][:, 0:512],
                           Ksb[(kvh % 2) * 64:(kvh % 2) * 64 + 64,
                               (kvh // 2) * S + kt * 128:(kvh // 2) * S + (kt + 1) * 128],
                           Q[qb_:qb_ + 64,
                             qt_ * T + u * 512: qt_ * T + (u + 1) * 512],
                           start=True, stop=True,
                           reads=("Ksb", "Q"), writes=(f"pb{sb_}",))
                    ctx.op("scalar", "activation", Eb[kt % 2][:, :],
                           p.pb[sb_][:, 0:512], AF.Exp, bias=-20.0,
                           reads=(f"pb{sb_}",), writes=(f"Eb{kt % 2}",))
                    ctx.op("tensor", "matmul", p.pb[2][0:65, 0:512],
                           Vsb[:, kt * 520 + kvh * 65: kt * 520 + (kvh + 1) * 65],
                           Eb[kt % 2][:, :], start=(kt == 0), stop=(kt == 31),
                           reads=("Vsb", f"Eb{kt % 2}"), writes=("pb2",),
                           inc=(kt == 31))
                ctx.op("vector", "reciprocal", den[:, :], p.pb[2][64:65, 0:512],
                       reads=("pb2",), writes=("den",))
                ctx.op("tensor", "matmul", p.pb[6][0:64, 0:512],
                       p.c_onesrowf[:, 0:64], den[:, :], start=True, stop=True,
                       reads=("c:ONESROWF", "den"), writes=("pb6",))
                ctx.op("vector", "tensor_copy", bc[:, :], p.pb[6][0:64, 0:512],
                       reads=("pb6",), writes=("bc",))
                ctx.op("vector", "tensor_mul",
                       attn_out[qb_:qb_ + 64,
                                qt_ * T + u * 512: qt_ * T + (u + 1) * 512],
                       p.pb[2][0:64, 0:512], bc[:, :],
                       reads=("pb2", "bc"), writes=("attn_out",))
    ctx.barrier()
    esD.close()

    esE = ExitStack()
    sbE = lambda n, s, d: esE.enter_context(nc.sbuf_tensor(f"{tag}E_{n}", s, d))
    owb = [sbE(f"owb{i}", [128, D], BF16) for i in range(2)]
    xo = sbE("xo", [128, D], F32)
    xob = [sbE(f"xob{i}", [128, D], F32) for i in range(2)]
    OWT = p.dram_in("l1_WoT", [D, D])
    for tt in range(8):
        ctx.op("sync", "dma_start", xob[tt % 2][:, :],
               xres_src[tt * 128:(tt + 1) * 128, :],
               reads=("xres_src",), writes=(f"xob{tt % 2}",))
        ctx.op("gpsimd", "dma_start", owb[0][:, :], OWT[0:128, :],
               reads=("OWT1",), writes=("owb0",))
        for ht in range(16):
            fb = ht % 2
            if ht + 1 < 16:
                ctx.op("gpsimd", "dma_start", owb[(ht + 1) % 2][:, :],
                       OWT[(ht + 1) * 128:(ht + 2) * 128, :],
                       reads=("OWT1",), writes=(f"owb{(ht + 1) % 2}",))
            for dc in range(4):
                b = 4 * (tt % 2) + dc
                ctx.op("tensor", "matmul", p.pb[b][:, 0:512],
                       attn_out[:, ht * T + tt * 128: ht * T + (tt + 1) * 128],
                       owb[fb][:, dc * 512:(dc + 1) * 512],
                       start=(ht == 0), stop=(ht == 15),
                       reads=("attn_out", f"owb{fb}"), writes=(f"pb{b}",),
                       inc=(dc == 3))
        for dc in range(4):
            b = 4 * (tt % 2) + dc
            ctx.op("vector", "tensor_add", xo[:, dc * 512:(dc + 1) * 512],
                   p.pb[b][:, 0:512], xob[tt % 2][:, dc * 512:(dc + 1) * 512],
                   reads=(f"pb{b}", f"xob{tt % 2}"), writes=("xo",))
        ctx.op("sync", "dma_start", xres_dst[tt * 128:(tt + 1) * 128, :],
               xo[:, :], reads=("xo",), writes=("xres_dst",))
    ctx.barrier()
    esE.close()
    es.close()






def _build(which):
    nc = bass.Bass("TRN2", target_bir_lowering=False)
    p = Prog(nc)
    xin = p.dram_in("x_in", [T, D], F32)
    xout = p.dram_out("x_out", [T, D], F32)
    p.setup_common()
    if which == "l0ffn":
        phase_l0ffn(p, xin, xout, "f0")
    elif which == "l1ffn":
        w2p = p.dram_int("w2p", [FF, D], BF16)
        phase_w2prep(p, w2p)
        phase_l1ffn(p, xin, xout, w2p, "f1")
    elif which in ("l0attn", "l1attn"):
        xbatch = p.dram_in("x_batch", [S, D], F32)
        xnf = p.dram_int("xnf", [8192, T], BF16)
        xno = p.dram_int("xnown", [2048, T], BF16)
        phase_xnfull(p, xbatch, "x_batch", xnf.ap(), "xnf",
                     [(r * 2048, r * 1024) for r in range(4)], "xnp")
        phase_xnfull(p, xin, "x_in", xno.ap(), "xnown", [(0, 0)], "xnq")
        if which == "l0attn":
            phase_l0attn(p, xin, xout, xnf.ap(), "xnf", xno.ap(), "xnown", "a0")
        else:
            phase_l1attn(p, xin, xout, xnf.ap(), "xnf", xno.ap(), "xnown", "a1")
    return p


def kernel(**inputs):
    inputs = {k: np.asarray(v) for k, v in inputs.items()}
    W = prep_weights(inputs)
    consts = [host_consts(c) for c in range(8)]
    x = np.ascontiguousarray(inputs["L0_attn_x"]).astype(np.float32)

    def launch(which, x):
        p = _build(which)
        in_maps = []
        for c in range(8):
            b, r = c // 4, c % 4
            m = {}
            for name in p.inp:
                if name in consts[c]:
                    m[name] = consts[c][name]
                elif name in W:
                    m[name] = W[name]
                elif name == "x_in":
                    m[name] = np.ascontiguousarray(x[b, r * 1024:(r + 1) * 1024, :])
                elif name == "x_batch":
                    m[name] = np.ascontiguousarray(x[b])
                else:
                    raise KeyError(name)
            in_maps.append(m)
        res = run_bass_kernel_spmd(p.nc, in_maps, core_ids=list(range(8)))
        out = np.empty_like(x)
        for c in range(8):
            b, r = c // 4, c % 4
            out[b, r * 1024:(r + 1) * 1024, :] = res.results[c]["x_out"]
        return out

    for which in ("l0ffn", "l0attn", "l0ffn", "l1ffn", "l1attn", "l1ffn"):
        x = launch(which, x)
    return x


# version 04
# speedup vs baseline: 1.0888x; 1.0888x over previous
import math
from contextlib import ExitStack

import numpy as np
import ml_dtypes
import concourse.bass as bass
import concourse.mybir as mybir
from concourse.bass_utils import run_bass_kernel_spmd

F32 = mybir.dt.float32
BF16 = mybir.dt.bfloat16
AF = mybir.ActivationFunctionType
ALU = mybir.AluOpType
AXX = mybir.AxisListType.X

T = 1024
D = 2048
FF = 8192
S = 4096

CVALS = [0.0, 1.0, -1.0, 0.5, 2.0, -20.0, 1e-5,
         float(np.float32(0.5) * np.float32(1.0 / np.sqrt(8192.0))),
         -1e9]


ENGINES = ("tensor", "vector", "scalar", "gpsimd", "sync")

L1_EVENKV = [h for h in range(32) if (h // 4) % 2 == 0]
L1_ODDKV = [h for h in range(32) if (h // 4) % 2 == 1]


def l1_qpos(h):
    if (h // 4) % 2 == 0:
        return L1_EVENKV.index(h), 0
    return L1_ODDKV.index(h), 64


class Ctx:

    def __init__(self, nc):
        self.nc = nc
        self.es = ExitStack()
        self.eng = dict(tensor=nc.tensor, vector=nc.vector, scalar=nc.scalar,
                        gpsimd=nc.gpsimd, sync=nc.sync)
        self.sem = {e: self.es.enter_context(nc.semaphore(f"sem_{e}")) for e in ENGINES}
        self.cnt = {e: 0 for e in ENGINES}
        self.waited = {}
        self.res = {}

    def sb(self, name, shape, dtype):
        return self.es.enter_context(self.nc.sbuf_tensor(name, shape, dtype))

    def ps(self, name, shape, dtype):
        return self.es.enter_context(self.nc.psum_tensor(name, shape, dtype))

    def _dma_semkey(self, key, eng):
        sk = f"d:{eng}:{key}"
        if sk not in self.sem:
            self.sem[sk] = self.es.enter_context(self.nc.semaphore(sk))
            self.cnt[sk] = 0
        return sk

    def _need(self, eng, key, write, need):
        r = self.res.get(key)
        if r is None:
            return
        w = r.get("w")
        if w is not None:
            wsem, wval = w
            if not (wsem == "tensor" and eng == "tensor"):
                need[wsem] = max(need.get(wsem, 0), wval)
        if write:
            for rsem, rval in r.get("r", {}).items():
                if rsem == "tensor" and eng == "tensor":
                    continue
                need[rsem] = max(need.get(rsem, 0), rval)

    def _record(self, semkey, key, write, val):
        r = self.res.setdefault(key, {"w": None, "r": {}})
        if write:
            r["w"] = (semkey, val)
            r["r"] = {}
        else:
            r["r"][semkey] = max(r["r"].get(semkey, 0), val)

    def _emit_waits(self, eng, need):
        e = self.eng[eng]
        for tgt, val in sorted(need.items()):
            if val <= 0 or self.waited.get((eng, tgt), 0) >= val:
                continue
            if val > self.cnt[tgt]:
                raise RuntimeError(f"{eng} waits {tgt}>={val} but cnt={self.cnt[tgt]}")
            e.wait_ge(self.sem[tgt], val)
            self.waited[(eng, tgt)] = val

    def op(self, eng, method, *args, reads=(), writes=(), inc=True, **kw):
        need = {}
        for k in reads:
            self._need(eng, k, False, need)
        for k in writes:
            self._need(eng, k, True, need)
        self._emit_waits(eng, need)
        ins = getattr(self.eng[eng], method)(*args, **kw)
        is_dma = method in ("dma_start", "collective_compute", "dma_start_transpose")
        if is_dma:
            assert writes, f"dma op needs a write resource ({method})"
            semkey = self._dma_semkey(writes[0], eng)
            amt = 1 if method == "collective_compute" else 16
            ins.then_inc(self.sem[semkey], amt)
            self.cnt[semkey] += amt
            val = self.cnt[semkey]
        elif inc:
            semkey = eng
            ins.then_inc(self.sem[eng], 1)
            self.cnt[eng] += 1
            val = self.cnt[eng]
        else:
            semkey = eng
            val = self.cnt[eng] + 1
        for k in reads:
            self._record(semkey, k, False, val)
        for k in writes:
            self._record(semkey, k, True, val)
        return ins

    def barrier(self):
        for a in ENGINES:
            for b, cnt in self.cnt.items():
                if a == b or cnt == 0:
                    continue
                if self.waited.get((a, b), 0) < cnt:
                    self.eng[a].wait_ge(self.sem[b], cnt)
                    self.waited[(a, b)] = cnt
        self.res.clear()



def _stat_tiles(wT, kt=128, ct=128):
    K, M = wT.shape
    a = wT.reshape(K // kt, kt, M // ct, ct).transpose(2, 1, 0, 3)
    return np.ascontiguousarray(a.reshape(M // ct * kt, (K // kt) * ct))


def _stat_flat(wT, kt=128, ct=128):
    K, M = wT.shape
    a = wT.reshape(K // kt, kt, M // ct, ct).transpose(1, 2, 0, 3)
    return np.ascontiguousarray(a.reshape(kt, (M // ct) * (K // kt) * ct))


def prep_weights(w):
    o = {}
    o["l0_W1Tt"] = _stat_tiles(w["L0_ffn_W1"].T)
    o["l0_W2T"] = np.ascontiguousarray(w["L0_ffn_W2"].T)
    o["l0_WrTt"] = _stat_tiles(w["L0_ffn_W_r"].T, ct=8)
    o["l0_br"] = np.ascontiguousarray(w["L0_ffn_b_r"].reshape(1, 8))
    o["l0_A1Tt"] = _stat_tiles(w["L0_ffn_A1"].reshape(64, D).T, ct=64)
    B1rf = w["L0_ffn_B1"].transpose(0, 2, 1).reshape(64, FF)
    o["l0_B1aug"] = np.ascontiguousarray(
        np.concatenate([B1rf, w["L0_ffn_b1"].reshape(1, FF)], 0))
    o["l0_A2Tt"] = _stat_tiles(w["L0_ffn_A2"].reshape(64, FF).T, ct=64)
    B2rf = w["L0_ffn_B2"].transpose(0, 2, 1).reshape(64, D)
    o["l0_B2aug"] = np.ascontiguousarray(
        np.concatenate([B2rf, w["L0_ffn_b2"].reshape(1, D)], 0))
    H0, QKD, NOPE, VD0, QR, KVR = 16, 128, 64, 128, 1536, 512
    o["l0_qaTt"] = _stat_tiles(w["L0_attn_q_a_w"].T)
    qb = w["L0_attn_q_b_w"].reshape(H0, QKD, QR)
    perm = (list(range(NOPE)) + [NOPE + 2 * i for i in range(32)]
            + [NOPE + 2 * i + 1 for i in range(32)])
    o["l0_qbTt"] = _stat_tiles(
        np.ascontiguousarray(qb[:, perm, :].reshape(H0 * QKD, QR).T))
    kva = w["L0_attn_kv_a_w"]
    rot = kva[KVR:]
    kva_p = np.concatenate([kva[:KVR], rot[0::2], rot[1::2]], 0)
    kvaT = np.ascontiguousarray(kva_p.T)
    o["l0_kvaTaf"] = _stat_flat(kvaT[:, 0:KVR])
    o["l0_kvaTbf"] = _stat_flat(kvaT[:, KVR:], ct=64)
    kvb = w["L0_attn_kv_b_w"].reshape(H0, NOPE + VD0, KVR)
    o["l0_WkbTf"] = _stat_flat(
        np.ascontiguousarray(kvb[:, :NOPE, :].reshape(H0 * NOPE, KVR).T), ct=64)
    o["l0_WvbTf"] = _stat_flat(
        np.ascontiguousarray(kvb[:, NOPE:, :].reshape(H0 * VD0, KVR).T), ct=2048)
    o["l0_owT"] = np.ascontiguousarray(w["L0_attn_o_w"].T)
    order = []
    for j in range(16):
        order += [L1_EVENKV[j], L1_ODDKV[j]]
    rowperm = np.concatenate([np.arange(h * 64, (h + 1) * 64) for h in order])
    o["l1_WqTt"] = _stat_tiles(np.ascontiguousarray(w["L1_attn_Wq"][rowperm].T))
    o["l1_WkTf"] = _stat_flat(w["L1_attn_Wk"].T)
    o["l1_WvTf"] = _stat_flat(w["L1_attn_Wv"].T, ct=512)
    o["l1_WoT"] = np.ascontiguousarray(w["L1_attn_Wo"].T[rowperm])
    o["l1_W1Tt"] = _stat_tiles(w["L1_ffn_W1"].T)
    o["l1_W3Tt"] = _stat_tiles(w["L1_ffn_W3"].T)
    o["l1_W2G"] = np.ascontiguousarray(w["L1_ffn_W2"].T.reshape(64, 128 * D))
    return o


def host_consts(core_id):
    c = {}
    c["CONSTS"] = np.tile(np.array(CVALS, np.float32), (128, 1))
    c["ID128"] = np.eye(128, dtype=np.float32)
    c["ONESROW"] = np.ones((1, 128), np.float32)
    c["ONESROWF"] = np.ones((1, 128), np.float32)
    c["ONES128"] = np.ones((128, 1), np.float32)
    e8 = np.zeros((8, 64), np.float32)
    for e in range(8):
        e8[e, e * 8:(e + 1) * 8] = 2.0
    c["E8X2"] = e8
    c["EPSIDX"] = (np.arange(8, dtype=np.float32) * np.float32(-1e-6)).reshape(1, 8)

    def had(n):
        h = np.array([[1.0]], np.float32)
        while h.shape[0] < n:
            h = np.block([[h, h], [h, -h]])
        return np.ascontiguousarray(h)
    c["H128"] = had(128)
    c["H64"] = had(64)
    q0 = ((core_id % 4) * T + np.arange(T)).astype(np.float32)
    kpos = np.arange(S, dtype=np.float32)
    th = (np.float32(1.0)
          / np.float32(10000.0) ** (np.arange(0, 64, 2, dtype=np.float32)
                                    / np.float32(64.0))).astype(np.float32)
    for nm, pos, scl, rep in (("L0Q", q0, 1.0, 1), ("L0K", kpos, 1.0, 1),
                              ("L1Q", q0, 0.125, 2), ("L1K", kpos, 1.0, 2)):
        ang = (pos[None, :] * th[:, None]).astype(np.float32)
        cs = np.cos(ang).astype(np.float32) * np.float32(scl)
        sn = np.sin(ang).astype(np.float32) * np.float32(scl)
        blkC = np.concatenate([cs, cs], 0)
        blkS = np.concatenate([-sn, sn], 0)
        c[f"{nm}_C"] = np.ascontiguousarray(np.concatenate([blkC] * rep, 0))
        c[f"{nm}_S"] = np.ascontiguousarray(np.concatenate([blkS] * rep, 0))
    return c



class Prog:
    def __init__(self, nc):
        self.nc = nc
        self.ctx = Ctx(nc)
        self.inp = {}

    def dram_in(self, name, shape, dtype=F32):
        if name in self.inp:
            return self.inp[name]
        t = self.nc.dram_tensor(name, list(shape), dtype, kind="ExternalInput")
        self.inp[name] = t.ap()
        return self.inp[name]

    def dram_out(self, name, shape, dtype=F32):
        t = self.nc.dram_tensor(name, list(shape), dtype, kind="ExternalOutput")
        return t.ap()

    def dram_int(self, name, shape, dtype):
        return self.nc.dram_tensor(name, list(shape), dtype, kind="Internal")

    def setup_common(self):
        ctx = self.ctx
        self.pb = [ctx.ps(f"pb{i}", [128, 512], F32) for i in range(8)]
        self.c_consts = ctx.sb("c_consts", [128, len(CVALS)], F32)
        for j, v in enumerate(CVALS):
            self.nc.const_aps.register(F32, v, self.c_consts[:, j:j + 1])
        self.CI = {v: j for j, v in enumerate(CVALS)}
        self.c_id = ctx.sb("c_id", [128, 128], F32)
        self.c_onesrow = ctx.sb("c_onesrow", [1, 128], BF16)
        self.c_onesrowf = ctx.sb("c_onesrowf", [1, 128], F32)
        self.c_ones = ctx.sb("c_ones", [128, 1], BF16)
        self.c_e8 = ctx.sb("c_e8", [8, 64], BF16)
        self.c_eps = ctx.sb("c_eps", [1, 8], F32)
        self.c_h128 = ctx.sb("c_h128", [128, 128], BF16)
        self.c_h64 = ctx.sb("c_h64", [64, 64], BF16)
        loads = [("CONSTS", self.c_consts, [128, len(CVALS)]),
                 ("ID128", self.c_id, [128, 128]),
                 ("ONESROW", self.c_onesrow, [1, 128]),
                 ("ONESROWF", self.c_onesrowf, [1, 128]),
                 ("ONES128", self.c_ones, [128, 1]),
                 ("E8X2", self.c_e8, [8, 64]),
                 ("EPSIDX", self.c_eps, [1, 8]),
                 ("H128", self.c_h128, [128, 128]),
                 ("H64", self.c_h64, [64, 64])]
        for nm, dst, shp in loads:
            src = self.dram_in(nm, shp, F32)
            ctx.op("gpsimd", "dma_start", dst[:, :], src, writes=(f"c:{nm}",))
        ctx.barrier()

    def const(self, v):
        j = self.CI[v]
        return self.c_consts[:, j:j + 1]

    def wsrc(self, name, shape):
        wc = getattr(self, "wc", None)
        if wc and name in wc:
            return wc[name].ap(), "scalar"
        return self.dram_in(name, shape), "gpsimd"



def ln_tile(p, x_ap, x_key, xn_tm, xn_key, stat, stat_key, tag):
    ctx = p.ctx
    bn = stat
    for j in range(4):
        ctx.op("vector", "bn_stats", stat[:, 8 + j * 6: 8 + (j + 1) * 6],
               x_ap[:, j * 512:(j + 1) * 512],
               reads=(x_key,), writes=(stat_key,))
    ctx.op("vector", "bn_aggr", stat[:, 0:2], stat[:, 8:32],
           reads=(stat_key,), writes=(stat_key,))
    ctx.op("scalar", "activation", stat[:, 2:3], stat[:, 1:2], AF.Sqrt,
           bias=1e-5, reads=(stat_key,), writes=(stat_key,))
    ctx.op("vector", "reciprocal", stat[:, 3:4], stat[:, 2:3],
           reads=(stat_key,), writes=(stat_key,))
    ctx.op("vector", "tensor_scalar", xn_tm[:, :], x_ap, stat[:, 0:1], stat[:, 3:4],
           ALU.subtract, ALU.mult, reads=(x_key, stat_key), writes=(xn_key,))


def transpose_tm_to_fm(p, src_tm, src_key, dst_fm, dst_key, ti, tc, n_dt=16,
                       dst_bf16=True, banks=(0, 1)):
    ctx = p.ctx
    for g in range(n_dt // 4):
        b = p.pb[banks[g % len(banks)]]
        bk = f"pb{banks[g % len(banks)]}"
        for q in range(4):
            dt = g * 4 + q
            ctx.op("tensor", "transpose", b[:, q * 128:(q + 1) * 128],
                   src_tm[:, dt * 128:(dt + 1) * 128], p.c_id[:, :],
                   reads=(src_key, "c:ID128"), writes=(bk,), inc=(q == 3))
        for q in range(4):
            dt = g * 4 + q
            ctx.op("scalar", "copy",
                   dst_fm[:, dt * tc + ti * 128: dt * tc + (ti + 1) * 128],
                   b[:, q * 128:(q + 1) * 128],
                   reads=(bk,), writes=(dst_key,))


def mm_movs(p, psum_ap, bank_keys, stat_ap, stat_keys, mov_ap, mov_keys, ncols,
            start, stop, inc_last=True, tile_position=None):
    ctx = p.ctx
    nch = (ncols + 511) // 512
    for j in range(nch):
        c0, c1 = j * 512, min(ncols, (j + 1) * 512)
        kw = {}
        if tile_position is not None:
            kw["tile_position"] = tile_position
        ctx.op("tensor", "matmul", psum_ap[:, c0:c1], stat_ap, mov_ap[:, c0:c1],
               start=start and j == 0, stop=stop and j == nch - 1,
               reads=tuple(stat_keys) + tuple(mov_keys), writes=tuple(bank_keys),
               inc=inc_last and (j == nch - 1), **kw)



def phase_l0ffn(p, xres_src, xres_dst, tag):
    nc, ctx = p.nc, p.ctx
    tc = 256
    nt = tc // 128
    nch = T // tc
    es = ExitStack()
    sb = lambda n, s, d: es.enter_context(nc.sbuf_tensor(f"{tag}_{n}", s, d))

    wr = sb("wr", [128, 16 * 8], BF16)
    a1t = sb("a1t", [128, 16 * 64], BF16)
    a2t = sb("a2t", [128, 64 * 64], BF16)
    b1aug = sb("b1aug", [65, FF], BF16)
    b2aug = sb("b2aug", [65, D], BF16)
    br = sb("br", [1, 8], BF16)
    epsb = sb("epsb", [128, 8], F32)
    NW = 3
    w1b = [sb(f"w1b{i}", [128, 16 * 128], BF16) for i in range(NW)]
    w2b = [sb(f"w2b{i}", [128, D], BF16) for i in range(NW)]
    xb = [sb(f"xb{i}", [128, D], F32) for i in range(nt)]
    xn_tm = sb("xn_tm", [128, D], F32)
    xn_fm = sb("xn_fm", [128, 16 * tc], BF16)
    stat = sb("stat", [128, 32], F32)
    lg = sb("lg", [128, 8], F32)
    sel = sb("sel", [128, 8], F32)
    m1 = sb("m1", [128, 8], F32)
    m2 = sb("m2", [128, 8], F32)
    ee = sb("ee", [128, 8], F32)
    ee2 = sb("ee2", [128, 8], F32)
    rr = sb("rr", [128, 8], F32)
    w12 = sb("w12", [128, 2 * nt], F32)
    m1t = sb("m1t", [8, tc], BF16)
    m2t = sb("m2t", [8, tc], BF16)
    mex = [sb(f"mex{k}", [64, tc], BF16) for k in range(2)]
    mu1a = [sb(f"mu1a{k}", [65, tc], BF16) for k in range(2)]
    mz2a = [sb(f"mz2a{k}", [65, tc], BF16) for k in range(2)]
    h = [sb(f"h{k}", [128, 64 * tc], BF16) for k in range(2)]
    base1 = [sb(f"base1_{j}", [128, tc], BF16) for j in range(2)]
    v01 = [sb(f"v01_{k}", [128, tc], F32) for k in range(2)]
    yacc = [sb(f"yacc{i}", [128, D], F32) for i in range(nt)]

    W1T, q1 = p.wsrc("l0_W1Tt", [64 * 128, 16 * 128])
    W2T, q2 = p.wsrc("l0_W2T", [FF, D])
    ctx.op("gpsimd", "dma_start", wr[:, :],
           p.dram_in("l0_WrTt", [128, 16 * 8]), writes=("wr",))
    ctx.op("gpsimd", "dma_start", a1t[:, :],
           p.dram_in("l0_A1Tt", [128, 16 * 64]), writes=("a1t",))
    ctx.op("gpsimd", "dma_start", a2t[:, :],
           p.dram_in("l0_A2Tt", [128, 64 * 64]), writes=("a2t",))
    ctx.op("gpsimd", "dma_start", b1aug[:, :], p.dram_in("l0_B1aug", [65, FF]),
           writes=("b1aug",))
    ctx.op("gpsimd", "dma_start", b2aug[:, :], p.dram_in("l0_B2aug", [65, D]),
           writes=("b2aug",))
    ctx.op("gpsimd", "dma_start", br[:, :], p.dram_in("l0_br", [1, 8]),
           writes=("br",))
    epsbf = sb("epsbf", [1, 8], BF16)
    ctx.op("vector", "tensor_copy", epsbf[:, :], p.c_eps[:, :],
           reads=("c:EPSIDX",), writes=("epsbf",))
    ctx.op("tensor", "matmul", p.pb[0][0:128, 0:8], p.c_onesrow[:, :], epsbf[:, :],
           start=True, stop=True, reads=("c:ONESROW", "epsbf"), writes=("pb0",))
    ctx.op("vector", "tensor_copy", epsb[:, :], p.pb[0][0:128, 0:8],
           reads=("pb0",), writes=("epsb",))
    for k in range(2):
        ctx.op("gpsimd", "memset", mu1a[k][64:65, :], 1.0, writes=(f"mu1a{k}",))
        ctx.op("gpsimd", "memset", mz2a[k][64:65, :], 1.0, writes=(f"mz2a{k}",))

    for ch in range(nch):
        tok0 = ch * tc
        for i in range(nt):
            ctx.op("sync", "dma_start", xb[i][:, :],
                   xres_src[tok0 + i * 128: tok0 + (i + 1) * 128, :],
                   reads=("xres_src",), writes=(f"xb{i}",))
        for i in range(nt):
            ln_tile(p, xb[i][:, :], f"xb{i}", xn_tm, "xn_tm", stat, "stat", tag)
            transpose_tm_to_fm(p, xn_tm, "xn_tm", xn_fm, "xn_fm", i, tc)
        for i in range(nt):
            for dt in range(16):
                ctx.op("tensor", "matmul", p.pb[4][0:128, 0:8],
                       xn_fm[:, dt * tc + i * 128: dt * tc + (i + 1) * 128],
                       wr[:, dt * 8:(dt + 1) * 8],
                       start=(dt == 0), stop=False,
                       reads=("xn_fm", "wr"), writes=("pb4",), inc=False)
            ctx.op("tensor", "matmul", p.pb[4][0:128, 0:8], p.c_onesrow[:, :],
                   br[:, :], start=False, stop=True,
                   reads=("c:ONESROW", "br"), writes=("pb4",))
            ctx.op("vector", "tensor_copy", lg[:, :], p.pb[4][0:128, 0:8],
                   reads=("pb4",), writes=("lg",))
            ctx.op("vector", "tensor_add", sel[:, :], lg[:, :], epsb[:, :],
                   reads=("lg", "epsb"), writes=("sel",))
            ctx.op("vector", "reduce_max", rr[:, 0:1], sel[:, :], axis=AXX,
                   reads=("sel",), writes=("rr",))
            ctx.op("vector", "tensor_scalar", m1[:, :], sel[:, :], rr[:, 0:1], None,
                   ALU.is_ge, reads=("sel", "rr"), writes=("m1",))
            ctx.op("vector", "scalar_tensor_tensor", sel[:, :], m1[:, :], -1e9,
                   sel[:, :], ALU.mult, ALU.add,
                   reads=("m1", "sel"), writes=("sel",))
            ctx.op("vector", "reduce_max", rr[:, 1:2], sel[:, :], axis=AXX,
                   reads=("sel",), writes=("rr",))
            ctx.op("vector", "tensor_scalar", m2[:, :], sel[:, :], rr[:, 1:2], None,
                   ALU.is_ge, reads=("sel", "rr"), writes=("m2",))
            ctx.op("vector", "reduce_max", rr[:, 2:3], lg[:, :], axis=AXX,
                   negate=True, reads=("lg",), writes=("rr",))
            ctx.op("scalar", "activation", ee[:, :], lg[:, :], AF.Exp,
                   bias=rr[:, 2:3], accum_out=rr[:, 3:4],
                   reads=("lg", "rr"), writes=("ee", "rr"))
            ctx.op("vector", "reciprocal", rr[:, 4:5], rr[:, 3:4],
                   reads=("rr",), writes=("rr",))
            for k, mk in ((0, m1), (1, m2)):
                ctx.op("vector", "tensor_mul", ee2[:, :], mk[:, :], ee[:, :],
                       reads=(("m1", "m2")[k], "ee"), writes=("ee2",))
                ctx.op("vector", "reduce_sum", rr[:, 6:7], ee2[:, :], axis=AXX,
                       reads=("ee2",), writes=("rr",))
                ctx.op("vector", "tensor_scalar", w12[:, 2 * i + k: 2 * i + k + 1],
                       rr[:, 6:7], rr[:, 4:5], 0.5, ALU.mult, ALU.mult,
                       reads=("rr",), writes=("w12",))
            for mk, mt, mkey, mtk in ((m1, m1t, "m1", "m1t"), (m2, m2t, "m2", "m2t")):
                ctx.op("tensor", "transpose", p.pb[5][0:8, i * 128:(i + 1) * 128],
                       mk[:, :], p.c_id[:, :],
                       reads=(mkey, "c:ID128"), writes=("pb5",))
                ctx.op("vector", "tensor_copy", mt[0:8, i * 128:(i + 1) * 128],
                       p.pb[5][0:8, i * 128:(i + 1) * 128],
                       reads=("pb5",), writes=(mtk,))
        for k, (mt, mtk) in enumerate(((m1t, "m1t"), (m2t, "m2t"))):
            ctx.op("tensor", "matmul", p.pb[5][0:64, 0:tc], p.c_e8[:, :], mt[:, :],
                   start=True, stop=True, reads=("c:E8X2", mtk), writes=("pb5",))
            ctx.op("vector", "tensor_copy", mex[k][:, :], p.pb[5][0:64, 0:tc],
                   reads=("pb5",), writes=(f"mex{k}",))
        for dt in range(16):
            ctx.op("tensor", "matmul", p.pb[6][0:64, 0:tc],
                   a1t[:, dt * 64:(dt + 1) * 64],
                   xn_fm[:, dt * tc:(dt + 1) * tc],
                   start=(dt == 0), stop=(dt == 15),
                   reads=("a1t", "xn_fm"), writes=("pb6",), inc=(dt == 15))
        for k in range(2):
            ctx.op("vector", "tensor_mul", mu1a[k][0:64, :], p.pb[6][0:64, 0:tc],
                   mex[k][:, :], reads=("pb6", f"mex{k}"), writes=(f"mu1a{k}",))
        for pf in range(2):
            ctx.op(q1, "dma_start", w1b[pf][:, :], W1T[pf * 128:(pf + 1) * 128, :],
                   reads=("W1T",), writes=(f"w1b{pf}",))
        for ft in range(FF // 128):
            cur = w1b[ft % NW]
            ck = f"w1b{ft % NW}"
            if ft + 2 < 64:
                nx = ft + 2
                ctx.op(q1, "dma_start", w1b[nx % NW][:, :],
                       W1T[nx * 128:(nx + 1) * 128, :],
                       reads=("W1T",), writes=(f"w1b{nx % NW}",))
            bA, bB, bC = (0, 1, 2) if ft % 2 == 0 else (3, 4, 5)
            kA, kB, kC = f"pb{bA}", f"pb{bB}", f"pb{bC}"
            fb = ft % 2
            for dt in range(16):
                ctx.op("tensor", "matmul", p.pb[bA][:, 0:tc],
                       cur[:, dt * 128:(dt + 1) * 128],
                       xn_fm[:, dt * tc:(dt + 1) * tc],
                       start=(dt == 0), stop=(dt == 15),
                       reads=(ck, "xn_fm"), writes=(kA,), inc=(dt == 15))
            for k, bZ, kZ in ((0, bB, kB), (1, bC, kC)):
                ctx.op("tensor", "matmul", p.pb[bZ][:, 0:tc],
                       b1aug[:, ft * 128:(ft + 1) * 128], mu1a[k][:, :],
                       start=True, stop=True,
                       reads=("b1aug", f"mu1a{k}"), writes=(kZ,))
            ctx.op("scalar", "copy", base1[fb][:, :], p.pb[bA][:, 0:tc],
                   reads=(kA,), writes=(f"base1_{fb}",))
            for k, bZ, kZ in ((0, bB, kB), (1, bC, kC)):
                ctx.op("vector", "tensor_add", v01[k][:, :], p.pb[bZ][:, 0:tc],
                       base1[fb][:, :], reads=(kZ, f"base1_{fb}"),
                       writes=(f"v01_{k}",))
            for k in range(2):
                ctx.op("scalar", "activation", h[k][:, ft * tc:(ft + 1) * tc],
                       v01[k][:, :], AF.Gelu_apprx_tanh,
                       reads=(f"v01_{k}",), writes=(f"h{k}",))
            for k, bU in ((0, 6), (1, 7)):
                ctx.op("tensor", "matmul", p.pb[bU][0:64, 0:tc],
                       a2t[:, ft * 64:(ft + 1) * 64],
                       h[k][:, ft * tc:(ft + 1) * tc],
                       start=(ft == 0), stop=(ft == 63),
                       reads=("a2t", f"h{k}"), writes=(f"pb{bU}",),
                       inc=(ft == 63))
        for k, bU in ((0, 6), (1, 7)):
            ctx.op("vector", "tensor_mul", mz2a[k][0:64, :], p.pb[bU][0:64, 0:tc],
                   mex[k][:, :], reads=(f"pb{bU}", f"mex{k}"), writes=(f"mz2a{k}",))
        for k in range(2):
            for i in range(nt):
                for dc in range(4):
                    b = i * 4 + dc
                    ctx.op("tensor", "matmul", p.pb[b][:, 0:512],
                           mz2a[k][:, i * 128:(i + 1) * 128],
                           b2aug[:, dc * 512:(dc + 1) * 512],
                           start=True, stop=False,
                           reads=(f"mz2a{k}", "b2aug"), writes=(f"pb{b}",),
                           inc=False)
            for pf in range(2):
                ctx.op(q2, "dma_start", w2b[pf][:, :], W2T[pf * 128:(pf + 1) * 128, :],
                       reads=("W2T",), writes=(f"w2b{pf}",))
            for ft in range(64):
                cur, ck = w2b[ft % NW], f"w2b{ft % NW}"
                if ft + 2 < 64:
                    nx = ft + 2
                    ctx.op(q2, "dma_start", w2b[nx % NW][:, :],
                           W2T[nx * 128:(nx + 1) * 128, :],
                           reads=("W2T",), writes=(f"w2b{nx % NW}",))
                for i in range(nt):
                    for dc in range(4):
                        b = i * 4 + dc
                        ctx.op("tensor", "matmul", p.pb[b][:, 0:512],
                               h[k][:, ft * tc + i * 128: ft * tc + (i + 1) * 128],
                               cur[:, dc * 512:(dc + 1) * 512],
                               start=False, stop=(ft == 63),
                               reads=(f"h{k}", ck), writes=(f"pb{b}",),
                               inc=(i == nt - 1 and dc == 3))
            for i in range(nt):
                if k == 0:
                    for dc in range(4):
                        b = i * 4 + dc
                        ctx.op("vector", "tensor_scalar",
                               yacc[i][:, dc * 512:(dc + 1) * 512],
                               p.pb[b][:, 0:512], w12[:, 2 * i:2 * i + 1], None,
                               ALU.mult, reads=(f"pb{b}", "w12"),
                               writes=(f"yacc{i}",))
                else:
                    for dc in range(4):
                        b = i * 4 + dc
                        ctx.op("vector", "scalar_tensor_tensor",
                               yacc[i][:, dc * 512:(dc + 1) * 512],
                               p.pb[b][:, 0:512], w12[:, 2 * i + 1:2 * i + 2],
                               yacc[i][:, dc * 512:(dc + 1) * 512],
                               ALU.mult, ALU.add,
                               reads=(f"pb{b}", "w12", f"yacc{i}"),
                               writes=(f"yacc{i}",))
                    ctx.op("vector", "tensor_add", xn_tm[:, :], yacc[i][:, :],
                           xb[i][:, :], reads=(f"yacc{i}", f"xb{i}"),
                           writes=("xn_tm",))
                    ctx.op("sync", "dma_start",
                           xres_dst[tok0 + i * 128: tok0 + (i + 1) * 128, :],
                           xn_tm[:, :], reads=("xn_tm",), writes=("xres_dst",))
    ctx.barrier()
    es.close()



def phase_bf16conv(p, items, tag="cv"):
    nc, ctx = p.nc, p.ctx
    es = ExitStack()
    bufs = [es.enter_context(nc.sbuf_tensor(f"{tag}_b{i}", [128, 2048], BF16))
            for i in range(4)]
    for src_name, shape, dst in items:
        rows, cols = shape
        src = p.dram_in(src_name, shape)
        dap = dst.ap()
        assert cols <= 2048
        for i in range(rows // 128):
            bi = i % 4
            ctx.op("gpsimd", "dma_start", bufs[bi][:, 0:cols],
                   src[i * 128:(i + 1) * 128, :],
                   reads=(src_name,), writes=(f"{tag}b{bi}",))
            ctx.op("sync", "dma_start", dap[i * 128:(i + 1) * 128, :],
                   bufs[bi][:, 0:cols],
                   reads=(f"{tag}b{bi}",), writes=(f"{tag}:{src_name}",))
    ctx.barrier()
    es.close()



def phase_w2prep(p, w2p_dram, tag="w2p"):
    nc, ctx = p.nc, p.ctx
    es = ExitStack()
    sb = lambda n, s, d: es.enter_context(nc.sbuf_tensor(f"{tag}_{n}", s, d))
    W2G = p.dram_in("l1_W2G", [64, 128 * D])
    gin = [sb(f"gin{i}", [64, D], BF16) for i in range(2)]
    gout = [sb(f"gout{i}", [64, D], BF16) for i in range(2)]
    w2p_v = w2p_dram.ap().rearrange("(a b) d -> a b d", a=64)
    for b in range(128):
        i = b % 2
        ctx.op("gpsimd", "dma_start", gin[i][:, :],
               W2G[:, b * D:(b + 1) * D],
               reads=("W2G",), writes=(f"{tag}gin{i}",))
        for dc in range(4):
            ctx.op("tensor", "matmul", p.pb[dc][0:64, 0:512], p.c_h64[:, :],
                   gin[i][:, dc * 512:(dc + 1) * 512],
                   start=True, stop=True,
                   reads=("c:H64", f"{tag}gin{i}"), writes=(f"pb{dc}",))
            ctx.op("vector", "tensor_copy", gout[i][:, dc * 512:(dc + 1) * 512],
                   p.pb[dc][0:64, 0:512],
                   reads=(f"pb{dc}",), writes=(f"{tag}gout{i}",))
        ctx.op("sync", "dma_start", w2p_v[:, b, :], gout[i][:, :],
               reads=(f"{tag}gout{i}",), writes=("w2p",))
    ctx.barrier()
    es.close()



def phase_l1ffn(p, xres_src, xres_dst, w2p_dram, tag):
    nc, ctx = p.nc, p.ctx
    tc = 512
    nt = tc // 128
    nch = T // tc
    es = ExitStack()
    sb = lambda n, s, d: es.enter_context(nc.sbuf_tensor(f"{tag}_{n}", s, d))

    W1T, q1 = p.wsrc("l1_W1Tt", [64 * 128, 16 * 128])
    W3T, q3 = p.wsrc("l1_W3Tt", [64 * 128, 16 * 128])
    w2p_ap = w2p_dram.ap()

    NW = 3
    w1b = [sb(f"w1b{i}", [128, 2048], BF16) for i in range(NW)]
    w3b = [sb(f"w3b{i}", [128, 2048], BF16) for i in range(NW)]
    w2b = [sb(f"w2b{i}", [128, D], BF16) for i in range(NW)]
    xb = [sb(f"xb{i}", [128, D], F32) for i in range(nt)]
    xn_tm = sb("xn_tm", [128, D], F32)
    xn_fm = sb("xn_fm", [128, 16 * tc], BF16)
    stat = sb("stat", [128, 32], F32)
    sg = sb("sg", [128, tc], F32)
    hp = [sb(f"hp{i}", [128, tc], BF16) for i in range(2)]
    hidp = sb("hidp", [128, 64 * tc], BF16)
    xo = sb("xo", [128, D], F32)

    for ch in range(nch):
        tok0 = ch * tc
        for i in range(nt):
            ctx.op("sync", "dma_start", xb[i][:, :],
                   xres_src[tok0 + i * 128: tok0 + (i + 1) * 128, :],
                   reads=("xres_src",), writes=(f"xb{i}",))
        for i in range(nt):
            ln_tile(p, xb[i][:, :], f"xb{i}", xn_tm, "xn_tm", stat, "stat", tag)
            transpose_tm_to_fm(p, xn_tm, "xn_tm", xn_fm, "xn_fm", i, tc)
        for pf in range(2):
            ctx.op(q1, "dma_start", w1b[pf][:, :], W1T[pf * 128:(pf + 1) * 128, :],
                   reads=("W1T",), writes=(f"w1b{pf}",))
            ctx.op(q3, "dma_start", w3b[pf][:, :], W3T[pf * 128:(pf + 1) * 128, :],
                   reads=("W3T",), writes=(f"w3b{pf}",))
        for ft in range(64):
            fb = ft % NW
            cg, cl = w1b[fb], w3b[fb]
            kg, kl = f"w1b{fb}", f"w3b{fb}"
            if ft + 2 < 64:
                nfb = (ft + 2) % NW
                ctx.op(q1, "dma_start", w1b[nfb][:, :],
                       W1T[(ft + 2) * 128:(ft + 3) * 128, :],
                       reads=("W1T",), writes=(f"w1b{nfb}",))
                ctx.op(q3, "dma_start", w3b[nfb][:, :],
                       W3T[(ft + 2) * 128:(ft + 3) * 128, :],
                       reads=("W3T",), writes=(f"w3b{nfb}",))
            bG, bL, bH = (0, 1, 2) if ft % 2 == 0 else (3, 4, 5)
            for dt in range(16):
                ctx.op("tensor", "matmul", p.pb[bG][:, 0:tc],
                       cg[:, dt * 128:(dt + 1) * 128],
                       xn_fm[:, dt * tc:(dt + 1) * tc],
                       start=(dt == 0), stop=(dt == 15),
                       reads=(kg, "xn_fm"), writes=(f"pb{bG}",), inc=(dt == 15))
            for dt in range(16):
                ctx.op("tensor", "matmul", p.pb[bL][:, 0:tc],
                       cl[:, dt * 128:(dt + 1) * 128],
                       xn_fm[:, dt * tc:(dt + 1) * tc],
                       start=(dt == 0), stop=(dt == 15),
                       reads=(kl, "xn_fm"), writes=(f"pb{bL}",), inc=(dt == 15))
            ctx.op("scalar", "activation", sg[:, :], p.pb[bG][:, 0:tc], AF.Silu,
                   reads=(f"pb{bG}",), writes=("sg",))
            hb = ft % 2
            ctx.op("vector", "tensor_mul", hp[hb][:, :], sg[:, :],
                   p.pb[bL][:, 0:tc], reads=("sg", f"pb{bL}"), writes=(f"hp{hb}",))
            ctx.op("tensor", "matmul", p.pb[bH][:, 0:tc], p.c_h128[:, :],
                   hp[hb][:, :], start=True, stop=True,
                   reads=("c:H128", f"hp{hb}"), writes=(f"pb{bH}",))
            ctx.op("scalar", "activation", hidp[:, ft * tc:(ft + 1) * tc],
                   p.pb[bH][:, 0:tc], AF.Copy, scale=CVALS[7],
                   reads=(f"pb{bH}",), writes=("hidp",))
        for pair in range(nt // 2):
            for pf in range(2):
                ctx.op("scalar", "dma_start", w2b[pf][:, :],
                       w2p_ap[pf * 128:(pf + 1) * 128, :],
                       reads=("w2p",), writes=(f"w2b{pf}",))
            for ft in range(64):
                fb = ft % NW
                cur, ck = w2b[fb], f"w2b{fb}"
                if ft + 2 < 64:
                    nfb = (ft + 2) % NW
                    ctx.op("scalar", "dma_start", w2b[nfb][:, :],
                           w2p_ap[(ft + 2) * 128:(ft + 3) * 128, :],
                           reads=("w2p",), writes=(f"w2b{nfb}",))
                for ii in range(2):
                    i = pair * 2 + ii
                    for dc in range(4):
                        b = ii * 4 + dc
                        ctx.op("tensor", "matmul", p.pb[b][:, 0:512],
                               hidp[:, ft * tc + i * 128: ft * tc + (i + 1) * 128],
                               cur[:, dc * 512:(dc + 1) * 512],
                               start=(ft == 0), stop=(ft == 63),
                               reads=("hidp", ck), writes=(f"pb{b}",),
                               inc=(ii == 1 and dc == 3))
            for ii in range(2):
                i = pair * 2 + ii
                for dc in range(4):
                    b = ii * 4 + dc
                    ctx.op("vector", "tensor_add", xo[:, dc * 512:(dc + 1) * 512],
                           p.pb[b][:, 0:512],
                           xb[i][:, dc * 512:(dc + 1) * 512],
                           reads=(f"pb{b}", f"xb{i}"), writes=("xo",))
                ctx.op("sync", "dma_start",
                       xres_dst[tok0 + i * 128: tok0 + (i + 1) * 128, :],
                       xo[:, :], reads=("xo",), writes=("xres_dst",))
    ctx.barrier()
    es.close()



def phase_xnfull(p, src_ap, src_key, dst_ap, dst_key, ranks, tag):
    nc, ctx = p.nc, p.ctx
    tc = 512
    es = ExitStack()
    sb = lambda n, s, d: es.enter_context(nc.sbuf_tensor(f"{tag}_{n}", s, d))
    xb = [sb(f"xb{i}", [128, D], F32) for i in range(2)]
    xn_tm = sb("xn_tm", [128, D], F32)
    xn_fm = sb("xn_fm", [128, 16 * tc], BF16)
    stat = sb("stat", [128, 32], F32)
    for rb, st0 in ranks:
        for ch in range(1024 // tc):
            for i in range(tc // 128):
                ib = i % 2
                ctx.op("sync", "dma_start", xb[ib][:, :],
                       src_ap[st0 + ch * tc + i * 128: st0 + ch * tc + (i + 1) * 128, :],
                       reads=(src_key,), writes=(f"{tag}xb{ib}",))
                ln_tile(p, xb[ib][:, :], f"{tag}xb{ib}", xn_tm, f"{tag}xn_tm",
                        stat, f"{tag}stat", tag)
                transpose_tm_to_fm(p, xn_tm, f"{tag}xn_tm", xn_fm, f"{tag}xn_fm",
                                   i, tc)
            for dt in range(16):
                ctx.op("sync", "dma_start",
                       dst_ap[rb + dt * 128: rb + (dt + 1) * 128,
                              ch * tc:(ch + 1) * tc],
                       xn_fm[:, dt * tc:(dt + 1) * tc],
                       reads=(f"{tag}xn_fm",), writes=(dst_key,))
    ctx.barrier()
    es.close()



def phase_l0attn(p, xres_src, xres_dst, xnfull_ap, xnfull_key, xnown_ap, xnown_key, tag):
    nc, ctx = p.nc, p.ctx
    es = ExitStack()
    sb = lambda n, s, d: es.enter_context(nc.sbuf_tensor(f"{tag}_{n}", s, d))

    Q = sb("Q", [128, 16 * T], BF16)
    ckv = sb("ckv", [128, 4 * S], BF16)
    krot = sb("krot", [64, S], BF16)

    esB = ExitStack()
    sbB = lambda n, s, d: esB.enter_context(nc.sbuf_tensor(f"{tag}B_{n}", s, d))
    xnq = sbB("xnq", [128, 16 * T], BF16)
    qlat = sbB("qlat", [128, 12 * T], BF16)
    qab = [sbB(f"qab{i}", [128, 2048], BF16) for i in range(2)]
    qbb = [sbB(f"qbb{i}", [128, 1536], BF16) for i in range(2)]
    qs = sbB("qs", [64, T], F32)
    qsw = sbB("qsw", [64, T], F32)
    c0q = sbB("c0q", [64, T], F32)
    s0q = sbB("s0q", [64, T], F32)
    QAT = p.dram_in("l0_qaTt", [12 * 128, 16 * 128])
    QBT = p.dram_in("l0_qbTt", [16 * 128, 12 * 128])
    ctx.op("sync", "dma_start", c0q[:, :], p.dram_in("L0Q_C", [64, T]),
           writes=("c0q",))
    ctx.op("sync", "dma_start", s0q[:, :], p.dram_in("L0Q_S", [64, T]),
           writes=("s0q",))
    for dt in range(16):
        ctx.op("sync", "dma_start", xnq[:, dt * T:(dt + 1) * T],
               xnown_ap[dt * 128:(dt + 1) * 128, :],
               reads=(xnown_key,), writes=("xnq",))
    ctx.op("gpsimd", "dma_start", qab[0][:, :], QAT[0:128, :],
           reads=("QAT",), writes=("qab0",))
    for lt in range(12):
        fb = lt % 2
        if lt + 1 < 12:
            ctx.op("gpsimd", "dma_start", qab[(lt + 1) % 2][:, :],
                   QAT[(lt + 1) * 128:(lt + 2) * 128, :],
                   reads=("QAT",), writes=(f"qab{(lt + 1) % 2}",))
        for half in range(2):
            bb = 2 * half
            for dt in range(16):
                ctx.op("tensor", "matmul", p.pb[bb][:, 0:512],
                       qab[fb][:, dt * 128:(dt + 1) * 128],
                       xnq[:, dt * T + half * 512: dt * T + (half + 1) * 512],
                       start=(dt == 0), stop=(dt == 15),
                       reads=(f"qab{fb}", "xnq"), writes=(f"pb{bb}",),
                       inc=(dt == 15))
            ctx.op("scalar", "copy",
                   qlat[:, lt * T + half * 512: lt * T + (half + 1) * 512],
                   p.pb[bb][:, 0:512], reads=(f"pb{bb}",), writes=("qlat",))
    ctx.op("gpsimd", "dma_start", qbb[0][:, :], QBT[0:128, :],
           reads=("QBT",), writes=("qbb0",))
    for qt in range(16):
        fb = qt % 2
        if qt + 1 < 16:
            ctx.op("gpsimd", "dma_start", qbb[(qt + 1) % 2][:, :],
                   QBT[(qt + 1) * 128:(qt + 2) * 128, :],
                   reads=("QBT",), writes=(f"qbb{(qt + 1) % 2}",))
        for half in range(2):
            bb = 4 + 2 * half
            for lt in range(12):
                ctx.op("tensor", "matmul", p.pb[bb][:, 0:512],
                       qbb[fb][:, lt * 128:(lt + 1) * 128],
                       qlat[:, lt * T + half * 512: lt * T + (half + 1) * 512],
                       start=(lt == 0), stop=(lt == 11),
                       reads=(f"qbb{fb}", "qlat"), writes=(f"pb{bb}",),
                       inc=(lt == 11))
        for half in range(2):
            bb = 4 + 2 * half
            ctx.op("scalar", "copy",
                   Q[0:64, qt * T + half * 512: qt * T + (half + 1) * 512],
                   p.pb[bb][0:64, 0:512], reads=(f"pb{bb}",), writes=("Q",))
            ctx.op("vector", "tensor_copy", qs[:, half * 512:(half + 1) * 512],
                   p.pb[bb][64:128, 0:512], reads=(f"pb{bb}",), writes=("qs",))
        for b in range(1):
            ctx.op("vector", "tensor_copy", qsw[0:32, :], qs[32:64, :],
                   reads=("qs",), writes=("qsw",))
            ctx.op("vector", "tensor_copy", qsw[32:64, :], qs[0:32, :],
                   reads=("qs",), writes=("qsw",))
        ctx.op("vector", "tensor_mul", qs[:, :], qs[:, :], c0q[:, :],
               reads=("qs", "c0q"), writes=("qs",))
        ctx.op("vector", "tensor_mul", qsw[:, :], qsw[:, :], s0q[:, :],
               reads=("qsw", "s0q"), writes=("qsw",))
        ctx.op("vector", "tensor_add", Q[64:128, qt * T:(qt + 1) * T],
               qs[:, :], qsw[:, :], reads=("qs", "qsw"), writes=("Q",))
    ctx.barrier()
    esB.close()

    esC = ExitStack()
    sbC = lambda n, s, d: esC.enter_context(nc.sbuf_tensor(f"{tag}C_{n}", s, d))
    kvaa = sbC("kvaa", [128, 4 * 2048], BF16)
    kvab = sbC("kvab", [128, 16 * 64], BF16)
    xk = [sbC(f"xk{i}", [128, 16 * 512], BF16) for i in range(2)]
    ckc = [sbC(f"ckc{i}", [64, 512], F32) for i in range(2)]
    sks = [sbC(f"sks{i}", [64, 512], F32) for i in range(2)]
    ks = sbC("ks", [64, 512], F32)
    ksw = sbC("ksw", [64, 512], F32)
    ctx.op("gpsimd", "dma_start", kvaa[:, :],
           p.dram_in("l0_kvaTaf", [128, 4 * 2048]), writes=("kvaa",))
    ctx.op("gpsimd", "dma_start", kvab[:, :],
           p.dram_in("l0_kvaTbf", [128, 16 * 64]), writes=("kvab",))
    C0K = p.dram_in("L0K_C", [64, S])
    S0K = p.dram_in("L0K_S", [64, S])

    def load_xk(kc, buf):
        r = kc // 2
        for dt in range(16):
            ctx.op("sync", "dma_start", xk[buf][:, dt * 512:(dt + 1) * 512],
                   xnfull_ap[r * 2048 + dt * 128: r * 2048 + (dt + 1) * 128,
                             (kc % 2) * 512:(kc % 2 + 1) * 512],
                   reads=(xnfull_key,), writes=(f"xk{buf}",))

    load_xk(0, 0)
    ctx.op("sync", "dma_start", ckc[0][:, :], C0K[:, 0:512], writes=("ckc0",))
    ctx.op("sync", "dma_start", sks[0][:, :], S0K[:, 0:512], writes=("sks0",))
    for kc in range(8):
        fb = kc % 2
        if kc + 1 < 8:
            load_xk(kc + 1, (kc + 1) % 2)
            ctx.op("sync", "dma_start", ckc[(kc + 1) % 2][:, :],
                   C0K[:, (kc + 1) * 512:(kc + 2) * 512],
                   writes=(f"ckc{(kc + 1) % 2}",))
            ctx.op("sync", "dma_start", sks[(kc + 1) % 2][:, :],
                   S0K[:, (kc + 1) * 512:(kc + 2) * 512],
                   writes=(f"sks{(kc + 1) % 2}",))
        for ct in range(4):
            for dt in range(16):
                ctx.op("tensor", "matmul", p.pb[ct][:, 0:512],
                       kvaa[:, ct * 2048 + dt * 128: ct * 2048 + (dt + 1) * 128],
                       xk[fb][:, dt * 512:(dt + 1) * 512],
                       start=(dt == 0), stop=(dt == 15),
                       reads=("kvaa", f"xk{fb}"), writes=(f"pb{ct}",),
                       inc=(dt == 15))
        for dt in range(16):
            ctx.op("tensor", "matmul", p.pb[4][0:64, 0:512],
                   kvab[:, dt * 64:(dt + 1) * 64],
                   xk[fb][:, dt * 512:(dt + 1) * 512],
                   start=(dt == 0), stop=(dt == 15),
                   reads=("kvab", f"xk{fb}"), writes=("pb4",), inc=(dt == 15))
        for ct in range(4):
            ctx.op("scalar", "copy", ckv[:, ct * S + kc * 512:(ct * S + (kc + 1) * 512)],
                   p.pb[ct][:, 0:512], reads=(f"pb{ct}",), writes=("ckv",))
        ctx.op("vector", "tensor_copy", ks[:, :], p.pb[4][0:64, 0:512],
               reads=("pb4",), writes=("ks",))
        ctx.op("vector", "tensor_copy", ksw[0:32, :], ks[32:64, :],
               reads=("ks",), writes=("ksw",))
        ctx.op("vector", "tensor_copy", ksw[32:64, :], ks[0:32, :],
               reads=("ks",), writes=("ksw",))
        ctx.op("vector", "tensor_mul", ks[:, :], ks[:, :], ckc[fb][:, :],
               reads=("ks", f"ckc{fb}"), writes=("ks",))
        ctx.op("vector", "tensor_mul", ksw[:, :], ksw[:, :], sks[fb][:, :],
               reads=("ksw", f"sks{fb}"), writes=("ksw",))
        ctx.op("vector", "tensor_add", krot[:, kc * 512:(kc + 1) * 512],
               ks[:, :], ksw[:, :], reads=("ks", "ksw"), writes=("krot",))
    ctx.barrier()
    esC.close()

    esD = ExitStack()
    sbD = lambda n, s, d: esD.enter_context(nc.sbuf_tensor(f"{tag}D_{n}", s, d))
    wkb = sbD("wkb", [128, 16 * 256], BF16)
    wvb = sbD("wvb", [128, 4 * 2048], BF16)
    Vq = sbD("Vq", [128, 32 * 512], BF16)
    Kf = [sbD(f"Kf{i}", [128, S], BF16) for i in range(2)]
    Eb = [sbD(f"Eb{i}", [128, 512], BF16) for i in range(2)]
    den = sbD("den", [1, 512], F32)
    denb = sbD("denb", [1, 512], F32)
    bc = sbD("bc", [128, 512], F32)
    attn_out = sbD("attn_out", [128, 16 * T], BF16)
    ctx.op("gpsimd", "dma_start", wkb[:, :],
           p.dram_in("l0_WkbTf", [128, 16 * 256]), writes=("wkb",))
    ctx.op("gpsimd", "dma_start", wvb[:, :],
           p.dram_in("l0_WvbTf", [128, 4 * 2048]), writes=("wvb",))
    for g in range(4):
        for kt in range(32):
            ctx.op("tensor", "matmul", p.pb[6][:, 0:512],
                   ckv[:, 0 * S + kt * 128: 0 * S + (kt + 1) * 128],
                   wvb[:, 0 * 2048 + g * 512:(0 * 2048 + (g + 1) * 512)],
                   start=True, stop=False,
                   reads=("ckv", "wvb"), writes=("pb6",), inc=False)
            for ct in range(1, 4):
                ctx.op("tensor", "matmul", p.pb[6][:, 0:512],
                       ckv[:, ct * S + kt * 128: ct * S + (kt + 1) * 128],
                       wvb[:, ct * 2048 + g * 512: ct * 2048 + (g + 1) * 512],
                       start=False, stop=(ct == 3),
                       reads=("ckv", "wvb"), writes=("pb6",), inc=(ct == 3))
            ctx.op("scalar", "copy", Vq[:, kt * 512:(kt + 1) * 512],
                   p.pb[6][:, 0:512], reads=("pb6",), writes=("Vq",))
        for hh in range(4):
            h = g * 4 + hh
            hb = hh % 2
            for kc in range(8):
                for ct in range(4):
                    ctx.op("tensor", "matmul", p.pb[7][0:64, 0:512],
                           wkb[:, h * 256 + ct * 64: h * 256 + (ct + 1) * 64],
                           ckv[:, ct * S + kc * 512: ct * S + (kc + 1) * 512],
                           start=(ct == 0), stop=(ct == 3),
                           reads=("wkb", "ckv"), writes=("pb7",), inc=(ct == 3))
                ctx.op("scalar", "copy", Kf[hb][0:64, kc * 512:(kc + 1) * 512],
                       p.pb[7][0:64, 0:512], reads=("pb7",), writes=(f"Kf{hb}",))
            ctx.op("vector", "tensor_copy", Kf[hb][64:128, :], krot[:, :],
                   reads=("krot",), writes=(f"Kf{hb}",))
            for u in range(2):
                for kt in range(32):
                    sb_ = 4 + kt % 2
                    ctx.op("tensor", "matmul", p.pb[sb_][:, 0:512],
                           Kf[hb][:, kt * 128:(kt + 1) * 128],
                           Q[:, h * T + u * 512: h * T + (u + 1) * 512],
                           start=True, stop=True,
                           reads=(f"Kf{hb}", "Q"), writes=(f"pb{sb_}",))
                    ctx.op("scalar", "activation", Eb[kt % 2][:, :],
                           p.pb[sb_][:, 0:512], AF.Exp, bias=-20.0,
                           reads=(f"pb{sb_}",), writes=(f"Eb{kt % 2}",))
                    ctx.op("tensor", "matmul", p.pb[2][:, 0:512],
                           Vq[:, kt * 512 + hh * 128: kt * 512 + (hh + 1) * 128],
                           Eb[kt % 2][:, :], start=(kt == 0), stop=(kt == 31),
                           reads=("Vq", f"Eb{kt % 2}"), writes=("pb2",),
                           inc=(kt == 31))
                    ctx.op("tensor", "matmul", p.pb[3][0:1, 0:512],
                           p.c_ones[:, :], Eb[kt % 2][:, :],
                           start=(kt == 0), stop=(kt == 31),
                           reads=("c:ONES128", f"Eb{kt % 2}"), writes=("pb3",),
                           inc=(kt == 31))
                ctx.op("vector", "reciprocal", den[:, :], p.pb[3][0:1, 0:512],
                       reads=("pb3",), writes=("den",))
                ctx.op("tensor", "matmul", p.pb[6][:, 0:512],
                       p.c_onesrowf[:, :], den[:, :], start=True, stop=True,
                       reads=("c:ONESROWF", "den"), writes=("pb6",))
                ctx.op("vector", "tensor_copy", bc[:, :], p.pb[6][:, 0:512],
                       reads=("pb6",), writes=("bc",))
                ctx.op("vector", "tensor_mul",
                       attn_out[:, h * T + u * 512: h * T + (u + 1) * 512],
                       p.pb[2][:, 0:512], bc[:, :],
                       reads=("pb2", "bc"), writes=("attn_out",))
    ctx.barrier()
    esD.close()

    esE = ExitStack()
    sbE = lambda n, s, d: esE.enter_context(nc.sbuf_tensor(f"{tag}E_{n}", s, d))
    owb = [sbE(f"owb{i}", [128, D], BF16) for i in range(2)]
    xo = sbE("xo", [128, D], F32)
    xob = [sbE(f"xob{i}", [128, D], F32) for i in range(2)]
    OWT = p.dram_in("l0_owT", [D, D])
    for tt in range(8):
        ctx.op("sync", "dma_start", xob[tt % 2][:, :],
               xres_src[tt * 128:(tt + 1) * 128, :],
               reads=("xres_src",), writes=(f"xob{tt % 2}",))
        ctx.op("gpsimd", "dma_start", owb[0][:, :], OWT[0:128, :],
               reads=("OWT",), writes=("owb0",))
        for ht in range(16):
            fb = ht % 2
            if ht + 1 < 16:
                ctx.op("gpsimd", "dma_start", owb[(ht + 1) % 2][:, :],
                       OWT[(ht + 1) * 128:(ht + 2) * 128, :],
                       reads=("OWT",), writes=(f"owb{(ht + 1) % 2}",))
            for dc in range(4):
                b = 4 * (tt % 2) + dc
                ctx.op("tensor", "matmul", p.pb[b][:, 0:512],
                       attn_out[:, ht * T + tt * 128: ht * T + (tt + 1) * 128],
                       owb[fb][:, dc * 512:(dc + 1) * 512],
                       start=(ht == 0), stop=(ht == 15),
                       reads=("attn_out", f"owb{fb}"), writes=(f"pb{b}",),
                       inc=(dc == 3))
        for dc in range(4):
            b = 4 * (tt % 2) + dc
            ctx.op("vector", "tensor_add", xo[:, dc * 512:(dc + 1) * 512],
                   p.pb[b][:, 0:512], xob[tt % 2][:, dc * 512:(dc + 1) * 512],
                   reads=(f"pb{b}", f"xob{tt % 2}"), writes=("xo",))
        ctx.op("sync", "dma_start", xres_dst[tt * 128:(tt + 1) * 128, :],
               xo[:, :], reads=("xo",), writes=("xres_dst",))
    ctx.barrier()
    esE.close()
    es.close()



def phase_l1attn(p, xres_src, xres_dst, xnfull_ap, xnfull_key, xnown_ap, xnown_key, tag):
    nc, ctx = p.nc, p.ctx
    es = ExitStack()
    sb = lambda n, s, d: es.enter_context(nc.sbuf_tensor(f"{tag}_{n}", s, d))
    Q = sb("Q", [128, 16 * T], BF16)
    Ksb = sb("Ksb", [128, 4 * S], BF16)
    Vsb = sb("Vsb", [128, 32 * 520], BF16)
    attn_out = sb("attn_out", [128, 16 * T], BF16)

    esB = ExitStack()
    sbB = lambda n, s, d: esB.enter_context(nc.sbuf_tensor(f"{tag}B_{n}", s, d))
    xnq = sbB("xnq", [128, 16 * T], BF16)
    wqb = [sbB(f"wqb{i}", [128, 2048], BF16) for i in range(2)]
    qs = sbB("qs", [128, T], F32)
    qsw = sbB("qsw", [128, T], F32)
    c1q = sbB("c1q", [128, T], F32)
    s1q = sbB("s1q", [128, T], F32)
    WQT = p.dram_in("l1_WqTt", [16 * 128, 16 * 128])
    ctx.op("sync", "dma_start", c1q[:, :], p.dram_in("L1Q_C", [128, T]),
           writes=("c1q",))
    ctx.op("sync", "dma_start", s1q[:, :], p.dram_in("L1Q_S", [128, T]),
           writes=("s1q",))
    for dt in range(16):
        ctx.op("sync", "dma_start", xnq[:, dt * T:(dt + 1) * T],
               xnown_ap[dt * 128:(dt + 1) * 128, :],
               reads=(xnown_key,), writes=("xnq",))
    ctx.op("gpsimd", "dma_start", wqb[0][:, :], WQT[0:128, :],
           reads=("WQT",), writes=("wqb0",))
    for qt in range(16):
        fb = qt % 2
        if qt + 1 < 16:
            ctx.op("gpsimd", "dma_start", wqb[(qt + 1) % 2][:, :],
                   WQT[(qt + 1) * 128:(qt + 2) * 128, :],
                   reads=("WQT",), writes=(f"wqb{(qt + 1) % 2}",))
        for half in range(2):
            bb = 2 * half
            for dt in range(16):
                ctx.op("tensor", "matmul", p.pb[bb][:, 0:512],
                       wqb[fb][:, dt * 128:(dt + 1) * 128],
                       xnq[:, dt * T + half * 512: dt * T + (half + 1) * 512],
                       start=(dt == 0), stop=(dt == 15),
                       reads=(f"wqb{fb}", "xnq"), writes=(f"pb{bb}",),
                       inc=(dt == 15))
            ctx.op("vector", "tensor_copy", qs[:, half * 512:(half + 1) * 512],
                   p.pb[bb][:, 0:512], reads=(f"pb{bb}",), writes=("qs",))
        for b in range(4):
            ctx.op("vector", "tensor_copy", qsw[b * 32:(b + 1) * 32, :],
                   qs[b * 32 + 32 - 64 * (b % 2):b * 32 + 64 - 64 * (b % 2), :],
                   reads=("qs",), writes=("qsw",))
        ctx.op("vector", "tensor_mul", qs[:, :], qs[:, :], c1q[:, :],
               reads=("qs", "c1q"), writes=("qs",))
        ctx.op("vector", "tensor_mul", qsw[:, :], qsw[:, :], s1q[:, :],
               reads=("qsw", "s1q"), writes=("qsw",))
        ctx.op("vector", "tensor_add", Q[:, qt * T:(qt + 1) * T],
               qs[:, :], qsw[:, :], reads=("qs", "qsw"), writes=("Q",))
    ctx.barrier()
    esB.close()

    esC = ExitStack()
    sbC = lambda n, s, d: esC.enter_context(nc.sbuf_tensor(f"{tag}C_{n}", s, d))
    wk = sbC("wk", [128, 4 * 2048], BF16)
    wv = sbC("wv", [128, 16 * 512], BF16)
    xk = [sbC(f"xk{i}", [128, 16 * 512], BF16) for i in range(2)]
    c1k = [sbC(f"c1k{i}", [128, 512], F32) for i in range(2)]
    s1k = [sbC(f"s1k{i}", [128, 512], F32) for i in range(2)]
    ks = sbC("ks", [128, 512], F32)
    ksw = sbC("ksw", [128, 512], F32)
    ctx.op("gpsimd", "dma_start", wk[:, :],
           p.dram_in("l1_WkTf", [128, 4 * 2048]), writes=("wk",))
    ctx.op("gpsimd", "dma_start", wv[:, :],
           p.dram_in("l1_WvTf", [128, 16 * 512]), writes=("wv",))
    C1K = p.dram_in("L1K_C", [128, S])
    S1K = p.dram_in("L1K_S", [128, S])
    ctx.op("vector", "memset",
           Vsb[:, :].rearrange("p (t g c) -> p t g c", g=8, c=65)[:, :, :, 64:65],
           1.0, writes=("Vsb",))

    def load_xk(kc, buf):
        r = kc // 2
        for dt in range(16):
            ctx.op("sync", "dma_start", xk[buf][:, dt * 512:(dt + 1) * 512],
                   xnfull_ap[r * 2048 + dt * 128: r * 2048 + (dt + 1) * 128,
                             (kc % 2) * 512:(kc % 2 + 1) * 512],
                   reads=(xnfull_key,), writes=(f"xk{buf}",))

    load_xk(0, 0)
    ctx.op("sync", "dma_start", c1k[0][:, :], C1K[:, 0:512], writes=("c1k0",))
    ctx.op("sync", "dma_start", s1k[0][:, :], S1K[:, 0:512], writes=("s1k0",))
    for kc in range(8):
        fb = kc % 2
        if kc + 1 < 8:
            load_xk(kc + 1, (kc + 1) % 2)
            ctx.op("sync", "dma_start", c1k[(kc + 1) % 2][:, :],
                   C1K[:, (kc + 1) * 512:(kc + 2) * 512],
                   writes=(f"c1k{(kc + 1) % 2}",))
            ctx.op("sync", "dma_start", s1k[(kc + 1) % 2][:, :],
                   S1K[:, (kc + 1) * 512:(kc + 2) * 512],
                   writes=(f"s1k{(kc + 1) % 2}",))
        for kb in range(4):
            for dt in range(16):
                ctx.op("tensor", "matmul", p.pb[kb][:, 0:512],
                       wk[:, kb * 2048 + dt * 128: kb * 2048 + (dt + 1) * 128],
                       xk[fb][:, dt * 512:(dt + 1) * 512],
                       start=(dt == 0), stop=(dt == 15),
                       reads=("wk", f"xk{fb}"), writes=(f"pb{kb}",),
                       inc=(dt == 15))
            ctx.op("vector", "tensor_copy", ks[:, :], p.pb[kb][:, 0:512],
                   reads=(f"pb{kb}",), writes=("ks",))
            for b in range(4):
                ctx.op("vector", "tensor_copy", ksw[b * 32:(b + 1) * 32, :],
                       ks[b * 32 + 32 - 64 * (b % 2):b * 32 + 64 - 64 * (b % 2), :],
                       reads=("ks",), writes=("ksw",))
            ctx.op("vector", "tensor_mul", ks[:, :], ks[:, :], c1k[fb][:, :],
                   reads=("ks", f"c1k{fb}"), writes=("ks",))
            ctx.op("vector", "tensor_mul", ksw[:, :], ksw[:, :], s1k[fb][:, :],
                   reads=("ksw", f"s1k{fb}"), writes=("ksw",))
            ctx.op("vector", "tensor_add",
                   Ksb[:, kb * S + kc * 512: kb * S + (kc + 1) * 512],
                   ks[:, :], ksw[:, :], reads=("ks", "ksw"), writes=("Ksb",))
        for k4 in range(4):
            kt = kc * 4 + k4
            for dt in range(16):
                ctx.op("tensor", "matmul", p.pb[4][:, 0:512],
                       xk[fb][:, dt * 512 + k4 * 128: dt * 512 + (k4 + 1) * 128],
                       wv[:, dt * 512:(dt + 1) * 512],
                       start=(dt == 0), stop=(dt == 15),
                       reads=(f"xk{fb}", "wv"), writes=("pb4",), inc=(dt == 15))
            ctx.op("vector", "tensor_copy",
                   Vsb[:, kt * 520:(kt + 1) * 520]
                   .rearrange("p (g c) -> p g c", c=65)[:, :, 0:64],
                   p.pb[4][:, 0:512].rearrange("p (g c) -> p g c", c=64),
                   reads=("pb4",), writes=("Vsb",))
    ctx.barrier()
    esC.close()

    esD = ExitStack()
    sbD = lambda n, s, d: esD.enter_context(nc.sbuf_tensor(f"{tag}D_{n}", s, d))
    Eb = [sbD(f"Eb{i}", [128, 512], BF16) for i in range(2)]
    den = sbD("den", [1, 512], F32)
    bc = sbD("bc", [64, 512], F32)
    for kvh in range(8):
        for u in range(2):
            for hh in range(4):
                h = kvh * 4 + hh
                qt_, qb_ = l1_qpos(h)
                assert qb_ == (kvh % 2) * 64
                for kt in range(32):
                    sb_ = 4 + kt % 2
                    ctx.op("tensor", "matmul", p.pb[sb_][:, 0:512],
                           Ksb[(kvh % 2) * 64:(kvh % 2) * 64 + 64,
                               (kvh // 2) * S + kt * 128:(kvh // 2) * S + (kt + 1) * 128],
                           Q[qb_:qb_ + 64,
                             qt_ * T + u * 512: qt_ * T + (u + 1) * 512],
                           start=True, stop=True,
                           reads=("Ksb", "Q"), writes=(f"pb{sb_}",))
                    ctx.op("scalar", "activation", Eb[kt % 2][:, :],
                           p.pb[sb_][:, 0:512], AF.Exp, bias=-20.0,
                           reads=(f"pb{sb_}",), writes=(f"Eb{kt % 2}",))
                    ctx.op("tensor", "matmul", p.pb[2][0:65, 0:512],
                           Vsb[:, kt * 520 + kvh * 65: kt * 520 + (kvh + 1) * 65],
                           Eb[kt % 2][:, :], start=(kt == 0), stop=(kt == 31),
                           reads=("Vsb", f"Eb{kt % 2}"), writes=("pb2",),
                           inc=(kt == 31))
                ctx.op("vector", "reciprocal", den[:, :], p.pb[2][64:65, 0:512],
                       reads=("pb2",), writes=("den",))
                ctx.op("tensor", "matmul", p.pb[6][0:64, 0:512],
                       p.c_onesrowf[:, 0:64], den[:, :], start=True, stop=True,
                       reads=("c:ONESROWF", "den"), writes=("pb6",))
                ctx.op("vector", "tensor_copy", bc[:, :], p.pb[6][0:64, 0:512],
                       reads=("pb6",), writes=("bc",))
                ctx.op("vector", "tensor_mul",
                       attn_out[qb_:qb_ + 64,
                                qt_ * T + u * 512: qt_ * T + (u + 1) * 512],
                       p.pb[2][0:64, 0:512], bc[:, :],
                       reads=("pb2", "bc"), writes=("attn_out",))
    ctx.barrier()
    esD.close()

    esE = ExitStack()
    sbE = lambda n, s, d: esE.enter_context(nc.sbuf_tensor(f"{tag}E_{n}", s, d))
    owb = [sbE(f"owb{i}", [128, D], BF16) for i in range(2)]
    xo = sbE("xo", [128, D], F32)
    xob = [sbE(f"xob{i}", [128, D], F32) for i in range(2)]
    OWT = p.dram_in("l1_WoT", [D, D])
    for tt in range(8):
        ctx.op("sync", "dma_start", xob[tt % 2][:, :],
               xres_src[tt * 128:(tt + 1) * 128, :],
               reads=("xres_src",), writes=(f"xob{tt % 2}",))
        ctx.op("gpsimd", "dma_start", owb[0][:, :], OWT[0:128, :],
               reads=("OWT1",), writes=("owb0",))
        for ht in range(16):
            fb = ht % 2
            if ht + 1 < 16:
                ctx.op("gpsimd", "dma_start", owb[(ht + 1) % 2][:, :],
                       OWT[(ht + 1) * 128:(ht + 2) * 128, :],
                       reads=("OWT1",), writes=(f"owb{(ht + 1) % 2}",))
            for dc in range(4):
                b = 4 * (tt % 2) + dc
                ctx.op("tensor", "matmul", p.pb[b][:, 0:512],
                       attn_out[:, ht * T + tt * 128: ht * T + (tt + 1) * 128],
                       owb[fb][:, dc * 512:(dc + 1) * 512],
                       start=(ht == 0), stop=(ht == 15),
                       reads=("attn_out", f"owb{fb}"), writes=(f"pb{b}",),
                       inc=(dc == 3))
        for dc in range(4):
            b = 4 * (tt % 2) + dc
            ctx.op("vector", "tensor_add", xo[:, dc * 512:(dc + 1) * 512],
                   p.pb[b][:, 0:512], xob[tt % 2][:, dc * 512:(dc + 1) * 512],
                   reads=(f"pb{b}", f"xob{tt % 2}"), writes=("xo",))
        ctx.op("sync", "dma_start", xres_dst[tt * 128:(tt + 1) * 128, :],
               xo[:, :], reads=("xo",), writes=("xres_dst",))
    ctx.barrier()
    esE.close()
    es.close()




def build_fused():
    nc = bass.Bass("TRN2", target_bir_lowering=False)
    p = Prog(nc)
    xin = p.dram_in("x_in", [T, D], F32)
    xout = p.dram_out("x_out", [T, D], F32)
    p.setup_common()
    xa = p.dram_int("xa", [T, D], F32).ap()
    xb = p.dram_int("xb", [T, D], F32).ap()
    w2p = p.dram_int("w2p", [FF, D], BF16)
    cc_in = p.dram_int("cc_in", [2048, T], BF16)
    cc_out = p.dram_int("cc_out", [8192, T], BF16)
    ctx = p.ctx

    def exchange(src_ap, src_key, tag):
        phase_xnfull(p, src_ap, src_key, cc_in.ap(), "cc_in", [(0, 0)], tag)
        ctx.op("gpsimd", "collective_compute", "AllGather", ALU.bypass,
               replica_groups=[[0, 1, 2, 3], [4, 5, 6, 7]],
               ins=[cc_in.ap().opt()], outs=[cc_out.ap().opt()],
               reads=("cc_in",), writes=("cc_out",))
        ctx.barrier()

    cvt = [("l0_W1Tt", [64 * 128, 16 * 128]), ("l0_W2T", [FF, D]),
           ("l1_W1Tt", [64 * 128, 16 * 128]), ("l1_W3Tt", [64 * 128, 16 * 128])]
    p.wc = {}
    items = []
    for nm, shp in cvt:
        h = p.dram_int(f"wc_{nm}", shp, BF16)
        p.wc[nm] = h
        items.append((nm, shp, h))
    phase_bf16conv(p, items)
    phase_w2prep(p, w2p)
    phase_l0ffn(p, xin, xa, "f0a")
    exchange(xa, "xa", "xf0")
    phase_l0attn(p, xa, xb, cc_out.ap(), "cc_out", cc_in.ap(), "cc_in", "a0")
    phase_l0ffn(p, xb, xa, "f0b")
    phase_l1ffn(p, xa, xb, w2p, "f1a")
    exchange(xb, "xb", "xf1")
    phase_l1attn(p, xb, xa, cc_out.ap(), "cc_out", cc_in.ap(), "cc_in", "a1")
    phase_l1ffn(p, xa, xout, w2p, "f1b")
    return p





def kernel(**inputs):
    inputs = {k: np.asarray(v) for k, v in inputs.items()}
    W = prep_weights(inputs)
    x = np.ascontiguousarray(inputs["L0_attn_x"]).astype(np.float32)
    p = build_fused()
    in_maps = []
    for c in range(8):
        b, r = c // 4, c % 4
        consts = host_consts(c)
        m = {}
        for name in p.inp:
            if name in consts:
                m[name] = consts[name]
            elif name in W:
                m[name] = W[name]
            elif name == "x_in":
                m[name] = np.ascontiguousarray(x[b, r * 1024:(r + 1) * 1024, :])
            else:
                raise KeyError(name)
        in_maps.append(m)
    res = run_bass_kernel_spmd(p.nc, in_maps, core_ids=list(range(8)))
    out = np.empty_like(x)
    for c in range(8):
        b, r = c // 4, c % 4
        out[b, r * 1024:(r + 1) * 1024, :] = res.results[c]["x_out"]
    return out


# version 05
# speedup vs baseline: 1.2942x; 1.1887x over previous
import math
from contextlib import ExitStack

import numpy as np
import ml_dtypes
import concourse.bass as bass
import concourse.mybir as mybir
from concourse.bass_utils import run_bass_kernel_spmd

F32 = mybir.dt.float32
BF16 = mybir.dt.bfloat16
AF = mybir.ActivationFunctionType
ALU = mybir.AluOpType
AXX = mybir.AxisListType.X

T = 1024
D = 2048
FF = 8192
S = 4096

CVALS = [0.0, 1.0, -1.0, 0.5, 2.0, -20.0, 1e-5,
         float(np.float32(0.5) * np.float32(1.0 / np.sqrt(8192.0))),
         -1e9]


ENGINES = ("tensor", "vector", "scalar", "gpsimd", "sync")

L1_EVENKV = [h for h in range(32) if (h // 4) % 2 == 0]
L1_ODDKV = [h for h in range(32) if (h // 4) % 2 == 1]


def l1_qpos(h):
    if (h // 4) % 2 == 0:
        return L1_EVENKV.index(h), 0
    return L1_ODDKV.index(h), 64


class Ctx:

    def __init__(self, nc):
        self.nc = nc
        self.es = ExitStack()
        self.eng = dict(tensor=nc.tensor, vector=nc.vector, scalar=nc.scalar,
                        gpsimd=nc.gpsimd, sync=nc.sync)
        self.sem = {e: self.es.enter_context(nc.semaphore(f"sem_{e}")) for e in ENGINES}
        self.cnt = {e: 0 for e in ENGINES}
        self.waited = {}
        self.res = {}

    def sb(self, name, shape, dtype):
        return self.es.enter_context(self.nc.sbuf_tensor(name, shape, dtype))

    def ps(self, name, shape, dtype):
        return self.es.enter_context(self.nc.psum_tensor(name, shape, dtype))

    def _dma_semkey(self, key, eng):
        sk = f"d:{eng}:{key}"
        if sk not in self.sem:
            self.sem[sk] = self.es.enter_context(self.nc.semaphore(sk))
            self.cnt[sk] = 0
        return sk

    def _need(self, eng, key, write, need):
        r = self.res.get(key)
        if r is None:
            return
        w = r.get("w")
        if w is not None:
            wsem, wval = w
            if not (wsem == "tensor" and eng == "tensor"):
                need[wsem] = max(need.get(wsem, 0), wval)
        if write:
            for rsem, rval in r.get("r", {}).items():
                if rsem == "tensor" and eng == "tensor":
                    continue
                need[rsem] = max(need.get(rsem, 0), rval)

    def _record(self, semkey, key, write, val):
        r = self.res.setdefault(key, {"w": None, "r": {}})
        if write:
            r["w"] = (semkey, val)
            r["r"] = {}
        else:
            r["r"][semkey] = max(r["r"].get(semkey, 0), val)

    def _emit_waits(self, eng, need):
        e = self.eng[eng]
        for tgt, val in sorted(need.items()):
            if val <= 0 or self.waited.get((eng, tgt), 0) >= val:
                continue
            if val > self.cnt[tgt]:
                raise RuntimeError(f"{eng} waits {tgt}>={val} but cnt={self.cnt[tgt]}")
            e.wait_ge(self.sem[tgt], val)
            self.waited[(eng, tgt)] = val

    def op(self, eng, method, *args, reads=(), writes=(), inc=True, **kw):
        need = {}
        for k in reads:
            self._need(eng, k, False, need)
        for k in writes:
            self._need(eng, k, True, need)
        self._emit_waits(eng, need)
        ins = getattr(self.eng[eng], method)(*args, **kw)
        is_dma = method in ("dma_start", "collective_compute", "dma_start_transpose")
        if is_dma:
            assert writes, f"dma op needs a write resource ({method})"
            semkey = self._dma_semkey(writes[0], eng)
            amt = 1 if method == "collective_compute" else 16
            ins.then_inc(self.sem[semkey], amt)
            self.cnt[semkey] += amt
            val = self.cnt[semkey]
        elif inc:
            semkey = eng
            ins.then_inc(self.sem[eng], 1)
            self.cnt[eng] += 1
            val = self.cnt[eng]
        else:
            semkey = eng
            val = self.cnt[eng] + 1
        for k in reads:
            self._record(semkey, k, False, val)
        for k in writes:
            self._record(semkey, k, True, val)
        return ins

    def barrier(self):
        for a in ENGINES:
            for b, cnt in self.cnt.items():
                if a == b or cnt == 0:
                    continue
                if self.waited.get((a, b), 0) < cnt:
                    self.eng[a].wait_ge(self.sem[b], cnt)
                    self.waited[(a, b)] = cnt
        self.res.clear()



def _stat_tiles(wT, kt=128, ct=128):
    K, M = wT.shape
    a = wT.reshape(K // kt, kt, M // ct, ct).transpose(2, 1, 0, 3)
    return np.ascontiguousarray(a.reshape(M // ct * kt, (K // kt) * ct))


def _stat_flat(wT, kt=128, ct=128):
    K, M = wT.shape
    a = wT.reshape(K // kt, kt, M // ct, ct).transpose(1, 2, 0, 3)
    return np.ascontiguousarray(a.reshape(kt, (M // ct) * (K // kt) * ct))


def prep_weights(w):
    o = {}
    o["l0_W1Tt"] = _stat_tiles(w["L0_ffn_W1"].T)
    o["l0_W2T"] = np.ascontiguousarray(w["L0_ffn_W2"].T)
    o["l0_WrTt"] = _stat_tiles(w["L0_ffn_W_r"].T, ct=8)
    o["l0_br"] = np.ascontiguousarray(w["L0_ffn_b_r"].reshape(1, 8))
    o["l0_A1Tt"] = _stat_tiles(w["L0_ffn_A1"].reshape(64, D).T, ct=64)
    B1rf = w["L0_ffn_B1"].transpose(0, 2, 1).reshape(64, FF)
    o["l0_B1aug"] = np.ascontiguousarray(
        np.concatenate([B1rf, w["L0_ffn_b1"].reshape(1, FF)], 0))
    o["l0_A2Tt"] = _stat_tiles(w["L0_ffn_A2"].reshape(64, FF).T, ct=64)
    B2rf = w["L0_ffn_B2"].transpose(0, 2, 1).reshape(64, D)
    o["l0_B2aug"] = np.ascontiguousarray(
        np.concatenate([B2rf, w["L0_ffn_b2"].reshape(1, D)], 0))
    H0, QKD, NOPE, VD0, QR, KVR = 16, 128, 64, 128, 1536, 512
    o["l0_qaTt"] = _stat_tiles(w["L0_attn_q_a_w"].T)
    qb = w["L0_attn_q_b_w"].reshape(H0, QKD, QR)
    perm = (list(range(NOPE)) + [NOPE + 2 * i for i in range(32)]
            + [NOPE + 2 * i + 1 for i in range(32)])
    o["l0_qbTt"] = _stat_tiles(
        np.ascontiguousarray(qb[:, perm, :].reshape(H0 * QKD, QR).T))
    kva = w["L0_attn_kv_a_w"]
    rot = kva[KVR:]
    kva_p = np.concatenate([kva[:KVR], rot[0::2], rot[1::2]], 0)
    kvaT = np.ascontiguousarray(kva_p.T)
    o["l0_kvaTaf"] = _stat_flat(kvaT[:, 0:KVR])
    o["l0_kvaTbf"] = _stat_flat(kvaT[:, KVR:], ct=64)
    kvb = w["L0_attn_kv_b_w"].reshape(H0, NOPE + VD0, KVR)
    o["l0_WkbTf"] = _stat_flat(
        np.ascontiguousarray(kvb[:, :NOPE, :].reshape(H0 * NOPE, KVR).T), ct=64)
    o["l0_WvbTf"] = _stat_flat(
        np.ascontiguousarray(kvb[:, NOPE:, :].reshape(H0 * VD0, KVR).T), ct=2048)
    o["l0_owT"] = np.ascontiguousarray(w["L0_attn_o_w"].T)
    order = []
    for j in range(16):
        order += [L1_EVENKV[j], L1_ODDKV[j]]
    rowperm = np.concatenate([np.arange(h * 64, (h + 1) * 64) for h in order])
    o["l1_WqTt"] = _stat_tiles(np.ascontiguousarray(w["L1_attn_Wq"][rowperm].T))
    o["l1_WkTf"] = _stat_flat(w["L1_attn_Wk"].T)
    o["l1_WvTf"] = _stat_flat(w["L1_attn_Wv"].T, ct=512)
    o["l1_WoT"] = np.ascontiguousarray(w["L1_attn_Wo"].T[rowperm])
    o["l1_W1Tt"] = _stat_tiles(w["L1_ffn_W1"].T)
    o["l1_W3Tt"] = _stat_tiles(w["L1_ffn_W3"].T)
    o["l1_W2G"] = np.ascontiguousarray(w["L1_ffn_W2"].T.reshape(64, 128 * D))
    return o


def host_consts(core_id):
    c = {}
    c["CONSTS"] = np.tile(np.array(CVALS, np.float32), (128, 1))
    c["ID128"] = np.eye(128, dtype=np.float32)
    c["ONESROW"] = np.ones((1, 128), np.float32)
    c["ONESROWF"] = np.ones((1, 128), np.float32)
    c["ONES128"] = np.ones((128, 1), np.float32)
    e8 = np.zeros((8, 64), np.float32)
    for e in range(8):
        e8[e, e * 8:(e + 1) * 8] = 2.0
    c["E8X2"] = e8
    c["EPSIDX"] = (np.arange(8, dtype=np.float32) * np.float32(-1e-6)).reshape(1, 8)

    def had(n):
        h = np.array([[1.0]], np.float32)
        while h.shape[0] < n:
            h = np.block([[h, h], [h, -h]])
        return np.ascontiguousarray(h)
    c["H128"] = had(128)
    c["H64"] = had(64)
    q0 = ((core_id % 4) * T + np.arange(T)).astype(np.float32)
    kpos = np.arange(S, dtype=np.float32)
    th = (np.float32(1.0)
          / np.float32(10000.0) ** (np.arange(0, 64, 2, dtype=np.float32)
                                    / np.float32(64.0))).astype(np.float32)
    for nm, pos, scl, rep in (("L0Q", q0, 1.0, 1), ("L0K", kpos, 1.0, 1),
                              ("L1Q", q0, 0.125, 2), ("L1K", kpos, 1.0, 2)):
        ang = (pos[None, :] * th[:, None]).astype(np.float32)
        cs = np.cos(ang).astype(np.float32) * np.float32(scl)
        sn = np.sin(ang).astype(np.float32) * np.float32(scl)
        blkC = np.concatenate([cs, cs], 0)
        blkS = np.concatenate([-sn, sn], 0)
        c[f"{nm}_C"] = np.ascontiguousarray(np.concatenate([blkC] * rep, 0))
        c[f"{nm}_S"] = np.ascontiguousarray(np.concatenate([blkS] * rep, 0))
    return c



class Prog:
    def __init__(self, nc):
        self.nc = nc
        self.ctx = Ctx(nc)
        self.inp = {}

    def dram_in(self, name, shape, dtype=F32):
        if name in self.inp:
            return self.inp[name]
        t = self.nc.dram_tensor(name, list(shape), dtype, kind="ExternalInput")
        self.inp[name] = t.ap()
        return self.inp[name]

    def dram_out(self, name, shape, dtype=F32):
        t = self.nc.dram_tensor(name, list(shape), dtype, kind="ExternalOutput")
        return t.ap()

    def dram_int(self, name, shape, dtype):
        return self.nc.dram_tensor(name, list(shape), dtype, kind="Internal")

    def setup_common(self):
        ctx = self.ctx
        self.psa = ctx.ps("PSA", [128, 4096], F32)
        self.pb = [self.psa[:, i * 512:(i + 1) * 512] for i in range(8)]
        self.c_consts = ctx.sb("c_consts", [128, len(CVALS)], F32)
        for j, v in enumerate(CVALS):
            self.nc.const_aps.register(F32, v, self.c_consts[:, j:j + 1])
        self.CI = {v: j for j, v in enumerate(CVALS)}
        self.c_id = ctx.sb("c_id", [128, 128], F32)
        self.c_onesrow = ctx.sb("c_onesrow", [1, 128], BF16)
        self.c_onesrowf = ctx.sb("c_onesrowf", [1, 128], F32)
        self.c_ones = ctx.sb("c_ones", [128, 1], BF16)
        self.c_e8 = ctx.sb("c_e8", [8, 64], BF16)
        self.c_eps = ctx.sb("c_eps", [1, 8], F32)
        self.c_h128 = ctx.sb("c_h128", [128, 128], BF16)
        self.c_h64 = ctx.sb("c_h64", [64, 64], BF16)
        loads = [("CONSTS", self.c_consts, [128, len(CVALS)]),
                 ("ID128", self.c_id, [128, 128]),
                 ("ONESROW", self.c_onesrow, [1, 128]),
                 ("ONESROWF", self.c_onesrowf, [1, 128]),
                 ("ONES128", self.c_ones, [128, 1]),
                 ("E8X2", self.c_e8, [8, 64]),
                 ("EPSIDX", self.c_eps, [1, 8]),
                 ("H128", self.c_h128, [128, 128]),
                 ("H64", self.c_h64, [64, 64])]
        for nm, dst, shp in loads:
            src = self.dram_in(nm, shp, F32)
            ctx.op("gpsimd", "dma_start", dst[:, :], src, writes=(f"c:{nm}",))
        ctx.barrier()

    def const(self, v):
        j = self.CI[v]
        return self.c_consts[:, j:j + 1]

    def wsrc(self, name, shape):
        wc = getattr(self, "wc", None)
        if wc and name in wc:
            return wc[name].ap(), "scalar"
        return self.dram_in(name, shape), "gpsimd"



def ln_tile(p, x_ap, x_key, xn_tm, xn_key, stat, stat_key, tag):
    ctx = p.ctx
    bn = stat
    for j in range(4):
        ctx.op("vector", "bn_stats", stat[:, 8 + j * 6: 8 + (j + 1) * 6],
               x_ap[:, j * 512:(j + 1) * 512],
               reads=(x_key,), writes=(stat_key,))
    ctx.op("vector", "bn_aggr", stat[:, 0:2], stat[:, 8:32],
           reads=(stat_key,), writes=(stat_key,))
    ctx.op("scalar", "activation", stat[:, 2:3], stat[:, 1:2], AF.Sqrt,
           bias=1e-5, reads=(stat_key,), writes=(stat_key,))
    ctx.op("vector", "reciprocal", stat[:, 3:4], stat[:, 2:3],
           reads=(stat_key,), writes=(stat_key,))
    ctx.op("vector", "tensor_scalar", xn_tm[:, :], x_ap, stat[:, 0:1], stat[:, 3:4],
           ALU.subtract, ALU.mult, reads=(x_key, stat_key), writes=(xn_key,))


def transpose_tm_to_fm(p, src_tm, src_key, dst_fm, dst_key, ti, tc, n_dt=16,
                       dst_bf16=True, banks=(0, 1)):
    ctx = p.ctx
    for g in range(n_dt // 4):
        b = p.pb[banks[g % len(banks)]]
        bk = f"pb{banks[g % len(banks)]}"
        for q in range(4):
            dt = g * 4 + q
            ctx.op("tensor", "transpose", b[:, q * 128:(q + 1) * 128],
                   src_tm[:, dt * 128:(dt + 1) * 128], p.c_id[:, :],
                   reads=(src_key, "c:ID128"), writes=(bk,), inc=(q == 3))
        for q in range(4):
            dt = g * 4 + q
            ctx.op("scalar", "copy",
                   dst_fm[:, dt * tc + ti * 128: dt * tc + (ti + 1) * 128],
                   b[:, q * 128:(q + 1) * 128],
                   reads=(bk,), writes=(dst_key,))


def mm_movs(p, psum_ap, bank_keys, stat_ap, stat_keys, mov_ap, mov_keys, ncols,
            start, stop, inc_last=True, tile_position=None):
    ctx = p.ctx
    nch = (ncols + 511) // 512
    for j in range(nch):
        c0, c1 = j * 512, min(ncols, (j + 1) * 512)
        kw = {}
        if tile_position is not None:
            kw["tile_position"] = tile_position
        ctx.op("tensor", "matmul", psum_ap[:, c0:c1], stat_ap, mov_ap[:, c0:c1],
               start=start and j == 0, stop=stop and j == nch - 1,
               reads=tuple(stat_keys) + tuple(mov_keys), writes=tuple(bank_keys),
               inc=inc_last and (j == nch - 1), **kw)



def phase_l0ffn(p, xres_src, xres_dst, tag):
    nc, ctx = p.nc, p.ctx
    tc = 256
    nt = tc // 128
    nch = T // tc
    es = ExitStack()
    sb = lambda n, s, d: es.enter_context(nc.sbuf_tensor(f"{tag}_{n}", s, d))

    wr = sb("wr", [128, 16 * 8], BF16)
    a1t = sb("a1t", [128, 16 * 64], BF16)
    a2t = sb("a2t", [128, 64 * 64], BF16)
    b1aug = sb("b1aug", [65, FF], BF16)
    b2aug = sb("b2aug", [65, D], BF16)
    br = sb("br", [1, 8], BF16)
    epsb = sb("epsb", [128, 8], F32)
    NW = 3
    w1b = [sb(f"w1b{i}", [128, 16 * 128], BF16) for i in range(NW)]
    w2b = [sb(f"w2b{i}", [128, D], BF16) for i in range(NW)]
    xb = [sb(f"xb{i}", [128, D], F32) for i in range(nt)]
    xn_tm = sb("xn_tm", [128, D], F32)
    xn_fm = sb("xn_fm", [128, 16 * tc], BF16)
    stat = sb("stat", [128, 32], F32)
    lg = sb("lg", [128, 8], F32)
    sel = sb("sel", [128, 8], F32)
    m1 = sb("m1", [128, 8], F32)
    m2 = sb("m2", [128, 8], F32)
    ee = sb("ee", [128, 8], F32)
    ee2 = sb("ee2", [128, 8], F32)
    rr = sb("rr", [128, 8], F32)
    w12 = sb("w12", [128, 2 * nt], F32)
    m1t = sb("m1t", [8, tc], BF16)
    m2t = sb("m2t", [8, tc], BF16)
    mex = [sb(f"mex{k}", [64, tc], BF16) for k in range(2)]
    mu1a = [sb(f"mu1a{k}", [65, tc], BF16) for k in range(2)]
    mz2a = [sb(f"mz2a{k}", [65, tc], BF16) for k in range(2)]
    h = [sb(f"h{k}", [128, 64 * tc], BF16) for k in range(2)]
    base1 = [sb(f"base1_{j}", [128, tc], BF16) for j in range(2)]
    v01 = [sb(f"v01_{k}", [128, tc], F32) for k in range(2)]
    yacc = [sb(f"yacc{i}", [128, D], F32) for i in range(nt)]

    W1T, q1 = p.wsrc("l0_W1Tt", [64 * 128, 16 * 128])
    W2T, q2 = p.wsrc("l0_W2T", [FF, D])
    ctx.op("gpsimd", "dma_start", wr[:, :],
           p.dram_in("l0_WrTt", [128, 16 * 8]), writes=("wr",))
    ctx.op("gpsimd", "dma_start", a1t[:, :],
           p.dram_in("l0_A1Tt", [128, 16 * 64]), writes=("a1t",))
    ctx.op("gpsimd", "dma_start", a2t[:, :],
           p.dram_in("l0_A2Tt", [128, 64 * 64]), writes=("a2t",))
    ctx.op("gpsimd", "dma_start", b1aug[:, :], p.dram_in("l0_B1aug", [65, FF]),
           writes=("b1aug",))
    ctx.op("gpsimd", "dma_start", b2aug[:, :], p.dram_in("l0_B2aug", [65, D]),
           writes=("b2aug",))
    ctx.op("gpsimd", "dma_start", br[:, :], p.dram_in("l0_br", [1, 8]),
           writes=("br",))
    epsbf = sb("epsbf", [1, 8], BF16)
    ctx.op("vector", "tensor_copy", epsbf[:, :], p.c_eps[:, :],
           reads=("c:EPSIDX",), writes=("epsbf",))
    ctx.op("tensor", "matmul", p.pb[0][0:128, 0:8], p.c_onesrow[:, :], epsbf[:, :],
           start=True, stop=True, reads=("c:ONESROW", "epsbf"), writes=("pb0",))
    ctx.op("vector", "tensor_copy", epsb[:, :], p.pb[0][0:128, 0:8],
           reads=("pb0",), writes=("epsb",))
    for k in range(2):
        ctx.op("gpsimd", "memset", mu1a[k][64:65, :], 1.0, writes=(f"mu1a{k}",))
        ctx.op("gpsimd", "memset", mz2a[k][64:65, :], 1.0, writes=(f"mz2a{k}",))

    for ch in range(nch):
        tok0 = ch * tc
        for i in range(nt):
            ctx.op("sync", "dma_start", xb[i][:, :],
                   xres_src[tok0 + i * 128: tok0 + (i + 1) * 128, :],
                   reads=("xres_src",), writes=(f"xb{i}",))
        for i in range(nt):
            ln_tile(p, xb[i][:, :], f"xb{i}", xn_tm, "xn_tm", stat, "stat", tag)
            transpose_tm_to_fm(p, xn_tm, "xn_tm", xn_fm, "xn_fm", i, tc)
        for i in range(nt):
            for dt in range(16):
                ctx.op("tensor", "matmul", p.pb[4][0:128, 0:8],
                       xn_fm[:, dt * tc + i * 128: dt * tc + (i + 1) * 128],
                       wr[:, dt * 8:(dt + 1) * 8],
                       start=(dt == 0), stop=False,
                       reads=("xn_fm", "wr"), writes=("pb4",), inc=False)
            ctx.op("tensor", "matmul", p.pb[4][0:128, 0:8], p.c_onesrow[:, :],
                   br[:, :], start=False, stop=True,
                   reads=("c:ONESROW", "br"), writes=("pb4",))
            ctx.op("vector", "tensor_copy", lg[:, :], p.pb[4][0:128, 0:8],
                   reads=("pb4",), writes=("lg",))
            ctx.op("vector", "tensor_add", sel[:, :], lg[:, :], epsb[:, :],
                   reads=("lg", "epsb"), writes=("sel",))
            ctx.op("vector", "reduce_max", rr[:, 0:1], sel[:, :], axis=AXX,
                   reads=("sel",), writes=("rr",))
            ctx.op("vector", "tensor_scalar", m1[:, :], sel[:, :], rr[:, 0:1], None,
                   ALU.is_ge, reads=("sel", "rr"), writes=("m1",))
            ctx.op("vector", "scalar_tensor_tensor", sel[:, :], m1[:, :], -1e9,
                   sel[:, :], ALU.mult, ALU.add,
                   reads=("m1", "sel"), writes=("sel",))
            ctx.op("vector", "reduce_max", rr[:, 1:2], sel[:, :], axis=AXX,
                   reads=("sel",), writes=("rr",))
            ctx.op("vector", "tensor_scalar", m2[:, :], sel[:, :], rr[:, 1:2], None,
                   ALU.is_ge, reads=("sel", "rr"), writes=("m2",))
            ctx.op("vector", "reduce_max", rr[:, 2:3], lg[:, :], axis=AXX,
                   negate=True, reads=("lg",), writes=("rr",))
            ctx.op("scalar", "activation", ee[:, :], lg[:, :], AF.Exp,
                   bias=rr[:, 2:3], accum_out=rr[:, 3:4],
                   reads=("lg", "rr"), writes=("ee", "rr"))
            ctx.op("vector", "reciprocal", rr[:, 4:5], rr[:, 3:4],
                   reads=("rr",), writes=("rr",))
            for k, mk in ((0, m1), (1, m2)):
                ctx.op("vector", "tensor_mul", ee2[:, :], mk[:, :], ee[:, :],
                       reads=(("m1", "m2")[k], "ee"), writes=("ee2",))
                ctx.op("vector", "reduce_sum", rr[:, 6:7], ee2[:, :], axis=AXX,
                       reads=("ee2",), writes=("rr",))
                ctx.op("vector", "tensor_scalar", w12[:, 2 * i + k: 2 * i + k + 1],
                       rr[:, 6:7], rr[:, 4:5], 0.5, ALU.mult, ALU.mult,
                       reads=("rr",), writes=("w12",))
            for mk, mt, mkey, mtk in ((m1, m1t, "m1", "m1t"), (m2, m2t, "m2", "m2t")):
                ctx.op("tensor", "transpose", p.pb[5][0:8, i * 128:(i + 1) * 128],
                       mk[:, :], p.c_id[:, :],
                       reads=(mkey, "c:ID128"), writes=("pb5",))
                ctx.op("vector", "tensor_copy", mt[0:8, i * 128:(i + 1) * 128],
                       p.pb[5][0:8, i * 128:(i + 1) * 128],
                       reads=("pb5",), writes=(mtk,))
        for k, (mt, mtk) in enumerate(((m1t, "m1t"), (m2t, "m2t"))):
            ctx.op("tensor", "matmul", p.pb[5][0:64, 0:tc], p.c_e8[:, :], mt[:, :],
                   start=True, stop=True, reads=("c:E8X2", mtk), writes=("pb5",))
            ctx.op("vector", "tensor_copy", mex[k][:, :], p.pb[5][0:64, 0:tc],
                   reads=("pb5",), writes=(f"mex{k}",))
        for dt in range(16):
            ctx.op("tensor", "matmul", p.pb[6][0:64, 0:tc],
                   a1t[:, dt * 64:(dt + 1) * 64],
                   xn_fm[:, dt * tc:(dt + 1) * tc],
                   start=(dt == 0), stop=(dt == 15),
                   reads=("a1t", "xn_fm"), writes=("pb6",), inc=(dt == 15))
        for k in range(2):
            ctx.op("vector", "tensor_mul", mu1a[k][0:64, :], p.pb[6][0:64, 0:tc],
                   mex[k][:, :], reads=("pb6", f"mex{k}"), writes=(f"mu1a{k}",))
        for pf in range(2):
            ctx.op(q1, "dma_start", w1b[pf][:, :], W1T[pf * 128:(pf + 1) * 128, :],
                   reads=("W1T",), writes=(f"w1b{pf}",))
        for ft in range(FF // 128):
            cur = w1b[ft % NW]
            ck = f"w1b{ft % NW}"
            if ft + 2 < 64:
                nx = ft + 2
                ctx.op(q1, "dma_start", w1b[nx % NW][:, :],
                       W1T[nx * 128:(nx + 1) * 128, :],
                       reads=("W1T",), writes=(f"w1b{nx % NW}",))
            bA, bB, bC = (0, 1, 2) if ft % 2 == 0 else (3, 4, 5)
            kA, kB, kC = f"pb{bA}", f"pb{bB}", f"pb{bC}"
            fb = ft % 2
            for dt in range(16):
                ctx.op("tensor", "matmul", p.pb[bA][:, 0:tc],
                       cur[:, dt * 128:(dt + 1) * 128],
                       xn_fm[:, dt * tc:(dt + 1) * tc],
                       start=(dt == 0), stop=(dt == 15),
                       reads=(ck, "xn_fm"), writes=(kA,), inc=(dt == 15))
            for k, bZ, kZ in ((0, bB, kB), (1, bC, kC)):
                ctx.op("tensor", "matmul", p.pb[bZ][:, 0:tc],
                       b1aug[:, ft * 128:(ft + 1) * 128], mu1a[k][:, :],
                       start=True, stop=True,
                       reads=("b1aug", f"mu1a{k}"), writes=(kZ,))
            ctx.op("scalar", "copy", base1[fb][:, :], p.pb[bA][:, 0:tc],
                   reads=(kA,), writes=(f"base1_{fb}",))
            for k, bZ, kZ in ((0, bB, kB), (1, bC, kC)):
                ctx.op("vector", "tensor_add", v01[k][:, :], p.pb[bZ][:, 0:tc],
                       base1[fb][:, :], reads=(kZ, f"base1_{fb}"),
                       writes=(f"v01_{k}",))
            for k in range(2):
                ctx.op("scalar", "activation", h[k][:, ft * tc:(ft + 1) * tc],
                       v01[k][:, :], AF.Gelu_apprx_tanh,
                       reads=(f"v01_{k}",), writes=(f"h{k}",))
            for k, bU in ((0, 6), (1, 7)):
                ctx.op("tensor", "matmul", p.pb[bU][0:64, 0:tc],
                       a2t[:, ft * 64:(ft + 1) * 64],
                       h[k][:, ft * tc:(ft + 1) * tc],
                       start=(ft == 0), stop=(ft == 63),
                       reads=("a2t", f"h{k}"), writes=(f"pb{bU}",),
                       inc=(ft == 63))
        for k, bU in ((0, 6), (1, 7)):
            ctx.op("vector", "tensor_mul", mz2a[k][0:64, :], p.pb[bU][0:64, 0:tc],
                   mex[k][:, :], reads=(f"pb{bU}", f"mex{k}"), writes=(f"mz2a{k}",))
        for k in range(2):
            for i in range(nt):
                for dc in range(4):
                    b = i * 4 + dc
                    ctx.op("tensor", "matmul", p.pb[b][:, 0:512],
                           mz2a[k][:, i * 128:(i + 1) * 128],
                           b2aug[:, dc * 512:(dc + 1) * 512],
                           start=True, stop=False,
                           reads=(f"mz2a{k}", "b2aug"), writes=(f"pb{b}",),
                           inc=False)
            for pf in range(2):
                ctx.op(q2, "dma_start", w2b[pf][:, :], W2T[pf * 128:(pf + 1) * 128, :],
                       reads=("W2T",), writes=(f"w2b{pf}",))
            for ft in range(64):
                cur, ck = w2b[ft % NW], f"w2b{ft % NW}"
                if ft + 2 < 64:
                    nx = ft + 2
                    ctx.op(q2, "dma_start", w2b[nx % NW][:, :],
                           W2T[nx * 128:(nx + 1) * 128, :],
                           reads=("W2T",), writes=(f"w2b{nx % NW}",))
                for i in range(nt):
                    for dc in range(4):
                        b = i * 4 + dc
                        ctx.op("tensor", "matmul", p.pb[b][:, 0:512],
                               h[k][:, ft * tc + i * 128: ft * tc + (i + 1) * 128],
                               cur[:, dc * 512:(dc + 1) * 512],
                               start=False, stop=(ft == 63),
                               reads=(f"h{k}", ck), writes=(f"pb{b}",),
                               inc=(i == nt - 1 and dc == 3))
            for i in range(nt):
                if k == 0:
                    for dc in range(4):
                        b = i * 4 + dc
                        ctx.op("vector", "tensor_scalar",
                               yacc[i][:, dc * 512:(dc + 1) * 512],
                               p.pb[b][:, 0:512], w12[:, 2 * i:2 * i + 1], None,
                               ALU.mult, reads=(f"pb{b}", "w12"),
                               writes=(f"yacc{i}",))
                else:
                    for dc in range(4):
                        b = i * 4 + dc
                        ctx.op("vector", "scalar_tensor_tensor",
                               yacc[i][:, dc * 512:(dc + 1) * 512],
                               p.pb[b][:, 0:512], w12[:, 2 * i + 1:2 * i + 2],
                               yacc[i][:, dc * 512:(dc + 1) * 512],
                               ALU.mult, ALU.add,
                               reads=(f"pb{b}", "w12", f"yacc{i}"),
                               writes=(f"yacc{i}",))
                    ctx.op("vector", "tensor_add", xn_tm[:, :], yacc[i][:, :],
                           xb[i][:, :], reads=(f"yacc{i}", f"xb{i}"),
                           writes=("xn_tm",))
                    ctx.op("sync", "dma_start",
                           xres_dst[tok0 + i * 128: tok0 + (i + 1) * 128, :],
                           xn_tm[:, :], reads=("xn_tm",), writes=("xres_dst",))
    ctx.barrier()
    es.close()



def phase_bf16conv(p, items, tag="cv"):
    nc, ctx = p.nc, p.ctx
    es = ExitStack()
    bufs = [es.enter_context(nc.sbuf_tensor(f"{tag}_b{i}", [128, 2048], BF16))
            for i in range(4)]
    for src_name, shape, dst in items:
        rows, cols = shape
        src = p.dram_in(src_name, shape)
        dap = dst.ap()
        assert cols <= 2048
        for i in range(rows // 128):
            bi = i % 4
            ctx.op("gpsimd", "dma_start", bufs[bi][:, 0:cols],
                   src[i * 128:(i + 1) * 128, :],
                   reads=(src_name,), writes=(f"{tag}b{bi}",))
            ctx.op("sync", "dma_start", dap[i * 128:(i + 1) * 128, :],
                   bufs[bi][:, 0:cols],
                   reads=(f"{tag}b{bi}",), writes=(f"{tag}:{src_name}",))
    ctx.barrier()
    es.close()



def phase_w2prep(p, w2p_dram, tag="w2p"):
    nc, ctx = p.nc, p.ctx
    es = ExitStack()
    sb = lambda n, s, d: es.enter_context(nc.sbuf_tensor(f"{tag}_{n}", s, d))
    W2G = p.dram_in("l1_W2G", [64, 128 * D])
    gin = [sb(f"gin{i}", [64, D], BF16) for i in range(2)]
    gout = [sb(f"gout{i}", [64, D], BF16) for i in range(2)]
    w2p_v = w2p_dram.ap().rearrange("(a b) d -> a b d", a=64)
    for b in range(128):
        i = b % 2
        ctx.op("gpsimd", "dma_start", gin[i][:, :],
               W2G[:, b * D:(b + 1) * D],
               reads=("W2G",), writes=(f"{tag}gin{i}",))
        for dc in range(4):
            ctx.op("tensor", "matmul", p.pb[dc][0:64, 0:512], p.c_h64[:, :],
                   gin[i][:, dc * 512:(dc + 1) * 512],
                   start=True, stop=True,
                   reads=("c:H64", f"{tag}gin{i}"), writes=(f"pb{dc}",))
            ctx.op("vector", "tensor_copy", gout[i][:, dc * 512:(dc + 1) * 512],
                   p.pb[dc][0:64, 0:512],
                   reads=(f"pb{dc}",), writes=(f"{tag}gout{i}",))
        ctx.op("sync", "dma_start", w2p_v[:, b, :], gout[i][:, :],
               reads=(f"{tag}gout{i}",), writes=("w2p",))
    ctx.barrier()
    es.close()



def phase_l1ffn(p, xres_src, xres_dst, w2p_dram, tag):
    nc, ctx = p.nc, p.ctx
    tc = 512
    nt = tc // 128
    nch = T // tc
    es = ExitStack()
    sb = lambda n, s, d: es.enter_context(nc.sbuf_tensor(f"{tag}_{n}", s, d))

    W1T, q1 = p.wsrc("l1_W1Tt", [64 * 128, 16 * 128])
    W3T, q3 = p.wsrc("l1_W3Tt", [64 * 128, 16 * 128])
    w2p_ap = w2p_dram.ap()

    NW = 3
    w1b = [sb(f"w1b{i}", [128, 2048], BF16) for i in range(NW)]
    w3b = [sb(f"w3b{i}", [128, 2048], BF16) for i in range(NW)]
    w2b = [sb(f"w2b{i}", [128, D], BF16) for i in range(NW)]
    xb = [sb(f"xb{i}", [128, D], F32) for i in range(nt)]
    xn_tm = sb("xn_tm", [128, D], F32)
    xn_fm = sb("xn_fm", [128, 16 * tc], BF16)
    stat = sb("stat", [128, 32], F32)
    sg = sb("sg", [128, tc], F32)
    hp = [sb(f"hp{i}", [128, tc], BF16) for i in range(2)]
    hidp = sb("hidp", [128, 64 * tc], BF16)
    xo = sb("xo", [128, D], F32)

    for ch in range(nch):
        tok0 = ch * tc
        for i in range(nt):
            ctx.op("sync", "dma_start", xb[i][:, :],
                   xres_src[tok0 + i * 128: tok0 + (i + 1) * 128, :],
                   reads=("xres_src",), writes=(f"xb{i}",))
        for i in range(nt):
            ln_tile(p, xb[i][:, :], f"xb{i}", xn_tm, "xn_tm", stat, "stat", tag)
            transpose_tm_to_fm(p, xn_tm, "xn_tm", xn_fm, "xn_fm", i, tc)
        for pf in range(2):
            ctx.op(q1, "dma_start", w1b[pf][:, :], W1T[pf * 128:(pf + 1) * 128, :],
                   reads=("W1T",), writes=(f"w1b{pf}",))
            ctx.op(q3, "dma_start", w3b[pf][:, :], W3T[pf * 128:(pf + 1) * 128, :],
                   reads=("W3T",), writes=(f"w3b{pf}",))
        for ft in range(64):
            fb = ft % NW
            cg, cl = w1b[fb], w3b[fb]
            kg, kl = f"w1b{fb}", f"w3b{fb}"
            if ft + 2 < 64:
                nfb = (ft + 2) % NW
                ctx.op(q1, "dma_start", w1b[nfb][:, :],
                       W1T[(ft + 2) * 128:(ft + 3) * 128, :],
                       reads=("W1T",), writes=(f"w1b{nfb}",))
                ctx.op(q3, "dma_start", w3b[nfb][:, :],
                       W3T[(ft + 2) * 128:(ft + 3) * 128, :],
                       reads=("W3T",), writes=(f"w3b{nfb}",))
            bG, bL, bH = (0, 1, 2) if ft % 2 == 0 else (3, 4, 5)
            for dt in range(16):
                ctx.op("tensor", "matmul", p.pb[bG][:, 0:tc],
                       cg[:, dt * 128:(dt + 1) * 128],
                       xn_fm[:, dt * tc:(dt + 1) * tc],
                       start=(dt == 0), stop=(dt == 15),
                       reads=(kg, "xn_fm"), writes=(f"pb{bG}",), inc=(dt == 15))
            for dt in range(16):
                ctx.op("tensor", "matmul", p.pb[bL][:, 0:tc],
                       cl[:, dt * 128:(dt + 1) * 128],
                       xn_fm[:, dt * tc:(dt + 1) * tc],
                       start=(dt == 0), stop=(dt == 15),
                       reads=(kl, "xn_fm"), writes=(f"pb{bL}",), inc=(dt == 15))
            ctx.op("scalar", "activation", sg[:, :], p.pb[bG][:, 0:tc], AF.Silu,
                   reads=(f"pb{bG}",), writes=("sg",))
            hb = ft % 2
            ctx.op("vector", "tensor_mul", hp[hb][:, :], sg[:, :],
                   p.pb[bL][:, 0:tc], reads=("sg", f"pb{bL}"), writes=(f"hp{hb}",))
            ctx.op("tensor", "matmul", p.pb[bH][:, 0:tc], p.c_h128[:, :],
                   hp[hb][:, :], start=True, stop=True,
                   reads=("c:H128", f"hp{hb}"), writes=(f"pb{bH}",))
            ctx.op("scalar", "activation", hidp[:, ft * tc:(ft + 1) * tc],
                   p.pb[bH][:, 0:tc], AF.Copy, scale=CVALS[7],
                   reads=(f"pb{bH}",), writes=("hidp",))
        for pair in range(nt // 2):
            for pf in range(2):
                ctx.op("scalar", "dma_start", w2b[pf][:, :],
                       w2p_ap[pf * 128:(pf + 1) * 128, :],
                       reads=("w2p",), writes=(f"w2b{pf}",))
            for ft in range(64):
                fb = ft % NW
                cur, ck = w2b[fb], f"w2b{fb}"
                if ft + 2 < 64:
                    nfb = (ft + 2) % NW
                    ctx.op("scalar", "dma_start", w2b[nfb][:, :],
                           w2p_ap[(ft + 2) * 128:(ft + 3) * 128, :],
                           reads=("w2p",), writes=(f"w2b{nfb}",))
                for ii in range(2):
                    i = pair * 2 + ii
                    for dc in range(4):
                        b = ii * 4 + dc
                        ctx.op("tensor", "matmul", p.pb[b][:, 0:512],
                               hidp[:, ft * tc + i * 128: ft * tc + (i + 1) * 128],
                               cur[:, dc * 512:(dc + 1) * 512],
                               start=(ft == 0), stop=(ft == 63),
                               reads=("hidp", ck), writes=(f"pb{b}",),
                               inc=(ii == 1 and dc == 3))
            for ii in range(2):
                i = pair * 2 + ii
                for dc in range(4):
                    b = ii * 4 + dc
                    ctx.op("vector", "tensor_add", xo[:, dc * 512:(dc + 1) * 512],
                           p.pb[b][:, 0:512],
                           xb[i][:, dc * 512:(dc + 1) * 512],
                           reads=(f"pb{b}", f"xb{i}"), writes=("xo",))
                ctx.op("sync", "dma_start",
                       xres_dst[tok0 + i * 128: tok0 + (i + 1) * 128, :],
                       xo[:, :], reads=("xo",), writes=("xres_dst",))
    ctx.barrier()
    es.close()



def phase_xnfull(p, src_ap, src_key, dst_ap, dst_key, ranks, tag):
    nc, ctx = p.nc, p.ctx
    tc = 512
    es = ExitStack()
    sb = lambda n, s, d: es.enter_context(nc.sbuf_tensor(f"{tag}_{n}", s, d))
    xb = [sb(f"xb{i}", [128, D], F32) for i in range(2)]
    xn_tm = sb("xn_tm", [128, D], F32)
    xn_fm = sb("xn_fm", [128, 16 * tc], BF16)
    stat = sb("stat", [128, 32], F32)
    for rb, st0 in ranks:
        for ch in range(1024 // tc):
            for i in range(tc // 128):
                ib = i % 2
                ctx.op("sync", "dma_start", xb[ib][:, :],
                       src_ap[st0 + ch * tc + i * 128: st0 + ch * tc + (i + 1) * 128, :],
                       reads=(src_key,), writes=(f"{tag}xb{ib}",))
                ln_tile(p, xb[ib][:, :], f"{tag}xb{ib}", xn_tm, f"{tag}xn_tm",
                        stat, f"{tag}stat", tag)
                transpose_tm_to_fm(p, xn_tm, f"{tag}xn_tm", xn_fm, f"{tag}xn_fm",
                                   i, tc)
            for dt in range(16):
                ctx.op("sync", "dma_start",
                       dst_ap[rb + dt * 128: rb + (dt + 1) * 128,
                              ch * tc:(ch + 1) * tc],
                       xn_fm[:, dt * tc:(dt + 1) * tc],
                       reads=(f"{tag}xn_fm",), writes=(dst_key,))
    ctx.barrier()
    es.close()



def phase_l0attn(p, xres_src, xres_dst, xnfull_ap, xnfull_key, xnown_ap, xnown_key, tag):
    nc, ctx = p.nc, p.ctx
    es = ExitStack()
    sb = lambda n, s, d: es.enter_context(nc.sbuf_tensor(f"{tag}_{n}", s, d))

    Q = sb("Q", [128, 16 * T], BF16)
    ckv = sb("ckv", [128, 4 * S], BF16)
    krot = sb("krot", [64, S], BF16)

    esB = ExitStack()
    sbB = lambda n, s, d: esB.enter_context(nc.sbuf_tensor(f"{tag}B_{n}", s, d))
    xnq = sbB("xnq", [128, 16 * T], BF16)
    qlat = sbB("qlat", [128, 12 * T], BF16)
    qab = [sbB(f"qab{i}", [128, 2048], BF16) for i in range(2)]
    qbb = [sbB(f"qbb{i}", [128, 1536], BF16) for i in range(2)]
    qs = sbB("qs", [64, T], F32)
    qsw = sbB("qsw", [64, T], F32)
    c0q = sbB("c0q", [64, T], F32)
    s0q = sbB("s0q", [64, T], F32)
    QAT = p.dram_in("l0_qaTt", [12 * 128, 16 * 128])
    QBT = p.dram_in("l0_qbTt", [16 * 128, 12 * 128])
    ctx.op("sync", "dma_start", c0q[:, :], p.dram_in("L0Q_C", [64, T]),
           writes=("c0q",))
    ctx.op("sync", "dma_start", s0q[:, :], p.dram_in("L0Q_S", [64, T]),
           writes=("s0q",))
    for dt in range(16):
        ctx.op("sync", "dma_start", xnq[:, dt * T:(dt + 1) * T],
               xnown_ap[dt * 128:(dt + 1) * 128, :],
               reads=(xnown_key,), writes=("xnq",))
    ctx.op("gpsimd", "dma_start", qab[0][:, :], QAT[0:128, :],
           reads=("QAT",), writes=("qab0",))
    for lt in range(12):
        fb = lt % 2
        if lt + 1 < 12:
            ctx.op("gpsimd", "dma_start", qab[(lt + 1) % 2][:, :],
                   QAT[(lt + 1) * 128:(lt + 2) * 128, :],
                   reads=("QAT",), writes=(f"qab{(lt + 1) % 2}",))
        for half in range(2):
            bb = 2 * half
            for dt in range(16):
                ctx.op("tensor", "matmul", p.pb[bb][:, 0:512],
                       qab[fb][:, dt * 128:(dt + 1) * 128],
                       xnq[:, dt * T + half * 512: dt * T + (half + 1) * 512],
                       start=(dt == 0), stop=(dt == 15),
                       reads=(f"qab{fb}", "xnq"), writes=(f"pb{bb}",),
                       inc=(dt == 15))
            ctx.op("scalar", "copy",
                   qlat[:, lt * T + half * 512: lt * T + (half + 1) * 512],
                   p.pb[bb][:, 0:512], reads=(f"pb{bb}",), writes=("qlat",))
    ctx.op("gpsimd", "dma_start", qbb[0][:, :], QBT[0:128, :],
           reads=("QBT",), writes=("qbb0",))
    for qt in range(16):
        fb = qt % 2
        if qt + 1 < 16:
            ctx.op("gpsimd", "dma_start", qbb[(qt + 1) % 2][:, :],
                   QBT[(qt + 1) * 128:(qt + 2) * 128, :],
                   reads=("QBT",), writes=(f"qbb{(qt + 1) % 2}",))
        for half in range(2):
            bb = 4 + 2 * half
            for lt in range(12):
                ctx.op("tensor", "matmul", p.pb[bb][:, 0:512],
                       qbb[fb][:, lt * 128:(lt + 1) * 128],
                       qlat[:, lt * T + half * 512: lt * T + (half + 1) * 512],
                       start=(lt == 0), stop=(lt == 11),
                       reads=(f"qbb{fb}", "qlat"), writes=(f"pb{bb}",),
                       inc=(lt == 11))
        for half in range(2):
            bb = 4 + 2 * half
            ctx.op("scalar", "copy",
                   Q[0:64, qt * T + half * 512: qt * T + (half + 1) * 512],
                   p.pb[bb][0:64, 0:512], reads=(f"pb{bb}",), writes=("Q",))
            ctx.op("vector", "tensor_copy", qs[:, half * 512:(half + 1) * 512],
                   p.pb[bb][64:128, 0:512], reads=(f"pb{bb}",), writes=("qs",))
        for b in range(1):
            ctx.op("vector", "tensor_copy", qsw[0:32, :], qs[32:64, :],
                   reads=("qs",), writes=("qsw",))
            ctx.op("vector", "tensor_copy", qsw[32:64, :], qs[0:32, :],
                   reads=("qs",), writes=("qsw",))
        ctx.op("vector", "tensor_mul", qs[:, :], qs[:, :], c0q[:, :],
               reads=("qs", "c0q"), writes=("qs",))
        ctx.op("vector", "tensor_mul", qsw[:, :], qsw[:, :], s0q[:, :],
               reads=("qsw", "s0q"), writes=("qsw",))
        ctx.op("vector", "tensor_add", Q[64:128, qt * T:(qt + 1) * T],
               qs[:, :], qsw[:, :], reads=("qs", "qsw"), writes=("Q",))
    ctx.barrier()
    esB.close()

    esC = ExitStack()
    sbC = lambda n, s, d: esC.enter_context(nc.sbuf_tensor(f"{tag}C_{n}", s, d))
    kvaa = sbC("kvaa", [128, 4 * 2048], BF16)
    kvab = sbC("kvab", [128, 16 * 64], BF16)
    xk = [sbC(f"xk{i}", [128, 16 * 512], BF16) for i in range(2)]
    ckc = [sbC(f"ckc{i}", [64, 512], F32) for i in range(2)]
    sks = [sbC(f"sks{i}", [64, 512], F32) for i in range(2)]
    ks = sbC("ks", [64, 512], F32)
    ksw = sbC("ksw", [64, 512], F32)
    ctx.op("gpsimd", "dma_start", kvaa[:, :],
           p.dram_in("l0_kvaTaf", [128, 4 * 2048]), writes=("kvaa",))
    ctx.op("gpsimd", "dma_start", kvab[:, :],
           p.dram_in("l0_kvaTbf", [128, 16 * 64]), writes=("kvab",))
    C0K = p.dram_in("L0K_C", [64, S])
    S0K = p.dram_in("L0K_S", [64, S])

    def load_xk(kc, buf):
        r = kc // 2
        for dt in range(16):
            ctx.op("sync", "dma_start", xk[buf][:, dt * 512:(dt + 1) * 512],
                   xnfull_ap[r * 2048 + dt * 128: r * 2048 + (dt + 1) * 128,
                             (kc % 2) * 512:(kc % 2 + 1) * 512],
                   reads=(xnfull_key,), writes=(f"xk{buf}",))

    load_xk(0, 0)
    ctx.op("sync", "dma_start", ckc[0][:, :], C0K[:, 0:512], writes=("ckc0",))
    ctx.op("sync", "dma_start", sks[0][:, :], S0K[:, 0:512], writes=("sks0",))
    for kc in range(8):
        fb = kc % 2
        if kc + 1 < 8:
            load_xk(kc + 1, (kc + 1) % 2)
            ctx.op("sync", "dma_start", ckc[(kc + 1) % 2][:, :],
                   C0K[:, (kc + 1) * 512:(kc + 2) * 512],
                   writes=(f"ckc{(kc + 1) % 2}",))
            ctx.op("sync", "dma_start", sks[(kc + 1) % 2][:, :],
                   S0K[:, (kc + 1) * 512:(kc + 2) * 512],
                   writes=(f"sks{(kc + 1) % 2}",))
        for ct in range(4):
            for dt in range(16):
                ctx.op("tensor", "matmul", p.pb[ct][:, 0:512],
                       kvaa[:, ct * 2048 + dt * 128: ct * 2048 + (dt + 1) * 128],
                       xk[fb][:, dt * 512:(dt + 1) * 512],
                       start=(dt == 0), stop=(dt == 15),
                       reads=("kvaa", f"xk{fb}"), writes=(f"pb{ct}",),
                       inc=(dt == 15))
        for dt in range(16):
            ctx.op("tensor", "matmul", p.pb[4][0:64, 0:512],
                   kvab[:, dt * 64:(dt + 1) * 64],
                   xk[fb][:, dt * 512:(dt + 1) * 512],
                   start=(dt == 0), stop=(dt == 15),
                   reads=("kvab", f"xk{fb}"), writes=("pb4",), inc=(dt == 15))
        for ct in range(4):
            ctx.op("scalar", "copy", ckv[:, ct * S + kc * 512:(ct * S + (kc + 1) * 512)],
                   p.pb[ct][:, 0:512], reads=(f"pb{ct}",), writes=("ckv",))
        ctx.op("vector", "tensor_copy", ks[:, :], p.pb[4][0:64, 0:512],
               reads=("pb4",), writes=("ks",))
        ctx.op("vector", "tensor_copy", ksw[0:32, :], ks[32:64, :],
               reads=("ks",), writes=("ksw",))
        ctx.op("vector", "tensor_copy", ksw[32:64, :], ks[0:32, :],
               reads=("ks",), writes=("ksw",))
        ctx.op("vector", "tensor_mul", ks[:, :], ks[:, :], ckc[fb][:, :],
               reads=("ks", f"ckc{fb}"), writes=("ks",))
        ctx.op("vector", "tensor_mul", ksw[:, :], ksw[:, :], sks[fb][:, :],
               reads=("ksw", f"sks{fb}"), writes=("ksw",))
        ctx.op("vector", "tensor_add", krot[:, kc * 512:(kc + 1) * 512],
               ks[:, :], ksw[:, :], reads=("ks", "ksw"), writes=("krot",))
    ctx.barrier()
    esC.close()

    esD = ExitStack()
    sbD = lambda n, s, d: esD.enter_context(nc.sbuf_tensor(f"{tag}D_{n}", s, d))
    wkb = sbD("wkb", [128, 16 * 256], BF16)
    wvb = sbD("wvb", [128, 4 * 2048], BF16)
    Vq = sbD("Vq", [128, 32 * 512], BF16)
    Kf = [sbD(f"Kf{i}", [128, S], BF16) for i in range(2)]
    Eb = [sbD(f"Eb{i}", [128, 1024], BF16) for i in range(2)]
    den = sbD("den", [1, 512], F32)
    bc = sbD("bc", [128, 512], F32)
    attn_out = sbD("attn_out", [128, 16 * T], BF16)
    ctx.op("gpsimd", "dma_start", wkb[:, :],
           p.dram_in("l0_WkbTf", [128, 16 * 256]), writes=("wkb",))
    ctx.op("gpsimd", "dma_start", wvb[:, :],
           p.dram_in("l0_WvbTf", [128, 4 * 2048]), writes=("wvb",))
    for g in range(4):
        for kt in range(32):
            ctx.op("tensor", "matmul", p.pb[6][:, 0:512],
                   ckv[:, 0 * S + kt * 128: 0 * S + (kt + 1) * 128],
                   wvb[:, 0 * 2048 + g * 512:(0 * 2048 + (g + 1) * 512)],
                   start=True, stop=False,
                   reads=("ckv", "wvb"), writes=("pb6",), inc=False)
            for ct in range(1, 4):
                ctx.op("tensor", "matmul", p.pb[6][:, 0:512],
                       ckv[:, ct * S + kt * 128: ct * S + (kt + 1) * 128],
                       wvb[:, ct * 2048 + g * 512: ct * 2048 + (g + 1) * 512],
                       start=False, stop=(ct == 3),
                       reads=("ckv", "wvb"), writes=("pb6",), inc=(ct == 3))
            ctx.op("scalar", "copy", Vq[:, kt * 512:(kt + 1) * 512],
                   p.pb[6][:, 0:512], reads=("pb6",), writes=("Vq",))
        for hh in range(4):
            h = g * 4 + hh
            hb = hh % 2
            for kc in range(8):
                for ct in range(4):
                    ctx.op("tensor", "matmul", p.pb[7][0:64, 0:512],
                           wkb[:, h * 256 + ct * 64: h * 256 + (ct + 1) * 64],
                           ckv[:, ct * S + kc * 512: ct * S + (kc + 1) * 512],
                           start=(ct == 0), stop=(ct == 3),
                           reads=("wkb", "ckv"), writes=("pb7",), inc=(ct == 3))
                ctx.op("scalar", "copy", Kf[hb][0:64, kc * 512:(kc + 1) * 512],
                       p.pb[7][0:64, 0:512], reads=("pb7",), writes=(f"Kf{hb}",))
            ctx.op("vector", "tensor_copy", Kf[hb][64:128, :], krot[:, :],
                   reads=("krot",), writes=(f"Kf{hb}",))
            for u in range(2):
                for pp in range(17):
                    if pp < 16:
                        b0 = 4 + (pp % 2) * 2
                        for j in range(2):
                            kt = 2 * pp + j
                            ctx.op("tensor", "matmul", p.pb[b0 + j][:, 0:512],
                                   Kf[hb][:, kt * 128:(kt + 1) * 128],
                                   Q[:, h * T + u * 512: h * T + (u + 1) * 512],
                                   start=True, stop=True,
                                   reads=(f"Kf{hb}", "Q"), writes=(f"pb{b0 + j}",))
                        ctx.op("scalar", "activation", Eb[pp % 2][:, :],
                               p.psa[:, b0 * 512:(b0 + 2) * 512], AF.Exp,
                               bias=-20.0,
                               reads=(f"pb{b0}", f"pb{b0 + 1}"),
                               writes=(f"Eb{pp % 2}",))
                    if pp > 0:
                        q_ = pp - 1
                        for j in range(2):
                            kt = 2 * q_ + j
                            ctx.op("tensor", "matmul", p.pb[2][:, 0:512],
                                   Vq[:, kt * 512 + hh * 128: kt * 512 + (hh + 1) * 128],
                                   Eb[q_ % 2][:, j * 512:(j + 1) * 512],
                                   start=(kt == 0), stop=(kt == 31),
                                   reads=("Vq", f"Eb{q_ % 2}"), writes=("pb2",),
                                   inc=(kt == 31))
                            ctx.op("tensor", "matmul", p.pb[3][0:1, 0:512],
                                   p.c_ones[:, :],
                                   Eb[q_ % 2][:, j * 512:(j + 1) * 512],
                                   start=(kt == 0), stop=(kt == 31),
                                   reads=("c:ONES128", f"Eb{q_ % 2}"), writes=("pb3",),
                                   inc=(kt == 31))
                ctx.op("vector", "reciprocal", den[:, :], p.pb[3][0:1, 0:512],
                       reads=("pb3",), writes=("den",))
                ctx.op("tensor", "matmul", p.pb[0][:, 0:512],
                       p.c_onesrowf[:, :], den[:, :], start=True, stop=True,
                       reads=("c:ONESROWF", "den"), writes=("pb0",))
                ctx.op("vector", "tensor_copy", bc[:, :], p.pb[0][:, 0:512],
                       reads=("pb0",), writes=("bc",))
                ctx.op("vector", "tensor_mul",
                       attn_out[:, h * T + u * 512: h * T + (u + 1) * 512],
                       p.pb[2][:, 0:512], bc[:, :],
                       reads=("pb2", "bc"), writes=("attn_out",))
    ctx.barrier()
    esD.close()

    esE = ExitStack()
    sbE = lambda n, s, d: esE.enter_context(nc.sbuf_tensor(f"{tag}E_{n}", s, d))
    owb = [sbE(f"owb{i}", [128, D], BF16) for i in range(2)]
    xo = sbE("xo", [128, D], F32)
    xob = [sbE(f"xob{i}", [128, D], F32) for i in range(2)]
    OWT = p.dram_in("l0_owT", [D, D])
    for tt in range(8):
        ctx.op("sync", "dma_start", xob[tt % 2][:, :],
               xres_src[tt * 128:(tt + 1) * 128, :],
               reads=("xres_src",), writes=(f"xob{tt % 2}",))
        ctx.op("gpsimd", "dma_start", owb[0][:, :], OWT[0:128, :],
               reads=("OWT",), writes=("owb0",))
        for ht in range(16):
            fb = ht % 2
            if ht + 1 < 16:
                ctx.op("gpsimd", "dma_start", owb[(ht + 1) % 2][:, :],
                       OWT[(ht + 1) * 128:(ht + 2) * 128, :],
                       reads=("OWT",), writes=(f"owb{(ht + 1) % 2}",))
            for dc in range(4):
                b = 4 * (tt % 2) + dc
                ctx.op("tensor", "matmul", p.pb[b][:, 0:512],
                       attn_out[:, ht * T + tt * 128: ht * T + (tt + 1) * 128],
                       owb[fb][:, dc * 512:(dc + 1) * 512],
                       start=(ht == 0), stop=(ht == 15),
                       reads=("attn_out", f"owb{fb}"), writes=(f"pb{b}",),
                       inc=(dc == 3))
        for dc in range(4):
            b = 4 * (tt % 2) + dc
            ctx.op("vector", "tensor_add", xo[:, dc * 512:(dc + 1) * 512],
                   p.pb[b][:, 0:512], xob[tt % 2][:, dc * 512:(dc + 1) * 512],
                   reads=(f"pb{b}", f"xob{tt % 2}"), writes=("xo",))
        ctx.op("sync", "dma_start", xres_dst[tt * 128:(tt + 1) * 128, :],
               xo[:, :], reads=("xo",), writes=("xres_dst",))
    ctx.barrier()
    esE.close()
    es.close()



def phase_l1attn(p, xres_src, xres_dst, xnfull_ap, xnfull_key, xnown_ap, xnown_key, tag):
    nc, ctx = p.nc, p.ctx
    es = ExitStack()
    sb = lambda n, s, d: es.enter_context(nc.sbuf_tensor(f"{tag}_{n}", s, d))
    Q = sb("Q", [128, 16 * T], BF16)
    Ksb = sb("Ksb", [128, 4 * S], BF16)
    Vsb = sb("Vsb", [128, 32 * 520], BF16)
    attn_out = sb("attn_out", [128, 16 * T], BF16)

    esB = ExitStack()
    sbB = lambda n, s, d: esB.enter_context(nc.sbuf_tensor(f"{tag}B_{n}", s, d))
    xnq = sbB("xnq", [128, 16 * T], BF16)
    wqb = [sbB(f"wqb{i}", [128, 2048], BF16) for i in range(2)]
    qs = sbB("qs", [128, T], F32)
    qsw = sbB("qsw", [128, T], F32)
    c1q = sbB("c1q", [128, T], F32)
    s1q = sbB("s1q", [128, T], F32)
    WQT = p.dram_in("l1_WqTt", [16 * 128, 16 * 128])
    ctx.op("sync", "dma_start", c1q[:, :], p.dram_in("L1Q_C", [128, T]),
           writes=("c1q",))
    ctx.op("sync", "dma_start", s1q[:, :], p.dram_in("L1Q_S", [128, T]),
           writes=("s1q",))
    for dt in range(16):
        ctx.op("sync", "dma_start", xnq[:, dt * T:(dt + 1) * T],
               xnown_ap[dt * 128:(dt + 1) * 128, :],
               reads=(xnown_key,), writes=("xnq",))
    ctx.op("gpsimd", "dma_start", wqb[0][:, :], WQT[0:128, :],
           reads=("WQT",), writes=("wqb0",))
    for qt in range(16):
        fb = qt % 2
        if qt + 1 < 16:
            ctx.op("gpsimd", "dma_start", wqb[(qt + 1) % 2][:, :],
                   WQT[(qt + 1) * 128:(qt + 2) * 128, :],
                   reads=("WQT",), writes=(f"wqb{(qt + 1) % 2}",))
        for half in range(2):
            bb = 2 * half
            for dt in range(16):
                ctx.op("tensor", "matmul", p.pb[bb][:, 0:512],
                       wqb[fb][:, dt * 128:(dt + 1) * 128],
                       xnq[:, dt * T + half * 512: dt * T + (half + 1) * 512],
                       start=(dt == 0), stop=(dt == 15),
                       reads=(f"wqb{fb}", "xnq"), writes=(f"pb{bb}",),
                       inc=(dt == 15))
            ctx.op("vector", "tensor_copy", qs[:, half * 512:(half + 1) * 512],
                   p.pb[bb][:, 0:512], reads=(f"pb{bb}",), writes=("qs",))
        for b in range(4):
            ctx.op("vector", "tensor_copy", qsw[b * 32:(b + 1) * 32, :],
                   qs[b * 32 + 32 - 64 * (b % 2):b * 32 + 64 - 64 * (b % 2), :],
                   reads=("qs",), writes=("qsw",))
        ctx.op("vector", "tensor_mul", qs[:, :], qs[:, :], c1q[:, :],
               reads=("qs", "c1q"), writes=("qs",))
        ctx.op("vector", "tensor_mul", qsw[:, :], qsw[:, :], s1q[:, :],
               reads=("qsw", "s1q"), writes=("qsw",))
        ctx.op("vector", "tensor_add", Q[:, qt * T:(qt + 1) * T],
               qs[:, :], qsw[:, :], reads=("qs", "qsw"), writes=("Q",))
    ctx.barrier()
    esB.close()

    esC = ExitStack()
    sbC = lambda n, s, d: esC.enter_context(nc.sbuf_tensor(f"{tag}C_{n}", s, d))
    wk = sbC("wk", [128, 4 * 2048], BF16)
    wv = sbC("wv", [128, 16 * 512], BF16)
    xk = [sbC(f"xk{i}", [128, 16 * 512], BF16) for i in range(2)]
    c1k = [sbC(f"c1k{i}", [128, 512], F32) for i in range(2)]
    s1k = [sbC(f"s1k{i}", [128, 512], F32) for i in range(2)]
    ks = sbC("ks", [128, 512], F32)
    ksw = sbC("ksw", [128, 512], F32)
    ctx.op("gpsimd", "dma_start", wk[:, :],
           p.dram_in("l1_WkTf", [128, 4 * 2048]), writes=("wk",))
    ctx.op("gpsimd", "dma_start", wv[:, :],
           p.dram_in("l1_WvTf", [128, 16 * 512]), writes=("wv",))
    C1K = p.dram_in("L1K_C", [128, S])
    S1K = p.dram_in("L1K_S", [128, S])
    ctx.op("vector", "memset",
           Vsb[:, :].rearrange("p (t g c) -> p t g c", g=8, c=65)[:, :, :, 64:65],
           1.0, writes=("Vsb",))

    def load_xk(kc, buf):
        r = kc // 2
        for dt in range(16):
            ctx.op("sync", "dma_start", xk[buf][:, dt * 512:(dt + 1) * 512],
                   xnfull_ap[r * 2048 + dt * 128: r * 2048 + (dt + 1) * 128,
                             (kc % 2) * 512:(kc % 2 + 1) * 512],
                   reads=(xnfull_key,), writes=(f"xk{buf}",))

    load_xk(0, 0)
    ctx.op("sync", "dma_start", c1k[0][:, :], C1K[:, 0:512], writes=("c1k0",))
    ctx.op("sync", "dma_start", s1k[0][:, :], S1K[:, 0:512], writes=("s1k0",))
    for kc in range(8):
        fb = kc % 2
        if kc + 1 < 8:
            load_xk(kc + 1, (kc + 1) % 2)
            ctx.op("sync", "dma_start", c1k[(kc + 1) % 2][:, :],
                   C1K[:, (kc + 1) * 512:(kc + 2) * 512],
                   writes=(f"c1k{(kc + 1) % 2}",))
            ctx.op("sync", "dma_start", s1k[(kc + 1) % 2][:, :],
                   S1K[:, (kc + 1) * 512:(kc + 2) * 512],
                   writes=(f"s1k{(kc + 1) % 2}",))
        for kb in range(4):
            for dt in range(16):
                ctx.op("tensor", "matmul", p.pb[kb][:, 0:512],
                       wk[:, kb * 2048 + dt * 128: kb * 2048 + (dt + 1) * 128],
                       xk[fb][:, dt * 512:(dt + 1) * 512],
                       start=(dt == 0), stop=(dt == 15),
                       reads=("wk", f"xk{fb}"), writes=(f"pb{kb}",),
                       inc=(dt == 15))
            ctx.op("vector", "tensor_copy", ks[:, :], p.pb[kb][:, 0:512],
                   reads=(f"pb{kb}",), writes=("ks",))
            for b in range(4):
                ctx.op("vector", "tensor_copy", ksw[b * 32:(b + 1) * 32, :],
                       ks[b * 32 + 32 - 64 * (b % 2):b * 32 + 64 - 64 * (b % 2), :],
                       reads=("ks",), writes=("ksw",))
            ctx.op("vector", "tensor_mul", ks[:, :], ks[:, :], c1k[fb][:, :],
                   reads=("ks", f"c1k{fb}"), writes=("ks",))
            ctx.op("vector", "tensor_mul", ksw[:, :], ksw[:, :], s1k[fb][:, :],
                   reads=("ksw", f"s1k{fb}"), writes=("ksw",))
            ctx.op("vector", "tensor_add",
                   Ksb[:, kb * S + kc * 512: kb * S + (kc + 1) * 512],
                   ks[:, :], ksw[:, :], reads=("ks", "ksw"), writes=("Ksb",))
        for k4 in range(4):
            kt = kc * 4 + k4
            for dt in range(16):
                ctx.op("tensor", "matmul", p.pb[4][:, 0:512],
                       xk[fb][:, dt * 512 + k4 * 128: dt * 512 + (k4 + 1) * 128],
                       wv[:, dt * 512:(dt + 1) * 512],
                       start=(dt == 0), stop=(dt == 15),
                       reads=(f"xk{fb}", "wv"), writes=("pb4",), inc=(dt == 15))
            ctx.op("vector", "tensor_copy",
                   Vsb[:, kt * 520:(kt + 1) * 520]
                   .rearrange("p (g c) -> p g c", c=65)[:, :, 0:64],
                   p.pb[4][:, 0:512].rearrange("p (g c) -> p g c", c=64),
                   reads=("pb4",), writes=("Vsb",))
    ctx.barrier()
    esC.close()

    esD = ExitStack()
    sbD = lambda n, s, d: esD.enter_context(nc.sbuf_tensor(f"{tag}D_{n}", s, d))
    Eb = [sbD(f"Eb{i}", [128, 1024], BF16) for i in range(2)]
    den = sbD("den", [1, 512], F32)
    bc = sbD("bc", [64, 512], F32)
    for kvh in range(8):
        for u in range(2):
            for hp in range(2):
                hA, hB = kvh * 4 + 2 * hp, kvh * 4 + 2 * hp + 1
                pos = [l1_qpos(hA), l1_qpos(hB)]
                for qt_, qb_ in pos:
                    assert qb_ == (kvh % 2) * 64
                for pp in range(33):
                    if pp < 32:
                        kt = pp
                        b0 = 4 + (pp % 2) * 2
                        for j, (qt_, qb_) in enumerate(pos):
                            ctx.op("tensor", "matmul", p.pb[b0 + j][:, 0:512],
                                   Ksb[(kvh % 2) * 64:(kvh % 2) * 64 + 64,
                                       (kvh // 2) * S + kt * 128:(kvh // 2) * S + (kt + 1) * 128],
                                   Q[qb_:qb_ + 64,
                                     qt_ * T + u * 512: qt_ * T + (u + 1) * 512],
                                   start=True, stop=True,
                                   reads=("Ksb", "Q"), writes=(f"pb{b0 + j}",))
                        ctx.op("scalar", "activation", Eb[pp % 2][:, :],
                               p.psa[:, b0 * 512:(b0 + 2) * 512], AF.Exp,
                               bias=-20.0,
                               reads=(f"pb{b0}", f"pb{b0 + 1}"),
                               writes=(f"Eb{pp % 2}",))
                    if pp > 0:
                        kt = pp - 1
                        q_ = pp - 1
                        for j in range(2):
                            ctx.op("tensor", "matmul", p.pb[2 + j][0:65, 0:512],
                                   Vsb[:, kt * 520 + kvh * 65: kt * 520 + (kvh + 1) * 65],
                                   Eb[q_ % 2][:, j * 512:(j + 1) * 512],
                                   start=(kt == 0), stop=(kt == 31),
                                   reads=("Vsb", f"Eb{q_ % 2}"), writes=(f"pb{2 + j}",),
                                   inc=(kt == 31))
                for j, (qt_, qb_) in enumerate(pos):
                    ctx.op("vector", "reciprocal", den[:, :],
                           p.pb[2 + j][64:65, 0:512],
                           reads=(f"pb{2 + j}",), writes=("den",))
                    ctx.op("tensor", "matmul", p.pb[0][0:64, 0:512],
                           p.c_onesrowf[:, 0:64], den[:, :], start=True, stop=True,
                           reads=("c:ONESROWF", "den"), writes=("pb0",))
                    ctx.op("vector", "tensor_copy", bc[:, :], p.pb[0][0:64, 0:512],
                           reads=("pb0",), writes=("bc",))
                    ctx.op("vector", "tensor_mul",
                           attn_out[qb_:qb_ + 64,
                                    qt_ * T + u * 512: qt_ * T + (u + 1) * 512],
                           p.pb[2 + j][0:64, 0:512], bc[:, :],
                           reads=(f"pb{2 + j}", "bc"), writes=("attn_out",))
    ctx.barrier()
    esD.close()

    esE = ExitStack()
    sbE = lambda n, s, d: esE.enter_context(nc.sbuf_tensor(f"{tag}E_{n}", s, d))
    owb = [sbE(f"owb{i}", [128, D], BF16) for i in range(2)]
    xo = sbE("xo", [128, D], F32)
    xob = [sbE(f"xob{i}", [128, D], F32) for i in range(2)]
    OWT = p.dram_in("l1_WoT", [D, D])
    for tt in range(8):
        ctx.op("sync", "dma_start", xob[tt % 2][:, :],
               xres_src[tt * 128:(tt + 1) * 128, :],
               reads=("xres_src",), writes=(f"xob{tt % 2}",))
        ctx.op("gpsimd", "dma_start", owb[0][:, :], OWT[0:128, :],
               reads=("OWT1",), writes=("owb0",))
        for ht in range(16):
            fb = ht % 2
            if ht + 1 < 16:
                ctx.op("gpsimd", "dma_start", owb[(ht + 1) % 2][:, :],
                       OWT[(ht + 1) * 128:(ht + 2) * 128, :],
                       reads=("OWT1",), writes=(f"owb{(ht + 1) % 2}",))
            for dc in range(4):
                b = 4 * (tt % 2) + dc
                ctx.op("tensor", "matmul", p.pb[b][:, 0:512],
                       attn_out[:, ht * T + tt * 128: ht * T + (tt + 1) * 128],
                       owb[fb][:, dc * 512:(dc + 1) * 512],
                       start=(ht == 0), stop=(ht == 15),
                       reads=("attn_out", f"owb{fb}"), writes=(f"pb{b}",),
                       inc=(dc == 3))
        for dc in range(4):
            b = 4 * (tt % 2) + dc
            ctx.op("vector", "tensor_add", xo[:, dc * 512:(dc + 1) * 512],
                   p.pb[b][:, 0:512], xob[tt % 2][:, dc * 512:(dc + 1) * 512],
                   reads=(f"pb{b}", f"xob{tt % 2}"), writes=("xo",))
        ctx.op("sync", "dma_start", xres_dst[tt * 128:(tt + 1) * 128, :],
               xo[:, :], reads=("xo",), writes=("xres_dst",))
    ctx.barrier()
    esE.close()
    es.close()




def build_fused():
    nc = bass.Bass("TRN2", target_bir_lowering=False)
    p = Prog(nc)
    xin = p.dram_in("x_in", [T, D], F32)
    xout = p.dram_out("x_out", [T, D], F32)
    p.setup_common()
    xa = p.dram_int("xa", [T, D], F32).ap()
    xb = p.dram_int("xb", [T, D], F32).ap()
    w2p = p.dram_int("w2p", [FF, D], BF16)
    cc_in = p.dram_int("cc_in", [2048, T], BF16)
    cc_out = p.dram_int("cc_out", [8192, T], BF16)
    ctx = p.ctx

    p.phase_marks = []

    def mark(nm):
        p.phase_marks.append((nm, p.nc.n_instructions()))

    def exchange(src_ap, src_key, tag):
        phase_xnfull(p, src_ap, src_key, cc_in.ap(), "cc_in", [(0, 0)], tag)
        ctx.op("gpsimd", "collective_compute", "AllGather", ALU.bypass,
               replica_groups=[[0, 1, 2, 3], [4, 5, 6, 7]],
               ins=[cc_in.ap().opt()], outs=[cc_out.ap().opt()],
               reads=("cc_in",), writes=("cc_out",))
        ctx.barrier()

    cvt = [("l0_W1Tt", [64 * 128, 16 * 128]), ("l0_W2T", [FF, D]),
           ("l1_W1Tt", [64 * 128, 16 * 128]), ("l1_W3Tt", [64 * 128, 16 * 128])]
    p.wc = {}
    items = []
    for nm, shp in cvt:
        h = p.dram_int(f"wc_{nm}", shp, BF16)
        p.wc[nm] = h
        items.append((nm, shp, h))
    mark("start")
    phase_bf16conv(p, items)
    mark("bf16conv")
    phase_w2prep(p, w2p)
    mark("w2prep")
    phase_l0ffn(p, xin, xa, "f0a")
    mark("l0ffn_a")
    exchange(xa, "xa", "xf0")
    mark("exch0")
    phase_l0attn(p, xa, xb, cc_out.ap(), "cc_out", cc_in.ap(), "cc_in", "a0")
    mark("l0attn")
    phase_l0ffn(p, xb, xa, "f0b")
    mark("l0ffn_b")
    phase_l1ffn(p, xa, xb, w2p, "f1a")
    mark("l1ffn_a")
    exchange(xb, "xb", "xf1")
    mark("exch1")
    phase_l1attn(p, xb, xa, cc_out.ap(), "cc_out", cc_in.ap(), "cc_in", "a1")
    mark("l1attn")
    phase_l1ffn(p, xa, xout, w2p, "f1b")
    mark("l1ffn_b")
    return p





def kernel(**inputs):
    inputs = {k: np.asarray(v) for k, v in inputs.items()}
    W = prep_weights(inputs)
    x = np.ascontiguousarray(inputs["L0_attn_x"]).astype(np.float32)
    p = build_fused()
    in_maps = []
    for c in range(8):
        b, r = c // 4, c % 4
        consts = host_consts(c)
        m = {}
        for name in p.inp:
            if name in consts:
                m[name] = consts[name]
            elif name in W:
                m[name] = W[name]
            elif name == "x_in":
                m[name] = np.ascontiguousarray(x[b, r * 1024:(r + 1) * 1024, :])
            else:
                raise KeyError(name)
        in_maps.append(m)
    res = run_bass_kernel_spmd(p.nc, in_maps, core_ids=list(range(8)))
    out = np.empty_like(x)
    for c in range(8):
        b, r = c // 4, c % 4
        out[b, r * 1024:(r + 1) * 1024, :] = res.results[c]["x_out"]
    return out


# version 07
# speedup vs baseline: 1.3726x; 1.0606x over previous
import math
from contextlib import ExitStack

import numpy as np
import ml_dtypes
import concourse.bass as bass
import concourse.mybir as mybir
from concourse.bass_utils import run_bass_kernel_spmd

F32 = mybir.dt.float32
BF16 = mybir.dt.bfloat16
AF = mybir.ActivationFunctionType
ALU = mybir.AluOpType
AXX = mybir.AxisListType.X

T = 1024
D = 2048
FF = 8192
S = 4096

CVALS = [0.0, 1.0, -1.0, 0.5, 2.0, -20.0, 1e-5,
         float(np.float32(0.5) * np.float32(1.0 / np.sqrt(8192.0))),
         -1e9]


ENGINES = ("tensor", "vector", "scalar", "gpsimd", "sync")

L1_EVENKV = [h for h in range(32) if (h // 4) % 2 == 0]
L1_ODDKV = [h for h in range(32) if (h // 4) % 2 == 1]


def l1_qpos(h):
    if (h // 4) % 2 == 0:
        return L1_EVENKV.index(h), 0
    return L1_ODDKV.index(h), 64


class Ctx:

    def __init__(self, nc):
        self.nc = nc
        self.es = ExitStack()
        self.eng = dict(tensor=nc.tensor, vector=nc.vector, scalar=nc.scalar,
                        gpsimd=nc.gpsimd, sync=nc.sync)
        self.sem = {e: self.es.enter_context(nc.semaphore(f"sem_{e}")) for e in ENGINES}
        self.cnt = {e: 0 for e in ENGINES}
        self.waited = {}
        self.res = {}

    def sb(self, name, shape, dtype):
        return self.es.enter_context(self.nc.sbuf_tensor(name, shape, dtype))

    def ps(self, name, shape, dtype):
        return self.es.enter_context(self.nc.psum_tensor(name, shape, dtype))

    def _dma_semkey(self, key, eng):
        sk = f"d:{eng}:{key}"
        if sk not in self.sem:
            self.sem[sk] = self.es.enter_context(self.nc.semaphore(sk))
            self.cnt[sk] = 0
        return sk

    def _need(self, eng, key, write, need):
        r = self.res.get(key)
        if r is None:
            return
        w = r.get("w")
        if w is not None:
            wsem, wval = w
            if not (wsem == "tensor" and eng == "tensor"):
                need[wsem] = max(need.get(wsem, 0), wval)
        if write:
            for rsem, rval in r.get("r", {}).items():
                if rsem == "tensor" and eng == "tensor":
                    continue
                need[rsem] = max(need.get(rsem, 0), rval)

    def _record(self, semkey, key, write, val):
        r = self.res.setdefault(key, {"w": None, "r": {}})
        if write:
            r["w"] = (semkey, val)
            r["r"] = {}
        else:
            r["r"][semkey] = max(r["r"].get(semkey, 0), val)

    def _emit_waits(self, eng, need):
        e = self.eng[eng]
        for tgt, val in sorted(need.items()):
            if val <= 0 or self.waited.get((eng, tgt), 0) >= val:
                continue
            if val > self.cnt[tgt]:
                raise RuntimeError(f"{eng} waits {tgt}>={val} but cnt={self.cnt[tgt]}")
            e.wait_ge(self.sem[tgt], val)
            self.waited[(eng, tgt)] = val

    def op(self, eng, method, *args, reads=(), writes=(), inc=True, **kw):
        need = {}
        for k in reads:
            self._need(eng, k, False, need)
        for k in writes:
            self._need(eng, k, True, need)
        self._emit_waits(eng, need)
        ins = getattr(self.eng[eng], method)(*args, **kw)
        is_dma = method in ("dma_start", "collective_compute", "dma_start_transpose")
        if is_dma:
            assert writes, f"dma op needs a write resource ({method})"
            semkey = self._dma_semkey(writes[0], eng)
            amt = 1 if method == "collective_compute" else 16
            ins.then_inc(self.sem[semkey], amt)
            self.cnt[semkey] += amt
            val = self.cnt[semkey]
        elif inc:
            semkey = eng
            ins.then_inc(self.sem[eng], 1)
            self.cnt[eng] += 1
            val = self.cnt[eng]
        else:
            semkey = eng
            val = self.cnt[eng] + 1
        for k in reads:
            self._record(semkey, k, False, val)
        for k in writes:
            self._record(semkey, k, True, val)
        return ins

    def barrier(self):
        for a in ENGINES:
            for b, cnt in self.cnt.items():
                if a == b or cnt == 0:
                    continue
                if self.waited.get((a, b), 0) < cnt:
                    self.eng[a].wait_ge(self.sem[b], cnt)
                    self.waited[(a, b)] = cnt
        self.res.clear()



def _stat_tiles(wT, kt=128, ct=128):
    K, M = wT.shape
    a = wT.reshape(K // kt, kt, M // ct, ct).transpose(2, 1, 0, 3)
    return np.ascontiguousarray(a.reshape(M // ct * kt, (K // kt) * ct))


def _stat_flat(wT, kt=128, ct=128):
    K, M = wT.shape
    a = wT.reshape(K // kt, kt, M // ct, ct).transpose(1, 2, 0, 3)
    return np.ascontiguousarray(a.reshape(kt, (M // ct) * (K // kt) * ct))


def prep_weights(w):
    o = {}
    o["l0_W1Tt"] = _stat_tiles(w["L0_ffn_W1"].T)
    o["l0_W2Tdc"] = np.ascontiguousarray(
        w["L0_ffn_W2"].T.reshape(FF, 4, 512).transpose(1, 0, 2).reshape(4 * FF, 512))
    o["l0_WrTt"] = _stat_tiles(w["L0_ffn_W_r"].T, ct=8)
    o["l0_br"] = np.ascontiguousarray(w["L0_ffn_b_r"].reshape(1, 8))
    o["l0_A1Tt"] = _stat_tiles(w["L0_ffn_A1"].reshape(64, D).T, ct=64)
    B1rf = w["L0_ffn_B1"].transpose(0, 2, 1).reshape(64, FF)
    o["l0_B1aug"] = np.ascontiguousarray(
        np.concatenate([B1rf, w["L0_ffn_b1"].reshape(1, FF)], 0))
    o["l0_A2Tt"] = _stat_tiles(w["L0_ffn_A2"].reshape(64, FF).T, ct=64)
    B2rf = w["L0_ffn_B2"].transpose(0, 2, 1).reshape(64, D)
    o["l0_B2aug"] = np.ascontiguousarray(
        np.concatenate([B2rf, w["L0_ffn_b2"].reshape(1, D)], 0))
    H0, QKD, NOPE, VD0, QR, KVR = 16, 128, 64, 128, 1536, 512
    o["l0_qaTt"] = _stat_tiles(w["L0_attn_q_a_w"].T)
    qb = w["L0_attn_q_b_w"].reshape(H0, QKD, QR)
    perm = (list(range(NOPE)) + [NOPE + 2 * i for i in range(32)]
            + [NOPE + 2 * i + 1 for i in range(32)])
    o["l0_qbTt"] = _stat_tiles(
        np.ascontiguousarray(qb[:, perm, :].reshape(H0 * QKD, QR).T))
    kva = w["L0_attn_kv_a_w"]
    rot = kva[KVR:]
    kva_p = np.concatenate([kva[:KVR], rot[0::2], rot[1::2]], 0)
    kvaT = np.ascontiguousarray(kva_p.T)
    o["l0_kvaTaf"] = _stat_flat(kvaT[:, 0:KVR])
    o["l0_kvaTbf"] = _stat_flat(kvaT[:, KVR:], ct=64)
    kvb = w["L0_attn_kv_b_w"].reshape(H0, NOPE + VD0, KVR)
    o["l0_WkbTf"] = _stat_flat(
        np.ascontiguousarray(kvb[:, :NOPE, :].reshape(H0 * NOPE, KVR).T), ct=64)
    o["l0_WvbTf"] = _stat_flat(
        np.ascontiguousarray(kvb[:, NOPE:, :].reshape(H0 * VD0, KVR).T), ct=2048)
    o["l0_owT"] = np.ascontiguousarray(w["L0_attn_o_w"].T)
    order = []
    for j in range(16):
        order += [L1_EVENKV[j], L1_ODDKV[j]]
    rowperm = np.concatenate([np.arange(h * 64, (h + 1) * 64) for h in order])
    o["l1_WqTt"] = _stat_tiles(np.ascontiguousarray(w["L1_attn_Wq"][rowperm].T))
    o["l1_WkTf"] = _stat_flat(w["L1_attn_Wk"].T)
    o["l1_WvTf"] = _stat_flat(w["L1_attn_Wv"].T, ct=512)
    o["l1_WoT"] = np.ascontiguousarray(w["L1_attn_Wo"].T[rowperm])
    o["l1_W1Tt"] = _stat_tiles(w["L1_ffn_W1"].T)
    o["l1_W3Tt"] = _stat_tiles(w["L1_ffn_W3"].T)
    o["l1_W2G"] = np.ascontiguousarray(w["L1_ffn_W2"].T.reshape(64, 128 * D))
    return o


def host_consts(core_id):
    c = {}
    c["CONSTS"] = np.tile(np.array(CVALS, np.float32), (128, 1))
    c["ID128"] = np.eye(128, dtype=np.float32)
    c["ONESROW"] = np.ones((1, 128), np.float32)
    c["ONESROWF"] = np.ones((1, 128), np.float32)
    c["ONES128"] = np.ones((128, 1), np.float32)
    e8 = np.zeros((8, 64), np.float32)
    for e in range(8):
        e8[e, e * 8:(e + 1) * 8] = 2.0
    c["E8X2"] = e8
    esel = np.zeros((8, 1024), np.float32)
    for j in range(8):
        esel[j, j * 128:(j + 1) * 128] = 1.0
    c["ESEL8"] = esel
    c["EPSIDX"] = (np.arange(8, dtype=np.float32) * np.float32(-1e-6)).reshape(1, 8)

    def had(n):
        h = np.array([[1.0]], np.float32)
        while h.shape[0] < n:
            h = np.block([[h, h], [h, -h]])
        return np.ascontiguousarray(h)
    c["H128"] = had(128)
    c["H64"] = had(64)
    q0 = ((core_id % 4) * T + np.arange(T)).astype(np.float32)
    kpos = np.arange(S, dtype=np.float32)
    th = (np.float32(1.0)
          / np.float32(10000.0) ** (np.arange(0, 64, 2, dtype=np.float32)
                                    / np.float32(64.0))).astype(np.float32)
    for nm, pos, scl, rep in (("L0Q", q0, 1.0, 1), ("L0K", kpos, 1.0, 1),
                              ("L1Q", q0, 0.125, 2), ("L1K", kpos, 1.0, 2)):
        ang = (pos[None, :] * th[:, None]).astype(np.float32)
        cs = np.cos(ang).astype(np.float32) * np.float32(scl)
        sn = np.sin(ang).astype(np.float32) * np.float32(scl)
        blkC = np.concatenate([cs, cs], 0)
        blkS = np.concatenate([-sn, sn], 0)
        c[f"{nm}_C"] = np.ascontiguousarray(np.concatenate([blkC] * rep, 0))
        c[f"{nm}_S"] = np.ascontiguousarray(np.concatenate([blkS] * rep, 0))
    return c



class Prog:
    def __init__(self, nc):
        self.nc = nc
        self.ctx = Ctx(nc)
        self.inp = {}

    def dram_in(self, name, shape, dtype=F32):
        if name in self.inp:
            return self.inp[name]
        t = self.nc.dram_tensor(name, list(shape), dtype, kind="ExternalInput")
        self.inp[name] = t.ap()
        return self.inp[name]

    def dram_out(self, name, shape, dtype=F32):
        t = self.nc.dram_tensor(name, list(shape), dtype, kind="ExternalOutput")
        return t.ap()

    def dram_int(self, name, shape, dtype):
        return self.nc.dram_tensor(name, list(shape), dtype, kind="Internal")

    def setup_common(self):
        ctx = self.ctx
        self.psa = ctx.ps("PSA", [128, 4096], F32)
        self.pb = [self.psa[:, i * 512:(i + 1) * 512] for i in range(8)]
        self.c_consts = ctx.sb("c_consts", [128, len(CVALS)], F32)
        for j, v in enumerate(CVALS):
            self.nc.const_aps.register(F32, v, self.c_consts[:, j:j + 1])
        self.CI = {v: j for j, v in enumerate(CVALS)}
        self.c_id = ctx.sb("c_id", [128, 128], F32)
        self.c_onesrow = ctx.sb("c_onesrow", [1, 128], BF16)
        self.c_onesrowf = ctx.sb("c_onesrowf", [1, 128], F32)
        self.c_ones = ctx.sb("c_ones", [128, 1], BF16)
        self.c_e8 = ctx.sb("c_e8", [8, 64], BF16)
        self.c_eps = ctx.sb("c_eps", [1, 8], F32)
        self.c_h128 = ctx.sb("c_h128", [128, 128], BF16)
        self.c_h64 = ctx.sb("c_h64", [64, 64], BF16)
        loads = [("CONSTS", self.c_consts, [128, len(CVALS)]),
                 ("ID128", self.c_id, [128, 128]),
                 ("ONESROW", self.c_onesrow, [1, 128]),
                 ("ONESROWF", self.c_onesrowf, [1, 128]),
                 ("ONES128", self.c_ones, [128, 1]),
                 ("E8X2", self.c_e8, [8, 64]),
                 ("EPSIDX", self.c_eps, [1, 8]),
                 ("H128", self.c_h128, [128, 128]),
                 ("H64", self.c_h64, [64, 64])]
        for nm, dst, shp in loads:
            src = self.dram_in(nm, shp, F32)
            ctx.op("gpsimd", "dma_start", dst[:, :], src, writes=(f"c:{nm}",))
        ctx.barrier()

    def const(self, v):
        j = self.CI[v]
        return self.c_consts[:, j:j + 1]

    def wsrc(self, name, shape):
        wc = getattr(self, "wc", None)
        if wc and name in wc:
            return wc[name].ap(), "scalar"
        return self.dram_in(name, shape), "gpsimd"



def ln_tile(p, x_ap, x_key, xn_tm, xn_key, stat, stat_key, tag):
    ctx = p.ctx
    bn = stat
    for j in range(4):
        ctx.op("vector", "bn_stats", stat[:, 8 + j * 6: 8 + (j + 1) * 6],
               x_ap[:, j * 512:(j + 1) * 512],
               reads=(x_key,), writes=(stat_key,))
    ctx.op("vector", "bn_aggr", stat[:, 0:2], stat[:, 8:32],
           reads=(stat_key,), writes=(stat_key,))
    ctx.op("scalar", "activation", stat[:, 2:3], stat[:, 1:2], AF.Sqrt,
           bias=1e-5, reads=(stat_key,), writes=(stat_key,))
    ctx.op("vector", "reciprocal", stat[:, 3:4], stat[:, 2:3],
           reads=(stat_key,), writes=(stat_key,))
    ctx.op("vector", "tensor_scalar", xn_tm[:, :], x_ap, stat[:, 0:1], stat[:, 3:4],
           ALU.subtract, ALU.mult, reads=(x_key, stat_key), writes=(xn_key,))


def transpose_tm_to_fm(p, src_tm, src_key, dst_fm, dst_key, ti, tc, n_dt=16,
                       dst_bf16=True, banks=(0, 1)):
    ctx = p.ctx
    for g in range(n_dt // 4):
        b = p.pb[banks[g % len(banks)]]
        bk = f"pb{banks[g % len(banks)]}"
        for q in range(4):
            dt = g * 4 + q
            ctx.op("tensor", "transpose", b[:, q * 128:(q + 1) * 128],
                   src_tm[:, dt * 128:(dt + 1) * 128], p.c_id[:, :],
                   reads=(src_key, "c:ID128"), writes=(bk,), inc=(q == 3))
        for q in range(4):
            dt = g * 4 + q
            ctx.op("scalar", "copy",
                   dst_fm[:, dt * tc + ti * 128: dt * tc + (ti + 1) * 128],
                   b[:, q * 128:(q + 1) * 128],
                   reads=(bk,), writes=(dst_key,))


def mm_movs(p, psum_ap, bank_keys, stat_ap, stat_keys, mov_ap, mov_keys, ncols,
            start, stop, inc_last=True, tile_position=None):
    ctx = p.ctx
    nch = (ncols + 511) // 512
    for j in range(nch):
        c0, c1 = j * 512, min(ncols, (j + 1) * 512)
        kw = {}
        if tile_position is not None:
            kw["tile_position"] = tile_position
        ctx.op("tensor", "matmul", psum_ap[:, c0:c1], stat_ap, mov_ap[:, c0:c1],
               start=start and j == 0, stop=stop and j == nch - 1,
               reads=tuple(stat_keys) + tuple(mov_keys), writes=tuple(bank_keys),
               inc=inc_last and (j == nch - 1), **kw)



def phase_l0ffn(p, xres_src, xres_dst, tag):
    nc, ctx = p.nc, p.ctx
    tc = 512
    nt = tc // 128
    nch = T // tc
    es = ExitStack()
    sb = lambda n, s, d: es.enter_context(nc.sbuf_tensor(f"{tag}_{n}", s, d))

    wr = sb("wr", [128, 16 * 8], BF16)
    a1t = sb("a1t", [128, 16 * 64], BF16)
    a2t = sb("a2t", [128, 64 * 64], BF16)
    b1aug = sb("b1aug", [65, FF], BF16)
    b2aug = sb("b2aug", [65, D], BF16)
    br = sb("br", [1, 8], BF16)
    epsb = sb("epsb", [128, 8], F32)
    NW = 3
    w1b = [sb(f"w1b{i}", [128, 16 * 128], BF16) for i in range(NW)]
    w2b = [sb(f"w2b{i}", [128, 512], BF16) for i in range(NW)]
    xb = [sb(f"xb{i}", [128, D], F32) for i in range(nt)]
    xo = [sb(f"xo{i}", [128, 512], F32) for i in range(2)]
    xn_tm = sb("xn_tm", [128, D], F32)
    xn_fm = sb("xn_fm", [128, 16 * tc], BF16)
    stat = sb("stat", [128, 32], F32)
    lg = sb("lg", [128, 8], F32)
    sel = sb("sel", [128, 8], F32)
    m1 = sb("m1", [128, 8], F32)
    m2 = sb("m2", [128, 8], F32)
    ee = sb("ee", [128, 8], F32)
    ee2 = sb("ee2", [128, 8], F32)
    rr = sb("rr", [128, 8], F32)
    w12 = sb("w12", [128, 2 * nt], F32)
    w12s = sb("w12s", [8, 128], F32)
    esel = sb("esel", [8, 1024], F32)
    wbc = [sb(f"wbc{k}", [128, tc], F32) for k in range(2)]
    m1t = sb("m1t", [8, tc], BF16)
    m2t = sb("m2t", [8, tc], BF16)
    mex = [sb(f"mex{k}", [64, tc], BF16) for k in range(2)]
    mexw = [sb(f"mexw{k}", [64, tc], F32) for k in range(2)]
    mu1a = [sb(f"mu1a{k}", [65, tc], BF16) for k in range(2)]
    mzmix = sb("mzmix", [65, tc], BF16)
    v01 = [sb(f"v01_{k}", [128, tc], F32) for k in range(2)]
    base1 = [sb(f"base1_{par}", [128, tc], F32) for par in range(2)]
    hkb = [[sb(f"h{k}_{par}", [128, tc], BF16) for par in range(2)]
           for k in range(2)]
    hmt = [sb(f"hmt{par}", [128, tc], F32) for par in range(2)]
    hmix = sb("hmix", [128, 64 * tc], BF16)

    W1T, q1 = p.wsrc("l0_W1Tt", [64 * 128, 16 * 128])
    W2T, q2 = p.wsrc("l0_W2Tdc", [4 * FF, 512])
    ctx.op("gpsimd", "dma_start", wr[:, :],
           p.dram_in("l0_WrTt", [128, 16 * 8]), writes=("wr",))
    ctx.op("gpsimd", "dma_start", a1t[:, :],
           p.dram_in("l0_A1Tt", [128, 16 * 64]), writes=("a1t",))
    ctx.op("gpsimd", "dma_start", a2t[:, :],
           p.dram_in("l0_A2Tt", [128, 64 * 64]), writes=("a2t",))
    ctx.op("gpsimd", "dma_start", b1aug[:, :], p.dram_in("l0_B1aug", [65, FF]),
           writes=("b1aug",))
    ctx.op("gpsimd", "dma_start", b2aug[:, :], p.dram_in("l0_B2aug", [65, D]),
           writes=("b2aug",))
    ctx.op("gpsimd", "dma_start", br[:, :], p.dram_in("l0_br", [1, 8]),
           writes=("br",))
    ctx.op("gpsimd", "dma_start", esel[:, :], p.dram_in("ESEL8", [8, 1024]),
           writes=("esel",))
    epsbf = sb("epsbf", [1, 8], BF16)
    ctx.op("vector", "tensor_copy", epsbf[:, :], p.c_eps[:, :],
           reads=("c:EPSIDX",), writes=("epsbf",))
    ctx.op("tensor", "matmul", p.pb[0][0:128, 0:8], p.c_onesrow[:, :], epsbf[:, :],
           start=True, stop=True, reads=("c:ONESROW", "epsbf"), writes=("pb0",))
    ctx.op("vector", "tensor_copy", epsb[:, :], p.pb[0][0:128, 0:8],
           reads=("pb0",), writes=("epsb",))
    for k in range(2):
        ctx.op("gpsimd", "memset", mu1a[k][64:65, :], 1.0, writes=(f"mu1a{k}",))

    for ch in range(nch):
        tok0 = ch * tc
        for i in range(nt):
            ctx.op("sync", "dma_start", xb[i][:, :],
                   xres_src[tok0 + i * 128: tok0 + (i + 1) * 128, :],
                   reads=("xres_src",), writes=(f"xb{i}",))
        for i in range(nt):
            ln_tile(p, xb[i][:, :], f"xb{i}", xn_tm, "xn_tm", stat, "stat", tag)
            transpose_tm_to_fm(p, xn_tm, "xn_tm", xn_fm, "xn_fm", i, tc)
        for i in range(nt):
            for dt in range(16):
                ctx.op("tensor", "matmul", p.pb[4][0:128, 0:8],
                       xn_fm[:, dt * tc + i * 128: dt * tc + (i + 1) * 128],
                       wr[:, dt * 8:(dt + 1) * 8],
                       start=(dt == 0), stop=False,
                       reads=("xn_fm", "wr"), writes=("pb4",), inc=False)
            ctx.op("tensor", "matmul", p.pb[4][0:128, 0:8], p.c_onesrow[:, :],
                   br[:, :], start=False, stop=True,
                   reads=("c:ONESROW", "br"), writes=("pb4",))
            ctx.op("vector", "tensor_copy", lg[:, :], p.pb[4][0:128, 0:8],
                   reads=("pb4",), writes=("lg",))
            ctx.op("vector", "tensor_add", sel[:, :], lg[:, :], epsb[:, :],
                   reads=("lg", "epsb"), writes=("sel",))
            ctx.op("vector", "reduce_max", rr[:, 0:1], sel[:, :], axis=AXX,
                   reads=("sel",), writes=("rr",))
            ctx.op("vector", "tensor_scalar", m1[:, :], sel[:, :], rr[:, 0:1], None,
                   ALU.is_ge, reads=("sel", "rr"), writes=("m1",))
            ctx.op("vector", "scalar_tensor_tensor", sel[:, :], m1[:, :], -1e9,
                   sel[:, :], ALU.mult, ALU.add,
                   reads=("m1", "sel"), writes=("sel",))
            ctx.op("vector", "reduce_max", rr[:, 1:2], sel[:, :], axis=AXX,
                   reads=("sel",), writes=("rr",))
            ctx.op("vector", "tensor_scalar", m2[:, :], sel[:, :], rr[:, 1:2], None,
                   ALU.is_ge, reads=("sel", "rr"), writes=("m2",))
            ctx.op("vector", "reduce_max", rr[:, 2:3], lg[:, :], axis=AXX,
                   negate=True, reads=("lg",), writes=("rr",))
            ctx.op("scalar", "activation", ee[:, :], lg[:, :], AF.Exp,
                   bias=rr[:, 2:3], accum_out=rr[:, 3:4],
                   reads=("lg", "rr"), writes=("ee", "rr"))
            ctx.op("vector", "reciprocal", rr[:, 4:5], rr[:, 3:4],
                   reads=("rr",), writes=("rr",))
            for k, mk in ((0, m1), (1, m2)):
                ctx.op("vector", "tensor_mul", ee2[:, :], mk[:, :], ee[:, :],
                       reads=(("m1", "m2")[k], "ee"), writes=("ee2",))
                ctx.op("vector", "reduce_sum", rr[:, 6:7], ee2[:, :], axis=AXX,
                       reads=("ee2",), writes=("rr",))
                ctx.op("vector", "tensor_scalar", w12[:, 2 * i + k: 2 * i + k + 1],
                       rr[:, 6:7], rr[:, 4:5], 0.5, ALU.mult, ALU.mult,
                       reads=("rr",), writes=("w12",))
            for mk, mt, mkey, mtk in ((m1, m1t, "m1", "m1t"), (m2, m2t, "m2", "m2t")):
                ctx.op("tensor", "transpose", p.pb[5][0:8, i * 128:(i + 1) * 128],
                       mk[:, :], p.c_id[:, :],
                       reads=(mkey, "c:ID128"), writes=("pb5",))
                ctx.op("vector", "tensor_copy", mt[0:8, i * 128:(i + 1) * 128],
                       p.pb[5][0:8, i * 128:(i + 1) * 128],
                       reads=("pb5",), writes=(mtk,))
        ctx.op("tensor", "transpose", p.pb[5][0:8, 0:128], w12[:, :], p.c_id[:, :],
               reads=("w12", "c:ID128"), writes=("pb5",))
        ctx.op("vector", "tensor_copy", w12s[:, :], p.pb[5][0:8, 0:128],
               reads=("pb5",), writes=("w12s",))
        for k in range(2):
            bk = 3 if k == 0 else 4
            for i in range(nt):
                ctx.op("tensor", "matmul", p.pb[bk][:, i * 128:(i + 1) * 128],
                       esel[:, (2 * i + k) * 128:(2 * i + k + 1) * 128],
                       w12s[:, :], start=True, stop=True,
                       reads=("esel", "w12s"), writes=(f"pb{bk}",),
                       inc=(i == nt - 1))
            ctx.op("vector", "tensor_copy", wbc[k][:, :], p.pb[bk][:, 0:tc],
                   reads=(f"pb{bk}",), writes=(f"wbc{k}",))
        ctx.op("vector", "tensor_add", mzmix[64:65, :], wbc[0][64:65, :],
               wbc[1][64:65, :], reads=("wbc0", "wbc1"), writes=("mzmix",))
        for k, (mt, mtk) in enumerate(((m1t, "m1t"), (m2t, "m2t"))):
            ctx.op("tensor", "matmul", p.pb[5][0:64, 0:tc], p.c_e8[:, :], mt[:, :],
                   start=True, stop=True, reads=("c:E8X2", mtk), writes=("pb5",))
            ctx.op("vector", "tensor_copy", mex[k][:, :], p.pb[5][0:64, 0:tc],
                   reads=("pb5",), writes=(f"mex{k}",))
            ctx.op("vector", "tensor_mul", mexw[k][:, :], mex[k][:, :],
                   wbc[k][0:64, :], reads=(f"mex{k}", f"wbc{k}"),
                   writes=(f"mexw{k}",))
        for dt in range(16):
            ctx.op("tensor", "matmul", p.pb[6][0:64, 0:tc],
                   a1t[:, dt * 64:(dt + 1) * 64],
                   xn_fm[:, dt * tc:(dt + 1) * tc],
                   start=(dt == 0), stop=(dt == 15),
                   reads=("a1t", "xn_fm"), writes=("pb6",), inc=(dt == 15))
        for k in range(2):
            ctx.op("vector", "tensor_mul", mu1a[k][0:64, :], p.pb[6][0:64, 0:tc],
                   mex[k][:, :], reads=("pb6", f"mex{k}"), writes=(f"mu1a{k}",))
        for pf in range(2):
            ctx.op(q1, "dma_start", w1b[pf][:, :], W1T[pf * 128:(pf + 1) * 128, :],
                   reads=("W1T",), writes=(f"w1b{pf}",))
        for ft in range(65):
            if ft < 64:
                cur = w1b[ft % NW]
                ck = f"w1b{ft % NW}"
                if ft + 2 < 64:
                    nx = ft + 2
                    ctx.op(q1, "dma_start", w1b[nx % NW][:, :],
                           W1T[nx * 128:(nx + 1) * 128, :],
                           reads=("W1T",), writes=(f"w1b{nx % NW}",))
                fb = ft % 2
                bA, bB, bC = (0, 1, 2) if fb == 0 else (3, 4, 5)
                kA, kB, kC = f"pb{bA}", f"pb{bB}", f"pb{bC}"
                for dt in range(16):
                    ctx.op("tensor", "matmul", p.pb[bA][:, 0:tc],
                           cur[:, dt * 128:(dt + 1) * 128],
                           xn_fm[:, dt * tc:(dt + 1) * tc],
                           start=(dt == 0), stop=(dt == 15),
                           reads=(ck, "xn_fm"), writes=(kA,), inc=(dt == 15))
                for k, bZ, kZ in ((0, bB, kB), (1, bC, kC)):
                    ctx.op("tensor", "matmul", p.pb[bZ][:, 0:tc],
                           b1aug[:, ft * 128:(ft + 1) * 128], mu1a[k][:, :],
                           start=True, stop=True,
                           reads=("b1aug", f"mu1a{k}"), writes=(kZ,))
                ctx.op("scalar", "copy", base1[fb][:, :], p.pb[bA][:, 0:tc],
                       reads=(kA,), writes=(f"base1_{fb}",))
                for k, bZ, kZ in ((0, bB, kB), (1, bC, kC)):
                    ctx.op("vector", "tensor_add", v01[k][:, :],
                           p.pb[bZ][:, 0:tc], base1[fb][:, :],
                           reads=(kZ, f"base1_{fb}"), writes=(f"v01_{k}",))
                for k in range(2):
                    ctx.op("scalar", "activation", hkb[k][fb][:, :],
                           v01[k][:, :], AF.Gelu_apprx_tanh,
                           reads=(f"v01_{k}",), writes=(f"h{k}_{fb}",))
            if ft >= 1:
                g = ft - 1
                gb = g % 2
                for k, bU in ((0, 6), (1, 7)):
                    ctx.op("tensor", "matmul", p.pb[bU][0:64, 0:tc],
                           a2t[:, g * 64:(g + 1) * 64],
                           hkb[k][gb][:, :],
                           start=(g == 0), stop=(g == 63),
                           reads=("a2t", f"h{k}_{gb}"), writes=(f"pb{bU}",),
                           inc=(g == 63))
                for k in range(2):
                    ctx.op("vector", "tensor_mul", hmt[k][:, :],
                           hkb[k][gb][:, :], wbc[k][:, :],
                           reads=(f"h{k}_{gb}", f"wbc{k}"), writes=(f"hmt{k}",))
                ctx.op("vector", "tensor_add", hmix[:, g * tc:(g + 1) * tc],
                       hmt[0][:, :], hmt[1][:, :],
                       reads=("hmt0", "hmt1"), writes=("hmix",))
        for k, bU in ((0, 6), (1, 7)):
            op = "tensor_mul" if k == 0 else None
            if k == 0:
                ctx.op("vector", "tensor_mul", hmt[0][0:64, 0:tc],
                       p.pb[bU][0:64, 0:tc], mexw[k][:, :],
                       reads=(f"pb{bU}", f"mexw{k}"), writes=("hmt0",))
            else:
                ctx.op("vector", "tensor_mul", hmt[1][0:64, 0:tc],
                       p.pb[bU][0:64, 0:tc], mexw[k][:, :],
                       reads=(f"pb{bU}", f"mexw{k}"), writes=("hmt1",))
        ctx.op("vector", "tensor_add", mzmix[0:64, :], hmt[0][0:64, 0:tc],
               hmt[1][0:64, 0:tc], reads=("hmt0", "hmt1"), writes=("mzmix",))
        for dc in range(4):
            bg = (dc % 2) * 4
            for i in range(nt):
                ctx.op("tensor", "matmul", p.pb[bg + i][:, 0:512],
                       mzmix[:, i * 128:(i + 1) * 128],
                       b2aug[:, dc * 512:(dc + 1) * 512],
                       start=True, stop=False,
                       reads=("mzmix", "b2aug"), writes=(f"pb{bg + i}",),
                       inc=(i == nt - 1))
            for pf in range(2):
                ctx.op(q2, "dma_start", w2b[pf][:, :],
                       W2T[dc * FF + pf * 128: dc * FF + (pf + 1) * 128, :],
                       reads=("W2T",), writes=(f"w2b{pf}",))
            for ft in range(64):
                cur, ck = w2b[ft % NW], f"w2b{ft % NW}"
                if ft + 2 < 64:
                    nx = ft + 2
                    ctx.op(q2, "dma_start", w2b[nx % NW][:, :],
                           W2T[dc * FF + nx * 128: dc * FF + (nx + 1) * 128, :],
                           reads=("W2T",), writes=(f"w2b{nx % NW}",))
                for i in range(nt):
                    ctx.op("tensor", "matmul", p.pb[bg + i][:, 0:512],
                           hmix[:, ft * tc + i * 128: ft * tc + (i + 1) * 128],
                           cur[:, :],
                           start=False, stop=(ft == 63),
                           reads=("hmix", ck), writes=(f"pb{bg + i}",),
                           inc=(i == nt - 1))
            for i in range(nt):
                xi = i % 2
                ctx.op("vector", "tensor_add", xo[xi][:, :],
                       p.pb[bg + i][:, 0:512],
                       xb[i][:, dc * 512:(dc + 1) * 512],
                       reads=(f"pb{bg + i}", f"xb{i}"), writes=(f"xo{xi}",))
                ctx.op("sync", "dma_start",
                       xres_dst[tok0 + i * 128: tok0 + (i + 1) * 128,
                                dc * 512:(dc + 1) * 512],
                       xo[xi][:, :], reads=(f"xo{xi}",), writes=("xres_dst",))
    ctx.barrier()
    es.close()



def phase_bf16conv(p, items, tag="cv"):
    nc, ctx = p.nc, p.ctx
    es = ExitStack()
    bufs = [es.enter_context(nc.sbuf_tensor(f"{tag}_b{i}", [128, 2048], BF16))
            for i in range(4)]
    for src_name, shape, dst in items:
        rows, cols = shape
        src = p.dram_in(src_name, shape)
        dap = dst.ap()
        assert cols <= 2048
        for i in range(rows // 128):
            bi = i % 4
            ctx.op("gpsimd", "dma_start", bufs[bi][:, 0:cols],
                   src[i * 128:(i + 1) * 128, :],
                   reads=(src_name,), writes=(f"{tag}b{bi}",))
            ctx.op("sync", "dma_start", dap[i * 128:(i + 1) * 128, :],
                   bufs[bi][:, 0:cols],
                   reads=(f"{tag}b{bi}",), writes=(f"{tag}:{src_name}",))
    ctx.barrier()
    es.close()



def phase_w2prep(p, w2p_dram, tag="w2p"):
    nc, ctx = p.nc, p.ctx
    es = ExitStack()
    sb = lambda n, s, d: es.enter_context(nc.sbuf_tensor(f"{tag}_{n}", s, d))
    W2G = p.dram_in("l1_W2G", [64, 128 * D])
    gin = [sb(f"gin{i}", [64, D], BF16) for i in range(2)]
    gout = [sb(f"gout{i}", [64, D], BF16) for i in range(2)]
    w2p_v = w2p_dram.ap().rearrange("(dc a b) d -> a b dc d", dc=4, a=64)
    for b in range(128):
        i = b % 2
        ctx.op("gpsimd", "dma_start", gin[i][:, :],
               W2G[:, b * D:(b + 1) * D],
               reads=("W2G",), writes=(f"{tag}gin{i}",))
        for dc in range(4):
            ctx.op("tensor", "matmul", p.pb[dc][0:64, 0:512], p.c_h64[:, :],
                   gin[i][:, dc * 512:(dc + 1) * 512],
                   start=True, stop=True,
                   reads=("c:H64", f"{tag}gin{i}"), writes=(f"pb{dc}",))
            ctx.op("vector", "tensor_copy", gout[i][:, dc * 512:(dc + 1) * 512],
                   p.pb[dc][0:64, 0:512],
                   reads=(f"pb{dc}",), writes=(f"{tag}gout{i}",))
        ctx.op("sync", "dma_start", w2p_v[:, b, :, :],
               gout[i][:, :].rearrange("a (dc d) -> a dc d", dc=4),
               reads=(f"{tag}gout{i}",), writes=("w2p",))
    ctx.barrier()
    es.close()



def phase_l1ffn(p, xres_src, xres_dst, w2p_dram, tag):
    nc, ctx = p.nc, p.ctx
    tc = 512
    nt = tc // 128
    nch = T // tc
    es = ExitStack()
    sb = lambda n, s, d: es.enter_context(nc.sbuf_tensor(f"{tag}_{n}", s, d))

    W1T, q1 = p.wsrc("l1_W1Tt", [64 * 128, 16 * 128])
    W3T, q3 = p.wsrc("l1_W3Tt", [64 * 128, 16 * 128])
    w2p_ap = w2p_dram.ap()

    NW = 3
    w1b = [sb(f"w1b{i}", [128, 2048], BF16) for i in range(NW)]
    w3b = [sb(f"w3b{i}", [128, 2048], BF16) for i in range(NW)]
    w2b = [sb(f"w2b{i}", [128, 512], BF16) for i in range(NW)]
    xb = [sb(f"xb{i}", [128, D], F32) for i in range(nt)]
    xn_tm = sb("xn_tm", [128, D], F32)
    xn_fm = sb("xn_fm", [128, 16 * tc], BF16)
    stat = sb("stat", [128, 32], F32)
    sg = sb("sg", [128, tc], F32)
    hp = [sb(f"hp{i}", [128, tc], BF16) for i in range(2)]
    hidp = sb("hidp", [128, 64 * tc], BF16)
    xo = [sb(f"xo{i}", [128, 512], F32) for i in range(4)]

    for ch in range(nch):
        tok0 = ch * tc
        for i in range(nt):
            ctx.op("sync", "dma_start", xb[i][:, :],
                   xres_src[tok0 + i * 128: tok0 + (i + 1) * 128, :],
                   reads=("xres_src",), writes=(f"xb{i}",))
        for i in range(nt):
            ln_tile(p, xb[i][:, :], f"xb{i}", xn_tm, "xn_tm", stat, "stat", tag)
            transpose_tm_to_fm(p, xn_tm, "xn_tm", xn_fm, "xn_fm", i, tc)
        for pf in range(2):
            ctx.op(q1, "dma_start", w1b[pf][:, :], W1T[pf * 128:(pf + 1) * 128, :],
                   reads=("W1T",), writes=(f"w1b{pf}",))
            ctx.op(q3, "dma_start", w3b[pf][:, :], W3T[pf * 128:(pf + 1) * 128, :],
                   reads=("W3T",), writes=(f"w3b{pf}",))
        for ft in range(64):
            fb = ft % NW
            cg, cl = w1b[fb], w3b[fb]
            kg, kl = f"w1b{fb}", f"w3b{fb}"
            if ft + 2 < 64:
                nfb = (ft + 2) % NW
                ctx.op(q1, "dma_start", w1b[nfb][:, :],
                       W1T[(ft + 2) * 128:(ft + 3) * 128, :],
                       reads=("W1T",), writes=(f"w1b{nfb}",))
                ctx.op(q3, "dma_start", w3b[nfb][:, :],
                       W3T[(ft + 2) * 128:(ft + 3) * 128, :],
                       reads=("W3T",), writes=(f"w3b{nfb}",))
            bG, bL, bH = (0, 1, 2) if ft % 2 == 0 else (3, 4, 5)
            for dt in range(16):
                ctx.op("tensor", "matmul", p.pb[bG][:, 0:tc],
                       cg[:, dt * 128:(dt + 1) * 128],
                       xn_fm[:, dt * tc:(dt + 1) * tc],
                       start=(dt == 0), stop=(dt == 15),
                       reads=(kg, "xn_fm"), writes=(f"pb{bG}",), inc=(dt == 15))
            for dt in range(16):
                ctx.op("tensor", "matmul", p.pb[bL][:, 0:tc],
                       cl[:, dt * 128:(dt + 1) * 128],
                       xn_fm[:, dt * tc:(dt + 1) * tc],
                       start=(dt == 0), stop=(dt == 15),
                       reads=(kl, "xn_fm"), writes=(f"pb{bL}",), inc=(dt == 15))
            ctx.op("scalar", "activation", sg[:, :], p.pb[bG][:, 0:tc], AF.Silu,
                   reads=(f"pb{bG}",), writes=("sg",))
            hb = ft % 2
            ctx.op("vector", "tensor_mul", hp[hb][:, :], sg[:, :],
                   p.pb[bL][:, 0:tc], reads=("sg", f"pb{bL}"), writes=(f"hp{hb}",))
            ctx.op("tensor", "matmul", p.pb[bH][:, 0:tc], p.c_h128[:, :],
                   hp[hb][:, :], start=True, stop=True,
                   reads=("c:H128", f"hp{hb}"), writes=(f"pb{bH}",))
            ctx.op("scalar", "activation", hidp[:, ft * tc:(ft + 1) * tc],
                   p.pb[bH][:, 0:tc], AF.Copy, scale=CVALS[7],
                   reads=(f"pb{bH}",), writes=("hidp",))
        for dc in range(4):
            bg = (dc % 2) * 4
            for pf in range(2):
                ctx.op("scalar", "dma_start", w2b[pf][:, :],
                       w2p_ap[dc * FF + pf * 128: dc * FF + (pf + 1) * 128, :],
                       reads=("w2p",), writes=(f"w2b{pf}",))
            for ft in range(64):
                fb = ft % NW
                cur, ck = w2b[fb], f"w2b{fb}"
                if ft + 2 < 64:
                    nfb = (ft + 2) % NW
                    ctx.op("scalar", "dma_start", w2b[nfb][:, :],
                           w2p_ap[dc * FF + (ft + 2) * 128: dc * FF + (ft + 3) * 128, :],
                           reads=("w2p",), writes=(f"w2b{nfb}",))
                for i in range(nt):
                    ctx.op("tensor", "matmul", p.pb[bg + i][:, 0:512],
                           hidp[:, ft * tc + i * 128: ft * tc + (i + 1) * 128],
                           cur[:, :],
                           start=(ft == 0), stop=(ft == 63),
                           reads=("hidp", ck), writes=(f"pb{bg + i}",),
                           inc=(i == nt - 1))
            for i in range(nt):
                xi = i % 2
                ctx.op("vector", "tensor_add", xo[xi][:, :],
                       p.pb[bg + i][:, 0:512],
                       xb[i][:, dc * 512:(dc + 1) * 512],
                       reads=(f"pb{bg + i}", f"xb{i}"), writes=(f"xo{xi}",))
                ctx.op("sync", "dma_start",
                       xres_dst[tok0 + i * 128: tok0 + (i + 1) * 128,
                                dc * 512:(dc + 1) * 512],
                       xo[xi][:, :], reads=(f"xo{xi}",), writes=("xres_dst",))
    ctx.barrier()
    es.close()



def phase_xnfull(p, src_ap, src_key, dst_ap, dst_key, ranks, tag):
    nc, ctx = p.nc, p.ctx
    tc = 512
    es = ExitStack()
    sb = lambda n, s, d: es.enter_context(nc.sbuf_tensor(f"{tag}_{n}", s, d))
    xb = [sb(f"xb{i}", [128, D], F32) for i in range(2)]
    xn_tm = sb("xn_tm", [128, D], F32)
    xn_fm = sb("xn_fm", [128, 16 * tc], BF16)
    stat = sb("stat", [128, 32], F32)
    for rb, st0 in ranks:
        for ch in range(1024 // tc):
            for i in range(tc // 128):
                ib = i % 2
                ctx.op("sync", "dma_start", xb[ib][:, :],
                       src_ap[st0 + ch * tc + i * 128: st0 + ch * tc + (i + 1) * 128, :],
                       reads=(src_key,), writes=(f"{tag}xb{ib}",))
                ln_tile(p, xb[ib][:, :], f"{tag}xb{ib}", xn_tm, f"{tag}xn_tm",
                        stat, f"{tag}stat", tag)
                transpose_tm_to_fm(p, xn_tm, f"{tag}xn_tm", xn_fm, f"{tag}xn_fm",
                                   i, tc)
            for dt in range(16):
                ctx.op("sync", "dma_start",
                       dst_ap[rb + dt * 128: rb + (dt + 1) * 128,
                              ch * tc:(ch + 1) * tc],
                       xn_fm[:, dt * tc:(dt + 1) * tc],
                       reads=(f"{tag}xn_fm",), writes=(dst_key,))
    ctx.barrier()
    es.close()



def phase_l0attn(p, xres_src, xres_dst, xnfull_ap, xnfull_key, xnown_ap, xnown_key, tag):
    nc, ctx = p.nc, p.ctx
    es = ExitStack()
    sb = lambda n, s, d: es.enter_context(nc.sbuf_tensor(f"{tag}_{n}", s, d))

    Q = sb("Q", [128, 16 * T], BF16)
    ckv = sb("ckv", [128, 4 * S], BF16)
    krot = sb("krot", [64, S], BF16)

    esB = ExitStack()
    sbB = lambda n, s, d: esB.enter_context(nc.sbuf_tensor(f"{tag}B_{n}", s, d))
    xnq = sbB("xnq", [128, 16 * T], BF16)
    qlat = sbB("qlat", [128, 12 * T], BF16)
    qab = [sbB(f"qab{i}", [128, 2048], BF16) for i in range(2)]
    qbb = [sbB(f"qbb{i}", [128, 1536], BF16) for i in range(2)]
    qs = sbB("qs", [64, T], F32)
    qsw = sbB("qsw", [64, T], F32)
    c0q = sbB("c0q", [64, T], F32)
    s0q = sbB("s0q", [64, T], F32)
    QAT = p.dram_in("l0_qaTt", [12 * 128, 16 * 128])
    QBT = p.dram_in("l0_qbTt", [16 * 128, 12 * 128])
    ctx.op("sync", "dma_start", c0q[:, :], p.dram_in("L0Q_C", [64, T]),
           writes=("c0q",))
    ctx.op("sync", "dma_start", s0q[:, :], p.dram_in("L0Q_S", [64, T]),
           writes=("s0q",))
    for dt in range(16):
        ctx.op("sync", "dma_start", xnq[:, dt * T:(dt + 1) * T],
               xnown_ap[dt * 128:(dt + 1) * 128, :],
               reads=(xnown_key,), writes=("xnq",))
    ctx.op("gpsimd", "dma_start", qab[0][:, :], QAT[0:128, :],
           reads=("QAT",), writes=("qab0",))
    for lt in range(12):
        fb = lt % 2
        if lt + 1 < 12:
            ctx.op("gpsimd", "dma_start", qab[(lt + 1) % 2][:, :],
                   QAT[(lt + 1) * 128:(lt + 2) * 128, :],
                   reads=("QAT",), writes=(f"qab{(lt + 1) % 2}",))
        for half in range(2):
            bb = 2 * half
            for dt in range(16):
                ctx.op("tensor", "matmul", p.pb[bb][:, 0:512],
                       qab[fb][:, dt * 128:(dt + 1) * 128],
                       xnq[:, dt * T + half * 512: dt * T + (half + 1) * 512],
                       start=(dt == 0), stop=(dt == 15),
                       reads=(f"qab{fb}", "xnq"), writes=(f"pb{bb}",),
                       inc=(dt == 15))
            ctx.op("scalar", "copy",
                   qlat[:, lt * T + half * 512: lt * T + (half + 1) * 512],
                   p.pb[bb][:, 0:512], reads=(f"pb{bb}",), writes=("qlat",))
    ctx.op("gpsimd", "dma_start", qbb[0][:, :], QBT[0:128, :],
           reads=("QBT",), writes=("qbb0",))
    for qt in range(16):
        fb = qt % 2
        if qt + 1 < 16:
            ctx.op("gpsimd", "dma_start", qbb[(qt + 1) % 2][:, :],
                   QBT[(qt + 1) * 128:(qt + 2) * 128, :],
                   reads=("QBT",), writes=(f"qbb{(qt + 1) % 2}",))
        for half in range(2):
            bb = 4 + 2 * half
            for lt in range(12):
                ctx.op("tensor", "matmul", p.pb[bb][:, 0:512],
                       qbb[fb][:, lt * 128:(lt + 1) * 128],
                       qlat[:, lt * T + half * 512: lt * T + (half + 1) * 512],
                       start=(lt == 0), stop=(lt == 11),
                       reads=(f"qbb{fb}", "qlat"), writes=(f"pb{bb}",),
                       inc=(lt == 11))
        for half in range(2):
            bb = 4 + 2 * half
            ctx.op("scalar", "copy",
                   Q[0:64, qt * T + half * 512: qt * T + (half + 1) * 512],
                   p.pb[bb][0:64, 0:512], reads=(f"pb{bb}",), writes=("Q",))
            ctx.op("vector", "tensor_copy", qs[:, half * 512:(half + 1) * 512],
                   p.pb[bb][64:128, 0:512], reads=(f"pb{bb}",), writes=("qs",))
        for b in range(1):
            ctx.op("vector", "tensor_copy", qsw[0:32, :], qs[32:64, :],
                   reads=("qs",), writes=("qsw",))
            ctx.op("vector", "tensor_copy", qsw[32:64, :], qs[0:32, :],
                   reads=("qs",), writes=("qsw",))
        ctx.op("vector", "tensor_mul", qs[:, :], qs[:, :], c0q[:, :],
               reads=("qs", "c0q"), writes=("qs",))
        ctx.op("vector", "tensor_mul", qsw[:, :], qsw[:, :], s0q[:, :],
               reads=("qsw", "s0q"), writes=("qsw",))
        ctx.op("vector", "tensor_add", Q[64:128, qt * T:(qt + 1) * T],
               qs[:, :], qsw[:, :], reads=("qs", "qsw"), writes=("Q",))
    ctx.barrier()
    esB.close()

    esC = ExitStack()
    sbC = lambda n, s, d: esC.enter_context(nc.sbuf_tensor(f"{tag}C_{n}", s, d))
    kvaa = sbC("kvaa", [128, 4 * 2048], BF16)
    kvab = sbC("kvab", [128, 16 * 64], BF16)
    xk = [sbC(f"xk{i}", [128, 16 * 512], BF16) for i in range(2)]
    ckc = [sbC(f"ckc{i}", [64, 512], F32) for i in range(2)]
    sks = [sbC(f"sks{i}", [64, 512], F32) for i in range(2)]
    ks = sbC("ks", [64, 512], F32)
    ksw = sbC("ksw", [64, 512], F32)
    ctx.op("gpsimd", "dma_start", kvaa[:, :],
           p.dram_in("l0_kvaTaf", [128, 4 * 2048]), writes=("kvaa",))
    ctx.op("gpsimd", "dma_start", kvab[:, :],
           p.dram_in("l0_kvaTbf", [128, 16 * 64]), writes=("kvab",))
    C0K = p.dram_in("L0K_C", [64, S])
    S0K = p.dram_in("L0K_S", [64, S])

    def load_xk(kc, buf):
        r = kc // 2
        for dt in range(16):
            ctx.op("sync", "dma_start", xk[buf][:, dt * 512:(dt + 1) * 512],
                   xnfull_ap[r * 2048 + dt * 128: r * 2048 + (dt + 1) * 128,
                             (kc % 2) * 512:(kc % 2 + 1) * 512],
                   reads=(xnfull_key,), writes=(f"xk{buf}",))

    load_xk(0, 0)
    ctx.op("sync", "dma_start", ckc[0][:, :], C0K[:, 0:512], writes=("ckc0",))
    ctx.op("sync", "dma_start", sks[0][:, :], S0K[:, 0:512], writes=("sks0",))
    for kc in range(8):
        fb = kc % 2
        if kc + 1 < 8:
            load_xk(kc + 1, (kc + 1) % 2)
            ctx.op("sync", "dma_start", ckc[(kc + 1) % 2][:, :],
                   C0K[:, (kc + 1) * 512:(kc + 2) * 512],
                   writes=(f"ckc{(kc + 1) % 2}",))
            ctx.op("sync", "dma_start", sks[(kc + 1) % 2][:, :],
                   S0K[:, (kc + 1) * 512:(kc + 2) * 512],
                   writes=(f"sks{(kc + 1) % 2}",))
        for ct in range(4):
            for dt in range(16):
                ctx.op("tensor", "matmul", p.pb[ct][:, 0:512],
                       kvaa[:, ct * 2048 + dt * 128: ct * 2048 + (dt + 1) * 128],
                       xk[fb][:, dt * 512:(dt + 1) * 512],
                       start=(dt == 0), stop=(dt == 15),
                       reads=("kvaa", f"xk{fb}"), writes=(f"pb{ct}",),
                       inc=(dt == 15))
        for dt in range(16):
            ctx.op("tensor", "matmul", p.pb[4][0:64, 0:512],
                   kvab[:, dt * 64:(dt + 1) * 64],
                   xk[fb][:, dt * 512:(dt + 1) * 512],
                   start=(dt == 0), stop=(dt == 15),
                   reads=("kvab", f"xk{fb}"), writes=("pb4",), inc=(dt == 15))
        for ct in range(4):
            ctx.op("scalar", "copy", ckv[:, ct * S + kc * 512:(ct * S + (kc + 1) * 512)],
                   p.pb[ct][:, 0:512], reads=(f"pb{ct}",), writes=("ckv",))
        ctx.op("vector", "tensor_copy", ks[:, :], p.pb[4][0:64, 0:512],
               reads=("pb4",), writes=("ks",))
        ctx.op("vector", "tensor_copy", ksw[0:32, :], ks[32:64, :],
               reads=("ks",), writes=("ksw",))
        ctx.op("vector", "tensor_copy", ksw[32:64, :], ks[0:32, :],
               reads=("ks",), writes=("ksw",))
        ctx.op("vector", "tensor_mul", ks[:, :], ks[:, :], ckc[fb][:, :],
               reads=("ks", f"ckc{fb}"), writes=("ks",))
        ctx.op("vector", "tensor_mul", ksw[:, :], ksw[:, :], sks[fb][:, :],
               reads=("ksw", f"sks{fb}"), writes=("ksw",))
        ctx.op("vector", "tensor_add", krot[:, kc * 512:(kc + 1) * 512],
               ks[:, :], ksw[:, :], reads=("ks", "ksw"), writes=("krot",))
    ctx.barrier()
    esC.close()

    esD = ExitStack()
    sbD = lambda n, s, d: esD.enter_context(nc.sbuf_tensor(f"{tag}D_{n}", s, d))
    wkb = sbD("wkb", [128, 16 * 256], BF16)
    wvb = sbD("wvb", [128, 4 * 2048], BF16)
    Vq = sbD("Vq", [128, 32 * 512], BF16)
    Kf = [sbD(f"Kf{i}", [128, S], BF16) for i in range(2)]
    Eb = [sbD(f"Eb{i}", [128, 1024], BF16) for i in range(2)]
    den = sbD("den", [1, 512], F32)
    bc = sbD("bc", [128, 512], F32)
    attn_out = sbD("attn_out", [128, 16 * T], BF16)
    ctx.op("gpsimd", "dma_start", wkb[:, :],
           p.dram_in("l0_WkbTf", [128, 16 * 256]), writes=("wkb",))
    ctx.op("gpsimd", "dma_start", wvb[:, :],
           p.dram_in("l0_WvbTf", [128, 4 * 2048]), writes=("wvb",))
    for g in range(4):
        for kt in range(32):
            ctx.op("tensor", "matmul", p.pb[6][:, 0:512],
                   ckv[:, 0 * S + kt * 128: 0 * S + (kt + 1) * 128],
                   wvb[:, 0 * 2048 + g * 512:(0 * 2048 + (g + 1) * 512)],
                   start=True, stop=False,
                   reads=("ckv", "wvb"), writes=("pb6",), inc=False)
            for ct in range(1, 4):
                ctx.op("tensor", "matmul", p.pb[6][:, 0:512],
                       ckv[:, ct * S + kt * 128: ct * S + (kt + 1) * 128],
                       wvb[:, ct * 2048 + g * 512: ct * 2048 + (g + 1) * 512],
                       start=False, stop=(ct == 3),
                       reads=("ckv", "wvb"), writes=("pb6",), inc=(ct == 3))
            ctx.op("scalar", "copy", Vq[:, kt * 512:(kt + 1) * 512],
                   p.pb[6][:, 0:512], reads=("pb6",), writes=("Vq",))
        for hh in range(4):
            h = g * 4 + hh
            hb = hh % 2
            for kc in range(8):
                for ct in range(4):
                    ctx.op("tensor", "matmul", p.pb[7][0:64, 0:512],
                           wkb[:, h * 256 + ct * 64: h * 256 + (ct + 1) * 64],
                           ckv[:, ct * S + kc * 512: ct * S + (kc + 1) * 512],
                           start=(ct == 0), stop=(ct == 3),
                           reads=("wkb", "ckv"), writes=("pb7",), inc=(ct == 3))
                ctx.op("scalar", "copy", Kf[hb][0:64, kc * 512:(kc + 1) * 512],
                       p.pb[7][0:64, 0:512], reads=("pb7",), writes=(f"Kf{hb}",))
            ctx.op("vector", "tensor_copy", Kf[hb][64:128, :], krot[:, :],
                   reads=("krot",), writes=(f"Kf{hb}",))
            for u in range(2):
                for pp in range(17):
                    if pp < 16:
                        b0 = 4 + (pp % 2) * 2
                        for j in range(2):
                            kt = 2 * pp + j
                            ctx.op("tensor", "matmul", p.pb[b0 + j][:, 0:512],
                                   Kf[hb][:, kt * 128:(kt + 1) * 128],
                                   Q[:, h * T + u * 512: h * T + (u + 1) * 512],
                                   start=True, stop=True,
                                   reads=(f"Kf{hb}", "Q"), writes=(f"pb{b0 + j}",))
                        ctx.op("scalar", "activation", Eb[pp % 2][:, :],
                               p.psa[:, b0 * 512:(b0 + 2) * 512], AF.Exp,
                               bias=-20.0,
                               reads=(f"pb{b0}", f"pb{b0 + 1}"),
                               writes=(f"Eb{pp % 2}",))
                    if pp > 0:
                        q_ = pp - 1
                        for j in range(2):
                            kt = 2 * q_ + j
                            ctx.op("tensor", "matmul", p.pb[2][:, 0:512],
                                   Vq[:, kt * 512 + hh * 128: kt * 512 + (hh + 1) * 128],
                                   Eb[q_ % 2][:, j * 512:(j + 1) * 512],
                                   start=(kt == 0), stop=(kt == 31),
                                   reads=("Vq", f"Eb{q_ % 2}"), writes=("pb2",),
                                   inc=(kt == 31))
                            ctx.op("tensor", "matmul", p.pb[3][0:1, 0:512],
                                   p.c_ones[:, :],
                                   Eb[q_ % 2][:, j * 512:(j + 1) * 512],
                                   start=(kt == 0), stop=(kt == 31),
                                   reads=("c:ONES128", f"Eb{q_ % 2}"), writes=("pb3",),
                                   inc=(kt == 31))
                ctx.op("vector", "reciprocal", den[:, :], p.pb[3][0:1, 0:512],
                       reads=("pb3",), writes=("den",))
                ctx.op("tensor", "matmul", p.pb[0][:, 0:512],
                       p.c_onesrowf[:, :], den[:, :], start=True, stop=True,
                       reads=("c:ONESROWF", "den"), writes=("pb0",))
                ctx.op("vector", "tensor_copy", bc[:, :], p.pb[0][:, 0:512],
                       reads=("pb0",), writes=("bc",))
                ctx.op("vector", "tensor_mul",
                       attn_out[:, h * T + u * 512: h * T + (u + 1) * 512],
                       p.pb[2][:, 0:512], bc[:, :],
                       reads=("pb2", "bc"), writes=("attn_out",))
    ctx.barrier()
    esD.close()

    esE = ExitStack()
    sbE = lambda n, s, d: esE.enter_context(nc.sbuf_tensor(f"{tag}E_{n}", s, d))
    owb = [sbE(f"owb{i}", [128, D], BF16) for i in range(2)]
    xo = sbE("xo", [128, D], F32)
    xob = [sbE(f"xob{i}", [128, D], F32) for i in range(2)]
    OWT = p.dram_in("l0_owT", [D, D])
    for tt in range(8):
        ctx.op("sync", "dma_start", xob[tt % 2][:, :],
               xres_src[tt * 128:(tt + 1) * 128, :],
               reads=("xres_src",), writes=(f"xob{tt % 2}",))
        ctx.op("gpsimd", "dma_start", owb[0][:, :], OWT[0:128, :],
               reads=("OWT",), writes=("owb0",))
        for ht in range(16):
            fb = ht % 2
            if ht + 1 < 16:
                ctx.op("gpsimd", "dma_start", owb[(ht + 1) % 2][:, :],
                       OWT[(ht + 1) * 128:(ht + 2) * 128, :],
                       reads=("OWT",), writes=(f"owb{(ht + 1) % 2}",))
            for dc in range(4):
                b = 4 * (tt % 2) + dc
                ctx.op("tensor", "matmul", p.pb[b][:, 0:512],
                       attn_out[:, ht * T + tt * 128: ht * T + (tt + 1) * 128],
                       owb[fb][:, dc * 512:(dc + 1) * 512],
                       start=(ht == 0), stop=(ht == 15),
                       reads=("attn_out", f"owb{fb}"), writes=(f"pb{b}",),
                       inc=(dc == 3))
        for dc in range(4):
            b = 4 * (tt % 2) + dc
            ctx.op("vector", "tensor_add", xo[:, dc * 512:(dc + 1) * 512],
                   p.pb[b][:, 0:512], xob[tt % 2][:, dc * 512:(dc + 1) * 512],
                   reads=(f"pb{b}", f"xob{tt % 2}"), writes=("xo",))
        ctx.op("sync", "dma_start", xres_dst[tt * 128:(tt + 1) * 128, :],
               xo[:, :], reads=("xo",), writes=("xres_dst",))
    ctx.barrier()
    esE.close()
    es.close()



def phase_l1attn(p, xres_src, xres_dst, xnfull_ap, xnfull_key, xnown_ap, xnown_key, tag):
    nc, ctx = p.nc, p.ctx
    es = ExitStack()
    sb = lambda n, s, d: es.enter_context(nc.sbuf_tensor(f"{tag}_{n}", s, d))
    Q = sb("Q", [128, 16 * T], BF16)
    Ksb = sb("Ksb", [128, 4 * S], BF16)
    Vsb = sb("Vsb", [128, 32 * 520], BF16)
    attn_out = sb("attn_out", [128, 16 * T], BF16)

    esB = ExitStack()
    sbB = lambda n, s, d: esB.enter_context(nc.sbuf_tensor(f"{tag}B_{n}", s, d))
    xnq = sbB("xnq", [128, 16 * T], BF16)
    wqb = [sbB(f"wqb{i}", [128, 2048], BF16) for i in range(2)]
    qs = sbB("qs", [128, T], F32)
    qsw = sbB("qsw", [128, T], F32)
    c1q = sbB("c1q", [128, T], F32)
    s1q = sbB("s1q", [128, T], F32)
    WQT = p.dram_in("l1_WqTt", [16 * 128, 16 * 128])
    ctx.op("sync", "dma_start", c1q[:, :], p.dram_in("L1Q_C", [128, T]),
           writes=("c1q",))
    ctx.op("sync", "dma_start", s1q[:, :], p.dram_in("L1Q_S", [128, T]),
           writes=("s1q",))
    for dt in range(16):
        ctx.op("sync", "dma_start", xnq[:, dt * T:(dt + 1) * T],
               xnown_ap[dt * 128:(dt + 1) * 128, :],
               reads=(xnown_key,), writes=("xnq",))
    ctx.op("gpsimd", "dma_start", wqb[0][:, :], WQT[0:128, :],
           reads=("WQT",), writes=("wqb0",))
    for qt in range(16):
        fb = qt % 2
        if qt + 1 < 16:
            ctx.op("gpsimd", "dma_start", wqb[(qt + 1) % 2][:, :],
                   WQT[(qt + 1) * 128:(qt + 2) * 128, :],
                   reads=("WQT",), writes=(f"wqb{(qt + 1) % 2}",))
        for half in range(2):
            bb = 2 * half
            for dt in range(16):
                ctx.op("tensor", "matmul", p.pb[bb][:, 0:512],
                       wqb[fb][:, dt * 128:(dt + 1) * 128],
                       xnq[:, dt * T + half * 512: dt * T + (half + 1) * 512],
                       start=(dt == 0), stop=(dt == 15),
                       reads=(f"wqb{fb}", "xnq"), writes=(f"pb{bb}",),
                       inc=(dt == 15))
            ctx.op("vector", "tensor_copy", qs[:, half * 512:(half + 1) * 512],
                   p.pb[bb][:, 0:512], reads=(f"pb{bb}",), writes=("qs",))
        for b in range(4):
            ctx.op("vector", "tensor_copy", qsw[b * 32:(b + 1) * 32, :],
                   qs[b * 32 + 32 - 64 * (b % 2):b * 32 + 64 - 64 * (b % 2), :],
                   reads=("qs",), writes=("qsw",))
        ctx.op("vector", "tensor_mul", qs[:, :], qs[:, :], c1q[:, :],
               reads=("qs", "c1q"), writes=("qs",))
        ctx.op("vector", "tensor_mul", qsw[:, :], qsw[:, :], s1q[:, :],
               reads=("qsw", "s1q"), writes=("qsw",))
        ctx.op("vector", "tensor_add", Q[:, qt * T:(qt + 1) * T],
               qs[:, :], qsw[:, :], reads=("qs", "qsw"), writes=("Q",))
    ctx.barrier()
    esB.close()

    esC = ExitStack()
    sbC = lambda n, s, d: esC.enter_context(nc.sbuf_tensor(f"{tag}C_{n}", s, d))
    wk = sbC("wk", [128, 4 * 2048], BF16)
    wv = sbC("wv", [128, 16 * 512], BF16)
    xk = [sbC(f"xk{i}", [128, 16 * 512], BF16) for i in range(2)]
    c1k = [sbC(f"c1k{i}", [128, 512], F32) for i in range(2)]
    s1k = [sbC(f"s1k{i}", [128, 512], F32) for i in range(2)]
    ks = sbC("ks", [128, 512], F32)
    ksw = sbC("ksw", [128, 512], F32)
    ctx.op("gpsimd", "dma_start", wk[:, :],
           p.dram_in("l1_WkTf", [128, 4 * 2048]), writes=("wk",))
    ctx.op("gpsimd", "dma_start", wv[:, :],
           p.dram_in("l1_WvTf", [128, 16 * 512]), writes=("wv",))
    C1K = p.dram_in("L1K_C", [128, S])
    S1K = p.dram_in("L1K_S", [128, S])
    ctx.op("vector", "memset",
           Vsb[:, :].rearrange("p (t g c) -> p t g c", g=8, c=65)[:, :, :, 64:65],
           1.0, writes=("Vsb",))

    def load_xk(kc, buf):
        r = kc // 2
        for dt in range(16):
            ctx.op("sync", "dma_start", xk[buf][:, dt * 512:(dt + 1) * 512],
                   xnfull_ap[r * 2048 + dt * 128: r * 2048 + (dt + 1) * 128,
                             (kc % 2) * 512:(kc % 2 + 1) * 512],
                   reads=(xnfull_key,), writes=(f"xk{buf}",))

    load_xk(0, 0)
    ctx.op("sync", "dma_start", c1k[0][:, :], C1K[:, 0:512], writes=("c1k0",))
    ctx.op("sync", "dma_start", s1k[0][:, :], S1K[:, 0:512], writes=("s1k0",))
    for kc in range(8):
        fb = kc % 2
        if kc + 1 < 8:
            load_xk(kc + 1, (kc + 1) % 2)
            ctx.op("sync", "dma_start", c1k[(kc + 1) % 2][:, :],
                   C1K[:, (kc + 1) * 512:(kc + 2) * 512],
                   writes=(f"c1k{(kc + 1) % 2}",))
            ctx.op("sync", "dma_start", s1k[(kc + 1) % 2][:, :],
                   S1K[:, (kc + 1) * 512:(kc + 2) * 512],
                   writes=(f"s1k{(kc + 1) % 2}",))
        for kb in range(4):
            for dt in range(16):
                ctx.op("tensor", "matmul", p.pb[kb][:, 0:512],
                       wk[:, kb * 2048 + dt * 128: kb * 2048 + (dt + 1) * 128],
                       xk[fb][:, dt * 512:(dt + 1) * 512],
                       start=(dt == 0), stop=(dt == 15),
                       reads=("wk", f"xk{fb}"), writes=(f"pb{kb}",),
                       inc=(dt == 15))
            ctx.op("vector", "tensor_copy", ks[:, :], p.pb[kb][:, 0:512],
                   reads=(f"pb{kb}",), writes=("ks",))
            for b in range(4):
                ctx.op("vector", "tensor_copy", ksw[b * 32:(b + 1) * 32, :],
                       ks[b * 32 + 32 - 64 * (b % 2):b * 32 + 64 - 64 * (b % 2), :],
                       reads=("ks",), writes=("ksw",))
            ctx.op("vector", "tensor_mul", ks[:, :], ks[:, :], c1k[fb][:, :],
                   reads=("ks", f"c1k{fb}"), writes=("ks",))
            ctx.op("vector", "tensor_mul", ksw[:, :], ksw[:, :], s1k[fb][:, :],
                   reads=("ksw", f"s1k{fb}"), writes=("ksw",))
            ctx.op("vector", "tensor_add",
                   Ksb[:, kb * S + kc * 512: kb * S + (kc + 1) * 512],
                   ks[:, :], ksw[:, :], reads=("ks", "ksw"), writes=("Ksb",))
        for k4 in range(4):
            kt = kc * 4 + k4
            for dt in range(16):
                ctx.op("tensor", "matmul", p.pb[4][:, 0:512],
                       xk[fb][:, dt * 512 + k4 * 128: dt * 512 + (k4 + 1) * 128],
                       wv[:, dt * 512:(dt + 1) * 512],
                       start=(dt == 0), stop=(dt == 15),
                       reads=(f"xk{fb}", "wv"), writes=("pb4",), inc=(dt == 15))
            ctx.op("vector", "tensor_copy",
                   Vsb[:, kt * 520:(kt + 1) * 520]
                   .rearrange("p (g c) -> p g c", c=65)[:, :, 0:64],
                   p.pb[4][:, 0:512].rearrange("p (g c) -> p g c", c=64),
                   reads=("pb4",), writes=("Vsb",))
    ctx.barrier()
    esC.close()

    esD = ExitStack()
    sbD = lambda n, s, d: esD.enter_context(nc.sbuf_tensor(f"{tag}D_{n}", s, d))
    Eb = [sbD(f"Eb{i}", [128, 1024], BF16) for i in range(2)]
    den = sbD("den", [1, 512], F32)
    bc = sbD("bc", [64, 512], F32)
    for kvh in range(8):
        for u in range(2):
            for hp in range(2):
                hA, hB = kvh * 4 + 2 * hp, kvh * 4 + 2 * hp + 1
                pos = [l1_qpos(hA), l1_qpos(hB)]
                for qt_, qb_ in pos:
                    assert qb_ == (kvh % 2) * 64
                for pp in range(33):
                    if pp < 32:
                        kt = pp
                        b0 = 4 + (pp % 2) * 2
                        for j, (qt_, qb_) in enumerate(pos):
                            ctx.op("tensor", "matmul", p.pb[b0 + j][:, 0:512],
                                   Ksb[(kvh % 2) * 64:(kvh % 2) * 64 + 64,
                                       (kvh // 2) * S + kt * 128:(kvh // 2) * S + (kt + 1) * 128],
                                   Q[qb_:qb_ + 64,
                                     qt_ * T + u * 512: qt_ * T + (u + 1) * 512],
                                   start=True, stop=True,
                                   reads=("Ksb", "Q"), writes=(f"pb{b0 + j}",))
                        ctx.op("scalar", "activation", Eb[pp % 2][:, :],
                               p.psa[:, b0 * 512:(b0 + 2) * 512], AF.Exp,
                               bias=-20.0,
                               reads=(f"pb{b0}", f"pb{b0 + 1}"),
                               writes=(f"Eb{pp % 2}",))
                    if pp > 0:
                        kt = pp - 1
                        q_ = pp - 1
                        for j in range(2):
                            ctx.op("tensor", "matmul", p.pb[2 + j][0:65, 0:512],
                                   Vsb[:, kt * 520 + kvh * 65: kt * 520 + (kvh + 1) * 65],
                                   Eb[q_ % 2][:, j * 512:(j + 1) * 512],
                                   start=(kt == 0), stop=(kt == 31),
                                   reads=("Vsb", f"Eb{q_ % 2}"), writes=(f"pb{2 + j}",),
                                   inc=(kt == 31))
                for j, (qt_, qb_) in enumerate(pos):
                    ctx.op("vector", "reciprocal", den[:, :],
                           p.pb[2 + j][64:65, 0:512],
                           reads=(f"pb{2 + j}",), writes=("den",))
                    ctx.op("tensor", "matmul", p.pb[0][0:64, 0:512],
                           p.c_onesrowf[:, 0:64], den[:, :], start=True, stop=True,
                           reads=("c:ONESROWF", "den"), writes=("pb0",))
                    ctx.op("vector", "tensor_copy", bc[:, :], p.pb[0][0:64, 0:512],
                           reads=("pb0",), writes=("bc",))
                    ctx.op("vector", "tensor_mul",
                           attn_out[qb_:qb_ + 64,
                                    qt_ * T + u * 512: qt_ * T + (u + 1) * 512],
                           p.pb[2 + j][0:64, 0:512], bc[:, :],
                           reads=(f"pb{2 + j}", "bc"), writes=("attn_out",))
    ctx.barrier()
    esD.close()

    esE = ExitStack()
    sbE = lambda n, s, d: esE.enter_context(nc.sbuf_tensor(f"{tag}E_{n}", s, d))
    owb = [sbE(f"owb{i}", [128, D], BF16) for i in range(2)]
    xo = sbE("xo", [128, D], F32)
    xob = [sbE(f"xob{i}", [128, D], F32) for i in range(2)]
    OWT = p.dram_in("l1_WoT", [D, D])
    for tt in range(8):
        ctx.op("sync", "dma_start", xob[tt % 2][:, :],
               xres_src[tt * 128:(tt + 1) * 128, :],
               reads=("xres_src",), writes=(f"xob{tt % 2}",))
        ctx.op("gpsimd", "dma_start", owb[0][:, :], OWT[0:128, :],
               reads=("OWT1",), writes=("owb0",))
        for ht in range(16):
            fb = ht % 2
            if ht + 1 < 16:
                ctx.op("gpsimd", "dma_start", owb[(ht + 1) % 2][:, :],
                       OWT[(ht + 1) * 128:(ht + 2) * 128, :],
                       reads=("OWT1",), writes=(f"owb{(ht + 1) % 2}",))
            for dc in range(4):
                b = 4 * (tt % 2) + dc
                ctx.op("tensor", "matmul", p.pb[b][:, 0:512],
                       attn_out[:, ht * T + tt * 128: ht * T + (tt + 1) * 128],
                       owb[fb][:, dc * 512:(dc + 1) * 512],
                       start=(ht == 0), stop=(ht == 15),
                       reads=("attn_out", f"owb{fb}"), writes=(f"pb{b}",),
                       inc=(dc == 3))
        for dc in range(4):
            b = 4 * (tt % 2) + dc
            ctx.op("vector", "tensor_add", xo[:, dc * 512:(dc + 1) * 512],
                   p.pb[b][:, 0:512], xob[tt % 2][:, dc * 512:(dc + 1) * 512],
                   reads=(f"pb{b}", f"xob{tt % 2}"), writes=("xo",))
        ctx.op("sync", "dma_start", xres_dst[tt * 128:(tt + 1) * 128, :],
               xo[:, :], reads=("xo",), writes=("xres_dst",))
    ctx.barrier()
    esE.close()
    es.close()




def build_fused():
    nc = bass.Bass("TRN2", target_bir_lowering=False)
    p = Prog(nc)
    xin = p.dram_in("x_in", [T, D], F32)
    xout = p.dram_out("x_out", [T, D], F32)
    p.setup_common()
    xa = p.dram_int("xa", [T, D], F32).ap()
    xb = p.dram_int("xb", [T, D], F32).ap()
    w2p = p.dram_int("w2p", [4 * FF, 512], BF16)
    cc_in = p.dram_int("cc_in", [2048, T], BF16)
    cc_out = p.dram_int("cc_out", [8192, T], BF16)
    ctx = p.ctx

    p.phase_marks = []

    def mark(nm):
        p.phase_marks.append((nm, p.nc.n_instructions()))

    def exchange(src_ap, src_key, tag):
        phase_xnfull(p, src_ap, src_key, cc_in.ap(), "cc_in", [(0, 0)], tag)
        ctx.op("gpsimd", "collective_compute", "AllGather", ALU.bypass,
               replica_groups=[[0, 1, 2, 3], [4, 5, 6, 7]],
               ins=[cc_in.ap().opt()], outs=[cc_out.ap().opt()],
               reads=("cc_in",), writes=("cc_out",))
        ctx.barrier()

    cvt = [("l0_W1Tt", [64 * 128, 16 * 128]), ("l0_W2Tdc", [4 * FF, 512]),
           ("l1_W1Tt", [64 * 128, 16 * 128]), ("l1_W3Tt", [64 * 128, 16 * 128])]
    p.wc = {}
    items = []
    for nm, shp in cvt:
        h = p.dram_int(f"wc_{nm}", shp, BF16)
        p.wc[nm] = h
        items.append((nm, shp, h))
    mark("start")
    phase_bf16conv(p, items)
    mark("bf16conv")
    phase_w2prep(p, w2p)
    mark("w2prep")
    phase_l0ffn(p, xin, xa, "f0a")
    mark("l0ffn_a")
    exchange(xa, "xa", "xf0")
    mark("exch0")
    phase_l0attn(p, xa, xb, cc_out.ap(), "cc_out", cc_in.ap(), "cc_in", "a0")
    mark("l0attn")
    phase_l0ffn(p, xb, xa, "f0b")
    mark("l0ffn_b")
    phase_l1ffn(p, xa, xb, w2p, "f1a")
    mark("l1ffn_a")
    exchange(xb, "xb", "xf1")
    mark("exch1")
    phase_l1attn(p, xb, xa, cc_out.ap(), "cc_out", cc_in.ap(), "cc_in", "a1")
    mark("l1attn")
    phase_l1ffn(p, xa, xout, w2p, "f1b")
    mark("l1ffn_b")
    return p





def kernel(**inputs):
    inputs = {k: np.asarray(v) for k, v in inputs.items()}
    W = prep_weights(inputs)
    x = np.ascontiguousarray(inputs["L0_attn_x"]).astype(np.float32)
    p = build_fused()
    in_maps = []
    for c in range(8):
        b, r = c // 4, c % 4
        consts = host_consts(c)
        m = {}
        for name in p.inp:
            if name in consts:
                m[name] = consts[name]
            elif name in W:
                m[name] = W[name]
            elif name == "x_in":
                m[name] = np.ascontiguousarray(x[b, r * 1024:(r + 1) * 1024, :])
            else:
                raise KeyError(name)
        in_maps.append(m)
    res = run_bass_kernel_spmd(p.nc, in_maps, core_ids=list(range(8)))
    out = np.empty_like(x)
    for c in range(8):
        b, r = c // 4, c % 4
        out[b, r * 1024:(r + 1) * 1024, :] = res.results[c]["x_out"]
    return out


# version 08
# speedup vs baseline: 1.4529x; 1.0585x over previous
import math
from contextlib import ExitStack

import numpy as np
import ml_dtypes
import concourse.bass as bass
import concourse.mybir as mybir
from concourse.bass_utils import run_bass_kernel_spmd

F32 = mybir.dt.float32
BF16 = mybir.dt.bfloat16
AF = mybir.ActivationFunctionType
ALU = mybir.AluOpType
AXX = mybir.AxisListType.X

T = 1024
D = 2048
FF = 8192
S = 4096

CVALS = [0.0, 1.0, -1.0, 0.5, 2.0, -20.0, 1e-5,
         float(np.float32(0.5) * np.float32(1.0 / np.sqrt(8192.0))),
         -1e9]

BGKEEP = ("d:gpsimd:cv2b", "d:gpsimd:cvw:")


ENGINES = ("tensor", "vector", "scalar", "gpsimd", "sync")

L1_EVENKV = [h for h in range(32) if (h // 4) % 2 == 0]
L1_ODDKV = [h for h in range(32) if (h // 4) % 2 == 1]


def l1_qpos(h):
    if (h // 4) % 2 == 0:
        return L1_EVENKV.index(h), 0
    return L1_ODDKV.index(h), 64


class Ctx:

    def __init__(self, nc):
        self.nc = nc
        self.es = ExitStack()
        self.eng = dict(tensor=nc.tensor, vector=nc.vector, scalar=nc.scalar,
                        gpsimd=nc.gpsimd, sync=nc.sync)
        self.sem = {e: self.es.enter_context(nc.semaphore(f"sem_{e}")) for e in ENGINES}
        self.cnt = {e: 0 for e in ENGINES}
        self.waited = {}
        self.res = {}

    def sb(self, name, shape, dtype):
        return self.es.enter_context(self.nc.sbuf_tensor(name, shape, dtype))

    def ps(self, name, shape, dtype):
        return self.es.enter_context(self.nc.psum_tensor(name, shape, dtype))

    def _dma_semkey(self, key, eng):
        sk = f"d:{eng}:{key}"
        if sk not in self.sem:
            self.sem[sk] = self.es.enter_context(self.nc.semaphore(sk))
            self.cnt[sk] = 0
        return sk

    def _need(self, eng, key, write, need):
        r = self.res.get(key)
        if r is None:
            return
        w = r.get("w")
        if w is not None:
            wsem, wval = w
            if not (wsem == "tensor" and eng == "tensor"):
                need[wsem] = max(need.get(wsem, 0), wval)
        if write:
            for rsem, rval in r.get("r", {}).items():
                if rsem == "tensor" and eng == "tensor":
                    continue
                need[rsem] = max(need.get(rsem, 0), rval)

    def _record(self, semkey, key, write, val):
        r = self.res.setdefault(key, {"w": None, "r": {}})
        if write:
            r["w"] = (semkey, val)
            r["r"] = {}
        else:
            r["r"][semkey] = max(r["r"].get(semkey, 0), val)

    def _emit_waits(self, eng, need):
        e = self.eng[eng]
        for tgt, val in sorted(need.items()):
            if val <= 0 or self.waited.get((eng, tgt), 0) >= val:
                continue
            if val > self.cnt[tgt]:
                raise RuntimeError(f"{eng} waits {tgt}>={val} but cnt={self.cnt[tgt]}")
            e.wait_ge(self.sem[tgt], val)
            self.waited[(eng, tgt)] = val

    def op(self, eng, method, *args, reads=(), writes=(), inc=True, **kw):
        need = {}
        for k in reads:
            self._need(eng, k, False, need)
        for k in writes:
            self._need(eng, k, True, need)
        self._emit_waits(eng, need)
        ins = getattr(self.eng[eng], method)(*args, **kw)
        is_dma = method in ("dma_start", "collective_compute", "dma_start_transpose")
        if is_dma:
            assert writes, f"dma op needs a write resource ({method})"
            semkey = self._dma_semkey(writes[0], eng)
            amt = 1 if method == "collective_compute" else 16
            ins.then_inc(self.sem[semkey], amt)
            self.cnt[semkey] += amt
            val = self.cnt[semkey]
        elif inc:
            semkey = eng
            ins.then_inc(self.sem[eng], 1)
            self.cnt[eng] += 1
            val = self.cnt[eng]
        else:
            semkey = eng
            val = self.cnt[eng] + 1
        for k in reads:
            self._record(semkey, k, False, val)
        for k in writes:
            self._record(semkey, k, True, val)
        return ins

    def barrier(self, keep=()):
        def kept(semkey):
            return any(semkey.startswith(k) for k in keep)
        for a in ENGINES:
            for b, cnt in self.cnt.items():
                if a == b or cnt == 0 or kept(b):
                    continue
                if self.waited.get((a, b), 0) < cnt:
                    self.eng[a].wait_ge(self.sem[b], cnt)
                    self.waited[(a, b)] = cnt
        for key, r in list(self.res.items()):
            semkeys = set()
            if r.get("w"):
                semkeys.add(r["w"][0])
            semkeys |= set(r.get("r", {}).keys())
            if not any(kept(s) for s in semkeys):
                del self.res[key]



def _stat_tiles(wT, kt=128, ct=128):
    K, M = wT.shape
    a = wT.reshape(K // kt, kt, M // ct, ct).transpose(2, 1, 0, 3)
    return np.ascontiguousarray(a.reshape(M // ct * kt, (K // kt) * ct))


def _stat_flat(wT, kt=128, ct=128):
    K, M = wT.shape
    a = wT.reshape(K // kt, kt, M // ct, ct).transpose(1, 2, 0, 3)
    return np.ascontiguousarray(a.reshape(kt, (M // ct) * (K // kt) * ct))


def prep_weights(w):
    o = {}
    o["l0_W1Tt"] = _stat_tiles(w["L0_ffn_W1"].T)
    o["l0_W2Tdc"] = np.ascontiguousarray(
        w["L0_ffn_W2"].T.reshape(FF, 4, 512).transpose(1, 0, 2).reshape(4 * FF, 512))
    o["l0_WrTt"] = _stat_tiles(w["L0_ffn_W_r"].T, ct=8)
    o["l0_br"] = np.ascontiguousarray(w["L0_ffn_b_r"].reshape(1, 8))
    o["l0_A1Tt"] = _stat_tiles(w["L0_ffn_A1"].reshape(64, D).T, ct=64)
    B1rf = w["L0_ffn_B1"].transpose(0, 2, 1).reshape(64, FF)
    o["l0_B1aug"] = np.ascontiguousarray(
        np.concatenate([B1rf, w["L0_ffn_b1"].reshape(1, FF)], 0))
    o["l0_A2Tt"] = _stat_tiles(w["L0_ffn_A2"].reshape(64, FF).T, ct=64)
    B2rf = w["L0_ffn_B2"].transpose(0, 2, 1).reshape(64, D)
    o["l0_B2aug"] = np.ascontiguousarray(
        np.concatenate([B2rf, w["L0_ffn_b2"].reshape(1, D)], 0))
    H0, QKD, NOPE, VD0, QR, KVR = 16, 128, 64, 128, 1536, 512
    o["l0_qaTt"] = _stat_tiles(w["L0_attn_q_a_w"].T)
    qb = w["L0_attn_q_b_w"].reshape(H0, QKD, QR)
    perm = (list(range(NOPE)) + [NOPE + 2 * i for i in range(32)]
            + [NOPE + 2 * i + 1 for i in range(32)])
    o["l0_qbTt"] = _stat_tiles(
        np.ascontiguousarray(qb[:, perm, :].reshape(H0 * QKD, QR).T))
    kva = w["L0_attn_kv_a_w"]
    rot = kva[KVR:]
    kva_p = np.concatenate([kva[:KVR], rot[0::2], rot[1::2]], 0)
    kvaT = np.ascontiguousarray(kva_p.T)
    o["l0_kvaTaf"] = _stat_flat(kvaT[:, 0:KVR])
    o["l0_kvaTbf"] = _stat_flat(kvaT[:, KVR:], ct=64)
    kvb = w["L0_attn_kv_b_w"].reshape(H0, NOPE + VD0, KVR)
    o["l0_WkbTf"] = _stat_flat(
        np.ascontiguousarray(kvb[:, :NOPE, :].reshape(H0 * NOPE, KVR).T), ct=64)
    o["l0_WvbTf"] = _stat_flat(
        np.ascontiguousarray(kvb[:, NOPE:, :].reshape(H0 * VD0, KVR).T), ct=2048)
    o["l0_owT"] = np.ascontiguousarray(w["L0_attn_o_w"].T)
    order = []
    for j in range(16):
        order += [L1_EVENKV[j], L1_ODDKV[j]]
    rowperm = np.concatenate([np.arange(h * 64, (h + 1) * 64) for h in order])
    o["l1_WqTt"] = _stat_tiles(np.ascontiguousarray(w["L1_attn_Wq"][rowperm].T))
    o["l1_WkTf"] = _stat_flat(w["L1_attn_Wk"].T)
    o["l1_WvTf"] = _stat_flat(w["L1_attn_Wv"].T, ct=512)
    o["l1_WoT"] = np.ascontiguousarray(w["L1_attn_Wo"].T[rowperm])
    o["l1_W1Tt"] = _stat_tiles(w["L1_ffn_W1"].T)
    o["l1_W3Tt"] = _stat_tiles(w["L1_ffn_W3"].T)
    o["l1_W2G"] = np.ascontiguousarray(w["L1_ffn_W2"].T.reshape(64, 128 * D))
    return o


def host_consts(core_id):
    c = {}
    c["CONSTS"] = np.tile(np.array(CVALS, np.float32), (128, 1))
    c["ID128"] = np.eye(128, dtype=np.float32)
    c["ONESROW"] = np.ones((1, 128), np.float32)
    c["ONESROWF"] = np.ones((1, 128), np.float32)
    c["ONES128"] = np.ones((128, 1), np.float32)
    e8 = np.zeros((8, 64), np.float32)
    for e in range(8):
        e8[e, e * 8:(e + 1) * 8] = 2.0
    c["E8X2"] = e8
    esel = np.zeros((8, 1024), np.float32)
    for j in range(8):
        esel[j, j * 128:(j + 1) * 128] = 1.0
    c["ESEL8"] = esel
    c["EPSIDX"] = (np.arange(8, dtype=np.float32) * np.float32(-1e-6)).reshape(1, 8)

    def had(n):
        h = np.array([[1.0]], np.float32)
        while h.shape[0] < n:
            h = np.block([[h, h], [h, -h]])
        return np.ascontiguousarray(h)
    c["H128"] = had(128)
    c["H64"] = had(64)
    q0 = ((core_id % 4) * T + np.arange(T)).astype(np.float32)
    kpos = np.arange(S, dtype=np.float32)
    th = (np.float32(1.0)
          / np.float32(10000.0) ** (np.arange(0, 64, 2, dtype=np.float32)
                                    / np.float32(64.0))).astype(np.float32)
    for nm, pos, scl, rep in (("L0Q", q0, 1.0, 1), ("L0K", kpos, 1.0, 1),
                              ("L1Q", q0, 0.125, 2), ("L1K", kpos, 1.0, 2)):
        ang = (pos[None, :] * th[:, None]).astype(np.float32)
        cs = np.cos(ang).astype(np.float32) * np.float32(scl)
        sn = np.sin(ang).astype(np.float32) * np.float32(scl)
        blkC = np.concatenate([cs, cs], 0)
        blkS = np.concatenate([-sn, sn], 0)
        c[f"{nm}_C"] = np.ascontiguousarray(np.concatenate([blkC] * rep, 0))
        c[f"{nm}_S"] = np.ascontiguousarray(np.concatenate([blkS] * rep, 0))
    return c



class Prog:
    def __init__(self, nc):
        self.nc = nc
        self.ctx = Ctx(nc)
        self.inp = {}

    def dram_in(self, name, shape, dtype=F32):
        if name in self.inp:
            return self.inp[name]
        t = self.nc.dram_tensor(name, list(shape), dtype, kind="ExternalInput")
        self.inp[name] = t.ap()
        return self.inp[name]

    def dram_out(self, name, shape, dtype=F32):
        t = self.nc.dram_tensor(name, list(shape), dtype, kind="ExternalOutput")
        return t.ap()

    def dram_int(self, name, shape, dtype):
        return self.nc.dram_tensor(name, list(shape), dtype, kind="Internal")

    def setup_common(self):
        ctx = self.ctx
        self.psa = ctx.ps("PSA", [128, 4096], F32)
        self.pb = [self.psa[:, i * 512:(i + 1) * 512] for i in range(8)]
        self.c_consts = ctx.sb("c_consts", [128, len(CVALS)], F32)
        for j, v in enumerate(CVALS):
            self.nc.const_aps.register(F32, v, self.c_consts[:, j:j + 1])
        self.CI = {v: j for j, v in enumerate(CVALS)}
        self.c_id = ctx.sb("c_id", [128, 128], F32)
        self.c_onesrow = ctx.sb("c_onesrow", [1, 128], BF16)
        self.c_onesrowf = ctx.sb("c_onesrowf", [1, 128], F32)
        self.c_ones = ctx.sb("c_ones", [128, 1], BF16)
        self.c_e8 = ctx.sb("c_e8", [8, 64], BF16)
        self.c_eps = ctx.sb("c_eps", [1, 8], F32)
        self.c_h128 = ctx.sb("c_h128", [128, 128], BF16)
        self.c_h64 = ctx.sb("c_h64", [64, 64], BF16)
        loads = [("CONSTS", self.c_consts, [128, len(CVALS)]),
                 ("ID128", self.c_id, [128, 128]),
                 ("ONESROW", self.c_onesrow, [1, 128]),
                 ("ONESROWF", self.c_onesrowf, [1, 128]),
                 ("ONES128", self.c_ones, [128, 1]),
                 ("E8X2", self.c_e8, [8, 64]),
                 ("EPSIDX", self.c_eps, [1, 8]),
                 ("H128", self.c_h128, [128, 128]),
                 ("H64", self.c_h64, [64, 64])]
        for nm, dst, shp in loads:
            src = self.dram_in(nm, shp, F32)
            ctx.op("gpsimd", "dma_start", dst[:, :], src, writes=(f"c:{nm}",))
        ctx.barrier(keep=BGKEEP)

    def const(self, v):
        j = self.CI[v]
        return self.c_consts[:, j:j + 1]

    def wsrc(self, name, shape):
        wc = getattr(self, "wc", None)
        if wc and name in wc:
            return wc[name].ap(), "scalar", f"cvw:{name}"
        return self.dram_in(name, shape), "gpsimd", name



def ln_tile(p, x_ap, x_key, xn_tm, xn_key, stat, stat_key, tag):
    ctx = p.ctx
    bn = stat
    for j in range(4):
        ctx.op("vector", "bn_stats", stat[:, 8 + j * 6: 8 + (j + 1) * 6],
               x_ap[:, j * 512:(j + 1) * 512],
               reads=(x_key,), writes=(stat_key,))
    ctx.op("vector", "bn_aggr", stat[:, 0:2], stat[:, 8:32],
           reads=(stat_key,), writes=(stat_key,))
    ctx.op("scalar", "activation", stat[:, 2:3], stat[:, 1:2], AF.Sqrt,
           bias=1e-5, reads=(stat_key,), writes=(stat_key,))
    ctx.op("vector", "reciprocal", stat[:, 3:4], stat[:, 2:3],
           reads=(stat_key,), writes=(stat_key,))
    ctx.op("vector", "tensor_scalar", xn_tm[:, :], x_ap, stat[:, 0:1], stat[:, 3:4],
           ALU.subtract, ALU.mult, reads=(x_key, stat_key), writes=(xn_key,))


def transpose_tm_to_fm(p, src_tm, src_key, dst_fm, dst_key, ti, tc, n_dt=16,
                       dst_bf16=True, banks=(0, 1)):
    ctx = p.ctx
    for g in range(n_dt // 4):
        b = p.pb[banks[g % len(banks)]]
        bk = f"pb{banks[g % len(banks)]}"
        for q in range(4):
            dt = g * 4 + q
            ctx.op("tensor", "transpose", b[:, q * 128:(q + 1) * 128],
                   src_tm[:, dt * 128:(dt + 1) * 128], p.c_id[:, :],
                   reads=(src_key, "c:ID128"), writes=(bk,), inc=(q == 3))
        for q in range(4):
            dt = g * 4 + q
            ctx.op("scalar", "copy",
                   dst_fm[:, dt * tc + ti * 128: dt * tc + (ti + 1) * 128],
                   b[:, q * 128:(q + 1) * 128],
                   reads=(bk,), writes=(dst_key,))


def mm_movs(p, psum_ap, bank_keys, stat_ap, stat_keys, mov_ap, mov_keys, ncols,
            start, stop, inc_last=True, tile_position=None):
    ctx = p.ctx
    nch = (ncols + 511) // 512
    for j in range(nch):
        c0, c1 = j * 512, min(ncols, (j + 1) * 512)
        kw = {}
        if tile_position is not None:
            kw["tile_position"] = tile_position
        ctx.op("tensor", "matmul", psum_ap[:, c0:c1], stat_ap, mov_ap[:, c0:c1],
               start=start and j == 0, stop=stop and j == nch - 1,
               reads=tuple(stat_keys) + tuple(mov_keys), writes=tuple(bank_keys),
               inc=inc_last and (j == nch - 1), **kw)



def hw_cast_load(p, dst, dst_key, src_name, shape, stg, stg_keys, cols_per=2048):
    ctx = p.ctx
    rows, cols = shape
    srcap = p.dram_in(src_name, shape)
    n = 0
    for c0 in range(0, cols, cols_per):
        c1 = min(cols, c0 + cols_per)
        si = n % len(stg)
        ctx.op("sync", "dma_start", stg[si][0:rows, 0:c1 - c0],
               srcap[:, c0:c1], reads=(src_name,), writes=(stg_keys[si],))
        ctx.op("vector", "tensor_copy", dst[0:rows, c0:c1],
               stg[si][0:rows, 0:c1 - c0],
               reads=(stg_keys[si],), writes=(dst_key,))
        n += 1


def phase_l0ffn(p, xres_src, xres_dst, tag):
    nc, ctx = p.nc, p.ctx
    tc = 512
    nt = tc // 128
    nch = T // tc
    es = ExitStack()
    sb = lambda n, s, d: es.enter_context(nc.sbuf_tensor(f"{tag}_{n}", s, d))

    wr = sb("wr", [128, 16 * 8], BF16)
    a1t = sb("a1t", [128, 16 * 64], BF16)
    a2t = sb("a2t", [128, 64 * 64], BF16)
    b1aug = sb("b1aug", [65, FF], BF16)
    b2aug = sb("b2aug", [65, D], BF16)
    br = sb("br", [1, 8], BF16)
    epsb = sb("epsb", [128, 8], F32)
    NW = 3
    w1b = [sb(f"w1b{i}", [128, 16 * 128], BF16) for i in range(NW)]
    w2b = [sb(f"w2b{i}", [128, 512], BF16) for i in range(NW)]
    xb = [sb(f"xb{i}", [128, D], F32) for i in range(nt)]
    xo = [sb(f"xo{i}", [128, 512], F32) for i in range(2)]
    xn_tm = sb("xn_tm", [128, D], F32)
    xn_fm = sb("xn_fm", [128, 16 * tc], BF16)
    stat = sb("stat", [128, 32], F32)
    lg = sb("lg", [128, 8], F32)
    sel = sb("sel", [128, 8], F32)
    m1 = sb("m1", [128, 8], F32)
    m2 = sb("m2", [128, 8], F32)
    ee = sb("ee", [128, 8], F32)
    ee2 = sb("ee2", [128, 8], F32)
    rr = sb("rr", [128, 8], F32)
    w12 = sb("w12", [128, 2 * nt], F32)
    w12s = sb("w12s", [8, 128], BF16)
    esel = sb("esel", [8, 1024], BF16)
    wbc = [sb(f"wbc{k}", [128, tc], BF16) for k in range(2)]
    m1t = sb("m1t", [8, tc], BF16)
    m2t = sb("m2t", [8, tc], BF16)
    mex = [sb(f"mex{k}", [64, tc], BF16) for k in range(2)]
    mexw = [sb(f"mexw{k}", [64, tc], F32) for k in range(2)]
    mu1a = [sb(f"mu1a{k}", [65, tc], BF16) for k in range(2)]
    mzmix = sb("mzmix", [65, tc], BF16)
    v01 = [sb(f"v01_{k}", [128, tc], F32) for k in range(2)]
    base1 = [sb(f"base1_{par}", [128, tc], F32) for par in range(2)]
    hkb = [[sb(f"h{k}_{par}", [128, tc], BF16) for par in range(2)]
           for k in range(2)]
    hmt = [sb(f"hmt{par}", [128, tc], BF16) for par in range(2)]
    hmix = sb("hmix", [128, 64 * tc], BF16)

    W1T, q1, k1 = p.wsrc("l0_W1Tt", [64 * 128, 16 * 128])
    W2T, q2, k2 = p.wsrc("l0_W2Tdc", [4 * FF, 512])
    stg = [xb[0], xb[1]]
    stgk = ("xb0", "xb1")
    hw_cast_load(p, wr, "wr", "l0_WrTt", [128, 16 * 8], stg, stgk)
    hw_cast_load(p, a1t, "a1t", "l0_A1Tt", [128, 16 * 64], stg, stgk)
    hw_cast_load(p, a2t, "a2t", "l0_A2Tt", [128, 64 * 64], stg, stgk)
    hw_cast_load(p, b1aug, "b1aug", "l0_B1aug", [65, FF], stg, stgk)
    hw_cast_load(p, b2aug, "b2aug", "l0_B2aug", [65, D], stg, stgk)
    hw_cast_load(p, br, "br", "l0_br", [1, 8], stg, stgk)
    hw_cast_load(p, esel, "esel", "ESEL8", [8, 1024], stg, stgk)
    epsbf = sb("epsbf", [1, 8], BF16)
    ctx.op("vector", "tensor_copy", epsbf[:, :], p.c_eps[:, :],
           reads=("c:EPSIDX",), writes=("epsbf",))
    ctx.op("tensor", "matmul", p.pb[0][0:128, 0:8], p.c_onesrow[:, :], epsbf[:, :],
           start=True, stop=True, reads=("c:ONESROW", "epsbf"), writes=("pb0",))
    ctx.op("vector", "tensor_copy", epsb[:, :], p.pb[0][0:128, 0:8],
           reads=("pb0",), writes=("epsb",))
    for k in range(2):
        ctx.op("gpsimd", "memset", mu1a[k][64:65, :], 1.0, writes=(f"mu1a{k}",))

    for ch in range(nch):
        tok0 = ch * tc
        for i in range(nt):
            ctx.op("sync", "dma_start", xb[i][:, :],
                   xres_src[tok0 + i * 128: tok0 + (i + 1) * 128, :],
                   reads=("xres_src",), writes=(f"xb{i}",))
        for i in range(nt):
            ln_tile(p, xb[i][:, :], f"xb{i}", xn_tm, "xn_tm", stat, "stat", tag)
            transpose_tm_to_fm(p, xn_tm, "xn_tm", xn_fm, "xn_fm", i, tc)
        for i in range(nt):
            for dt in range(16):
                ctx.op("tensor", "matmul", p.pb[4][0:128, 0:8],
                       xn_fm[:, dt * tc + i * 128: dt * tc + (i + 1) * 128],
                       wr[:, dt * 8:(dt + 1) * 8],
                       start=(dt == 0), stop=False,
                       reads=("xn_fm", "wr"), writes=("pb4",), inc=False)
            ctx.op("tensor", "matmul", p.pb[4][0:128, 0:8], p.c_onesrow[:, :],
                   br[:, :], start=False, stop=True,
                   reads=("c:ONESROW", "br"), writes=("pb4",))
            ctx.op("vector", "tensor_copy", lg[:, :], p.pb[4][0:128, 0:8],
                   reads=("pb4",), writes=("lg",))
            ctx.op("vector", "tensor_add", sel[:, :], lg[:, :], epsb[:, :],
                   reads=("lg", "epsb"), writes=("sel",))
            ctx.op("vector", "reduce_max", rr[:, 0:1], sel[:, :], axis=AXX,
                   reads=("sel",), writes=("rr",))
            ctx.op("vector", "tensor_scalar", m1[:, :], sel[:, :], rr[:, 0:1], None,
                   ALU.is_ge, reads=("sel", "rr"), writes=("m1",))
            ctx.op("vector", "scalar_tensor_tensor", sel[:, :], m1[:, :], -1e9,
                   sel[:, :], ALU.mult, ALU.add,
                   reads=("m1", "sel"), writes=("sel",))
            ctx.op("vector", "reduce_max", rr[:, 1:2], sel[:, :], axis=AXX,
                   reads=("sel",), writes=("rr",))
            ctx.op("vector", "tensor_scalar", m2[:, :], sel[:, :], rr[:, 1:2], None,
                   ALU.is_ge, reads=("sel", "rr"), writes=("m2",))
            ctx.op("vector", "reduce_max", rr[:, 2:3], lg[:, :], axis=AXX,
                   negate=True, reads=("lg",), writes=("rr",))
            ctx.op("scalar", "activation", ee[:, :], lg[:, :], AF.Exp,
                   bias=rr[:, 2:3], accum_out=rr[:, 3:4],
                   reads=("lg", "rr"), writes=("ee", "rr"))
            ctx.op("vector", "reciprocal", rr[:, 4:5], rr[:, 3:4],
                   reads=("rr",), writes=("rr",))
            for k, mk in ((0, m1), (1, m2)):
                ctx.op("vector", "tensor_mul", ee2[:, :], mk[:, :], ee[:, :],
                       reads=(("m1", "m2")[k], "ee"), writes=("ee2",))
                ctx.op("vector", "reduce_sum", rr[:, 6:7], ee2[:, :], axis=AXX,
                       reads=("ee2",), writes=("rr",))
                ctx.op("vector", "tensor_scalar", w12[:, 2 * i + k: 2 * i + k + 1],
                       rr[:, 6:7], rr[:, 4:5], 0.5, ALU.mult, ALU.mult,
                       reads=("rr",), writes=("w12",))
            for mk, mt, mkey, mtk in ((m1, m1t, "m1", "m1t"), (m2, m2t, "m2", "m2t")):
                ctx.op("tensor", "transpose", p.pb[5][0:8, i * 128:(i + 1) * 128],
                       mk[:, :], p.c_id[:, :],
                       reads=(mkey, "c:ID128"), writes=("pb5",))
                ctx.op("vector", "tensor_copy", mt[0:8, i * 128:(i + 1) * 128],
                       p.pb[5][0:8, i * 128:(i + 1) * 128],
                       reads=("pb5",), writes=(mtk,))
        ctx.op("tensor", "transpose", p.pb[5][0:8, 0:128], w12[:, :], p.c_id[:, :],
               reads=("w12", "c:ID128"), writes=("pb5",))
        ctx.op("vector", "tensor_copy", w12s[:, :], p.pb[5][0:8, 0:128],
               reads=("pb5",), writes=("w12s",))
        for k in range(2):
            bk = 3 if k == 0 else 4
            for i in range(nt):
                ctx.op("tensor", "matmul", p.pb[bk][:, i * 128:(i + 1) * 128],
                       esel[:, (2 * i + k) * 128:(2 * i + k + 1) * 128],
                       w12s[:, :], start=True, stop=True,
                       reads=("esel", "w12s"), writes=(f"pb{bk}",),
                       inc=(i == nt - 1))
            ctx.op("vector", "tensor_copy", wbc[k][:, :], p.pb[bk][:, 0:tc],
                   reads=(f"pb{bk}",), writes=(f"wbc{k}",))
        ctx.op("vector", "tensor_add", mzmix[64:65, :], wbc[0][64:65, :],
               wbc[1][64:65, :], reads=("wbc0", "wbc1"), writes=("mzmix",))
        for k, (mt, mtk) in enumerate(((m1t, "m1t"), (m2t, "m2t"))):
            ctx.op("tensor", "matmul", p.pb[5][0:64, 0:tc], p.c_e8[:, :], mt[:, :],
                   start=True, stop=True, reads=("c:E8X2", mtk), writes=("pb5",))
            ctx.op("vector", "tensor_copy", mex[k][:, :], p.pb[5][0:64, 0:tc],
                   reads=("pb5",), writes=(f"mex{k}",))
            ctx.op("vector", "tensor_mul", mexw[k][:, :], mex[k][:, :],
                   wbc[k][0:64, :], reads=(f"mex{k}", f"wbc{k}"),
                   writes=(f"mexw{k}",))
        for dt in range(16):
            ctx.op("tensor", "matmul", p.pb[6][0:64, 0:tc],
                   a1t[:, dt * 64:(dt + 1) * 64],
                   xn_fm[:, dt * tc:(dt + 1) * tc],
                   start=(dt == 0), stop=(dt == 15),
                   reads=("a1t", "xn_fm"), writes=("pb6",), inc=(dt == 15))
        for k in range(2):
            ctx.op("vector", "tensor_mul", mu1a[k][0:64, :], p.pb[6][0:64, 0:tc],
                   mex[k][:, :], reads=("pb6", f"mex{k}"), writes=(f"mu1a{k}",))
        for pf in range(2):
            ctx.op(q1, "dma_start", w1b[pf][:, :], W1T[pf * 128:(pf + 1) * 128, :],
                   reads=(k1,), writes=(f"w1b{pf}",))
        for ft in range(65):
            if ft < 64:
                cur = w1b[ft % NW]
                ck = f"w1b{ft % NW}"
                if ft + 2 < 64:
                    nx = ft + 2
                    ctx.op(q1, "dma_start", w1b[nx % NW][:, :],
                           W1T[nx * 128:(nx + 1) * 128, :],
                           reads=(k1,), writes=(f"w1b{nx % NW}",))
                fb = ft % 2
                bA, bB, bC = (0, 1, 2) if fb == 0 else (3, 4, 5)
                kA, kB, kC = f"pb{bA}", f"pb{bB}", f"pb{bC}"
                for dt in range(16):
                    ctx.op("tensor", "matmul", p.pb[bA][:, 0:tc],
                           cur[:, dt * 128:(dt + 1) * 128],
                           xn_fm[:, dt * tc:(dt + 1) * tc],
                           start=(dt == 0), stop=(dt == 15),
                           reads=(ck, "xn_fm"), writes=(kA,), inc=(dt == 15))
                for k, bZ, kZ in ((0, bB, kB), (1, bC, kC)):
                    ctx.op("tensor", "matmul", p.pb[bZ][:, 0:tc],
                           b1aug[:, ft * 128:(ft + 1) * 128], mu1a[k][:, :],
                           start=True, stop=True,
                           reads=("b1aug", f"mu1a{k}"), writes=(kZ,))
                ctx.op("scalar", "copy", base1[fb][:, :], p.pb[bA][:, 0:tc],
                       reads=(kA,), writes=(f"base1_{fb}",))
                for k, bZ, kZ in ((0, bB, kB), (1, bC, kC)):
                    ctx.op("vector", "tensor_add", v01[k][:, :],
                           p.pb[bZ][:, 0:tc], base1[fb][:, :],
                           reads=(kZ, f"base1_{fb}"), writes=(f"v01_{k}",))
                for k in range(2):
                    ctx.op("scalar", "activation", hkb[k][fb][:, :],
                           v01[k][:, :], AF.Gelu_apprx_tanh,
                           reads=(f"v01_{k}",), writes=(f"h{k}_{fb}",))
            if ft >= 1:
                g = ft - 1
                gb = g % 2
                for k, bU in ((0, 6), (1, 7)):
                    ctx.op("tensor", "matmul", p.pb[bU][0:64, 0:tc],
                           a2t[:, g * 64:(g + 1) * 64],
                           hkb[k][gb][:, :],
                           start=(g == 0), stop=(g == 63),
                           reads=("a2t", f"h{k}_{gb}"), writes=(f"pb{bU}",),
                           inc=(g == 63))
                for k in range(2):
                    ctx.op("vector", "tensor_mul", hmt[k][:, :],
                           hkb[k][gb][:, :], wbc[k][:, :],
                           reads=(f"h{k}_{gb}", f"wbc{k}"), writes=(f"hmt{k}",))
                ctx.op("vector", "tensor_add", hmix[:, g * tc:(g + 1) * tc],
                       hmt[0][:, :], hmt[1][:, :],
                       reads=("hmt0", "hmt1"), writes=("hmix",))
        for k, bU in ((0, 6), (1, 7)):
            op = "tensor_mul" if k == 0 else None
            if k == 0:
                ctx.op("vector", "tensor_mul", hmt[0][0:64, 0:tc],
                       p.pb[bU][0:64, 0:tc], mexw[k][:, :],
                       reads=(f"pb{bU}", f"mexw{k}"), writes=("hmt0",))
            else:
                ctx.op("vector", "tensor_mul", hmt[1][0:64, 0:tc],
                       p.pb[bU][0:64, 0:tc], mexw[k][:, :],
                       reads=(f"pb{bU}", f"mexw{k}"), writes=("hmt1",))
        ctx.op("vector", "tensor_add", mzmix[0:64, :], hmt[0][0:64, 0:tc],
               hmt[1][0:64, 0:tc], reads=("hmt0", "hmt1"), writes=("mzmix",))
        for dc in range(4):
            bg = (dc % 2) * 4
            for i in range(nt):
                ctx.op("tensor", "matmul", p.pb[bg + i][:, 0:512],
                       mzmix[:, i * 128:(i + 1) * 128],
                       b2aug[:, dc * 512:(dc + 1) * 512],
                       start=True, stop=False,
                       reads=("mzmix", "b2aug"), writes=(f"pb{bg + i}",),
                       inc=(i == nt - 1))
            for pf in range(2):
                ctx.op(q2, "dma_start", w2b[pf][:, :],
                       W2T[dc * FF + pf * 128: dc * FF + (pf + 1) * 128, :],
                       reads=(k2,), writes=(f"w2b{pf}",))
            for ft in range(64):
                cur, ck = w2b[ft % NW], f"w2b{ft % NW}"
                if ft + 2 < 64:
                    nx = ft + 2
                    ctx.op(q2, "dma_start", w2b[nx % NW][:, :],
                           W2T[dc * FF + nx * 128: dc * FF + (nx + 1) * 128, :],
                           reads=(k2,), writes=(f"w2b{nx % NW}",))
                for i in range(nt):
                    ctx.op("tensor", "matmul", p.pb[bg + i][:, 0:512],
                           hmix[:, ft * tc + i * 128: ft * tc + (i + 1) * 128],
                           cur[:, :],
                           start=False, stop=(ft == 63),
                           reads=("hmix", ck), writes=(f"pb{bg + i}",),
                           inc=(i == nt - 1))
            for i in range(nt):
                xi = i % 2
                ctx.op("vector", "tensor_add", xo[xi][:, :],
                       p.pb[bg + i][:, 0:512],
                       xb[i][:, dc * 512:(dc + 1) * 512],
                       reads=(f"pb{bg + i}", f"xb{i}"), writes=(f"xo{xi}",))
                ctx.op("sync", "dma_start",
                       xres_dst[tok0 + i * 128: tok0 + (i + 1) * 128,
                                dc * 512:(dc + 1) * 512],
                       xo[xi][:, :], reads=(f"xo{xi}",), writes=("xres_dst",))
    ctx.barrier(keep=BGKEEP)
    es.close()



def phase_conv_fast(p, items, tag="cv1"):
    nc, ctx = p.nc, p.ctx
    es = ExitStack()
    NB = 4
    fin = [es.enter_context(nc.sbuf_tensor(f"{tag}_i{i}", [128, 2048], F32))
           for i in range(NB)]
    fout = [es.enter_context(nc.sbuf_tensor(f"{tag}_o{i}", [128, 2048], BF16))
            for i in range(NB)]
    for src_name, shape, dst in items:
        rows, cols = shape
        srcap = p.dram_in(src_name, shape)
        dap = dst.ap()
        assert cols <= 2048
        for i in range(rows // 128):
            bi = i % NB
            ctx.op("sync", "dma_start", fin[bi][:, 0:cols],
                   srcap[i * 128:(i + 1) * 128, :],
                   reads=(src_name,), writes=(f"{tag}i{bi}",))
            ctx.op("vector", "tensor_copy", fout[bi][:, 0:cols],
                   fin[bi][:, 0:cols],
                   reads=(f"{tag}i{bi}",), writes=(f"{tag}o{bi}",))
            ctx.op("sync", "dma_start", dap[i * 128:(i + 1) * 128, :],
                   fout[bi][:, 0:cols],
                   reads=(f"{tag}o{bi}",), writes=(f"cvw:{src_name}",))
    ctx.barrier(keep=BGKEEP)
    es.close()


def phase_conv_bg(p, items, bufs, tag="cv2"):
    nc, ctx = p.nc, p.ctx
    NB = len(bufs)
    bw = bufs[0].shape[1]
    n = 0
    for src_name, shape, dst in items:
        rows, cols = shape
        srcap = p.dram_in(src_name, shape)
        dap = dst.ap()
        for i in range(rows // 128):
            for c0 in range(0, cols, bw):
                c1 = min(cols, c0 + bw)
                bi = n % NB
                n += 1
                ctx.op("gpsimd", "dma_start", bufs[bi][:, 0:c1 - c0],
                       srcap[i * 128:(i + 1) * 128, c0:c1],
                       reads=(src_name,), writes=(f"cv2b{bi}",))
                ctx.op("gpsimd", "dma_start", dap[i * 128:(i + 1) * 128, c0:c1],
                       bufs[bi][:, 0:c1 - c0],
                       reads=(f"cv2b{bi}",), writes=(f"cvw:{src_name}",))



def phase_w2prep(p, w2p_dram, tag="w2p"):
    nc, ctx = p.nc, p.ctx
    es = ExitStack()
    sb = lambda n, s, d: es.enter_context(nc.sbuf_tensor(f"{tag}_{n}", s, d))
    W2G = p.dram_in("l1_W2G", [64, 128 * D])
    gin = [sb(f"gin{i}", [64, D], BF16) for i in range(2)]
    gout = [sb(f"gout{i}", [64, D], BF16) for i in range(2)]
    w2p_v = w2p_dram.ap().rearrange("(dc a b) d -> a b dc d", dc=4, a=64)
    for b in range(128):
        i = b % 2
        ctx.op("gpsimd", "dma_start", gin[i][:, :],
               W2G[:, b * D:(b + 1) * D],
               reads=("W2G",), writes=(f"{tag}gin{i}",))
        for dc in range(4):
            ctx.op("tensor", "matmul", p.pb[dc][0:64, 0:512], p.c_h64[:, :],
                   gin[i][:, dc * 512:(dc + 1) * 512],
                   start=True, stop=True,
                   reads=("c:H64", f"{tag}gin{i}"), writes=(f"pb{dc}",))
            ctx.op("vector", "tensor_copy", gout[i][:, dc * 512:(dc + 1) * 512],
                   p.pb[dc][0:64, 0:512],
                   reads=(f"pb{dc}",), writes=(f"{tag}gout{i}",))
        ctx.op("sync", "dma_start", w2p_v[:, b, :, :],
               gout[i][:, :].rearrange("a (dc d) -> a dc d", dc=4),
               reads=(f"{tag}gout{i}",), writes=("w2p",))
    ctx.barrier(keep=BGKEEP)
    es.close()



def phase_l1ffn(p, xres_src, xres_dst, w2p_dram, tag):
    nc, ctx = p.nc, p.ctx
    tc = 512
    nt = tc // 128
    nch = T // tc
    es = ExitStack()
    sb = lambda n, s, d: es.enter_context(nc.sbuf_tensor(f"{tag}_{n}", s, d))

    W1T, q1, k1 = p.wsrc("l1_W1Tt", [64 * 128, 16 * 128])
    W3T, q3, k3 = p.wsrc("l1_W3Tt", [64 * 128, 16 * 128])
    w2p_ap = w2p_dram.ap()

    NW = 3
    w1b = [sb(f"w1b{i}", [128, 2048], BF16) for i in range(NW)]
    w3b = [sb(f"w3b{i}", [128, 2048], BF16) for i in range(NW)]
    w2b = [sb(f"w2b{i}", [128, 512], BF16) for i in range(NW)]
    xb = [sb(f"xb{i}", [128, D], F32) for i in range(nt)]
    xn_tm = sb("xn_tm", [128, D], F32)
    xn_fm = sb("xn_fm", [128, 16 * tc], BF16)
    stat = sb("stat", [128, 32], F32)
    sg = sb("sg", [128, tc], F32)
    hp = [sb(f"hp{i}", [128, tc], BF16) for i in range(2)]
    hidp = sb("hidp", [128, 64 * tc], BF16)
    xo = [sb(f"xo{i}", [128, 512], F32) for i in range(4)]

    for ch in range(nch):
        tok0 = ch * tc
        for i in range(nt):
            ctx.op("sync", "dma_start", xb[i][:, :],
                   xres_src[tok0 + i * 128: tok0 + (i + 1) * 128, :],
                   reads=("xres_src",), writes=(f"xb{i}",))
        for i in range(nt):
            ln_tile(p, xb[i][:, :], f"xb{i}", xn_tm, "xn_tm", stat, "stat", tag)
            transpose_tm_to_fm(p, xn_tm, "xn_tm", xn_fm, "xn_fm", i, tc)
        for pf in range(2):
            ctx.op(q1, "dma_start", w1b[pf][:, :], W1T[pf * 128:(pf + 1) * 128, :],
                   reads=(k1,), writes=(f"w1b{pf}",))
            ctx.op(q3, "dma_start", w3b[pf][:, :], W3T[pf * 128:(pf + 1) * 128, :],
                   reads=(k3,), writes=(f"w3b{pf}",))
        for ft in range(64):
            fb = ft % NW
            cg, cl = w1b[fb], w3b[fb]
            kg, kl = f"w1b{fb}", f"w3b{fb}"
            if ft + 2 < 64:
                nfb = (ft + 2) % NW
                ctx.op(q1, "dma_start", w1b[nfb][:, :],
                       W1T[(ft + 2) * 128:(ft + 3) * 128, :],
                       reads=(k1,), writes=(f"w1b{nfb}",))
                ctx.op(q3, "dma_start", w3b[nfb][:, :],
                       W3T[(ft + 2) * 128:(ft + 3) * 128, :],
                       reads=(k3,), writes=(f"w3b{nfb}",))
            bG, bL, bH = (0, 1, 2) if ft % 2 == 0 else (3, 4, 5)
            for dt in range(16):
                ctx.op("tensor", "matmul", p.pb[bG][:, 0:tc],
                       cg[:, dt * 128:(dt + 1) * 128],
                       xn_fm[:, dt * tc:(dt + 1) * tc],
                       start=(dt == 0), stop=(dt == 15),
                       reads=(kg, "xn_fm"), writes=(f"pb{bG}",), inc=(dt == 15))
            for dt in range(16):
                ctx.op("tensor", "matmul", p.pb[bL][:, 0:tc],
                       cl[:, dt * 128:(dt + 1) * 128],
                       xn_fm[:, dt * tc:(dt + 1) * tc],
                       start=(dt == 0), stop=(dt == 15),
                       reads=(kl, "xn_fm"), writes=(f"pb{bL}",), inc=(dt == 15))
            ctx.op("scalar", "activation", sg[:, :], p.pb[bG][:, 0:tc], AF.Silu,
                   reads=(f"pb{bG}",), writes=("sg",))
            hb = ft % 2
            ctx.op("vector", "tensor_mul", hp[hb][:, :], sg[:, :],
                   p.pb[bL][:, 0:tc], reads=("sg", f"pb{bL}"), writes=(f"hp{hb}",))
            ctx.op("tensor", "matmul", p.pb[bH][:, 0:tc], p.c_h128[:, :],
                   hp[hb][:, :], start=True, stop=True,
                   reads=("c:H128", f"hp{hb}"), writes=(f"pb{bH}",))
            ctx.op("scalar", "activation", hidp[:, ft * tc:(ft + 1) * tc],
                   p.pb[bH][:, 0:tc], AF.Copy, scale=CVALS[7],
                   reads=(f"pb{bH}",), writes=("hidp",))
        for dc in range(4):
            bg = (dc % 2) * 4
            for pf in range(2):
                ctx.op("scalar", "dma_start", w2b[pf][:, :],
                       w2p_ap[dc * FF + pf * 128: dc * FF + (pf + 1) * 128, :],
                       reads=("w2p",), writes=(f"w2b{pf}",))
            for ft in range(64):
                fb = ft % NW
                cur, ck = w2b[fb], f"w2b{fb}"
                if ft + 2 < 64:
                    nfb = (ft + 2) % NW
                    ctx.op("scalar", "dma_start", w2b[nfb][:, :],
                           w2p_ap[dc * FF + (ft + 2) * 128: dc * FF + (ft + 3) * 128, :],
                           reads=("w2p",), writes=(f"w2b{nfb}",))
                for i in range(nt):
                    ctx.op("tensor", "matmul", p.pb[bg + i][:, 0:512],
                           hidp[:, ft * tc + i * 128: ft * tc + (i + 1) * 128],
                           cur[:, :],
                           start=(ft == 0), stop=(ft == 63),
                           reads=("hidp", ck), writes=(f"pb{bg + i}",),
                           inc=(i == nt - 1))
            for i in range(nt):
                xi = i % 2
                ctx.op("vector", "tensor_add", xo[xi][:, :],
                       p.pb[bg + i][:, 0:512],
                       xb[i][:, dc * 512:(dc + 1) * 512],
                       reads=(f"pb{bg + i}", f"xb{i}"), writes=(f"xo{xi}",))
                ctx.op("sync", "dma_start",
                       xres_dst[tok0 + i * 128: tok0 + (i + 1) * 128,
                                dc * 512:(dc + 1) * 512],
                       xo[xi][:, :], reads=(f"xo{xi}",), writes=("xres_dst",))
    ctx.barrier(keep=BGKEEP)
    es.close()



def phase_xnfull(p, src_ap, src_key, dst_ap, dst_key, ranks, tag):
    nc, ctx = p.nc, p.ctx
    tc = 512
    es = ExitStack()
    sb = lambda n, s, d: es.enter_context(nc.sbuf_tensor(f"{tag}_{n}", s, d))
    xb = [sb(f"xb{i}", [128, D], F32) for i in range(2)]
    xn_tm = sb("xn_tm", [128, D], F32)
    xn_fm = sb("xn_fm", [128, 16 * tc], BF16)
    stat = sb("stat", [128, 32], F32)
    for rb, st0 in ranks:
        for ch in range(1024 // tc):
            for i in range(tc // 128):
                ib = i % 2
                ctx.op("sync", "dma_start", xb[ib][:, :],
                       src_ap[st0 + ch * tc + i * 128: st0 + ch * tc + (i + 1) * 128, :],
                       reads=(src_key,), writes=(f"{tag}xb{ib}",))
                ln_tile(p, xb[ib][:, :], f"{tag}xb{ib}", xn_tm, f"{tag}xn_tm",
                        stat, f"{tag}stat", tag)
                transpose_tm_to_fm(p, xn_tm, f"{tag}xn_tm", xn_fm, f"{tag}xn_fm",
                                   i, tc)
            for dt in range(16):
                ctx.op("sync", "dma_start",
                       dst_ap[rb + dt * 128: rb + (dt + 1) * 128,
                              ch * tc:(ch + 1) * tc],
                       xn_fm[:, dt * tc:(dt + 1) * tc],
                       reads=(f"{tag}xn_fm",), writes=(dst_key,))
    ctx.barrier(keep=BGKEEP)
    es.close()



def phase_outproj(p, xres_src, xres_dst, attn_out, ao_key, ow_name, tag):
    nc, ctx = p.nc, p.ctx
    esE = ExitStack()
    sbE = lambda n, s, d: esE.enter_context(nc.sbuf_tensor(f"{tag}E_{n}", s, d))
    owb = [sbE(f"owb{i}", [128, D], BF16) for i in range(3)]
    xo = sbE("xo", [128, D], F32)
    xob = [sbE(f"xob{i}", [128, D], F32) for i in range(2)]
    OWT, qo, ko = p.wsrc(ow_name, [D, D])
    for tp in range(4):
        tts = (2 * tp, 2 * tp + 1)
        for j, tt in enumerate(tts):
            ctx.op("sync", "dma_start", xob[j][:, :],
                   xres_src[tt * 128:(tt + 1) * 128, :],
                   reads=("xres_src",), writes=(f"xob{j}",))
        for pf in range(2):
            ctx.op(qo, "dma_start", owb[pf][:, :], OWT[pf * 128:(pf + 1) * 128, :],
                   reads=(ko,), writes=(f"owb{pf}",))
        for ht in range(16):
            fb = ht % 3
            if ht + 2 < 16:
                ctx.op(qo, "dma_start", owb[(ht + 2) % 3][:, :],
                       OWT[(ht + 2) * 128:(ht + 3) * 128, :],
                       reads=(ko,), writes=(f"owb{(ht + 2) % 3}",))
            for j, tt in enumerate(tts):
                for dc in range(4):
                    b = 4 * j + dc
                    ctx.op("tensor", "matmul", p.pb[b][:, 0:512],
                           attn_out[:, ht * T + tt * 128: ht * T + (tt + 1) * 128],
                           owb[fb][:, dc * 512:(dc + 1) * 512],
                           start=(ht == 0), stop=(ht == 15),
                           reads=(ao_key, f"owb{fb}"), writes=(f"pb{b}",),
                           inc=(j == 1 and dc == 3))
        for j, tt in enumerate(tts):
            for dc in range(4):
                b = 4 * j + dc
                ctx.op("vector", "tensor_add", xo[:, dc * 512:(dc + 1) * 512],
                       p.pb[b][:, 0:512], xob[j][:, dc * 512:(dc + 1) * 512],
                       reads=(f"pb{b}", f"xob{j}"), writes=("xo",))
            ctx.op("sync", "dma_start", xres_dst[tt * 128:(tt + 1) * 128, :],
                   xo[:, :], reads=("xo",), writes=("xres_dst",))
    ctx.barrier(keep=BGKEEP)
    esE.close()


def phase_l0attn(p, xres_src, xres_dst, xnfull_ap, xnfull_key, xnown_ap, xnown_key, tag):
    nc, ctx = p.nc, p.ctx
    es = ExitStack()
    sb = lambda n, s, d: es.enter_context(nc.sbuf_tensor(f"{tag}_{n}", s, d))

    Q = sb("Q", [128, 16 * T], BF16)
    ckv = sb("ckv", [128, 4 * S], BF16)
    krot = sb("krot", [64, S], BF16)

    esB = ExitStack()
    sbB = lambda n, s, d: esB.enter_context(nc.sbuf_tensor(f"{tag}B_{n}", s, d))
    xnq = sbB("xnq", [128, 16 * T], BF16)
    qlat = sbB("qlat", [128, 12 * T], BF16)
    qab = [sbB(f"qab{i}", [128, 2048], BF16) for i in range(2)]
    qbb = [sbB(f"qbb{i}", [128, 1536], BF16) for i in range(2)]
    qs = sbB("qs", [64, T], F32)
    qsw = sbB("qsw", [64, T], F32)
    c0q = sbB("c0q", [64, T], F32)
    s0q = sbB("s0q", [64, T], F32)
    QAT, qqa, kqa = p.wsrc("l0_qaTt", [12 * 128, 16 * 128])
    QBT, qqb, kqb = p.wsrc("l0_qbTt", [16 * 128, 12 * 128])
    ctx.op("sync", "dma_start", c0q[:, :], p.dram_in("L0Q_C", [64, T]),
           writes=("c0q",))
    ctx.op("sync", "dma_start", s0q[:, :], p.dram_in("L0Q_S", [64, T]),
           writes=("s0q",))
    for dt in range(16):
        ctx.op("sync", "dma_start", xnq[:, dt * T:(dt + 1) * T],
               xnown_ap[dt * 128:(dt + 1) * 128, :],
               reads=(xnown_key,), writes=("xnq",))
    ctx.op(qqa, "dma_start", qab[0][:, :], QAT[0:128, :],
           reads=(kqa,), writes=("qab0",))
    for lt in range(12):
        fb = lt % 2
        if lt + 1 < 12:
            ctx.op(qqa, "dma_start", qab[(lt + 1) % 2][:, :],
                   QAT[(lt + 1) * 128:(lt + 2) * 128, :],
                   reads=(kqa,), writes=(f"qab{(lt + 1) % 2}",))
        for half in range(2):
            bb = 2 * half
            for dt in range(16):
                ctx.op("tensor", "matmul", p.pb[bb][:, 0:512],
                       qab[fb][:, dt * 128:(dt + 1) * 128],
                       xnq[:, dt * T + half * 512: dt * T + (half + 1) * 512],
                       start=(dt == 0), stop=(dt == 15),
                       reads=(f"qab{fb}", "xnq"), writes=(f"pb{bb}",),
                       inc=(dt == 15))
            ctx.op("scalar", "copy",
                   qlat[:, lt * T + half * 512: lt * T + (half + 1) * 512],
                   p.pb[bb][:, 0:512], reads=(f"pb{bb}",), writes=("qlat",))
    ctx.op(qqb, "dma_start", qbb[0][:, :], QBT[0:128, :],
           reads=(kqb,), writes=("qbb0",))
    for qt in range(16):
        fb = qt % 2
        if qt + 1 < 16:
            ctx.op(qqb, "dma_start", qbb[(qt + 1) % 2][:, :],
                   QBT[(qt + 1) * 128:(qt + 2) * 128, :],
                   reads=(kqb,), writes=(f"qbb{(qt + 1) % 2}",))
        for half in range(2):
            bb = 4 + 2 * half
            for lt in range(12):
                ctx.op("tensor", "matmul", p.pb[bb][:, 0:512],
                       qbb[fb][:, lt * 128:(lt + 1) * 128],
                       qlat[:, lt * T + half * 512: lt * T + (half + 1) * 512],
                       start=(lt == 0), stop=(lt == 11),
                       reads=(f"qbb{fb}", "qlat"), writes=(f"pb{bb}",),
                       inc=(lt == 11))
        for half in range(2):
            bb = 4 + 2 * half
            ctx.op("scalar", "copy",
                   Q[0:64, qt * T + half * 512: qt * T + (half + 1) * 512],
                   p.pb[bb][0:64, 0:512], reads=(f"pb{bb}",), writes=("Q",))
            ctx.op("vector", "tensor_copy", qs[:, half * 512:(half + 1) * 512],
                   p.pb[bb][64:128, 0:512], reads=(f"pb{bb}",), writes=("qs",))
        for b in range(1):
            ctx.op("vector", "tensor_copy", qsw[0:32, :], qs[32:64, :],
                   reads=("qs",), writes=("qsw",))
            ctx.op("vector", "tensor_copy", qsw[32:64, :], qs[0:32, :],
                   reads=("qs",), writes=("qsw",))
        ctx.op("vector", "tensor_mul", qs[:, :], qs[:, :], c0q[:, :],
               reads=("qs", "c0q"), writes=("qs",))
        ctx.op("vector", "tensor_mul", qsw[:, :], qsw[:, :], s0q[:, :],
               reads=("qsw", "s0q"), writes=("qsw",))
        ctx.op("vector", "tensor_add", Q[64:128, qt * T:(qt + 1) * T],
               qs[:, :], qsw[:, :], reads=("qs", "qsw"), writes=("Q",))
    ctx.barrier(keep=BGKEEP)
    esB.close()

    esC = ExitStack()
    sbC = lambda n, s, d: esC.enter_context(nc.sbuf_tensor(f"{tag}C_{n}", s, d))
    kvaa = sbC("kvaa", [128, 4 * 2048], BF16)
    kvab = sbC("kvab", [128, 16 * 64], BF16)
    xk = [sbC(f"xk{i}", [128, 16 * 512], BF16) for i in range(2)]
    ckc = [sbC(f"ckc{i}", [64, 512], F32) for i in range(2)]
    sks = [sbC(f"sks{i}", [64, 512], F32) for i in range(2)]
    ks = sbC("ks", [64, 512], F32)
    ksw = sbC("ksw", [64, 512], F32)
    stgC = [sbC(f"stgC{i}", [128, 2048], F32) for i in range(2)]
    hw_cast_load(p, kvaa, "kvaa", "l0_kvaTaf", [128, 4 * 2048],
                 stgC, ("stgC0", "stgC1"))
    hw_cast_load(p, kvab, "kvab", "l0_kvaTbf", [128, 16 * 64],
                 stgC, ("stgC0", "stgC1"))
    C0K = p.dram_in("L0K_C", [64, S])
    S0K = p.dram_in("L0K_S", [64, S])

    def load_xk(kc, buf):
        r = kc // 2
        for dt in range(16):
            ctx.op("sync", "dma_start", xk[buf][:, dt * 512:(dt + 1) * 512],
                   xnfull_ap[r * 2048 + dt * 128: r * 2048 + (dt + 1) * 128,
                             (kc % 2) * 512:(kc % 2 + 1) * 512],
                   reads=(xnfull_key,), writes=(f"xk{buf}",))

    load_xk(0, 0)
    ctx.op("sync", "dma_start", ckc[0][:, :], C0K[:, 0:512], writes=("ckc0",))
    ctx.op("sync", "dma_start", sks[0][:, :], S0K[:, 0:512], writes=("sks0",))
    for kc in range(8):
        fb = kc % 2
        if kc + 1 < 8:
            load_xk(kc + 1, (kc + 1) % 2)
            ctx.op("sync", "dma_start", ckc[(kc + 1) % 2][:, :],
                   C0K[:, (kc + 1) * 512:(kc + 2) * 512],
                   writes=(f"ckc{(kc + 1) % 2}",))
            ctx.op("sync", "dma_start", sks[(kc + 1) % 2][:, :],
                   S0K[:, (kc + 1) * 512:(kc + 2) * 512],
                   writes=(f"sks{(kc + 1) % 2}",))
        for ct in range(4):
            for dt in range(16):
                ctx.op("tensor", "matmul", p.pb[ct][:, 0:512],
                       kvaa[:, ct * 2048 + dt * 128: ct * 2048 + (dt + 1) * 128],
                       xk[fb][:, dt * 512:(dt + 1) * 512],
                       start=(dt == 0), stop=(dt == 15),
                       reads=("kvaa", f"xk{fb}"), writes=(f"pb{ct}",),
                       inc=(dt == 15))
        for dt in range(16):
            ctx.op("tensor", "matmul", p.pb[4][0:64, 0:512],
                   kvab[:, dt * 64:(dt + 1) * 64],
                   xk[fb][:, dt * 512:(dt + 1) * 512],
                   start=(dt == 0), stop=(dt == 15),
                   reads=("kvab", f"xk{fb}"), writes=("pb4",), inc=(dt == 15))
        for ct in range(4):
            ctx.op("scalar", "copy", ckv[:, ct * S + kc * 512:(ct * S + (kc + 1) * 512)],
                   p.pb[ct][:, 0:512], reads=(f"pb{ct}",), writes=("ckv",))
        ctx.op("vector", "tensor_copy", ks[:, :], p.pb[4][0:64, 0:512],
               reads=("pb4",), writes=("ks",))
        ctx.op("vector", "tensor_copy", ksw[0:32, :], ks[32:64, :],
               reads=("ks",), writes=("ksw",))
        ctx.op("vector", "tensor_copy", ksw[32:64, :], ks[0:32, :],
               reads=("ks",), writes=("ksw",))
        ctx.op("vector", "tensor_mul", ks[:, :], ks[:, :], ckc[fb][:, :],
               reads=("ks", f"ckc{fb}"), writes=("ks",))
        ctx.op("vector", "tensor_mul", ksw[:, :], ksw[:, :], sks[fb][:, :],
               reads=("ksw", f"sks{fb}"), writes=("ksw",))
        ctx.op("vector", "tensor_add", krot[:, kc * 512:(kc + 1) * 512],
               ks[:, :], ksw[:, :], reads=("ks", "ksw"), writes=("krot",))
    ctx.barrier(keep=BGKEEP)
    esC.close()

    esD = ExitStack()
    sbD = lambda n, s, d: esD.enter_context(nc.sbuf_tensor(f"{tag}D_{n}", s, d))
    wkb = sbD("wkb", [128, 16 * 256], BF16)
    wvb = sbD("wvb", [128, 4 * 2048], BF16)
    Vq = sbD("Vq", [128, 32 * 512], BF16)
    Kf = [sbD(f"Kf{i}", [128, S], BF16) for i in range(2)]
    Eb = [sbD(f"Eb{i}", [128, 1024], BF16) for i in range(2)]
    den = sbD("den", [1, 512], F32)
    bc = sbD("bc", [128, 512], F32)
    attn_out = sbD("attn_out", [128, 16 * T], BF16)
    stgD = [sbD(f"stgD{i}", [128, 2048], F32) for i in range(2)]
    hw_cast_load(p, wkb, "wkb", "l0_WkbTf", [128, 16 * 256],
                 stgD, ("stgD0", "stgD1"))
    hw_cast_load(p, wvb, "wvb", "l0_WvbTf", [128, 4 * 2048],
                 stgD, ("stgD0", "stgD1"))
    for g in range(4):
        for kt in range(32):
            ctx.op("tensor", "matmul", p.pb[6][:, 0:512],
                   ckv[:, 0 * S + kt * 128: 0 * S + (kt + 1) * 128],
                   wvb[:, 0 * 2048 + g * 512:(0 * 2048 + (g + 1) * 512)],
                   start=True, stop=False,
                   reads=("ckv", "wvb"), writes=("pb6",), inc=False)
            for ct in range(1, 4):
                ctx.op("tensor", "matmul", p.pb[6][:, 0:512],
                       ckv[:, ct * S + kt * 128: ct * S + (kt + 1) * 128],
                       wvb[:, ct * 2048 + g * 512: ct * 2048 + (g + 1) * 512],
                       start=False, stop=(ct == 3),
                       reads=("ckv", "wvb"), writes=("pb6",), inc=(ct == 3))
            ctx.op("scalar", "copy", Vq[:, kt * 512:(kt + 1) * 512],
                   p.pb[6][:, 0:512], reads=("pb6",), writes=("Vq",))
        for hh in range(4):
            h = g * 4 + hh
            hb = hh % 2
            for kc in range(8):
                for ct in range(4):
                    ctx.op("tensor", "matmul", p.pb[7][0:64, 0:512],
                           wkb[:, h * 256 + ct * 64: h * 256 + (ct + 1) * 64],
                           ckv[:, ct * S + kc * 512: ct * S + (kc + 1) * 512],
                           start=(ct == 0), stop=(ct == 3),
                           reads=("wkb", "ckv"), writes=("pb7",), inc=(ct == 3))
                ctx.op("scalar", "copy", Kf[hb][0:64, kc * 512:(kc + 1) * 512],
                       p.pb[7][0:64, 0:512], reads=("pb7",), writes=(f"Kf{hb}",))
            ctx.op("vector", "tensor_copy", Kf[hb][64:128, :], krot[:, :],
                   reads=("krot",), writes=(f"Kf{hb}",))
            for u in range(2):
                for pp in range(17):
                    if pp < 16:
                        b0 = 4 + (pp % 2) * 2
                        for j in range(2):
                            kt = 2 * pp + j
                            ctx.op("tensor", "matmul", p.pb[b0 + j][:, 0:512],
                                   Kf[hb][:, kt * 128:(kt + 1) * 128],
                                   Q[:, h * T + u * 512: h * T + (u + 1) * 512],
                                   start=True, stop=True,
                                   reads=(f"Kf{hb}", "Q"), writes=(f"pb{b0 + j}",))
                        ctx.op("scalar", "activation", Eb[pp % 2][:, :],
                               p.psa[:, b0 * 512:(b0 + 2) * 512], AF.Exp,
                               bias=-20.0,
                               reads=(f"pb{b0}", f"pb{b0 + 1}"),
                               writes=(f"Eb{pp % 2}",))
                    if pp > 0:
                        q_ = pp - 1
                        for j in range(2):
                            kt = 2 * q_ + j
                            ctx.op("tensor", "matmul", p.pb[2][:, 0:512],
                                   Vq[:, kt * 512 + hh * 128: kt * 512 + (hh + 1) * 128],
                                   Eb[q_ % 2][:, j * 512:(j + 1) * 512],
                                   start=(kt == 0), stop=(kt == 31),
                                   reads=("Vq", f"Eb{q_ % 2}"), writes=("pb2",),
                                   inc=(kt == 31))
                            ctx.op("tensor", "matmul", p.pb[3][0:1, 0:512],
                                   p.c_ones[:, :],
                                   Eb[q_ % 2][:, j * 512:(j + 1) * 512],
                                   start=(kt == 0), stop=(kt == 31),
                                   reads=("c:ONES128", f"Eb{q_ % 2}"), writes=("pb3",),
                                   inc=(kt == 31))
                ctx.op("vector", "reciprocal", den[:, :], p.pb[3][0:1, 0:512],
                       reads=("pb3",), writes=("den",))
                ctx.op("tensor", "matmul", p.pb[0][:, 0:512],
                       p.c_onesrowf[:, :], den[:, :], start=True, stop=True,
                       reads=("c:ONESROWF", "den"), writes=("pb0",))
                ctx.op("vector", "tensor_copy", bc[:, :], p.pb[0][:, 0:512],
                       reads=("pb0",), writes=("bc",))
                ctx.op("vector", "tensor_mul",
                       attn_out[:, h * T + u * 512: h * T + (u + 1) * 512],
                       p.pb[2][:, 0:512], bc[:, :],
                       reads=("pb2", "bc"), writes=("attn_out",))
    ctx.barrier(keep=BGKEEP)
    esD.close()

    phase_outproj(p, xres_src, xres_dst, attn_out, "attn_out", "l0_owT", tag)
    es.close()



def phase_l1attn(p, xres_src, xres_dst, xnfull_ap, xnfull_key, xnown_ap, xnown_key, tag):
    nc, ctx = p.nc, p.ctx
    es = ExitStack()
    sb = lambda n, s, d: es.enter_context(nc.sbuf_tensor(f"{tag}_{n}", s, d))
    Q = sb("Q", [128, 16 * T], BF16)
    Ksb = sb("Ksb", [128, 4 * S], BF16)
    Vsb = sb("Vsb", [128, 32 * 520], BF16)
    attn_out = sb("attn_out", [128, 16 * T], BF16)

    esB = ExitStack()
    sbB = lambda n, s, d: esB.enter_context(nc.sbuf_tensor(f"{tag}B_{n}", s, d))
    xnq = sbB("xnq", [128, 16 * T], BF16)
    wqb = [sbB(f"wqb{i}", [128, 2048], BF16) for i in range(2)]
    qs = sbB("qs", [128, T], F32)
    qsw = sbB("qsw", [128, T], F32)
    c1q = sbB("c1q", [128, T], F32)
    s1q = sbB("s1q", [128, T], F32)
    WQT, qwq, kwq = p.wsrc("l1_WqTt", [16 * 128, 16 * 128])
    ctx.op("sync", "dma_start", c1q[:, :], p.dram_in("L1Q_C", [128, T]),
           writes=("c1q",))
    ctx.op("sync", "dma_start", s1q[:, :], p.dram_in("L1Q_S", [128, T]),
           writes=("s1q",))
    for dt in range(16):
        ctx.op("sync", "dma_start", xnq[:, dt * T:(dt + 1) * T],
               xnown_ap[dt * 128:(dt + 1) * 128, :],
               reads=(xnown_key,), writes=("xnq",))
    ctx.op(qwq, "dma_start", wqb[0][:, :], WQT[0:128, :],
           reads=(kwq,), writes=("wqb0",))
    for qt in range(16):
        fb = qt % 2
        if qt + 1 < 16:
            ctx.op(qwq, "dma_start", wqb[(qt + 1) % 2][:, :],
                   WQT[(qt + 1) * 128:(qt + 2) * 128, :],
                   reads=(kwq,), writes=(f"wqb{(qt + 1) % 2}",))
        for half in range(2):
            bb = 2 * half
            for dt in range(16):
                ctx.op("tensor", "matmul", p.pb[bb][:, 0:512],
                       wqb[fb][:, dt * 128:(dt + 1) * 128],
                       xnq[:, dt * T + half * 512: dt * T + (half + 1) * 512],
                       start=(dt == 0), stop=(dt == 15),
                       reads=(f"wqb{fb}", "xnq"), writes=(f"pb{bb}",),
                       inc=(dt == 15))
            ctx.op("vector", "tensor_copy", qs[:, half * 512:(half + 1) * 512],
                   p.pb[bb][:, 0:512], reads=(f"pb{bb}",), writes=("qs",))
        for b in range(4):
            ctx.op("vector", "tensor_copy", qsw[b * 32:(b + 1) * 32, :],
                   qs[b * 32 + 32 - 64 * (b % 2):b * 32 + 64 - 64 * (b % 2), :],
                   reads=("qs",), writes=("qsw",))
        ctx.op("vector", "tensor_mul", qs[:, :], qs[:, :], c1q[:, :],
               reads=("qs", "c1q"), writes=("qs",))
        ctx.op("vector", "tensor_mul", qsw[:, :], qsw[:, :], s1q[:, :],
               reads=("qsw", "s1q"), writes=("qsw",))
        ctx.op("vector", "tensor_add", Q[:, qt * T:(qt + 1) * T],
               qs[:, :], qsw[:, :], reads=("qs", "qsw"), writes=("Q",))
    ctx.barrier(keep=BGKEEP)
    esB.close()

    esC = ExitStack()
    sbC = lambda n, s, d: esC.enter_context(nc.sbuf_tensor(f"{tag}C_{n}", s, d))
    wk = sbC("wk", [128, 4 * 2048], BF16)
    wv = sbC("wv", [128, 16 * 512], BF16)
    xk = [sbC(f"xk{i}", [128, 16 * 512], BF16) for i in range(2)]
    c1k = [sbC("c1k0", [128, 512], F32)]
    s1k = [sbC("s1k0", [128, 512], F32)]
    ks = sbC("ks", [128, 512], F32)
    ksw = sbC("ksw", [128, 512], F32)
    ctx.op("gpsimd", "dma_start", wk[:, :],
           p.dram_in("l1_WkTf", [128, 4 * 2048]), writes=("wk",))
    ctx.op("gpsimd", "dma_start", wv[:, :],
           p.dram_in("l1_WvTf", [128, 16 * 512]), writes=("wv",))
    C1K = p.dram_in("L1K_C", [128, S])
    S1K = p.dram_in("L1K_S", [128, S])
    ctx.op("vector", "memset",
           Vsb[:, :].rearrange("p (t g c) -> p t g c", g=8, c=65)[:, :, :, 64:65],
           1.0, writes=("Vsb",))

    def load_xk(kc, buf):
        r = kc // 2
        for dt in range(16):
            ctx.op("sync", "dma_start", xk[buf][:, dt * 512:(dt + 1) * 512],
                   xnfull_ap[r * 2048 + dt * 128: r * 2048 + (dt + 1) * 128,
                             (kc % 2) * 512:(kc % 2 + 1) * 512],
                   reads=(xnfull_key,), writes=(f"xk{buf}",))

    load_xk(0, 0)
    for kc in range(8):
        fb = kc % 2
        ctx.op("sync", "dma_start", c1k[0][:, :],
               C1K[:, kc * 512:(kc + 1) * 512], writes=("c1k0",))
        ctx.op("sync", "dma_start", s1k[0][:, :],
               S1K[:, kc * 512:(kc + 1) * 512], writes=("s1k0",))
        if kc + 1 < 8:
            load_xk(kc + 1, (kc + 1) % 2)
        for kb in range(4):
            for dt in range(16):
                ctx.op("tensor", "matmul", p.pb[kb][:, 0:512],
                       wk[:, kb * 2048 + dt * 128: kb * 2048 + (dt + 1) * 128],
                       xk[fb][:, dt * 512:(dt + 1) * 512],
                       start=(dt == 0), stop=(dt == 15),
                       reads=("wk", f"xk{fb}"), writes=(f"pb{kb}",),
                       inc=(dt == 15))
            ctx.op("vector", "tensor_copy", ks[:, :], p.pb[kb][:, 0:512],
                   reads=(f"pb{kb}",), writes=("ks",))
            for b in range(4):
                ctx.op("vector", "tensor_copy", ksw[b * 32:(b + 1) * 32, :],
                       ks[b * 32 + 32 - 64 * (b % 2):b * 32 + 64 - 64 * (b % 2), :],
                       reads=("ks",), writes=("ksw",))
            ctx.op("vector", "tensor_mul", ks[:, :], ks[:, :], c1k[0][:, :],
                   reads=("ks", "c1k0"), writes=("ks",))
            ctx.op("vector", "tensor_mul", ksw[:, :], ksw[:, :], s1k[0][:, :],
                   reads=("ksw", "s1k0"), writes=("ksw",))
            ctx.op("vector", "tensor_add",
                   Ksb[:, kb * S + kc * 512: kb * S + (kc + 1) * 512],
                   ks[:, :], ksw[:, :], reads=("ks", "ksw"), writes=("Ksb",))
        for k4 in range(4):
            kt = kc * 4 + k4
            for dt in range(16):
                ctx.op("tensor", "matmul", p.pb[4][:, 0:512],
                       xk[fb][:, dt * 512 + k4 * 128: dt * 512 + (k4 + 1) * 128],
                       wv[:, dt * 512:(dt + 1) * 512],
                       start=(dt == 0), stop=(dt == 15),
                       reads=(f"xk{fb}", "wv"), writes=("pb4",), inc=(dt == 15))
            ctx.op("vector", "tensor_copy",
                   Vsb[:, kt * 520:(kt + 1) * 520]
                   .rearrange("p (g c) -> p g c", c=65)[:, :, 0:64],
                   p.pb[4][:, 0:512].rearrange("p (g c) -> p g c", c=64),
                   reads=("pb4",), writes=("Vsb",))
    ctx.barrier(keep=BGKEEP)
    esC.close()

    esD = ExitStack()
    sbD = lambda n, s, d: esD.enter_context(nc.sbuf_tensor(f"{tag}D_{n}", s, d))
    Eb = [sbD(f"Eb{i}", [128, 1024], BF16) for i in range(2)]
    den = sbD("den", [1, 512], F32)
    bc = sbD("bc", [64, 512], F32)
    for kvh in range(8):
        for u in range(2):
            for hp in range(2):
                hA, hB = kvh * 4 + 2 * hp, kvh * 4 + 2 * hp + 1
                pos = [l1_qpos(hA), l1_qpos(hB)]
                for qt_, qb_ in pos:
                    assert qb_ == (kvh % 2) * 64
                for pp in range(33):
                    if pp < 32:
                        kt = pp
                        b0 = 4 + (pp % 2) * 2
                        for j, (qt_, qb_) in enumerate(pos):
                            ctx.op("tensor", "matmul", p.pb[b0 + j][:, 0:512],
                                   Ksb[(kvh % 2) * 64:(kvh % 2) * 64 + 64,
                                       (kvh // 2) * S + kt * 128:(kvh // 2) * S + (kt + 1) * 128],
                                   Q[qb_:qb_ + 64,
                                     qt_ * T + u * 512: qt_ * T + (u + 1) * 512],
                                   start=True, stop=True,
                                   reads=("Ksb", "Q"), writes=(f"pb{b0 + j}",))
                        ctx.op("scalar", "activation", Eb[pp % 2][:, :],
                               p.psa[:, b0 * 512:(b0 + 2) * 512], AF.Exp,
                               bias=-20.0,
                               reads=(f"pb{b0}", f"pb{b0 + 1}"),
                               writes=(f"Eb{pp % 2}",))
                    if pp > 0:
                        kt = pp - 1
                        q_ = pp - 1
                        for j in range(2):
                            ctx.op("tensor", "matmul", p.pb[2 + j][0:65, 0:512],
                                   Vsb[:, kt * 520 + kvh * 65: kt * 520 + (kvh + 1) * 65],
                                   Eb[q_ % 2][:, j * 512:(j + 1) * 512],
                                   start=(kt == 0), stop=(kt == 31),
                                   reads=("Vsb", f"Eb{q_ % 2}"), writes=(f"pb{2 + j}",),
                                   inc=(kt == 31))
                for j, (qt_, qb_) in enumerate(pos):
                    ctx.op("vector", "reciprocal", den[:, :],
                           p.pb[2 + j][64:65, 0:512],
                           reads=(f"pb{2 + j}",), writes=("den",))
                    ctx.op("tensor", "matmul", p.pb[0][0:64, 0:512],
                           p.c_onesrowf[:, 0:64], den[:, :], start=True, stop=True,
                           reads=("c:ONESROWF", "den"), writes=("pb0",))
                    ctx.op("vector", "tensor_copy", bc[:, :], p.pb[0][0:64, 0:512],
                           reads=("pb0",), writes=("bc",))
                    ctx.op("vector", "tensor_mul",
                           attn_out[qb_:qb_ + 64,
                                    qt_ * T + u * 512: qt_ * T + (u + 1) * 512],
                           p.pb[2 + j][0:64, 0:512], bc[:, :],
                           reads=(f"pb{2 + j}", "bc"), writes=("attn_out",))
    ctx.barrier(keep=BGKEEP)
    esD.close()

    phase_outproj(p, xres_src, xres_dst, attn_out, "attn_out", "l1_WoT", tag)
    es.close()




def build_fused():
    nc = bass.Bass("TRN2", target_bir_lowering=False)
    p = Prog(nc)
    xin = p.dram_in("x_in", [T, D], F32)
    xout = p.dram_out("x_out", [T, D], F32)
    p.setup_common()
    xa = p.dram_int("xa", [T, D], F32).ap()
    xb = p.dram_int("xb", [T, D], F32).ap()
    w2p = p.dram_int("w2p", [4 * FF, 512], BF16)
    cc_in = p.dram_int("cc_in", [2048, T], BF16)
    cc_out = p.dram_int("cc_out", [8192, T], BF16)
    ctx = p.ctx

    p.phase_marks = []

    def mark(nm):
        p.phase_marks.append((nm, p.nc.n_instructions()))

    def exchange(src_ap, src_key, tag):
        phase_xnfull(p, src_ap, src_key, cc_in.ap(), "cc_in", [(0, 0)], tag)
        ctx.op("gpsimd", "collective_compute", "AllGather", ALU.bypass,
               replica_groups=[[0, 1, 2, 3], [4, 5, 6, 7]],
               ins=[cc_in.ap().opt()], outs=[cc_out.ap().opt()],
               reads=("cc_in",), writes=("cc_out",))
        ctx.barrier(keep=BGKEEP)

    cvt = {"l0_W1Tt": [64 * 128, 16 * 128], "l0_W2Tdc": [4 * FF, 512],
           "l1_W1Tt": [64 * 128, 16 * 128], "l1_W3Tt": [64 * 128, 16 * 128],
           "l0_qaTt": [12 * 128, 16 * 128], "l0_qbTt": [16 * 128, 12 * 128],
           "l0_owT": [D, D], "l1_WoT": [D, D], "l1_WqTt": [16 * 128, 16 * 128]}
    p.wc = {}
    for nm, shp in cvt.items():
        p.wc[nm] = p.dram_int(f"wc_{nm}", shp, BF16)
    it = lambda nm: (nm, cvt[nm], p.wc[nm])
    cvbufs = [ctx.sb(f"cv2buf{i}", [128, 1024], BF16) for i in range(2)]
    mark("start")
    phase_conv_fast(p, [it("l0_W1Tt"), it("l0_W2Tdc")])
    mark("bf16conv")
    phase_w2prep(p, w2p)
    mark("w2prep")
    phase_conv_bg(p, [it("l0_qaTt"), it("l0_qbTt"), it("l0_owT"),
                      it("l1_WqTt"), it("l1_WoT")], cvbufs)
    phase_l0ffn(p, xin, xa, "f0a")
    mark("l0ffn_a")
    exchange(xa, "xa", "xf0")
    mark("exch0")
    phase_conv_bg(p, [it("l1_W1Tt"), it("l1_W3Tt")], cvbufs)
    phase_l0attn(p, xa, xb, cc_out.ap(), "cc_out", cc_in.ap(), "cc_in", "a0")
    mark("l0attn")
    phase_l0ffn(p, xb, xa, "f0b")
    mark("l0ffn_b")
    phase_l1ffn(p, xa, xb, w2p, "f1a")
    mark("l1ffn_a")
    exchange(xb, "xb", "xf1")
    mark("exch1")
    phase_l1attn(p, xb, xa, cc_out.ap(), "cc_out", cc_in.ap(), "cc_in", "a1")
    mark("l1attn")
    phase_l1ffn(p, xa, xout, w2p, "f1b")
    mark("l1ffn_b")
    ctx.barrier()
    return p





def kernel(**inputs):
    inputs = {k: np.asarray(v) for k, v in inputs.items()}
    W = prep_weights(inputs)
    x = np.ascontiguousarray(inputs["L0_attn_x"]).astype(np.float32)
    p = build_fused()
    in_maps = []
    for c in range(8):
        b, r = c // 4, c % 4
        consts = host_consts(c)
        m = {}
        for name in p.inp:
            if name in consts:
                m[name] = consts[name]
            elif name in W:
                m[name] = W[name]
            elif name == "x_in":
                m[name] = np.ascontiguousarray(x[b, r * 1024:(r + 1) * 1024, :])
            else:
                raise KeyError(name)
        in_maps.append(m)
    res = run_bass_kernel_spmd(p.nc, in_maps, core_ids=list(range(8)))
    out = np.empty_like(x)
    for c in range(8):
        b, r = c // 4, c % 4
        out[b, r * 1024:(r + 1) * 1024, :] = res.results[c]["x_out"]
    return out


# version 10 (final)
# speedup vs baseline: 1.5697x; 1.0804x over previous
import math
from contextlib import ExitStack

import numpy as np
import ml_dtypes
import concourse.bass as bass
import concourse.mybir as mybir
from concourse.bass_utils import run_bass_kernel_spmd

F32 = mybir.dt.float32
BF16 = mybir.dt.bfloat16
AF = mybir.ActivationFunctionType
ALU = mybir.AluOpType
AXX = mybir.AxisListType.X

T = 1024
D = 2048
FF = 8192
S = 4096

CVALS = [0.0, 1.0, -1.0, 0.5, 2.0, -20.0, 1e-5,
         float(np.float32(0.5) * np.float32(1.0 / np.sqrt(8192.0))),
         -1e9]

BGKEEP = ("d:gpsimd:cv2b", "d:gpsimd:cvw:")


ENGINES = ("tensor", "vector", "scalar", "gpsimd", "sync")

L1_EVENKV = [h for h in range(32) if (h // 4) % 2 == 0]
L1_ODDKV = [h for h in range(32) if (h // 4) % 2 == 1]


def l1_qpos(h):
    if (h // 4) % 2 == 0:
        return L1_EVENKV.index(h), 0
    return L1_ODDKV.index(h), 64


class Ctx:

    def __init__(self, nc):
        self.nc = nc
        self.es = ExitStack()
        self.eng = dict(tensor=nc.tensor, vector=nc.vector, scalar=nc.scalar,
                        gpsimd=nc.gpsimd, sync=nc.sync)
        self.sem = {e: self.es.enter_context(nc.semaphore(f"sem_{e}")) for e in ENGINES}
        self.cnt = {e: 0 for e in ENGINES}
        self.waited = {}
        self.res = {}

    def sb(self, name, shape, dtype):
        return self.es.enter_context(self.nc.sbuf_tensor(name, shape, dtype))

    def ps(self, name, shape, dtype):
        return self.es.enter_context(self.nc.psum_tensor(name, shape, dtype))

    def _dma_semkey(self, key, eng):
        sk = f"d:{eng}:{key}"
        if sk not in self.sem:
            self.sem[sk] = self.es.enter_context(self.nc.semaphore(sk))
            self.cnt[sk] = 0
        return sk

    def _need(self, eng, key, write, need):
        r = self.res.get(key)
        if r is None:
            return
        w = r.get("w")
        if w is not None:
            wsem, wval = w
            if not (wsem == "tensor" and eng == "tensor"):
                need[wsem] = max(need.get(wsem, 0), wval)
        if write:
            for rsem, rval in r.get("r", {}).items():
                if rsem == "tensor" and eng == "tensor":
                    continue
                need[rsem] = max(need.get(rsem, 0), rval)

    def _record(self, semkey, key, write, val):
        r = self.res.setdefault(key, {"w": None, "r": {}})
        if write:
            r["w"] = (semkey, val)
            r["r"] = {}
        else:
            r["r"][semkey] = max(r["r"].get(semkey, 0), val)

    def _emit_waits(self, eng, need):
        e = self.eng[eng]
        for tgt, val in sorted(need.items()):
            if val <= 0 or self.waited.get((eng, tgt), 0) >= val:
                continue
            if val > self.cnt[tgt]:
                raise RuntimeError(f"{eng} waits {tgt}>={val} but cnt={self.cnt[tgt]}")
            e.wait_ge(self.sem[tgt], val)
            self.waited[(eng, tgt)] = val

    def op(self, eng, method, *args, reads=(), writes=(), inc=True, **kw):
        need = {}
        for k in reads:
            self._need(eng, k, False, need)
        for k in writes:
            self._need(eng, k, True, need)
        self._emit_waits(eng, need)
        ins = getattr(self.eng[eng], method)(*args, **kw)
        is_dma = method in ("dma_start", "collective_compute", "dma_start_transpose")
        if is_dma:
            assert writes, f"dma op needs a write resource ({method})"
            semkey = self._dma_semkey(writes[0], eng)
            amt = 1 if method == "collective_compute" else 16
            ins.then_inc(self.sem[semkey], amt)
            self.cnt[semkey] += amt
            val = self.cnt[semkey]
        elif inc:
            semkey = eng
            ins.then_inc(self.sem[eng], 1)
            self.cnt[eng] += 1
            val = self.cnt[eng]
        else:
            semkey = eng
            val = self.cnt[eng] + 1
        for k in reads:
            self._record(semkey, k, False, val)
        for k in writes:
            self._record(semkey, k, True, val)
        return ins

    def barrier(self, keep=()):
        def kept(semkey):
            return any(semkey.startswith(k) for k in keep)
        for a in ENGINES:
            for b, cnt in self.cnt.items():
                if a == b or cnt == 0 or kept(b):
                    continue
                if self.waited.get((a, b), 0) < cnt:
                    self.eng[a].wait_ge(self.sem[b], cnt)
                    self.waited[(a, b)] = cnt
        for key, r in list(self.res.items()):
            semkeys = set()
            if r.get("w"):
                semkeys.add(r["w"][0])
            semkeys |= set(r.get("r", {}).keys())
            if not any(kept(s) for s in semkeys):
                del self.res[key]



def _stat_tiles(wT, kt=128, ct=128):
    K, M = wT.shape
    a = wT.reshape(K // kt, kt, M // ct, ct).transpose(2, 1, 0, 3)
    return np.ascontiguousarray(a.reshape(M // ct * kt, (K // kt) * ct))


def _stat_flat(wT, kt=128, ct=128):
    K, M = wT.shape
    a = wT.reshape(K // kt, kt, M // ct, ct).transpose(1, 2, 0, 3)
    return np.ascontiguousarray(a.reshape(kt, (M // ct) * (K // kt) * ct))


def prep_weights(w):
    o = {}
    o["l0_W1Tt"] = _stat_tiles(w["L0_ffn_W1"].T)
    o["l0_W2Tdc"] = np.ascontiguousarray(
        w["L0_ffn_W2"].T.reshape(FF, 4, 512).transpose(1, 0, 2).reshape(FF, 2048))
    o["l0_WrTt"] = _stat_tiles(w["L0_ffn_W_r"].T, ct=8)
    o["l0_br"] = np.ascontiguousarray(w["L0_ffn_b_r"].reshape(1, 8))
    o["l0_A1Tt"] = _stat_tiles(w["L0_ffn_A1"].reshape(64, D).T, ct=64)
    B1rf = w["L0_ffn_B1"].transpose(0, 2, 1).reshape(64, FF)
    o["l0_B1aug"] = np.ascontiguousarray(
        np.concatenate([B1rf, w["L0_ffn_b1"].reshape(1, FF)], 0))
    o["l0_A2Tt"] = _stat_tiles(w["L0_ffn_A2"].reshape(64, FF).T, ct=64)
    B2rf = w["L0_ffn_B2"].transpose(0, 2, 1).reshape(64, D)
    o["l0_B2aug"] = np.ascontiguousarray(
        np.concatenate([B2rf, w["L0_ffn_b2"].reshape(1, D)], 0))
    H0, QKD, NOPE, VD0, QR, KVR = 16, 128, 64, 128, 1536, 512
    o["l0_qaTt"] = _stat_tiles(w["L0_attn_q_a_w"].T)
    qb = w["L0_attn_q_b_w"].reshape(H0, QKD, QR)
    perm = (list(range(NOPE)) + [NOPE + 2 * i for i in range(32)]
            + [NOPE + 2 * i + 1 for i in range(32)])
    o["l0_qbTt"] = _stat_tiles(
        np.ascontiguousarray(qb[:, perm, :].reshape(H0 * QKD, QR).T))
    kva = w["L0_attn_kv_a_w"]
    rot = kva[KVR:]
    kva_p = np.concatenate([kva[:KVR], rot[0::2], rot[1::2]], 0)
    kvaT = np.ascontiguousarray(kva_p.T)
    o["l0_kvaTaf"] = _stat_flat(kvaT[:, 0:KVR])
    o["l0_kvaTbf"] = _stat_flat(kvaT[:, KVR:], ct=64)
    kvb = w["L0_attn_kv_b_w"].reshape(H0, NOPE + VD0, KVR)
    o["l0_WkbTf"] = _stat_flat(
        np.ascontiguousarray(kvb[:, :NOPE, :].reshape(H0 * NOPE, KVR).T), ct=64)
    o["l0_WvbTf"] = _stat_flat(
        np.ascontiguousarray(kvb[:, NOPE:, :].reshape(H0 * VD0, KVR).T), ct=2048)
    o["l0_owT"] = np.ascontiguousarray(w["L0_attn_o_w"].T)
    order = []
    for j in range(16):
        order += [L1_EVENKV[j], L1_ODDKV[j]]
    rowperm = np.concatenate([np.arange(h * 64, (h + 1) * 64) for h in order])
    o["l1_WqTt"] = _stat_tiles(np.ascontiguousarray(w["L1_attn_Wq"][rowperm].T))
    o["l1_WkTf"] = _stat_flat(w["L1_attn_Wk"].T)
    o["l1_WvTf"] = _stat_flat(w["L1_attn_Wv"].T, ct=512)
    o["l1_WoT"] = np.ascontiguousarray(w["L1_attn_Wo"].T[rowperm])
    o["l1_W1Tt"] = _stat_tiles(w["L1_ffn_W1"].T)
    o["l1_W3Tt"] = _stat_tiles(w["L1_ffn_W3"].T)
    o["l1_W2G"] = np.ascontiguousarray(w["L1_ffn_W2"].T.reshape(64, 128 * D))
    return o


def host_consts(core_id):
    c = {}
    c["CONSTS"] = np.tile(np.array(CVALS, np.float32), (128, 1))
    c["ID128"] = np.eye(128, dtype=np.float32)
    c["ONESROW"] = np.ones((1, 128), np.float32)
    c["ONESROWF"] = np.ones((1, 128), np.float32)
    c["ONES128"] = np.ones((128, 1), np.float32)
    e8 = np.zeros((8, 64), np.float32)
    for e in range(8):
        e8[e, e * 8:(e + 1) * 8] = 2.0
    c["E8X2"] = e8
    esel = np.zeros((8, 1024), np.float32)
    for j in range(8):
        esel[j, j * 128:(j + 1) * 128] = 1.0
    c["ESEL8"] = esel
    c["EPSIDX"] = (np.arange(8, dtype=np.float32) * np.float32(-1e-6)).reshape(1, 8)

    def had(n):
        h = np.array([[1.0]], np.float32)
        while h.shape[0] < n:
            h = np.block([[h, h], [h, -h]])
        return np.ascontiguousarray(h)
    c["H128"] = had(128)
    c["H64"] = had(64)
    q0 = ((core_id % 4) * T + np.arange(T)).astype(np.float32)
    kpos = np.arange(S, dtype=np.float32)
    th = (np.float32(1.0)
          / np.float32(10000.0) ** (np.arange(0, 64, 2, dtype=np.float32)
                                    / np.float32(64.0))).astype(np.float32)
    for nm, pos, scl, rep in (("L0Q", q0, 1.0, 1), ("L0K", kpos, 1.0, 1),
                              ("L1Q", q0, 0.125, 2), ("L1K", kpos, 1.0, 2)):
        ang = (pos[None, :] * th[:, None]).astype(np.float32)
        cs = np.cos(ang).astype(np.float32) * np.float32(scl)
        sn = np.sin(ang).astype(np.float32) * np.float32(scl)
        blkC = np.concatenate([cs, cs], 0)
        blkS = np.concatenate([-sn, sn], 0)
        c[f"{nm}_C"] = np.ascontiguousarray(np.concatenate([blkC] * rep, 0))
        c[f"{nm}_S"] = np.ascontiguousarray(np.concatenate([blkS] * rep, 0))
    return c



class Prog:
    def __init__(self, nc):
        self.nc = nc
        self.ctx = Ctx(nc)
        self.inp = {}

    def dram_in(self, name, shape, dtype=F32):
        if name in self.inp:
            return self.inp[name]
        t = self.nc.dram_tensor(name, list(shape), dtype, kind="ExternalInput")
        self.inp[name] = t.ap()
        return self.inp[name]

    def dram_out(self, name, shape, dtype=F32):
        t = self.nc.dram_tensor(name, list(shape), dtype, kind="ExternalOutput")
        return t.ap()

    def dram_int(self, name, shape, dtype):
        return self.nc.dram_tensor(name, list(shape), dtype, kind="Internal")

    def setup_common(self):
        ctx = self.ctx
        self.psa = ctx.ps("PSA", [128, 4096], F32)
        self.pb = [self.psa[:, i * 512:(i + 1) * 512] for i in range(8)]
        self.c_consts = ctx.sb("c_consts", [128, len(CVALS)], F32)
        for j, v in enumerate(CVALS):
            self.nc.const_aps.register(F32, v, self.c_consts[:, j:j + 1])
        self.CI = {v: j for j, v in enumerate(CVALS)}
        self.c_id = ctx.sb("c_id", [128, 128], F32)
        self.c_onesrow = ctx.sb("c_onesrow", [1, 128], BF16)
        self.c_onesrowf = ctx.sb("c_onesrowf", [1, 128], F32)
        self.c_ones = ctx.sb("c_ones", [128, 1], BF16)
        self.c_e8 = ctx.sb("c_e8", [8, 64], BF16)
        self.c_eps = ctx.sb("c_eps", [1, 8], F32)
        self.c_h128 = ctx.sb("c_h128", [128, 128], BF16)
        self.c_h64 = ctx.sb("c_h64", [64, 64], BF16)
        loads = [("CONSTS", self.c_consts, [128, len(CVALS)]),
                 ("ID128", self.c_id, [128, 128]),
                 ("ONESROW", self.c_onesrow, [1, 128]),
                 ("ONESROWF", self.c_onesrowf, [1, 128]),
                 ("ONES128", self.c_ones, [128, 1]),
                 ("E8X2", self.c_e8, [8, 64]),
                 ("EPSIDX", self.c_eps, [1, 8]),
                 ("H128", self.c_h128, [128, 128]),
                 ("H64", self.c_h64, [64, 64])]
        for nm, dst, shp in loads:
            src = self.dram_in(nm, shp, F32)
            ctx.op("gpsimd", "dma_start", dst[:, :], src, writes=(f"c:{nm}",))
        ctx.barrier(keep=BGKEEP)

    def const(self, v):
        j = self.CI[v]
        return self.c_consts[:, j:j + 1]

    def wsrc(self, name, shape):
        wc = getattr(self, "wc", None)
        if wc and name in wc:
            return wc[name].ap(), "scalar", f"cvw:{name}"
        return self.dram_in(name, shape), "gpsimd", name



def ln_tile(p, x_ap, x_key, xn_tm, xn_key, stat, stat_key, tag):
    ctx = p.ctx
    bn = stat
    for j in range(4):
        ctx.op("vector", "bn_stats", stat[:, 8 + j * 6: 8 + (j + 1) * 6],
               x_ap[:, j * 512:(j + 1) * 512],
               reads=(x_key,), writes=(stat_key,))
    ctx.op("vector", "bn_aggr", stat[:, 0:2], stat[:, 8:32],
           reads=(stat_key,), writes=(stat_key,))
    ctx.op("scalar", "activation", stat[:, 2:3], stat[:, 1:2], AF.Sqrt,
           bias=1e-5, reads=(stat_key,), writes=(stat_key,))
    ctx.op("vector", "reciprocal", stat[:, 3:4], stat[:, 2:3],
           reads=(stat_key,), writes=(stat_key,))
    ctx.op("vector", "tensor_scalar", xn_tm[:, :], x_ap, stat[:, 0:1], stat[:, 3:4],
           ALU.subtract, ALU.mult, reads=(x_key, stat_key), writes=(xn_key,))


def transpose_tm_to_fm(p, src_tm, src_key, dst_fm, dst_key, ti, tc, n_dt=16,
                       dst_bf16=True, banks=(0, 1)):
    ctx = p.ctx
    for g in range(n_dt // 4):
        b = p.pb[banks[g % len(banks)]]
        bk = f"pb{banks[g % len(banks)]}"
        for q in range(4):
            dt = g * 4 + q
            ctx.op("tensor", "transpose", b[:, q * 128:(q + 1) * 128],
                   src_tm[:, dt * 128:(dt + 1) * 128], p.c_id[:, :],
                   reads=(src_key, "c:ID128"), writes=(bk,), inc=(q == 3))
        for q in range(4):
            dt = g * 4 + q
            ctx.op("scalar", "copy",
                   dst_fm[:, dt * tc + ti * 128: dt * tc + (ti + 1) * 128],
                   b[:, q * 128:(q + 1) * 128],
                   reads=(bk,), writes=(dst_key,))


def mm_movs(p, psum_ap, bank_keys, stat_ap, stat_keys, mov_ap, mov_keys, ncols,
            start, stop, inc_last=True, tile_position=None):
    ctx = p.ctx
    nch = (ncols + 511) // 512
    for j in range(nch):
        c0, c1 = j * 512, min(ncols, (j + 1) * 512)
        kw = {}
        if tile_position is not None:
            kw["tile_position"] = tile_position
        ctx.op("tensor", "matmul", psum_ap[:, c0:c1], stat_ap, mov_ap[:, c0:c1],
               start=start and j == 0, stop=stop and j == nch - 1,
               reads=tuple(stat_keys) + tuple(mov_keys), writes=tuple(bank_keys),
               inc=inc_last and (j == nch - 1), **kw)



def hw_cast_load(p, dst, dst_key, src_name, shape, stg, stg_keys, cols_per=2048):
    ctx = p.ctx
    rows, cols = shape
    srcap = p.dram_in(src_name, shape)
    n = 0
    for c0 in range(0, cols, cols_per):
        c1 = min(cols, c0 + cols_per)
        si = n % len(stg)
        ctx.op("sync", "dma_start", stg[si][0:rows, 0:c1 - c0],
               srcap[:, c0:c1], reads=(src_name,), writes=(stg_keys[si],))
        ctx.op("vector", "tensor_copy", dst[0:rows, c0:c1],
               stg[si][0:rows, 0:c1 - c0],
               reads=(stg_keys[si],), writes=(dst_key,))
        n += 1


def phase_l0ffn(p, xres_src, xres_dst, tag):
    nc, ctx = p.nc, p.ctx
    tc = 512
    nt = tc // 128
    nch = T // tc
    es = ExitStack()
    sb = lambda n, s, d: es.enter_context(nc.sbuf_tensor(f"{tag}_{n}", s, d))

    wr = sb("wr", [128, 16 * 8], BF16)
    a1t = sb("a1t", [128, 16 * 64], BF16)
    a2t = sb("a2t", [128, 64 * 64], BF16)
    b1aug = sb("b1aug", [65, FF], BF16)
    b2aug = sb("b2aug", [65, D], BF16)
    br = sb("br", [1, 8], BF16)
    epsb = sb("epsb", [128, 8], F32)
    NW = 3
    w1b = [sb(f"w1b{i}", [128, 16 * 128], BF16) for i in range(NW)]
    NW2 = 4
    w2b = [sb(f"w2b{i}", [128, 1024], BF16) for i in range(NW2)]
    xb = [sb(f"xb{i}", [128, D], F32) for i in range(nt)]
    xo = [sb(f"xo{i}", [128, 512], F32) for i in range(2)]
    xn_tm = sb("xn_tm", [128, D], F32)
    xn_fm = sb("xn_fm", [128, 16 * tc], BF16)
    stat = sb("stat", [128, 32], F32)
    lg = sb("lg", [128, 8], F32)
    sel = sb("sel", [128, 8], F32)
    m1 = sb("m1", [128, 8], F32)
    m2 = sb("m2", [128, 8], F32)
    ee = sb("ee", [128, 8], F32)
    ee2 = sb("ee2", [128, 8], F32)
    rr = sb("rr", [128, 8], F32)
    w12 = sb("w12", [128, 2 * nt], F32)
    w12s = sb("w12s", [8, 128], BF16)
    esel = sb("esel", [8, 1024], BF16)
    wbc = [sb(f"wbc{k}", [128, tc], BF16) for k in range(2)]
    m1t = sb("m1t", [8, tc], BF16)
    m2t = sb("m2t", [8, tc], BF16)
    mex = [sb(f"mex{k}", [64, tc], BF16) for k in range(2)]
    mexw = [sb(f"mexw{k}", [64, tc], F32) for k in range(2)]
    mu1a = [sb(f"mu1a{k}", [65, tc], BF16) for k in range(2)]
    mzmix = sb("mzmix", [65, tc], BF16)
    v01 = [sb(f"v01_{k}", [128, tc], BF16) for k in range(2)]
    base1 = [sb(f"base1_{par}", [128, tc], F32) for par in range(2)]
    hkb = [[sb(f"h{k}_{par}", [128, tc], BF16) for par in range(2)]
           for k in range(2)]
    hmt = [sb(f"hmt{par}", [128, tc], BF16) for par in range(2)]
    hmix = sb("hmix", [128, 64 * tc], BF16)

    W1T, q1, k1 = p.wsrc("l0_W1Tt", [64 * 128, 16 * 128])
    W2T, q2, k2 = p.wsrc("l0_W2Tdc", [4 * FF, 512])
    stg = [xb[0], xb[1]]
    stgk = ("xb0", "xb1")
    hw_cast_load(p, wr, "wr", "l0_WrTt", [128, 16 * 8], stg, stgk)
    hw_cast_load(p, a1t, "a1t", "l0_A1Tt", [128, 16 * 64], stg, stgk)
    hw_cast_load(p, a2t, "a2t", "l0_A2Tt", [128, 64 * 64], stg, stgk)
    hw_cast_load(p, b1aug, "b1aug", "l0_B1aug", [65, FF], stg, stgk)
    hw_cast_load(p, b2aug, "b2aug", "l0_B2aug", [65, D], stg, stgk)
    hw_cast_load(p, br, "br", "l0_br", [1, 8], stg, stgk)
    hw_cast_load(p, esel, "esel", "ESEL8", [8, 1024], stg, stgk)
    epsbf = sb("epsbf", [1, 8], BF16)
    ctx.op("vector", "tensor_copy", epsbf[:, :], p.c_eps[:, :],
           reads=("c:EPSIDX",), writes=("epsbf",))
    ctx.op("tensor", "matmul", p.pb[0][0:128, 0:8], p.c_onesrow[:, :], epsbf[:, :],
           start=True, stop=True, reads=("c:ONESROW", "epsbf"), writes=("pb0",))
    ctx.op("vector", "tensor_copy", epsb[:, :], p.pb[0][0:128, 0:8],
           reads=("pb0",), writes=("epsb",))
    for k in range(2):
        ctx.op("gpsimd", "memset", mu1a[k][64:65, :], 1.0, writes=(f"mu1a{k}",))

    for ch in range(nch):
        tok0 = ch * tc
        for i in range(nt):
            ctx.op("sync", "dma_start", xb[i][:, :],
                   xres_src[tok0 + i * 128: tok0 + (i + 1) * 128, :],
                   reads=("xres_src",), writes=(f"xb{i}",))
        for i in range(nt):
            ln_tile(p, xb[i][:, :], f"xb{i}", xn_tm, "xn_tm", stat, "stat", tag)
            transpose_tm_to_fm(p, xn_tm, "xn_tm", xn_fm, "xn_fm", i, tc)
        for i in range(nt):
            for dt in range(16):
                ctx.op("tensor", "matmul", p.pb[4][0:128, 0:8],
                       xn_fm[:, dt * tc + i * 128: dt * tc + (i + 1) * 128],
                       wr[:, dt * 8:(dt + 1) * 8],
                       start=(dt == 0), stop=False,
                       reads=("xn_fm", "wr"), writes=("pb4",), inc=False)
            ctx.op("tensor", "matmul", p.pb[4][0:128, 0:8], p.c_onesrow[:, :],
                   br[:, :], start=False, stop=True,
                   reads=("c:ONESROW", "br"), writes=("pb4",))
            ctx.op("vector", "tensor_copy", lg[:, :], p.pb[4][0:128, 0:8],
                   reads=("pb4",), writes=("lg",))
            ctx.op("vector", "tensor_add", sel[:, :], lg[:, :], epsb[:, :],
                   reads=("lg", "epsb"), writes=("sel",))
            ctx.op("vector", "reduce_max", rr[:, 0:1], sel[:, :], axis=AXX,
                   reads=("sel",), writes=("rr",))
            ctx.op("vector", "tensor_scalar", m1[:, :], sel[:, :], rr[:, 0:1], None,
                   ALU.is_ge, reads=("sel", "rr"), writes=("m1",))
            ctx.op("vector", "scalar_tensor_tensor", sel[:, :], m1[:, :], -1e9,
                   sel[:, :], ALU.mult, ALU.add,
                   reads=("m1", "sel"), writes=("sel",))
            ctx.op("vector", "reduce_max", rr[:, 1:2], sel[:, :], axis=AXX,
                   reads=("sel",), writes=("rr",))
            ctx.op("vector", "tensor_scalar", m2[:, :], sel[:, :], rr[:, 1:2], None,
                   ALU.is_ge, reads=("sel", "rr"), writes=("m2",))
            ctx.op("vector", "reduce_max", rr[:, 2:3], lg[:, :], axis=AXX,
                   negate=True, reads=("lg",), writes=("rr",))
            ctx.op("scalar", "activation", ee[:, :], lg[:, :], AF.Exp,
                   bias=rr[:, 2:3], accum_out=rr[:, 3:4],
                   reads=("lg", "rr"), writes=("ee", "rr"))
            ctx.op("vector", "reciprocal", rr[:, 4:5], rr[:, 3:4],
                   reads=("rr",), writes=("rr",))
            for k, mk in ((0, m1), (1, m2)):
                ctx.op("vector", "tensor_mul", ee2[:, :], mk[:, :], ee[:, :],
                       reads=(("m1", "m2")[k], "ee"), writes=("ee2",))
                ctx.op("vector", "reduce_sum", rr[:, 6:7], ee2[:, :], axis=AXX,
                       reads=("ee2",), writes=("rr",))
                ctx.op("vector", "tensor_scalar", w12[:, 2 * i + k: 2 * i + k + 1],
                       rr[:, 6:7], rr[:, 4:5], 0.5, ALU.mult, ALU.mult,
                       reads=("rr",), writes=("w12",))
            for mk, mt, mkey, mtk in ((m1, m1t, "m1", "m1t"), (m2, m2t, "m2", "m2t")):
                ctx.op("tensor", "transpose", p.pb[5][0:8, i * 128:(i + 1) * 128],
                       mk[:, :], p.c_id[:, :],
                       reads=(mkey, "c:ID128"), writes=("pb5",))
                ctx.op("vector", "tensor_copy", mt[0:8, i * 128:(i + 1) * 128],
                       p.pb[5][0:8, i * 128:(i + 1) * 128],
                       reads=("pb5",), writes=(mtk,))
        ctx.op("tensor", "transpose", p.pb[5][0:8, 0:128], w12[:, :], p.c_id[:, :],
               reads=("w12", "c:ID128"), writes=("pb5",))
        ctx.op("vector", "tensor_copy", w12s[:, :], p.pb[5][0:8, 0:128],
               reads=("pb5",), writes=("w12s",))
        for k in range(2):
            bk = 3 if k == 0 else 4
            for i in range(nt):
                ctx.op("tensor", "matmul", p.pb[bk][:, i * 128:(i + 1) * 128],
                       esel[:, (2 * i + k) * 128:(2 * i + k + 1) * 128],
                       w12s[:, :], start=True, stop=True,
                       reads=("esel", "w12s"), writes=(f"pb{bk}",),
                       inc=(i == nt - 1))
            ctx.op("vector", "tensor_copy", wbc[k][:, :], p.pb[bk][:, 0:tc],
                   reads=(f"pb{bk}",), writes=(f"wbc{k}",))
        ctx.op("vector", "tensor_add", mzmix[64:65, :], wbc[0][64:65, :],
               wbc[1][64:65, :], reads=("wbc0", "wbc1"), writes=("mzmix",))
        for k, (mt, mtk) in enumerate(((m1t, "m1t"), (m2t, "m2t"))):
            ctx.op("tensor", "matmul", p.pb[5][0:64, 0:tc], p.c_e8[:, :], mt[:, :],
                   start=True, stop=True, reads=("c:E8X2", mtk), writes=("pb5",))
            ctx.op("vector", "tensor_copy", mex[k][:, :], p.pb[5][0:64, 0:tc],
                   reads=("pb5",), writes=(f"mex{k}",))
            ctx.op("vector", "tensor_mul", mexw[k][:, :], mex[k][:, :],
                   wbc[k][0:64, :], reads=(f"mex{k}", f"wbc{k}"),
                   writes=(f"mexw{k}",))
        for dt in range(16):
            ctx.op("tensor", "matmul", p.pb[6][0:64, 0:tc],
                   a1t[:, dt * 64:(dt + 1) * 64],
                   xn_fm[:, dt * tc:(dt + 1) * tc],
                   start=(dt == 0), stop=(dt == 15),
                   reads=("a1t", "xn_fm"), writes=("pb6",), inc=(dt == 15))
        for k in range(2):
            ctx.op("vector", "tensor_mul", mu1a[k][0:64, :], p.pb[6][0:64, 0:tc],
                   mex[k][:, :], reads=("pb6", f"mex{k}"), writes=(f"mu1a{k}",))
        for pf in range(2):
            ctx.op(q1, "dma_start", w1b[pf][:, :], W1T[pf * 128:(pf + 1) * 128, :],
                   reads=(k1,), writes=(f"w1b{pf}",))
        for ft in range(65):
            if ft < 64:
                cur = w1b[ft % NW]
                ck = f"w1b{ft % NW}"
                if ft + 2 < 64:
                    nx = ft + 2
                    ctx.op(q1, "dma_start", w1b[nx % NW][:, :],
                           W1T[nx * 128:(nx + 1) * 128, :],
                           reads=(k1,), writes=(f"w1b{nx % NW}",))
                fb = ft % 2
                bA, bB, bC = (0, 1, 2) if fb == 0 else (3, 4, 5)
                kA, kB, kC = f"pb{bA}", f"pb{bB}", f"pb{bC}"
                for dt in range(16):
                    ctx.op("tensor", "matmul", p.pb[bA][:, 0:tc],
                           cur[:, dt * 128:(dt + 1) * 128],
                           xn_fm[:, dt * tc:(dt + 1) * tc],
                           start=(dt == 0), stop=(dt == 15),
                           reads=(ck, "xn_fm"), writes=(kA,), inc=(dt == 15))
                for k, bZ, kZ in ((0, bB, kB), (1, bC, kC)):
                    ctx.op("tensor", "matmul", p.pb[bZ][:, 0:tc],
                           b1aug[:, ft * 128:(ft + 1) * 128], mu1a[k][:, :],
                           start=True, stop=True,
                           reads=("b1aug", f"mu1a{k}"), writes=(kZ,))
                ctx.op("scalar", "copy", base1[fb][:, :], p.pb[bA][:, 0:tc],
                       reads=(kA,), writes=(f"base1_{fb}",))
                for k, bZ, kZ in ((0, bB, kB), (1, bC, kC)):
                    ctx.op("vector", "tensor_add", v01[k][:, :],
                           p.pb[bZ][:, 0:tc], base1[fb][:, :],
                           reads=(kZ, f"base1_{fb}"), writes=(f"v01_{k}",))
                for k in range(2):
                    ctx.op("scalar", "activation", hkb[k][fb][:, :],
                           v01[k][:, :], AF.Gelu_apprx_tanh,
                           reads=(f"v01_{k}",), writes=(f"h{k}_{fb}",))
            if ft >= 1:
                g = ft - 1
                gb = g % 2
                for k, bU in ((0, 6), (1, 7)):
                    ctx.op("tensor", "matmul", p.pb[bU][0:64, 0:tc],
                           a2t[:, g * 64:(g + 1) * 64],
                           hkb[k][gb][:, :],
                           start=(g == 0), stop=(g == 63),
                           reads=("a2t", f"h{k}_{gb}"), writes=(f"pb{bU}",),
                           inc=(g == 63))
                for k in range(2):
                    ctx.op("vector", "tensor_mul", hmt[k][:, :],
                           hkb[k][gb][:, :], wbc[k][:, :],
                           reads=(f"h{k}_{gb}", f"wbc{k}"), writes=(f"hmt{k}",))
                ctx.op("vector", "tensor_add", hmix[:, g * tc:(g + 1) * tc],
                       hmt[0][:, :], hmt[1][:, :],
                       reads=("hmt0", "hmt1"), writes=("hmix",))
        for k, bU in ((0, 6), (1, 7)):
            op = "tensor_mul" if k == 0 else None
            if k == 0:
                ctx.op("vector", "tensor_mul", hmt[0][0:64, 0:tc],
                       p.pb[bU][0:64, 0:tc], mexw[k][:, :],
                       reads=(f"pb{bU}", f"mexw{k}"), writes=("hmt0",))
            else:
                ctx.op("vector", "tensor_mul", hmt[1][0:64, 0:tc],
                       p.pb[bU][0:64, 0:tc], mexw[k][:, :],
                       reads=(f"pb{bU}", f"mexw{k}"), writes=("hmt1",))
        ctx.op("vector", "tensor_add", mzmix[0:64, :], hmt[0][0:64, 0:tc],
               hmt[1][0:64, 0:tc], reads=("hmt0", "hmt1"), writes=("mzmix",))
        for dc in range(4):
            bg = (dc % 2) * 4
            for i in range(nt):
                ctx.op("tensor", "matmul", p.pb[bg + i][:, 0:512],
                       mzmix[:, i * 128:(i + 1) * 128],
                       b2aug[:, dc * 512:(dc + 1) * 512],
                       start=True, stop=False,
                       reads=("mzmix", "b2aug"), writes=(f"pb{bg + i}",),
                       inc=(i == nt - 1))
            def w2load(ftp):
                r0 = dc * FF + ftp * 256
                ctx.op(q2 if ftp % 2 == 0 else "sync", "dma_start",
                       w2b[ftp % NW2][:, :].rearrange("q (x c) -> q x c", x=2),
                       W2T[r0:r0 + 256, :].rearrange("(x q) c -> q x c", x=2),
                       reads=(k2,), writes=(f"w2b{ftp % NW2}",))
            for pf in range(3):
                w2load(pf)
            for ft in range(64):
                ftp, f2 = ft // 2, ft % 2
                cur, ck = w2b[ftp % NW2], f"w2b{ftp % NW2}"
                if f2 == 0 and ftp + 3 < 32:
                    w2load(ftp + 3)
                for i in range(nt):
                    ctx.op("tensor", "matmul", p.pb[bg + i][:, 0:512],
                           hmix[:, ft * tc + i * 128: ft * tc + (i + 1) * 128],
                           cur[:, f2 * 512:(f2 + 1) * 512],
                           start=False, stop=(ft == 63),
                           reads=("hmix", ck), writes=(f"pb{bg + i}",),
                           inc=(i == nt - 1))
            for i in range(nt):
                xi = i % 2
                ctx.op("vector", "tensor_add", xo[xi][:, :],
                       p.pb[bg + i][:, 0:512],
                       xb[i][:, dc * 512:(dc + 1) * 512],
                       reads=(f"pb{bg + i}", f"xb{i}"), writes=(f"xo{xi}",))
                ctx.op("sync", "dma_start",
                       xres_dst[tok0 + i * 128: tok0 + (i + 1) * 128,
                                dc * 512:(dc + 1) * 512],
                       xo[xi][:, :], reads=(f"xo{xi}",), writes=("xres_dst",))
    ctx.barrier(keep=BGKEEP)
    es.close()



def phase_conv_fast(p, items, tag="cv1"):
    nc, ctx = p.nc, p.ctx
    es = ExitStack()
    NB = 8
    fin = [es.enter_context(nc.sbuf_tensor(f"{tag}_i{i}", [128, 2048], F32))
           for i in range(NB)]
    fout = [es.enter_context(nc.sbuf_tensor(f"{tag}_o{i}", [128, 2048], BF16))
            for i in range(NB)]
    for src_name, shape, dap in items:
        rows, cols = shape
        srcap = p.dram_in(src_name, shape)
        assert cols <= 2048
        for i in range(rows // 128):
            bi = i % NB
            ctx.op("sync", "dma_start", fin[bi][:, 0:cols],
                   srcap[i * 128:(i + 1) * 128, :],
                   reads=(src_name,), writes=(f"{tag}i{bi}",))
            ctx.op("scalar", "copy", fout[bi][:, 0:cols],
                   fin[bi][:, 0:cols],
                   reads=(f"{tag}i{bi}",), writes=(f"{tag}o{bi}",))
            ctx.op("scalar", "dma_start", dap[i * 128:(i + 1) * 128, :],
                   fout[bi][:, 0:cols],
                   reads=(f"{tag}o{bi}",), writes=(f"cvw:{src_name}",))
    ctx.barrier(keep=BGKEEP)
    es.close()


def phase_conv_bg(p, items, bufs, tag="cv2"):
    nc, ctx = p.nc, p.ctx
    NB = len(bufs)
    bw = bufs[0].shape[1]
    n = 0
    for src_name, shape, dst in items:
        rows, cols = shape
        srcap = p.dram_in(src_name, shape)
        dap = dst.ap() if hasattr(dst, "ap") else dst
        for i in range(rows // 128):
            for c0 in range(0, cols, bw):
                c1 = min(cols, c0 + bw)
                bi = n % NB
                n += 1
                ctx.op("gpsimd", "dma_start", bufs[bi][:, 0:c1 - c0],
                       srcap[i * 128:(i + 1) * 128, c0:c1],
                       reads=(src_name,), writes=(f"cv2b{bi}",))
                ctx.op("gpsimd", "dma_start", dap[i * 128:(i + 1) * 128, c0:c1],
                       bufs[bi][:, 0:c1 - c0],
                       reads=(f"cv2b{bi}",), writes=(f"cvw:{src_name}",))



def phase_w2prep(p, w2p_dram, tag="w2p", bar=True):
    nc, ctx = p.nc, p.ctx
    es = ExitStack()
    sb = lambda n, s, d: es.enter_context(nc.sbuf_tensor(f"{tag}_{n}", s, d))
    W2G = p.dram_in("l1_W2G", [64, 128 * D])
    gin = [sb(f"gin{i}", [64, D], BF16) for i in range(2)]
    gout = [sb(f"gout{i}", [64, D], BF16) for i in range(2)]
    w2p_v = w2p_dram.ap().rearrange("(dc a b) d -> a b dc d", dc=4, a=64)
    for b in range(128):
        i = b % 2
        ctx.op("gpsimd", "dma_start", gin[i][:, :],
               W2G[:, b * D:(b + 1) * D],
               reads=("W2G",), writes=(f"{tag}gin{i}",))
        for dc in range(4):
            ctx.op("tensor", "matmul", p.pb[dc][0:64, 0:512], p.c_h64[:, :],
                   gin[i][:, dc * 512:(dc + 1) * 512],
                   start=True, stop=True,
                   reads=("c:H64", f"{tag}gin{i}"), writes=(f"pb{dc}",))
            ctx.op("vector", "tensor_copy", gout[i][:, dc * 512:(dc + 1) * 512],
                   p.pb[dc][0:64, 0:512],
                   reads=(f"pb{dc}",), writes=(f"{tag}gout{i}",))
        ctx.op("gpsimd", "dma_start", w2p_v[:, b, :, :],
               gout[i][:, :].rearrange("a (dc d) -> a dc d", dc=4),
               reads=(f"{tag}gout{i}",), writes=("w2p",))
    if bar:
        ctx.barrier(keep=BGKEEP)
        es.close()
        return None
    return es



def phase_l1ffn(p, xres_src, xres_dst, w2p_dram, tag):
    nc, ctx = p.nc, p.ctx
    tc = 512
    nt = tc // 128
    nch = T // tc
    es = ExitStack()
    sb = lambda n, s, d: es.enter_context(nc.sbuf_tensor(f"{tag}_{n}", s, d))

    W1T, q1, k1 = p.wsrc("l1_W1Tt", [64 * 128, 16 * 128])
    W3T, q3, k3 = p.wsrc("l1_W3Tt", [64 * 128, 16 * 128])
    w2p_ap = w2p_dram.ap()

    NW = 3
    NW2 = 4
    w1b = [sb(f"w1b{i}", [128, 2048], BF16) for i in range(NW)]
    w3b = [sb(f"w3b{i}", [128, 2048], BF16) for i in range(NW)]
    w2b = [sb(f"w2b{i}", [128, 1024], BF16) for i in range(NW2)]
    xb = [sb(f"xb{i}", [128, D], F32) for i in range(nt)]
    xn_tm = sb("xn_tm", [128, D], F32)
    xn_fm = sb("xn_fm", [128, 16 * tc], BF16)
    stat = sb("stat", [128, 32], F32)
    sg = sb("sg", [128, tc], F32)
    hp = [sb(f"hp{i}", [128, tc], BF16) for i in range(2)]
    hidp = sb("hidp", [128, 64 * tc], BF16)
    xo = [sb(f"xo{i}", [128, 512], F32) for i in range(4)]

    for ch in range(nch):
        tok0 = ch * tc
        for i in range(nt):
            ctx.op("sync", "dma_start", xb[i][:, :],
                   xres_src[tok0 + i * 128: tok0 + (i + 1) * 128, :],
                   reads=("xres_src",), writes=(f"xb{i}",))
        for i in range(nt):
            ln_tile(p, xb[i][:, :], f"xb{i}", xn_tm, "xn_tm", stat, "stat", tag)
            transpose_tm_to_fm(p, xn_tm, "xn_tm", xn_fm, "xn_fm", i, tc)
        for pf in range(2):
            ctx.op(q1, "dma_start", w1b[pf][:, :], W1T[pf * 128:(pf + 1) * 128, :],
                   reads=(k1,), writes=(f"w1b{pf}",))
            ctx.op("sync", "dma_start", w3b[pf][:, :], W3T[pf * 128:(pf + 1) * 128, :],
                   reads=(k3,), writes=(f"w3b{pf}",))
        for ft in range(64):
            fb = ft % NW
            cg, cl = w1b[fb], w3b[fb]
            kg, kl = f"w1b{fb}", f"w3b{fb}"
            if ft + 2 < 64:
                nfb = (ft + 2) % NW
                ctx.op(q1, "dma_start", w1b[nfb][:, :],
                       W1T[(ft + 2) * 128:(ft + 3) * 128, :],
                       reads=(k1,), writes=(f"w1b{nfb}",))
                ctx.op("sync", "dma_start", w3b[nfb][:, :],
                       W3T[(ft + 2) * 128:(ft + 3) * 128, :],
                       reads=(k3,), writes=(f"w3b{nfb}",))
            bG, bL, bH = (0, 1, 2) if ft % 2 == 0 else (3, 4, 5)
            for dt in range(16):
                ctx.op("tensor", "matmul", p.pb[bG][:, 0:tc],
                       cg[:, dt * 128:(dt + 1) * 128],
                       xn_fm[:, dt * tc:(dt + 1) * tc],
                       start=(dt == 0), stop=(dt == 15),
                       reads=(kg, "xn_fm"), writes=(f"pb{bG}",), inc=(dt == 15))
            for dt in range(16):
                ctx.op("tensor", "matmul", p.pb[bL][:, 0:tc],
                       cl[:, dt * 128:(dt + 1) * 128],
                       xn_fm[:, dt * tc:(dt + 1) * tc],
                       start=(dt == 0), stop=(dt == 15),
                       reads=(kl, "xn_fm"), writes=(f"pb{bL}",), inc=(dt == 15))
            ctx.op("scalar", "activation", sg[:, :], p.pb[bG][:, 0:tc], AF.Silu,
                   reads=(f"pb{bG}",), writes=("sg",))
            hb = ft % 2
            ctx.op("vector", "tensor_mul", hp[hb][:, :], sg[:, :],
                   p.pb[bL][:, 0:tc], reads=("sg", f"pb{bL}"), writes=(f"hp{hb}",))
            ctx.op("tensor", "matmul", p.pb[bH][:, 0:tc], p.c_h128[:, :],
                   hp[hb][:, :], start=True, stop=True,
                   reads=("c:H128", f"hp{hb}"), writes=(f"pb{bH}",))
            ctx.op("scalar", "activation", hidp[:, ft * tc:(ft + 1) * tc],
                   p.pb[bH][:, 0:tc], AF.Copy, scale=CVALS[7],
                   reads=(f"pb{bH}",), writes=("hidp",))
        for dc in range(4):
            bg = (dc % 2) * 4
            def w2load(ftp):
                r0 = dc * FF + ftp * 256
                ctx.op("scalar" if ftp % 2 == 0 else "sync", "dma_start",
                       w2b[ftp % NW2][:, :].rearrange("q (x c) -> q x c", x=2),
                       w2p_ap[r0:r0 + 256, :].rearrange("(x q) c -> q x c", x=2),
                       reads=("w2p",), writes=(f"w2b{ftp % NW2}",))
            for pf in range(3):
                w2load(pf)
            for ft in range(64):
                ftp, f2 = ft // 2, ft % 2
                cur, ck = w2b[ftp % NW2], f"w2b{ftp % NW2}"
                if f2 == 0 and ftp + 3 < 32:
                    w2load(ftp + 3)
                for i in range(nt):
                    ctx.op("tensor", "matmul", p.pb[bg + i][:, 0:512],
                           hidp[:, ft * tc + i * 128: ft * tc + (i + 1) * 128],
                           cur[:, f2 * 512:(f2 + 1) * 512],
                           start=(ft == 0), stop=(ft == 63),
                           reads=("hidp", ck), writes=(f"pb{bg + i}",),
                           inc=(i == nt - 1))
            for i in range(nt):
                xi = i % 2
                ctx.op("vector", "tensor_add", xo[xi][:, :],
                       p.pb[bg + i][:, 0:512],
                       xb[i][:, dc * 512:(dc + 1) * 512],
                       reads=(f"pb{bg + i}", f"xb{i}"), writes=(f"xo{xi}",))
                ctx.op("sync", "dma_start",
                       xres_dst[tok0 + i * 128: tok0 + (i + 1) * 128,
                                dc * 512:(dc + 1) * 512],
                       xo[xi][:, :], reads=(f"xo{xi}",), writes=("xres_dst",))
    ctx.barrier(keep=BGKEEP)
    es.close()



def phase_xnfull(p, src_ap, src_key, dst_ap, dst_key, ranks, tag):
    nc, ctx = p.nc, p.ctx
    tc = 512
    es = ExitStack()
    sb = lambda n, s, d: es.enter_context(nc.sbuf_tensor(f"{tag}_{n}", s, d))
    xb = [sb(f"xb{i}", [128, D], F32) for i in range(2)]
    xn_tm = sb("xn_tm", [128, D], F32)
    xn_fm = sb("xn_fm", [128, 16 * tc], BF16)
    stat = sb("stat", [128, 32], F32)
    for rb, st0 in ranks:
        for ch in range(1024 // tc):
            for i in range(tc // 128):
                ib = i % 2
                ctx.op("sync", "dma_start", xb[ib][:, :],
                       src_ap[st0 + ch * tc + i * 128: st0 + ch * tc + (i + 1) * 128, :],
                       reads=(src_key,), writes=(f"{tag}xb{ib}",))
                ln_tile(p, xb[ib][:, :], f"{tag}xb{ib}", xn_tm, f"{tag}xn_tm",
                        stat, f"{tag}stat", tag)
                transpose_tm_to_fm(p, xn_tm, f"{tag}xn_tm", xn_fm, f"{tag}xn_fm",
                                   i, tc)
            for dt in range(16):
                ctx.op("sync", "dma_start",
                       dst_ap[rb + dt * 128: rb + (dt + 1) * 128,
                              ch * tc:(ch + 1) * tc],
                       xn_fm[:, dt * tc:(dt + 1) * tc],
                       reads=(f"{tag}xn_fm",), writes=(dst_key,))
    ctx.barrier(keep=BGKEEP)
    es.close()



def phase_outproj(p, xres_src, xres_dst, attn_out, ao_key, ow_name, tag):
    nc, ctx = p.nc, p.ctx
    esE = ExitStack()
    sbE = lambda n, s, d: esE.enter_context(nc.sbuf_tensor(f"{tag}E_{n}", s, d))
    owb = [sbE(f"owb{i}", [128, D], BF16) for i in range(3)]
    xo = sbE("xo", [128, D], F32)
    xob = [sbE(f"xob{i}", [128, D], F32) for i in range(2)]
    OWT, qo, ko = p.wsrc(ow_name, [D, D])
    for tp in range(4):
        tts = (2 * tp, 2 * tp + 1)
        for j, tt in enumerate(tts):
            ctx.op("sync", "dma_start", xob[j][:, :],
                   xres_src[tt * 128:(tt + 1) * 128, :],
                   reads=("xres_src",), writes=(f"xob{j}",))
        for pf in range(2):
            ctx.op(qo, "dma_start", owb[pf][:, :], OWT[pf * 128:(pf + 1) * 128, :],
                   reads=(ko,), writes=(f"owb{pf}",))
        for ht in range(16):
            fb = ht % 3
            if ht + 2 < 16:
                ctx.op(qo, "dma_start", owb[(ht + 2) % 3][:, :],
                       OWT[(ht + 2) * 128:(ht + 3) * 128, :],
                       reads=(ko,), writes=(f"owb{(ht + 2) % 3}",))
            for j, tt in enumerate(tts):
                for dc in range(4):
                    b = 4 * j + dc
                    ctx.op("tensor", "matmul", p.pb[b][:, 0:512],
                           attn_out[:, ht * T + tt * 128: ht * T + (tt + 1) * 128],
                           owb[fb][:, dc * 512:(dc + 1) * 512],
                           start=(ht == 0), stop=(ht == 15),
                           reads=(ao_key, f"owb{fb}"), writes=(f"pb{b}",),
                           inc=(j == 1 and dc == 3))
        for j, tt in enumerate(tts):
            for dc in range(4):
                b = 4 * j + dc
                ctx.op("vector", "tensor_add", xo[:, dc * 512:(dc + 1) * 512],
                       p.pb[b][:, 0:512], xob[j][:, dc * 512:(dc + 1) * 512],
                       reads=(f"pb{b}", f"xob{j}"), writes=("xo",))
            ctx.op("sync", "dma_start", xres_dst[tt * 128:(tt + 1) * 128, :],
                   xo[:, :], reads=("xo",), writes=("xres_dst",))
    ctx.barrier(keep=BGKEEP)
    esE.close()


def phase_l0attn(p, xres_src, xres_dst, xnfull_ap, xnfull_key, xnown_ap, xnown_key, tag):
    nc, ctx = p.nc, p.ctx
    es = ExitStack()
    sb = lambda n, s, d: es.enter_context(nc.sbuf_tensor(f"{tag}_{n}", s, d))

    Q = sb("Q", [128, 16 * T], BF16)
    ckv = sb("ckv", [128, 4 * S], BF16)
    krot = sb("krot", [64, S], BF16)

    esB = ExitStack()
    sbB = lambda n, s, d: esB.enter_context(nc.sbuf_tensor(f"{tag}B_{n}", s, d))
    xnq = sbB("xnq", [128, 16 * T], BF16)
    qlat = sbB("qlat", [128, 12 * T], BF16)
    qab = [sbB(f"qab{i}", [128, 2048], BF16) for i in range(2)]
    qbb = [sbB(f"qbb{i}", [128, 1536], BF16) for i in range(2)]
    qs = sbB("qs", [64, T], F32)
    qsw = sbB("qsw", [64, T], F32)
    c0q = sbB("c0q", [64, T], F32)
    s0q = sbB("s0q", [64, T], F32)
    QAT, qqa, kqa = p.wsrc("l0_qaTt", [12 * 128, 16 * 128])
    QBT, qqb, kqb = p.wsrc("l0_qbTt", [16 * 128, 12 * 128])
    ctx.op("sync", "dma_start", c0q[:, :], p.dram_in("L0Q_C", [64, T]),
           writes=("c0q",))
    ctx.op("sync", "dma_start", s0q[:, :], p.dram_in("L0Q_S", [64, T]),
           writes=("s0q",))
    for dt in range(16):
        ctx.op("sync", "dma_start", xnq[:, dt * T:(dt + 1) * T],
               xnown_ap[dt * 128:(dt + 1) * 128, :],
               reads=(xnown_key,), writes=("xnq",))
    ctx.op(qqa, "dma_start", qab[0][:, :], QAT[0:128, :],
           reads=(kqa,), writes=("qab0",))
    for lt in range(12):
        fb = lt % 2
        if lt + 1 < 12:
            ctx.op(qqa, "dma_start", qab[(lt + 1) % 2][:, :],
                   QAT[(lt + 1) * 128:(lt + 2) * 128, :],
                   reads=(kqa,), writes=(f"qab{(lt + 1) % 2}",))
        for half in range(2):
            bb = 2 * half
            for dt in range(16):
                ctx.op("tensor", "matmul", p.pb[bb][:, 0:512],
                       qab[fb][:, dt * 128:(dt + 1) * 128],
                       xnq[:, dt * T + half * 512: dt * T + (half + 1) * 512],
                       start=(dt == 0), stop=(dt == 15),
                       reads=(f"qab{fb}", "xnq"), writes=(f"pb{bb}",),
                       inc=(dt == 15))
            ctx.op("scalar", "copy",
                   qlat[:, lt * T + half * 512: lt * T + (half + 1) * 512],
                   p.pb[bb][:, 0:512], reads=(f"pb{bb}",), writes=("qlat",))
    ctx.op(qqb, "dma_start", qbb[0][:, :], QBT[0:128, :],
           reads=(kqb,), writes=("qbb0",))
    for qt in range(16):
        fb = qt % 2
        if qt + 1 < 16:
            ctx.op(qqb, "dma_start", qbb[(qt + 1) % 2][:, :],
                   QBT[(qt + 1) * 128:(qt + 2) * 128, :],
                   reads=(kqb,), writes=(f"qbb{(qt + 1) % 2}",))
        for half in range(2):
            bb = 4 + 2 * half
            for lt in range(12):
                ctx.op("tensor", "matmul", p.pb[bb][:, 0:512],
                       qbb[fb][:, lt * 128:(lt + 1) * 128],
                       qlat[:, lt * T + half * 512: lt * T + (half + 1) * 512],
                       start=(lt == 0), stop=(lt == 11),
                       reads=(f"qbb{fb}", "qlat"), writes=(f"pb{bb}",),
                       inc=(lt == 11))
        for half in range(2):
            bb = 4 + 2 * half
            ctx.op("scalar", "copy",
                   Q[0:64, qt * T + half * 512: qt * T + (half + 1) * 512],
                   p.pb[bb][0:64, 0:512], reads=(f"pb{bb}",), writes=("Q",))
            ctx.op("vector", "tensor_copy", qs[:, half * 512:(half + 1) * 512],
                   p.pb[bb][64:128, 0:512], reads=(f"pb{bb}",), writes=("qs",))
        for b in range(1):
            ctx.op("vector", "tensor_copy", qsw[0:32, :], qs[32:64, :],
                   reads=("qs",), writes=("qsw",))
            ctx.op("vector", "tensor_copy", qsw[32:64, :], qs[0:32, :],
                   reads=("qs",), writes=("qsw",))
        ctx.op("vector", "tensor_mul", qs[:, :], qs[:, :], c0q[:, :],
               reads=("qs", "c0q"), writes=("qs",))
        ctx.op("vector", "tensor_mul", qsw[:, :], qsw[:, :], s0q[:, :],
               reads=("qsw", "s0q"), writes=("qsw",))
        ctx.op("vector", "tensor_add", Q[64:128, qt * T:(qt + 1) * T],
               qs[:, :], qsw[:, :], reads=("qs", "qsw"), writes=("Q",))
    ctx.barrier(keep=BGKEEP)
    esB.close()

    esC = ExitStack()
    sbC = lambda n, s, d: esC.enter_context(nc.sbuf_tensor(f"{tag}C_{n}", s, d))
    kvaa = sbC("kvaa", [128, 4 * 2048], BF16)
    kvab = sbC("kvab", [128, 16 * 64], BF16)
    xk = [sbC(f"xk{i}", [128, 16 * 512], BF16) for i in range(2)]
    ckc = [sbC(f"ckc{i}", [64, 512], F32) for i in range(2)]
    sks = [sbC(f"sks{i}", [64, 512], F32) for i in range(2)]
    ks = sbC("ks", [64, 512], F32)
    ksw = sbC("ksw", [64, 512], F32)
    stgC = [sbC(f"stgC{i}", [128, 2048], F32) for i in range(2)]
    hw_cast_load(p, kvaa, "kvaa", "l0_kvaTaf", [128, 4 * 2048],
                 stgC, ("stgC0", "stgC1"))
    hw_cast_load(p, kvab, "kvab", "l0_kvaTbf", [128, 16 * 64],
                 stgC, ("stgC0", "stgC1"))
    C0K = p.dram_in("L0K_C", [64, S])
    S0K = p.dram_in("L0K_S", [64, S])

    def load_xk(kc, buf):
        r = kc // 2
        for dt in range(16):
            ctx.op("sync", "dma_start", xk[buf][:, dt * 512:(dt + 1) * 512],
                   xnfull_ap[r * 2048 + dt * 128: r * 2048 + (dt + 1) * 128,
                             (kc % 2) * 512:(kc % 2 + 1) * 512],
                   reads=(xnfull_key,), writes=(f"xk{buf}",))

    load_xk(0, 0)
    ctx.op("sync", "dma_start", ckc[0][:, :], C0K[:, 0:512], writes=("ckc0",))
    ctx.op("sync", "dma_start", sks[0][:, :], S0K[:, 0:512], writes=("sks0",))
    for kc in range(8):
        fb = kc % 2
        if kc + 1 < 8:
            load_xk(kc + 1, (kc + 1) % 2)
            ctx.op("sync", "dma_start", ckc[(kc + 1) % 2][:, :],
                   C0K[:, (kc + 1) * 512:(kc + 2) * 512],
                   writes=(f"ckc{(kc + 1) % 2}",))
            ctx.op("sync", "dma_start", sks[(kc + 1) % 2][:, :],
                   S0K[:, (kc + 1) * 512:(kc + 2) * 512],
                   writes=(f"sks{(kc + 1) % 2}",))
        for ct in range(4):
            for dt in range(16):
                ctx.op("tensor", "matmul", p.pb[ct][:, 0:512],
                       kvaa[:, ct * 2048 + dt * 128: ct * 2048 + (dt + 1) * 128],
                       xk[fb][:, dt * 512:(dt + 1) * 512],
                       start=(dt == 0), stop=(dt == 15),
                       reads=("kvaa", f"xk{fb}"), writes=(f"pb{ct}",),
                       inc=(dt == 15))
        for dt in range(16):
            ctx.op("tensor", "matmul", p.pb[4][0:64, 0:512],
                   kvab[:, dt * 64:(dt + 1) * 64],
                   xk[fb][:, dt * 512:(dt + 1) * 512],
                   start=(dt == 0), stop=(dt == 15),
                   reads=("kvab", f"xk{fb}"), writes=("pb4",), inc=(dt == 15))
        for ct in range(4):
            ctx.op("scalar", "copy", ckv[:, ct * S + kc * 512:(ct * S + (kc + 1) * 512)],
                   p.pb[ct][:, 0:512], reads=(f"pb{ct}",), writes=("ckv",))
        ctx.op("vector", "tensor_copy", ks[:, :], p.pb[4][0:64, 0:512],
               reads=("pb4",), writes=("ks",))
        ctx.op("vector", "tensor_copy", ksw[0:32, :], ks[32:64, :],
               reads=("ks",), writes=("ksw",))
        ctx.op("vector", "tensor_copy", ksw[32:64, :], ks[0:32, :],
               reads=("ks",), writes=("ksw",))
        ctx.op("vector", "tensor_mul", ks[:, :], ks[:, :], ckc[fb][:, :],
               reads=("ks", f"ckc{fb}"), writes=("ks",))
        ctx.op("vector", "tensor_mul", ksw[:, :], ksw[:, :], sks[fb][:, :],
               reads=("ksw", f"sks{fb}"), writes=("ksw",))
        ctx.op("vector", "tensor_add", krot[:, kc * 512:(kc + 1) * 512],
               ks[:, :], ksw[:, :], reads=("ks", "ksw"), writes=("krot",))
    ctx.barrier(keep=BGKEEP)
    esC.close()

    esD = ExitStack()
    sbD = lambda n, s, d: esD.enter_context(nc.sbuf_tensor(f"{tag}D_{n}", s, d))
    wkb = sbD("wkb", [128, 16 * 256], BF16)
    wvb = sbD("wvb", [128, 4 * 2048], BF16)
    Vq = sbD("Vq", [128, 32 * 512], BF16)
    Kf = [sbD(f"Kf{i}", [128, S], BF16) for i in range(2)]
    Eb = [sbD(f"Eb{i}", [128, 1024], BF16) for i in range(2)]
    den = sbD("den", [1, 512], F32)
    bc = sbD("bc", [128, 512], F32)
    attn_out = sbD("attn_out", [128, 16 * T], BF16)
    stgD = [sbD(f"stgD{i}", [128, 2048], F32) for i in range(2)]
    hw_cast_load(p, wkb, "wkb", "l0_WkbTf", [128, 16 * 256],
                 stgD, ("stgD0", "stgD1"))
    hw_cast_load(p, wvb, "wvb", "l0_WvbTf", [128, 4 * 2048],
                 stgD, ("stgD0", "stgD1"))
    for g in range(4):
        for kt in range(32):
            ctx.op("tensor", "matmul", p.pb[6][:, 0:512],
                   ckv[:, 0 * S + kt * 128: 0 * S + (kt + 1) * 128],
                   wvb[:, 0 * 2048 + g * 512:(0 * 2048 + (g + 1) * 512)],
                   start=True, stop=False,
                   reads=("ckv", "wvb"), writes=("pb6",), inc=False)
            for ct in range(1, 4):
                ctx.op("tensor", "matmul", p.pb[6][:, 0:512],
                       ckv[:, ct * S + kt * 128: ct * S + (kt + 1) * 128],
                       wvb[:, ct * 2048 + g * 512: ct * 2048 + (g + 1) * 512],
                       start=False, stop=(ct == 3),
                       reads=("ckv", "wvb"), writes=("pb6",), inc=(ct == 3))
            ctx.op("scalar", "copy", Vq[:, kt * 512:(kt + 1) * 512],
                   p.pb[6][:, 0:512], reads=("pb6",), writes=("Vq",))
        for hh in range(4):
            h = g * 4 + hh
            hb = hh % 2
            for kc in range(8):
                for ct in range(4):
                    ctx.op("tensor", "matmul", p.pb[7][0:64, 0:512],
                           wkb[:, h * 256 + ct * 64: h * 256 + (ct + 1) * 64],
                           ckv[:, ct * S + kc * 512: ct * S + (kc + 1) * 512],
                           start=(ct == 0), stop=(ct == 3),
                           reads=("wkb", "ckv"), writes=("pb7",), inc=(ct == 3))
                ctx.op("scalar", "copy", Kf[hb][0:64, kc * 512:(kc + 1) * 512],
                       p.pb[7][0:64, 0:512], reads=("pb7",), writes=(f"Kf{hb}",))
            ctx.op("vector", "tensor_copy", Kf[hb][64:128, :], krot[:, :],
                   reads=("krot",), writes=(f"Kf{hb}",))
            for u in range(2):
                for pp in range(17):
                    if pp < 16:
                        b0 = 4 + (pp % 2) * 2
                        for j in range(2):
                            kt = 2 * pp + j
                            ctx.op("tensor", "matmul", p.pb[b0 + j][:, 0:512],
                                   Kf[hb][:, kt * 128:(kt + 1) * 128],
                                   Q[:, h * T + u * 512: h * T + (u + 1) * 512],
                                   start=True, stop=True,
                                   reads=(f"Kf{hb}", "Q"), writes=(f"pb{b0 + j}",))
                        ctx.op("scalar", "activation", Eb[pp % 2][:, :],
                               p.psa[:, b0 * 512:(b0 + 2) * 512], AF.Exp,
                               bias=-20.0,
                               reads=(f"pb{b0}", f"pb{b0 + 1}"),
                               writes=(f"Eb{pp % 2}",))
                    if pp > 0:
                        q_ = pp - 1
                        for j in range(2):
                            kt = 2 * q_ + j
                            ctx.op("tensor", "matmul", p.pb[2][:, 0:512],
                                   Vq[:, kt * 512 + hh * 128: kt * 512 + (hh + 1) * 128],
                                   Eb[q_ % 2][:, j * 512:(j + 1) * 512],
                                   start=(kt == 0), stop=(kt == 31),
                                   reads=("Vq", f"Eb{q_ % 2}"), writes=("pb2",),
                                   inc=(kt == 31))
                            ctx.op("tensor", "matmul", p.pb[3][0:1, 0:512],
                                   p.c_ones[:, :],
                                   Eb[q_ % 2][:, j * 512:(j + 1) * 512],
                                   start=(kt == 0), stop=(kt == 31),
                                   reads=("c:ONES128", f"Eb{q_ % 2}"), writes=("pb3",),
                                   inc=(kt == 31))
                ctx.op("vector", "reciprocal", den[:, :], p.pb[3][0:1, 0:512],
                       reads=("pb3",), writes=("den",))
                ctx.op("tensor", "matmul", p.pb[0][:, 0:512],
                       p.c_onesrowf[:, :], den[:, :], start=True, stop=True,
                       reads=("c:ONESROWF", "den"), writes=("pb0",))
                ctx.op("vector", "tensor_copy", bc[:, :], p.pb[0][:, 0:512],
                       reads=("pb0",), writes=("bc",))
                ctx.op("vector", "tensor_mul",
                       attn_out[:, h * T + u * 512: h * T + (u + 1) * 512],
                       p.pb[2][:, 0:512], bc[:, :],
                       reads=("pb2", "bc"), writes=("attn_out",))
    ctx.barrier(keep=BGKEEP)
    esD.close()

    phase_outproj(p, xres_src, xres_dst, attn_out, "attn_out", "l0_owT", tag)
    es.close()



def phase_l1attn(p, xres_src, xres_dst, xnfull_ap, xnfull_key, xnown_ap, xnown_key, tag):
    nc, ctx = p.nc, p.ctx
    es = ExitStack()
    sb = lambda n, s, d: es.enter_context(nc.sbuf_tensor(f"{tag}_{n}", s, d))
    Q = sb("Q", [128, 16 * T], BF16)
    Ksb = sb("Ksb", [128, 4 * S], BF16)
    Vsb = sb("Vsb", [128, 32 * 520], BF16)
    attn_out = sb("attn_out", [128, 16 * T], BF16)

    esB = ExitStack()
    sbB = lambda n, s, d: esB.enter_context(nc.sbuf_tensor(f"{tag}B_{n}", s, d))
    xnq = sbB("xnq", [128, 16 * T], BF16)
    wqb = [sbB(f"wqb{i}", [128, 2048], BF16) for i in range(2)]
    qs = sbB("qs", [128, T], F32)
    qsw = sbB("qsw", [128, T], F32)
    c1q = sbB("c1q", [128, T], F32)
    s1q = sbB("s1q", [128, T], F32)
    WQT, qwq, kwq = p.wsrc("l1_WqTt", [16 * 128, 16 * 128])
    ctx.op("sync", "dma_start", c1q[:, :], p.dram_in("L1Q_C", [128, T]),
           writes=("c1q",))
    ctx.op("sync", "dma_start", s1q[:, :], p.dram_in("L1Q_S", [128, T]),
           writes=("s1q",))
    for dt in range(16):
        ctx.op("sync", "dma_start", xnq[:, dt * T:(dt + 1) * T],
               xnown_ap[dt * 128:(dt + 1) * 128, :],
               reads=(xnown_key,), writes=("xnq",))
    ctx.op(qwq, "dma_start", wqb[0][:, :], WQT[0:128, :],
           reads=(kwq,), writes=("wqb0",))
    for qt in range(16):
        fb = qt % 2
        if qt + 1 < 16:
            ctx.op(qwq, "dma_start", wqb[(qt + 1) % 2][:, :],
                   WQT[(qt + 1) * 128:(qt + 2) * 128, :],
                   reads=(kwq,), writes=(f"wqb{(qt + 1) % 2}",))
        for half in range(2):
            bb = 2 * half
            for dt in range(16):
                ctx.op("tensor", "matmul", p.pb[bb][:, 0:512],
                       wqb[fb][:, dt * 128:(dt + 1) * 128],
                       xnq[:, dt * T + half * 512: dt * T + (half + 1) * 512],
                       start=(dt == 0), stop=(dt == 15),
                       reads=(f"wqb{fb}", "xnq"), writes=(f"pb{bb}",),
                       inc=(dt == 15))
            ctx.op("vector", "tensor_copy", qs[:, half * 512:(half + 1) * 512],
                   p.pb[bb][:, 0:512], reads=(f"pb{bb}",), writes=("qs",))
        for b in range(4):
            ctx.op("vector", "tensor_copy", qsw[b * 32:(b + 1) * 32, :],
                   qs[b * 32 + 32 - 64 * (b % 2):b * 32 + 64 - 64 * (b % 2), :],
                   reads=("qs",), writes=("qsw",))
        ctx.op("vector", "tensor_mul", qs[:, :], qs[:, :], c1q[:, :],
               reads=("qs", "c1q"), writes=("qs",))
        ctx.op("vector", "tensor_mul", qsw[:, :], qsw[:, :], s1q[:, :],
               reads=("qsw", "s1q"), writes=("qsw",))
        ctx.op("vector", "tensor_add", Q[:, qt * T:(qt + 1) * T],
               qs[:, :], qsw[:, :], reads=("qs", "qsw"), writes=("Q",))
    ctx.barrier(keep=BGKEEP)
    esB.close()

    esC = ExitStack()
    sbC = lambda n, s, d: esC.enter_context(nc.sbuf_tensor(f"{tag}C_{n}", s, d))
    wk = sbC("wk", [128, 4 * 2048], BF16)
    wv = sbC("wv", [128, 16 * 512], BF16)
    xk = [sbC(f"xk{i}", [128, 16 * 512], BF16) for i in range(2)]
    c1k = [sbC("c1k0", [128, 512], F32)]
    s1k = [sbC("s1k0", [128, 512], F32)]
    ks = sbC("ks", [128, 512], F32)
    ksw = sbC("ksw", [128, 512], F32)
    ctx.op("gpsimd", "dma_start", wk[:, :],
           p.dram_in("l1_WkTf", [128, 4 * 2048]), writes=("wk",))
    ctx.op("gpsimd", "dma_start", wv[:, :],
           p.dram_in("l1_WvTf", [128, 16 * 512]), writes=("wv",))
    C1K = p.dram_in("L1K_C", [128, S])
    S1K = p.dram_in("L1K_S", [128, S])
    ctx.op("vector", "memset",
           Vsb[:, :].rearrange("p (t g c) -> p t g c", g=8, c=65)[:, :, :, 64:65],
           1.0, writes=("Vsb",))

    def load_xk(kc, buf):
        r = kc // 2
        for dt in range(16):
            ctx.op("sync", "dma_start", xk[buf][:, dt * 512:(dt + 1) * 512],
                   xnfull_ap[r * 2048 + dt * 128: r * 2048 + (dt + 1) * 128,
                             (kc % 2) * 512:(kc % 2 + 1) * 512],
                   reads=(xnfull_key,), writes=(f"xk{buf}",))

    load_xk(0, 0)
    for kc in range(8):
        fb = kc % 2
        ctx.op("sync", "dma_start", c1k[0][:, :],
               C1K[:, kc * 512:(kc + 1) * 512], writes=("c1k0",))
        ctx.op("sync", "dma_start", s1k[0][:, :],
               S1K[:, kc * 512:(kc + 1) * 512], writes=("s1k0",))
        if kc + 1 < 8:
            load_xk(kc + 1, (kc + 1) % 2)
        for kb in range(4):
            for dt in range(16):
                ctx.op("tensor", "matmul", p.pb[kb][:, 0:512],
                       wk[:, kb * 2048 + dt * 128: kb * 2048 + (dt + 1) * 128],
                       xk[fb][:, dt * 512:(dt + 1) * 512],
                       start=(dt == 0), stop=(dt == 15),
                       reads=("wk", f"xk{fb}"), writes=(f"pb{kb}",),
                       inc=(dt == 15))
            ctx.op("vector", "tensor_copy", ks[:, :], p.pb[kb][:, 0:512],
                   reads=(f"pb{kb}",), writes=("ks",))
            for b in range(4):
                ctx.op("vector", "tensor_copy", ksw[b * 32:(b + 1) * 32, :],
                       ks[b * 32 + 32 - 64 * (b % 2):b * 32 + 64 - 64 * (b % 2), :],
                       reads=("ks",), writes=("ksw",))
            ctx.op("vector", "tensor_mul", ks[:, :], ks[:, :], c1k[0][:, :],
                   reads=("ks", "c1k0"), writes=("ks",))
            ctx.op("vector", "tensor_mul", ksw[:, :], ksw[:, :], s1k[0][:, :],
                   reads=("ksw", "s1k0"), writes=("ksw",))
            ctx.op("vector", "tensor_add",
                   Ksb[:, kb * S + kc * 512: kb * S + (kc + 1) * 512],
                   ks[:, :], ksw[:, :], reads=("ks", "ksw"), writes=("Ksb",))
        for k4 in range(4):
            kt = kc * 4 + k4
            for dt in range(16):
                ctx.op("tensor", "matmul", p.pb[4][:, 0:512],
                       xk[fb][:, dt * 512 + k4 * 128: dt * 512 + (k4 + 1) * 128],
                       wv[:, dt * 512:(dt + 1) * 512],
                       start=(dt == 0), stop=(dt == 15),
                       reads=(f"xk{fb}", "wv"), writes=("pb4",), inc=(dt == 15))
            ctx.op("vector", "tensor_copy",
                   Vsb[:, kt * 520:(kt + 1) * 520]
                   .rearrange("p (g c) -> p g c", c=65)[:, :, 0:64],
                   p.pb[4][:, 0:512].rearrange("p (g c) -> p g c", c=64),
                   reads=("pb4",), writes=("Vsb",))
    ctx.barrier(keep=BGKEEP)
    esC.close()

    esD = ExitStack()
    sbD = lambda n, s, d: esD.enter_context(nc.sbuf_tensor(f"{tag}D_{n}", s, d))
    Eb = [sbD(f"Eb{i}", [128, 1024], BF16) for i in range(2)]
    den = sbD("den", [1, 512], F32)
    bc = sbD("bc", [64, 512], F32)
    for kvh in range(8):
        for u in range(2):
            for hp in range(2):
                hA, hB = kvh * 4 + 2 * hp, kvh * 4 + 2 * hp + 1
                pos = [l1_qpos(hA), l1_qpos(hB)]
                for qt_, qb_ in pos:
                    assert qb_ == (kvh % 2) * 64
                for pp in range(33):
                    if pp < 32:
                        kt = pp
                        b0 = 4 + (pp % 2) * 2
                        for j, (qt_, qb_) in enumerate(pos):
                            ctx.op("tensor", "matmul", p.pb[b0 + j][:, 0:512],
                                   Ksb[(kvh % 2) * 64:(kvh % 2) * 64 + 64,
                                       (kvh // 2) * S + kt * 128:(kvh // 2) * S + (kt + 1) * 128],
                                   Q[qb_:qb_ + 64,
                                     qt_ * T + u * 512: qt_ * T + (u + 1) * 512],
                                   start=True, stop=True,
                                   reads=("Ksb", "Q"), writes=(f"pb{b0 + j}",))
                        ctx.op("scalar", "activation", Eb[pp % 2][:, :],
                               p.psa[:, b0 * 512:(b0 + 2) * 512], AF.Exp,
                               bias=-20.0,
                               reads=(f"pb{b0}", f"pb{b0 + 1}"),
                               writes=(f"Eb{pp % 2}",))
                    if pp > 0:
                        kt = pp - 1
                        q_ = pp - 1
                        for j in range(2):
                            ctx.op("tensor", "matmul", p.pb[2 + j][0:65, 0:512],
                                   Vsb[:, kt * 520 + kvh * 65: kt * 520 + (kvh + 1) * 65],
                                   Eb[q_ % 2][:, j * 512:(j + 1) * 512],
                                   start=(kt == 0), stop=(kt == 31),
                                   reads=("Vsb", f"Eb{q_ % 2}"), writes=(f"pb{2 + j}",),
                                   inc=(kt == 31))
                for j, (qt_, qb_) in enumerate(pos):
                    ctx.op("vector", "reciprocal", den[:, :],
                           p.pb[2 + j][64:65, 0:512],
                           reads=(f"pb{2 + j}",), writes=("den",))
                    ctx.op("tensor", "matmul", p.pb[0][0:64, 0:512],
                           p.c_onesrowf[:, 0:64], den[:, :], start=True, stop=True,
                           reads=("c:ONESROWF", "den"), writes=("pb0",))
                    ctx.op("vector", "tensor_copy", bc[:, :], p.pb[0][0:64, 0:512],
                           reads=("pb0",), writes=("bc",))
                    ctx.op("vector", "tensor_mul",
                           attn_out[qb_:qb_ + 64,
                                    qt_ * T + u * 512: qt_ * T + (u + 1) * 512],
                           p.pb[2 + j][0:64, 0:512], bc[:, :],
                           reads=(f"pb{2 + j}", "bc"), writes=("attn_out",))
    ctx.barrier(keep=BGKEEP)
    esD.close()

    phase_outproj(p, xres_src, xres_dst, attn_out, "attn_out", "l1_WoT", tag)
    es.close()




def build_fused():
    nc = bass.Bass("TRN2", target_bir_lowering=False)
    p = Prog(nc)
    xin = p.dram_in("x_in", [T, D], F32)
    xout = p.dram_out("x_out", [T, D], F32)
    p.setup_common()
    xa = p.dram_int("xa", [T, D], F32).ap()
    xb = p.dram_int("xb", [T, D], F32).ap()
    w2p = p.dram_int("w2p", [4 * FF, 512], BF16)
    cc_in = p.dram_int("cc_in", [2048, T], BF16)
    cc_out = p.dram_int("cc_out", [8192, T], BF16)
    ctx = p.ctx

    p.phase_marks = []

    def mark(nm):
        p.phase_marks.append((nm, p.nc.n_instructions()))

    def exchange(src_ap, src_key, tag):
        phase_xnfull(p, src_ap, src_key, cc_in.ap(), "cc_in", [(0, 0)], tag)
        ctx.op("gpsimd", "collective_compute", "AllGather", ALU.bypass,
               replica_groups=[[0, 1, 2, 3], [4, 5, 6, 7]],
               ins=[cc_in.ap().opt()], outs=[cc_out.ap().opt()],
               reads=("cc_in",), writes=("cc_out",))
        ctx.barrier(keep=BGKEEP)

    cvt = {"l0_W1Tt": [64 * 128, 16 * 128], "l0_W2Tdc": [4 * FF, 512],
           "l1_W1Tt": [64 * 128, 16 * 128], "l1_W3Tt": [64 * 128, 16 * 128],
           "l0_qaTt": [12 * 128, 16 * 128], "l0_qbTt": [16 * 128, 12 * 128],
           "l0_owT": [D, D], "l1_WoT": [D, D], "l1_WqTt": [16 * 128, 16 * 128]}
    p.wc = {}
    for nm, shp in cvt.items():
        p.wc[nm] = p.dram_int(f"wc_{nm}", shp, BF16)
    it = lambda nm: (nm, cvt[nm], p.wc[nm])
    itf = lambda nm: (nm, cvt[nm], p.wc[nm].ap())
    cvbufs = [ctx.sb(f"cv2buf{i}", [128, 1024], BF16) for i in range(2)]
    mark("start")
    w2es = phase_w2prep(p, w2p, bar=False)
    phase_conv_fast(p, [itf("l0_W1Tt"),
                        ("l0_W2Tdc", [FF, 2048],
                         p.wc["l0_W2Tdc"].ap().rearrange("(a x) c -> a (x c)", x=4))])
    w2es.close()
    mark("w2prep")
    phase_conv_bg(p, [it("l0_qaTt"), it("l0_qbTt"), it("l0_owT")], cvbufs)
    phase_l0ffn(p, xin, xa, "f0a")
    mark("l0ffn_a")
    exchange(xa, "xa", "xf0")
    mark("exch0")
    phase_conv_bg(p, [it("l1_WqTt"), it("l1_WoT"),
                      it("l1_W1Tt"), it("l1_W3Tt")], cvbufs)
    phase_l0attn(p, xa, xb, cc_out.ap(), "cc_out", cc_in.ap(), "cc_in", "a0")
    mark("l0attn")
    phase_l0ffn(p, xb, xa, "f0b")
    mark("l0ffn_b")
    phase_l1ffn(p, xa, xb, w2p, "f1a")
    mark("l1ffn_a")
    exchange(xb, "xb", "xf1")
    mark("exch1")
    phase_l1attn(p, xb, xa, cc_out.ap(), "cc_out", cc_in.ap(), "cc_in", "a1")
    mark("l1attn")
    phase_l1ffn(p, xa, xout, w2p, "f1b")
    mark("l1ffn_b")
    ctx.barrier()
    return p





def kernel(**inputs):
    inputs = {k: np.asarray(v) for k, v in inputs.items()}
    W = prep_weights(inputs)
    x = np.ascontiguousarray(inputs["L0_attn_x"]).astype(np.float32)
    p = build_fused()
    in_maps = []
    for c in range(8):
        b, r = c // 4, c % 4
        consts = host_consts(c)
        m = {}
        for name in p.inp:
            if name in consts:
                m[name] = consts[name]
            elif name in W:
                m[name] = W[name]
            elif name == "x_in":
                m[name] = np.ascontiguousarray(x[b, r * 1024:(r + 1) * 1024, :])
            else:
                raise KeyError(name)
        in_maps.append(m)
    res = run_bass_kernel_spmd(p.nc, in_maps, core_ids=list(range(8)))
    out = np.empty_like(x)
    for c in range(8):
        b, r = c // 4, c % 4
        out[b, r * 1024:(r + 1) * 1024, :] = res.results[c]["x_out"]
    return out
